# version 02
# baseline (speedup 1.0000x reference)
import numpy as np
import ml_dtypes
from contextlib import ExitStack

import concourse.bass as bass
import concourse.mybir as mybir
from concourse.bass_utils import run_bass_kernel_spmd

F32 = mybir.dt.float32
BF16 = mybir.dt.bfloat16
U8 = mybir.dt.uint8
U32 = mybir.dt.uint32
AF = mybir.ActivationFunctionType
ALU = mybir.AluOpType
AX = mybir.AxisListType

NCORES = 8
B, T, C = 2, 8192, 1024
TCORE = 2048
HALO = 256
NTT = 16
NTQ = 18
NH, HD = 16, 64
W3 = 384
FF = 4096
NE = 8
NKC = 8
NEG = -1.0e30
EPS1 = 1e-5
EPS2 = 1e-12

ENGINES = ("sync", "gpsimd", "scalar", "vector", "tensor")


class Sched:

    def __init__(self):
        self.blocks = []
        self.cur = None

    def new_block(self):
        self.cur = []
        self.blocks.append(self.cur)

    def op(self, eng, fn, waits=()):
        rec = {"eng": eng, "fn": fn, "waits": [w for w in waits if w is not None],
               "inc": None}
        self.cur.append(rec)
        return rec

    def finalize(self):
        for blk in self.blocks:
            for rec in blk:
                for wtg in rec["waits"]:
                    if isinstance(wtg, dict) and wtg["inc"] is None:
                        wtg["inc"] = -1
        counters = {e: 0 for e in ENGINES}
        for blk in self.blocks:
            for rec in blk:
                if rec["inc"] == -1:
                    counters[rec["eng"]] += 1
                    rec["inc"] = counters[rec["eng"]]

    def emit_block(self, blk_ops, eng_name, engine, eng_sems):
        highwater = {}
        for rec in blk_ops:
            if rec["eng"] != eng_name:
                continue
            for wtg in rec["waits"]:
                if isinstance(wtg, dict):
                    sem, val, key = eng_sems[wtg["eng"]], wtg["inc"], ("e", wtg["eng"])
                else:
                    sem, val = wtg
                    key = ("s", id(sem))
                if highwater.get(key, -1) >= val:
                    continue
                highwater[key] = val
                engine.wait_ge(sem, val)
            inst = rec["fn"](engine)
            if rec["inc"] is not None:
                assert inst is not None, f"wait-target op returned no inst on {eng_name}"
                inst.then_inc(eng_sems[rec["eng"]], 1)


def build_program():
    nc = bass.Bass("TRN2", target_bir_lowering=False)

    d_x = nc.dram_tensor("x_ext", [NTQ * 128, C], F32, kind="ExternalInput").ap()
    d_wa = nc.dram_tensor("w_attn", [C, 3 * C], F32, kind="ExternalInput").ap()
    d_ba = nc.dram_tensor("b_attn", [3 * C], F32, kind="ExternalInput").ap()
    d_wp = nc.dram_tensor("w_proj", [C, C], F32, kind="ExternalInput").ap()
    d_bp = nc.dram_tensor("b_proj", [C], F32, kind="ExternalInput").ap()
    d_wrt = nc.dram_tensor("wr_t", [C, NE], F32, kind="ExternalInput").ap()
    d_w1 = nc.dram_tensor("w1", [NE, C, FF], F32, kind="ExternalInput").ap()
    d_b1 = nc.dram_tensor("b1", [NE, FF], F32, kind="ExternalInput").ap()
    d_wo = nc.dram_tensor("wo", [FF, C], F32, kind="ExternalInput").ap()
    d_bo = nc.dram_tensor("bo", [C], F32, kind="ExternalInput").ap()
    d_gam = nc.dram_tensor("gamma", [C], F32, kind="ExternalInput").ap()
    d_bet = nc.dram_tensor("beta", [C], F32, kind="ExternalInput").ap()
    d_msk = nc.dram_tensor("masks", [3, 128, W3], F32, kind="ExternalInput").ap()
    d_out = nc.dram_tensor("out", [TCORE, C], F32, kind="ExternalOutput").ap()
    d_x1 = nc.dram_tensor("x1_scratch", [TCORE, C], F32, kind="Internal").ap()
    d_idx = nc.dram_tensor("idx_scratch", [TCORE], F32, kind="Internal").ap()

    ident_bf = nc.inline_tensor(np.eye(128, dtype=ml_dtypes.bfloat16), "ident_bf").ap()
    ident_f32 = nc.inline_tensor(np.eye(128, dtype=np.float32), "ident_f32").ap()

    st = ExitStack()
    sb = lambda name, shape, dt: st.enter_context(nc.sbuf_tensor(name, shape, dt))
    psal = lambda name, shape, dt: st.enter_context(nc.psum_tensor(name, shape, dt))
    sem = lambda name: st.enter_context(nc.semaphore(name))

    stream = sb("stream", [128, NTT * C], F32)
    acc_full = stream[:].bitcast(BF16)
    t_identb = sb("identb", [128, 128], BF16)
    t_identf = sb("identf", [128, 128], F32)
    t_bqk = sb("bqk", [128, 16], F32)
    t_bqs = sb("bqs", [128, 8], F32)
    t_wrt = sb("wrt", [128, NKC * NE], F32)
    t_b1c = sb("b1cols", [128, NE * 32], F32)
    t_bo = sb("bo_bc", [128, C], F32)
    t_gam = sb("gam_bc", [128, C], F32)
    t_bet = sb("bet_bc", [128, C], F32)

    eng_sems = {e: sem("e_" + e) for e in ENGINES}
    q_setup = sem("q_setup")
    q_wsetup = sem("q_wsetup")
    q_x = [sem(f"q_x{i}") for i in range(6)]
    q_spill = sem("q_spill")
    q_idx = sem("q_idx")
    q_idxbc = sem("q_idxbc")
    q_xrel = [sem(f"q_xrel{i}") for i in range(4)]
    q_xrel2 = [sem(f"q_xrel2{i}") for i in range(4)]
    q_slot = [sem(f"q_slot{i}") for i in range(32)]
    q_out2 = [sem("q_out0"), sem("q_out1")]

    sched = Sched()

    a_st = ExitStack()
    asb = lambda name, shape, dt: a_st.enter_context(nc.sbuf_tensor(name, shape, dt))

    t_wab = asb("wab", [128, NKC * 3 * C], BF16)
    t_wpb = asb("wpb", [128, NKC * C], BF16)
    t_kring = asb("kring", [128, 3 * C], BF16)
    t_vring = asb("vring", [128, 3 * C], BF16)
    t_masks = asb("masks_sb", [128, 3 * W3], F32)
    t_bvbc = asb("bv_bc", [128, C], F32)
    t_bpbc = asb("bp_bc", [128, C], F32)
    t_xhalo = asb("xhalo", [128, 2 * C], F32)
    t_xc = asb("xc", [128, 2 * C], BF16)
    t_xTb = asb("xTb", [128, 2 * C], BF16)
    t_q = asb("qtile", [128, C], BF16)
    t_scf = asb("scf", [128, 2 * W3], F32)
    t_prob = asb("prob", [128, 2 * W3], BF16)
    t_probT = asb("probT", [128, 2 * W3], BF16)
    t_ao = asb("attn_out", [128, C], BF16)
    t_aoT = asb("attn_outT", [128, C], BF16)
    t_nmx = asb("negmax", [128, NH], F32)
    t_den = asb("den", [128, NH], F32)
    t_rden = asb("rden", [128, NH], F32)
    t_w = asb("wbuf", [128, C], F32)
    t_cent = asb("centbuf", [128, C], F32)
    t_xT32 = asb("xT32", [128, C], F32)
    t_lg = asb("lgs", [128, 8], F32)
    t_mx8 = asb("mx8", [128, 8], F32)
    t_idxu = asb("idxu", [128, 8], U32)
    t_idxf = asb("idxf", [128, 1], F32)
    t_stat = asb("stat", [128, 8], F32)

    apsal = lambda name, shape, dt: a_st.enter_context(nc.psum_tensor(name, shape, dt))
    p_tra = apsal("p_tra", [128, 1024], BF16)
    p_trb = apsal("p_trb", [128, 1024], BF16)
    p_kq = [apsal("p_kq0", [128, 128], F32), apsal("p_kq1", [128, 128], F32)]
    p_v = apsal("p_v", [128, 512], F32)
    p_sc = [apsal("p_sc0", [128, W3], F32), apsal("p_sc1", [128, W3], F32)]
    p_misc = apsal("p_misc", [128, 512], F32)

    sched.new_block()
    S = sched.op

    n_setup = [0]

    def setup_dma(engname, fn):
        S(engname, lambda e, fn=fn: fn(e).then_inc(q_setup, 16))
        n_setup[0] += 16

    setup_dma("sync", lambda e: e.dma_start(t_identb[:], ident_bf))
    setup_dma("sync", lambda e: e.dma_start(t_identf[:], ident_f32))
    for v in range(3):
        setup_dma("sync", lambda e, v=v: e.dma_start(
            t_masks[:, v * W3:(v + 1) * W3], d_msk[v]))
    ba_col = d_ba.rearrange("(j p) -> j p", p=128)
    for j in range(16):
        setup_dma("sync", lambda e, j=j: e.dma_start(
            t_bqk[:, j:j + 1], ba_col[j].rearrange("(p o) -> p o", o=1)))

    def bcast_src(dap, off, n):
        return bass.AP(dap.tensor, off, [[0, 128], [1, n]])

    setup_dma("sync", lambda e: e.dma_start(t_bvbc[:], bcast_src(d_ba, 2048, C)))
    setup_dma("sync", lambda e: e.dma_start(t_bpbc[:], bcast_src(d_bp, 0, C)))
    setup_dma("sync", lambda e: e.dma_start(t_bo[:], bcast_src(d_bo, 0, C)))
    setup_dma("sync", lambda e: e.dma_start(t_gam[:], bcast_src(d_gam, 0, C)))
    setup_dma("sync", lambda e: e.dma_start(t_bet[:], bcast_src(d_bet, 0, C)))
    for kc in range(NKC):
        setup_dma("sync", lambda e, kc=kc: e.dma_start(
            t_wrt[:, kc * NE:(kc + 1) * NE], d_wrt[kc * 128:(kc + 1) * 128, :]))
    b1_col = d_b1.rearrange("e (j p) -> e j p", p=128)
    for eidx in range(NE):
        for j in range(32):
            setup_dma("scalar", lambda e, eidx=eidx, j=j: e.dma_start(
                t_b1c[:, eidx * 32 + j:eidx * 32 + j + 1],
                b1_col[eidx, j].rearrange("(p o) -> p o", o=1)))
    SETUP_TOTAL = n_setup[0]

    n_wsetup = [0]
    for kc in range(NKC):
        S("gpsimd", lambda e, kc=kc: e.dma_start(
            t_wab[:, kc * 3 * C:(kc + 1) * 3 * C],
            d_wa[kc * 128:(kc + 1) * 128, :]).then_inc(q_wsetup, 16))
        n_wsetup[0] += 16
    for kc in range(NKC):
        S("gpsimd", lambda e, kc=kc: e.dma_start(
            t_wpb[:, kc * C:(kc + 1) * C],
            d_wp[kc * 128:(kc + 1) * 128, :]).then_inc(q_wsetup, 16))
        n_wsetup[0] += 16
    w_setup = (q_setup, SETUP_TOTAL)
    w_wsetup = (q_wsetup, n_wsetup[0])

    tok_bqs = S("vector", lambda e: e.tensor_scalar(
        t_bqs[:], t_bqk[:, 0:8], 0.125, None, op0=ALU.mult), waits=[w_setup])

    trb_ev = [None, None]

    tok_cast = {}
    tok_trE = {}
    tok_kev = {}
    tok_vev = {}
    tok_qev = {}
    tok_kq_mm = {}
    tok_v_mm = {}
    tok_sc = {}
    tok_maskTT = {}
    tok_exp = {}
    tok_recip = {}
    tok_probT_pe = {}
    tok_probT_ev = {}
    tok_av = {}
    tok_norm = {}
    tok_aoT_pe = {}
    tok_aoT_ev = {}
    tok_y1_mm = {}
    tok_proj_ev = {}
    tok_logit_ev = {}
    tok_xnew = {}
    tok_trP = {}
    tok_heads_last_pe = {}

    nx_load = [0] * 6

    for tq in range(NTQ):
        tt = tq - 2
        kslot = tq % 3

        qi = tq % 6
        nx_load[qi] += 16
        xw = (q_x[qi], nx_load[qi])
        dst = stream[:, tt * C:(tt + 1) * C] if tq >= 2 else t_xhalo[:, tq * C:(tq + 1) * C]
        lw = [tok_cast[tq - 6]] if tq - 6 >= 0 else []
        S("sync", lambda e, dst=dst, tq=tq, qi=qi: e.dma_start(
            dst, d_x[tq * 128:(tq + 1) * 128, :]).then_inc(q_x[qi], 16), waits=lw)

        cbuf = t_xc[:, (tq % 2) * C:(tq % 2 + 1) * C]
        cw = [xw]
        if tq - 2 >= 0:
            cw.append(tok_trP[tq - 2])
        tok_cast[tq] = S("scalar", lambda e, cbuf=cbuf, dst=dst: e.copy(cbuf, dst), waits=cw)

        def mk_tr(eng, cbuf=cbuf):
            last = None
            for j in range(NKC):
                last = eng.transpose(p_tra[:, j * 128:(j + 1) * 128],
                                     cbuf[:, j * 128:(j + 1) * 128], t_identb[:])
            return last
        tok_trP[tq] = S("tensor", mk_tr, waits=[tok_cast[tq], w_setup, trb_ev[0]])
        xTbuf = t_xTb[:, (tq % 2) * C:(tq % 2 + 1) * C]
        evw = [tok_trP[tq]]
        if tq - 2 >= 2:
            evw.append(tok_kq_mm[(tq - 2, "q", NKC - 1)])
        elif tq - 2 >= 0:
            evw.append(tok_v_mm[(tq - 2, 1)])
        tok_trE[tq] = S("vector", lambda e, xTbuf=xTbuf: e.tensor_copy(xTbuf, p_tra[:, 0:C]),
                        waits=evw)
        trb_ev[0] = tok_trE[tq]

        for fc in range(NKC):
            bank = p_kq[fc % 2]
            def mk_k(eng, fc=fc, bank=bank, xTbuf=xTbuf):
                last = None
                for kc in range(NKC):
                    last = eng.matmul(
                        bank[:],
                        t_wab[:, kc * 3 * C + C + fc * 128: kc * 3 * C + C + (fc + 1) * 128],
                        xTbuf[:, kc * 128:(kc + 1) * 128],
                        start=(kc == 0), stop=(kc == NKC - 1))
                return last
            kw = [tok_trE[tq], w_wsetup]
            if fc >= 2:
                kw.append(tok_kev[(tq, fc - 2)])
            elif tq >= 3:
                kw.append(tok_qev[(tq - 1, 6 + fc)])
            elif tq >= 1:
                kw.append(tok_kev[(tq - 1, 6 + fc)])
            tok_kq_mm[(tq, "k", fc)] = S("tensor", mk_k, waits=kw)
            kdst = t_kring[:, kslot * C + fc * 128: kslot * C + (fc + 1) * 128]
            ew = [tok_kq_mm[(tq, "k", fc)], w_setup]
            if tq - 1 >= 2:
                ew.append(tok_heads_last_pe[tq - 1])
            tok_kev[(tq, fc)] = S("scalar", lambda e, kdst=kdst, bank=bank, fc=fc:
                                  e.activation(kdst, bank[:], AF.Identity,
                                               bias=t_bqk[:, 8 + fc:9 + fc]), waits=ew)

        for vh in range(2):
            def mk_v(eng, vh=vh, xTbuf=xTbuf):
                last = None
                for kc in range(NKC):
                    last = eng.matmul(
                        p_v[:],
                        xTbuf[:, kc * 128:(kc + 1) * 128],
                        t_wab[:, kc * 3 * C + 2 * C + vh * 512: kc * 3 * C + 2 * C + (vh + 1) * 512],
                        start=(kc == 0), stop=(kc == NKC - 1))
                return last
            vw = [tok_trE[tq], w_wsetup]
            if vh == 1:
                vw.append(tok_vev[(tq, 0)])
            elif (tq - 1, 1) in tok_vev:
                vw.append(tok_vev[(tq - 1, 1)])
            tok_v_mm[(tq, vh)] = S("tensor", mk_v, waits=vw)
            vdst = t_vring[:, kslot * C + vh * 512: kslot * C + (vh + 1) * 512]
            ew = [tok_v_mm[(tq, vh)], w_setup]
            if tq - 1 >= 2:
                ew.append(tok_heads_last_pe[tq - 1])
            tok_vev[(tq, vh)] = S("vector", lambda e, vdst=vdst, vh=vh:
                                  e.scalar_tensor_tensor(
                                      vdst, p_v[:], 1.0, t_bvbc[:, vh * 512:(vh + 1) * 512],
                                      op0=ALU.mult, op1=ALU.add), waits=ew)

        if tt < 0:
            continue

        for fc in range(NKC):
            bank = p_kq[fc % 2]
            def mk_q(eng, fc=fc, bank=bank, xTbuf=xTbuf):
                last = None
                for kc in range(NKC):
                    last = eng.matmul(
                        bank[:],
                        t_wab[:, kc * 3 * C + fc * 128: kc * 3 * C + (fc + 1) * 128],
                        xTbuf[:, kc * 128:(kc + 1) * 128],
                        start=(kc == 0), stop=(kc == NKC - 1))
                return last
            qw = [tok_trE[tq], tok_kev[(tq, 6 + (fc % 2))], tok_bqs]
            if fc >= 2:
                qw.append(tok_qev[(tq, fc - 2)])
            tok_kq_mm[(tq, "q", fc)] = S("tensor", mk_q, waits=qw)
            qdst = t_q[:, fc * 128:(fc + 1) * 128]
            ew = [tok_kq_mm[(tq, "q", fc)]]
            if tt >= 1:
                ew.append(tok_sc[(tt - 1, NH - 1)])
            tok_qev[(tq, fc)] = S("scalar", lambda e, qdst=qdst, bank=bank, fc=fc:
                                  e.activation(qdst, bank[:], AF.Identity, scale=0.125,
                                               bias=t_bqs[:, fc:fc + 1]), waits=ew)

        mvar = tt if tt < 2 else 2
        for h in range(NH + 1):
            if h < NH:
                fc, r0, hb = h // 2, (h % 2) * 64, h % 2

                def mk_sc(eng, fc=fc, r0=r0, hb=hb, tq=tq):
                    last = None
                    for kr in range(3):
                        ks = (tq - 2 + kr) % 3
                        last = eng.matmul(
                            p_sc[hb][:, kr * 128:(kr + 1) * 128],
                            t_q[r0:r0 + 64, fc * 128:(fc + 1) * 128],
                            t_kring[r0:r0 + 64, ks * C + fc * 128: ks * C + (fc + 1) * 128],
                            start=True, stop=True)
                    return last
                sw = [tok_qev[(tq, fc)], tok_kev[(tq, NKC - 1)]]
                if h >= 2:
                    sw.append(tok_maskTT[(tt, h - 2)])
                elif tt >= 1:
                    sw.append(tok_maskTT[(tt - 1, 14 + hb)])
                tok_sc[(tt, h)] = S("tensor", mk_sc, waits=sw)

                scbuf = t_scf[:, hb * W3:(hb + 1) * W3]
                mw = [tok_sc[(tt, h)], w_setup]
                if h >= 2:
                    mw.append(tok_exp[(tt, h - 2)])
                elif tt >= 1:
                    mw.append(tok_exp[(tt - 1, 14 + hb)])
                tok_maskTT[(tt, h)] = S("vector", lambda e, scbuf=scbuf, hb=hb, mvar=mvar:
                                        e.tensor_tensor(scbuf, p_sc[hb][:],
                                                        t_masks[:, mvar * W3:(mvar + 1) * W3],
                                                        op=ALU.add), waits=mw)
                tok_rmax = S("vector", lambda e, scbuf=scbuf, h=h:
                             e.reduce_max(t_nmx[:, h:h + 1], scbuf, axis=AX.X, negate=True),
                             waits=[tok_maskTT[(tt, h)]])

                pbuf = t_prob[:, hb * W3:(hb + 1) * W3]
                ew = [tok_rmax]
                if h >= 2:
                    ew.append(tok_probT_pe[(tt, h - 2)])
                elif tt >= 1:
                    ew.append(tok_probT_pe[(tt - 1, 14 + hb)])
                tok_exp[(tt, h)] = S("scalar", lambda e, pbuf=pbuf, scbuf=scbuf, h=h:
                                     e.activation(pbuf, scbuf, AF.Exp,
                                                  bias=t_nmx[:, h:h + 1],
                                                  accum_out=t_den[:, h:h + 1]), waits=ew)
                tok_recip[(tt, h)] = S("vector", lambda e, h=h:
                                       e.reciprocal(t_rden[:, h:h + 1], t_den[:, h:h + 1]),
                                       waits=[tok_exp[(tt, h)]])

            if h >= 1:
                hh = h - 1
                hb2 = hh % 2
                bankT = p_tra if hb2 == 0 else p_trb
                pbuf2 = t_prob[:, hb2 * W3:(hb2 + 1) * W3]

                def mk_pt(eng, pbuf2=pbuf2, bankT=bankT):
                    last = None
                    for kr in range(3):
                        last = eng.transpose(bankT[:, kr * 128:(kr + 1) * 128],
                                             pbuf2[:, kr * 128:(kr + 1) * 128], t_identb[:])
                    return last
                tok_probT_pe[(tt, hh)] = S("tensor", mk_pt,
                                           waits=[tok_exp[(tt, hh)], trb_ev[hb2]])
                ptbuf = t_probT[:, hb2 * W3:(hb2 + 1) * W3]
                ew = [tok_probT_pe[(tt, hh)]]
                if hh >= 2:
                    ew.append(tok_av[(tt, hh - 2)])
                elif tt >= 1:
                    ew.append(tok_av[(tt - 1, 14 + hb2)])
                tok_probT_ev[(tt, hh)] = S("vector", lambda e, ptbuf=ptbuf, bankT=bankT:
                                           e.tensor_copy(ptbuf, bankT[:, 0:W3]), waits=ew)
                trb_ev[hb2] = tok_probT_ev[(tt, hh)]

                def mk_av(eng, ptbuf=ptbuf, hh=hh, tq=tq):
                    last = None
                    for kr in range(3):
                        ks = (tq - 2 + kr) % 3
                        last = eng.matmul(
                            p_misc[:, 0:HD],
                            ptbuf[:, kr * 128:(kr + 1) * 128],
                            t_vring[:, ks * C + hh * 64: ks * C + hh * 64 + 64],
                            start=(kr == 0), stop=(kr == 2))
                    return last
                aw = [tok_probT_ev[(tt, hh)], tok_vev[(tq, 1)]]
                if hh >= 1:
                    aw.append(tok_norm[(tt, hh - 1)])
                elif tt >= 1:
                    aw.append(tok_logit_ev[tt - 1])
                tok_av[(tt, hh)] = S("tensor", mk_av, waits=aw)

                nw = [tok_av[(tt, hh)], tok_recip[(tt, hh)]]
                if tt >= 1 and hh == 0:
                    nw.append(tok_aoT_pe[tt - 1])
                tok_norm[(tt, hh)] = S("vector", lambda e, hh=hh:
                                       e.tensor_scalar(t_ao[:, hh * 64:(hh + 1) * 64],
                                                       p_misc[:, 0:HD],
                                                       t_rden[:, hh:hh + 1], None,
                                                       op0=ALU.mult), waits=nw)
        tok_heads_last_pe[tq] = tok_av[(tt, NH - 1)]

        def mk_aoT(eng):
            last = None
            for j in range(NKC):
                last = eng.transpose(p_trb[:, j * 128:(j + 1) * 128],
                                     t_ao[:, j * 128:(j + 1) * 128], t_identb[:])
            return last
        aw = [tok_norm[(tt, NH - 1)], trb_ev[1]]
        tok_aoT_pe[tt] = S("tensor", mk_aoT, waits=aw)
        ew = [tok_aoT_pe[tt]]
        if tt >= 1:
            ew.append(tok_y1_mm[(tt - 1, 1)])
        tok_aoT_ev[tt] = S("vector", lambda e: e.tensor_copy(t_aoT[:], p_trb[:, 0:C]),
                           waits=ew)
        trb_ev[1] = tok_aoT_ev[tt]

        for nh2 in range(2):
            def mk_pj(eng, nh2=nh2):
                last = None
                for kc in range(NKC):
                    last = eng.matmul(
                        p_misc[:],
                        t_aoT[:, kc * 128:(kc + 1) * 128],
                        t_wpb[:, kc * C + nh2 * 512: kc * C + (nh2 + 1) * 512],
                        start=(kc == 0), stop=(kc == NKC - 1))
                return last
            pw = [tok_aoT_ev[tt], w_wsetup, tok_norm[(tt, NH - 1)]]
            if nh2 == 1:
                pw.append(tok_proj_ev[(tt, 0)])
            tok_y1_mm[(tt, nh2)] = S("tensor", mk_pj, waits=pw)
            ew = [tok_y1_mm[(tt, nh2)], w_setup]
            tok_proj_ev[(tt, nh2)] = S("vector", lambda e, nh2=nh2:
                                       e.scalar_tensor_tensor(
                                           t_w[:, nh2 * 512:(nh2 + 1) * 512], p_misc[:], 1.0,
                                           t_bpbc[:, nh2 * 512:(nh2 + 1) * 512],
                                           op0=ALU.mult, op1=ALU.add), waits=ew)

        tok_res = S("vector", lambda e, tt=tt: e.tensor_tensor(
            t_w[:], t_w[:], stream[:, tt * C:(tt + 1) * C], op=ALU.add),
            waits=[tok_proj_ev[(tt, 0)], tok_proj_ev[(tt, 1)]])

        tok_ns = S("vector", lambda e: e.reduce_sum(t_stat[:, 0:1], t_w[:], axis=AX.X,
                                                    negate=True), waits=[tok_res])
        tok_nm = S("vector", lambda e: e.tensor_scalar(t_stat[:, 1:2], t_stat[:, 0:1],
                                                       1.0 / C, None, op0=ALU.mult),
                   waits=[tok_ns])
        tok_ct = S("vector", lambda e: e.tensor_scalar(t_cent[:], t_w[:], t_stat[:, 1:2],
                                                       None, op0=ALU.add), waits=[tok_nm])
        tok_sq = S("vector", lambda e: e.tensor_tensor(t_w[:], t_cent[:], t_cent[:],
                                                       op=ALU.mult), waits=[tok_ct])
        tok_vr = S("vector", lambda e: e.reduce_sum(t_stat[:, 2:3], t_w[:], axis=AX.X),
                   waits=[tok_sq])
        tok_v2 = S("vector", lambda e: e.tensor_scalar(t_stat[:, 3:4], t_stat[:, 2:3],
                                                       1.0 / C, EPS1, op0=ALU.mult,
                                                       op1=ALU.add), waits=[tok_vr])
        tok_ln = S("scalar", lambda e: e.activation(t_stat[:, 4:5], t_stat[:, 3:4], AF.Ln),
                   waits=[tok_v2])
        tok_rs = S("scalar", lambda e: e.activation(t_stat[:, 5:6], t_stat[:, 4:5], AF.Exp,
                                                    scale=-0.5), waits=[tok_ln])
        tok_xnew[tt] = S("vector", lambda e, tt=tt: e.tensor_scalar(
            stream[:, tt * C:(tt + 1) * C], t_cent[:], t_stat[:, 5:6], None, op0=ALU.mult),
            waits=[tok_rs])

        S("sync", lambda e, tt=tt: e.dma_start(
            d_x1[tt * 128:(tt + 1) * 128, :],
            stream[:, tt * C:(tt + 1) * C]).then_inc(q_spill, 16), waits=[tok_xnew[tt]])

        tok_rtr_ev = {}
        for bt in range(2):
            def mk_rtr(eng, bt=bt, tt=tt):
                last = None
                for j in range(4):
                    last = eng.transpose(
                        p_misc[:, j * 128:(j + 1) * 128],
                        stream[:, tt * C + (bt * 4 + j) * 128: tt * C + (bt * 4 + j + 1) * 128],
                        t_identf[:])
                return last
            rw = [tok_xnew[tt]]
            rw.append(tok_proj_ev[(tt, 1)] if bt == 0 else tok_rtr_ev[0])
            tok_rtr_pe = S("tensor", mk_rtr, waits=rw)
            tok_rtr_ev[bt] = S("vector", lambda e, bt=bt: e.tensor_copy(
                t_xT32[:, bt * 512:(bt + 1) * 512], p_misc[:]), waits=[tok_rtr_pe])

        def mk_rmm(eng):
            last = None
            for kc in range(NKC):
                last = eng.matmul(p_misc[:, 0:NE],
                                  t_xT32[:, kc * 128:(kc + 1) * 128],
                                  t_wrt[:, kc * NE:(kc + 1) * NE],
                                  start=(kc == 0), stop=(kc == NKC - 1))
            return last
        tok_rmm = S("tensor", mk_rmm, waits=[tok_rtr_ev[0], tok_rtr_ev[1], w_setup])
        tok_logit_ev[tt] = S("vector", lambda e: e.tensor_copy(t_lg[:], p_misc[:, 0:NE]),
                             waits=[tok_rmm])
        tok_m8 = S("vector", lambda e: e.max(t_mx8[:], t_lg[:]), waits=[tok_logit_ev[tt]])
        tok_mi = S("vector", lambda e: e.max_index(t_idxu[:], t_mx8[:], t_lg[:]),
                   waits=[tok_m8])
        tok_if = S("vector", lambda e: e.tensor_scalar(t_idxf[:], t_idxu[:, 0:1], 0.0,
                                                       None, op0=ALU.add),
                   waits=[tok_mi] + ([(q_idx, 16 * tt)] if tt >= 1 else []))
        S("sync", lambda e, tt=tt: e.dma_start(
            d_idx.rearrange("(a p) -> a p", p=128)[tt].rearrange("(p o) -> p o", o=1),
            t_idxf[:]).then_inc(q_idx, 16), waits=[tok_if])

    S("sync", lambda e: e.nop(), waits=[(q_spill, 16 * NTT), (q_idx, 16 * NTT)])

    a_st.close()
    b_st = ExitStack()
    bsb = lambda name, shape, dt: b_st.enter_context(nc.sbuf_tensor(name, shape, dt))

    t_ring = bsb("ring", [128, 32 * 1024], BF16)
    t_xTg = bsb("xTg", [128, NKC * 1024], BF16)
    t_idxbc = bsb("idx_bc", [128, 1024], F32)
    t_eq = bsb("eq", [128, 2 * 1024], U8)
    t_gelu = bsb("gelu", [128, 3 * 512], BF16)
    t_xrel2 = bsb("xrel2", [128, 2 * C], F32)
    t_xrb = bsb("xrb", [128, 2 * C], BF16)
    t_y2 = bsb("y2", [128, 2 * C], F32)
    t_ct2 = bsb("ct2", [128, 2 * C], F32)
    t_st2 = bsb("stat2", [128, 16], F32)

    bpsal = lambda name, shape, dt: b_st.enter_context(nc.psum_tensor(name, shape, dt))
    p_i = [bpsal(f"p_i{j}", [128, 512], F32) for j in range(4)]
    p_y2 = [bpsal(f"p_y2{j}", [128, 512], F32) for j in range(2)]
    p_trB = bpsal("p_trB", [128, 1024], BF16)

    sched.new_block()
    S = sched.op

    nw_slot = [0] * 32
    slot_last_reader = [None] * 32
    load_ctr = [0]
    n_xrel = [0] * 4
    n_xrel2 = [0] * 4
    n_out2 = [0, 0]

    tok_sel = {}
    tok_gelu = {}
    tok_immP = {}
    tok_y2mm = {}
    tok_y2ev = {}
    tok_tail_last = {}
    tok_eqtok = {}
    tok_xTg_ev = {}
    tok_trB_tok = {}
    tok_tail_res = {}
    idxbc_readers = []
    gelu_slot_reader = [None, None, None]
    eq_slot_reader = [None, None]

    tok_acc0 = S("vector", lambda e: e.memset(acc_full[:], 0.0),
                 waits=[(q_spill, 16 * NTT), tok_logit_ev[NTT - 1]])

    for g in range(2):
        iw = [(q_idx, 16 * NTT)] + idxbc_readers
        idxbc_readers = []
        S("sync", lambda e, g=g: e.dma_start(
            t_idxbc[:], bass.AP(d_idx.tensor, g * 1024, [[0, 128], [1, 1024]])).then_inc(
            q_idxbc, 16), waits=iw)
        w_idxbc = (q_idxbc, 16 * (g + 1))

        for ttl in range(8):
            tt = g * 8 + ttl
            qi = ttl % 4
            rw = [(q_spill, 16 * NTT)]
            if (g, ttl - 2) in tok_trB_tok:
                rw.append(tok_trB_tok[(g, ttl - 2)])
            elif g == 1 and ttl < 2:
                rw.append(tok_trB_tok[(0, 6 + ttl)])
            n_xrel[qi] += 1
            S("gpsimd", lambda e, tt=tt, ttl=ttl, qi=qi: e.dma_start(
                t_xrb[:, (ttl % 2) * C:(ttl % 2 + 1) * C],
                d_x1[tt * 128:(tt + 1) * 128, :]).then_inc(q_xrel[qi], 16), waits=rw)
            xrw = (q_xrel[qi], 16 * n_xrel[qi])

            def mk_trB(eng, ttl=ttl):
                last = None
                for j in range(NKC):
                    last = eng.transpose(p_trB[:, j * 128:(j + 1) * 128],
                                         t_xrb[:, (ttl % 2) * C + j * 128:
                                               (ttl % 2) * C + (j + 1) * 128], t_identb[:])
                return last
            tw = [xrw, w_setup]
            if (g, ttl - 1) in tok_xTg_ev:
                tw.append(tok_xTg_ev[(g, ttl - 1)])
            elif g == 1 and ttl == 0:
                tw.append(tok_xTg_ev[(0, 7)])
            tok_trB_pe = S("tensor", mk_trB, waits=tw)
            tok_trB_tok[(g, ttl)] = tok_trB_pe
            xTg_v = t_xTg.rearrange("p (k t) -> p k t", k=NKC)
            ew = [tok_trB_pe]
            if g == 1 and ttl == 0:
                ew.append(tok_y2mm[(0, 7, 1)])
            tok_xTg_ev[(g, ttl)] = S("vector", lambda e, xTg_v=xTg_v, ttl=ttl:
                                     e.tensor_copy(xTg_v[:, :, ttl * 128:(ttl + 1) * 128],
                                                   p_trB[:, 0:C]), waits=ew)
        w_xTg = [tok_xTg_ev[(g, t2)] for t2 in range(8)]

        for eix in range(NE):
            eqbuf = t_eq[:, (eix % 2) * 1024:(eix % 2 + 1) * 1024]
            eqw = [w_idxbc]
            if eq_slot_reader[eix % 2] is not None:
                eqw.append(eq_slot_reader[eix % 2])
            tok_eqtok[(g, eix)] = S("vector", lambda e, eqbuf=eqbuf, eix=eix:
                                    e.tensor_scalar(eqbuf, t_idxbc[:], float(eix), None,
                                                    op0=ALU.is_equal), waits=eqw)
            idxbc_readers.append(tok_eqtok[(g, eix)])

            for iq in range(4):
                slot_of_kc = {}
                for kc in range(NKC):
                    L = load_ctr[0]
                    s = L % 32
                    nw_slot[s] += 1
                    slot_of_kc[kc] = (s, nw_slot[s])
                    lw = [slot_last_reader[s]] if slot_last_reader[s] is not None else []
                    S("gpsimd", lambda e, s=s, eix=eix, kc=kc, iq=iq: e.dma_start(
                        t_ring[:, s * 1024:(s + 1) * 1024],
                        d_w1[eix, kc * 128:(kc + 1) * 128,
                             iq * 1024:(iq + 1) * 1024]).then_inc(q_slot[s], 16), waits=lw)
                    load_ctr[0] += 1

                for ps2 in range(2):
                    for icl in range(8):
                        seq = ps2 * 8 + icl
                        bank = p_i[seq % 4]
                        icg = iq * 8 + icl
                        key = (g, eix, iq, ps2, icl)

                        def mk_im(eng, bank=bank, icl=icl, ps2=ps2,
                                  slot_of_kc=dict(slot_of_kc)):
                            last = None
                            for kc in range(NKC):
                                s2, _ = slot_of_kc[kc]
                                last = eng.matmul(
                                    bank[:],
                                    t_ring[:, s2 * 1024 + icl * 128: s2 * 1024 + (icl + 1) * 128],
                                    t_xTg[:, kc * 1024 + ps2 * 512: kc * 1024 + (ps2 + 1) * 512],
                                    start=(kc == 0), stop=(kc == NKC - 1))
                            return last

                        def back_key(delta):
                            sq = seq - delta
                            e2, i2, g2 = eix, iq, g
                            while sq < 0:
                                sq += 16
                                i2 -= 1
                                if i2 < 0:
                                    i2 = 3
                                    e2 -= 1
                                    if e2 < 0:
                                        e2 = NE - 1
                                        g2 -= 1
                                        if g2 < 0:
                                            return None
                            return (g2, e2, i2, sq // 8, sq % 8)

                        mw = list(w_xTg)
                        mw += [(q_slot[s2], 16 * n2) for (s2, n2) in slot_of_kc.values()]
                        bk = back_key(4)
                        if bk is not None and bk in tok_gelu:
                            mw.append(tok_gelu[bk])
                        tok_immP[key] = S("tensor", mk_im, waits=mw)

                        gb = seq % 3
                        gbuf = t_gelu[:, gb * 512:(gb + 1) * 512]
                        ew = [tok_immP[key]]
                        if gelu_slot_reader[gb] is not None:
                            ew.append(gelu_slot_reader[gb])
                        tok_gelu[key] = S("scalar", lambda e, gbuf=gbuf, bank=bank,
                                          eix=eix, icg=icg:
                                          e.activation(gbuf, bank[:], AF.Gelu,
                                                       bias=t_b1c[:, eix * 32 + icg:
                                                                  eix * 32 + icg + 1]),
                                          waits=ew)
                        accsl = acc_full[:, icg * 1024 + ps2 * 512: icg * 1024 + (ps2 + 1) * 512]
                        sw = [tok_gelu[key], tok_eqtok[(g, eix)]]
                        if g == 1 and eix == 0:
                            sw.append(tok_y2ev[(0, 7, 1)])
                        tok_sel[key] = S("vector", lambda e, accsl=accsl, eqbuf=eqbuf,
                                         gbuf=gbuf, ps2=ps2:
                                         e.copy_predicated(
                                             accsl, eqbuf[:, ps2 * 512:(ps2 + 1) * 512],
                                             gbuf), waits=sw)
                        gelu_slot_reader[gb] = tok_sel[key]
                for kc in range(NKC):
                    s2, _ = slot_of_kc[kc]
                    slot_last_reader[s2] = tok_immP[(g, eix, iq, 1, 7)]
            eq_slot_reader[eix % 2] = tok_sel[(g, eix, 3, 1, 7)]

        last_sel = tok_sel[(g, NE - 1, 3, 1, 7)]

        wo_slot = {}
        for ic in range(32):
            L = load_ctr[0]
            s = L % 32
            nw_slot[s] += 1
            wo_slot[ic] = (s, nw_slot[s])
            lw = [slot_last_reader[s]] if slot_last_reader[s] is not None else []
            S("gpsimd", lambda e, s=s, ic=ic: e.dma_start(
                t_ring[:, s * 1024:(s + 1) * 1024],
                d_wo[ic * 128:(ic + 1) * 128, :]).then_inc(q_slot[s], 16), waits=lw)
            load_ctr[0] += 1

        for ttl in range(8):
            tt = g * 8 + ttl
            qi = ttl % 4
            tb = ttl % 2
            rw = [(q_spill, 16 * NTT)]
            if (g, ttl - 2) in tok_tail_res:
                rw.append(tok_tail_res[(g, ttl - 2)])
            elif g == 1 and ttl < 2:
                rw.append(tok_tail_res[(0, 6 + ttl)])
            n_xrel2[qi] += 1
            S("sync", lambda e, tt=tt, tb=tb, qi=qi: e.dma_start(
                t_xrel2[:, tb * C:(tb + 1) * C],
                d_x1[tt * 128:(tt + 1) * 128, :]).then_inc(q_xrel2[qi], 16), waits=rw)
            xr2w = (q_xrel2[qi], 16 * n_xrel2[qi])

            for ch in range(2):
                def mk_wo(eng, ttl=ttl, ch=ch, wo_slot=dict(wo_slot)):
                    last = None
                    for ic in range(32):
                        s2, _ = wo_slot[ic]
                        last = eng.matmul(
                            p_y2[ch][:],
                            acc_full[:, ic * 1024 + ttl * 128: ic * 1024 + (ttl + 1) * 128],
                            t_ring[:, s2 * 1024 + ch * 512: s2 * 1024 + (ch + 1) * 512],
                            start=(ic == 0), stop=(ic == 31))
                    return last
                ww = [last_sel]
                ww += [(q_slot[s2], 16 * n2) for (s2, n2) in wo_slot.values()]
                if ttl >= 1:
                    ww.append(tok_y2ev[(g, ttl - 1, ch)])
                elif g == 1:
                    ww.append(tok_y2ev[(0, 7, ch)])
                tok_y2mm[(g, ttl, ch)] = S("tensor", mk_wo, waits=ww)
                ew = [tok_y2mm[(g, ttl, ch)], w_setup, xr2w]
                if (g, ttl - 2) in tok_tail_last:
                    ew.append(tok_tail_last[(g, ttl - 2)])
                elif g == 1 and ttl < 2:
                    ew.append(tok_tail_last[(0, 6 + ttl)])
                tok_y2ev[(g, ttl, ch)] = S("vector", lambda e, ch=ch, tb=tb:
                                           e.scalar_tensor_tensor(
                                               t_y2[:, tb * C + ch * 512:
                                                    tb * C + (ch + 1) * 512],
                                               p_y2[ch][:], 1.0,
                                               t_bo[:, ch * 512:(ch + 1) * 512],
                                               op0=ALU.mult, op1=ALU.add), waits=ew)
            if ttl == 7:
                for ic in range(32):
                    s2, _ = wo_slot[ic]
                    slot_last_reader[s2] = tok_y2mm[(g, 7, 1)]

            yb = t_y2[:, tb * C:(tb + 1) * C]
            cb = t_ct2[:, tb * C:(tb + 1) * C]
            xr = t_xrel2[:, tb * C:(tb + 1) * C]
            sc = t_st2[:, tb * 8:(tb + 1) * 8]

            def ln_chain(first_w, src, cent, dst, eps, affine, sc=sc):
                tns = S("vector", lambda e, src=src, sc=sc: e.reduce_sum(
                    sc[:, 0:1], src, axis=AX.X, negate=True), waits=first_w)
                tnm = S("vector", lambda e, sc=sc: e.tensor_scalar(
                    sc[:, 1:2], sc[:, 0:1], 1.0 / C, None, op0=ALU.mult), waits=[tns])
                tct = S("vector", lambda e, src=src, cent=cent, sc=sc: e.tensor_scalar(
                    cent, src, sc[:, 1:2], None, op0=ALU.add), waits=[tnm])
                tsq = S("vector", lambda e, src=src, cent=cent: e.tensor_tensor(
                    src, cent, cent, op=ALU.mult), waits=[tct])
                tvr = S("vector", lambda e, src=src, sc=sc: e.reduce_sum(
                    sc[:, 2:3], src, axis=AX.X), waits=[tsq])
                tv2 = S("vector", lambda e, sc=sc, eps=eps: e.tensor_scalar(
                    sc[:, 3:4], sc[:, 2:3], 1.0 / C, eps, op0=ALU.mult, op1=ALU.add),
                    waits=[tvr])
                tln = S("scalar", lambda e, sc=sc: e.activation(sc[:, 4:5], sc[:, 3:4],
                                                                AF.Ln), waits=[tv2])
                trs = S("scalar", lambda e, sc=sc: e.activation(sc[:, 5:6], sc[:, 4:5],
                                                                AF.Exp, scale=-0.5),
                        waits=[tln])
                if affine:
                    tz = S("vector", lambda e, cent=cent, dst=dst, sc=sc:
                           e.scalar_tensor_tensor(dst, cent, sc[:, 5:6], t_gam[:],
                                                  op0=ALU.mult, op1=ALU.mult), waits=[trs])
                    tz = S("vector", lambda e, dst=dst: e.tensor_tensor(
                        dst, dst, t_bet[:], op=ALU.add), waits=[tz])
                else:
                    tz = S("vector", lambda e, cent=cent, dst=dst, sc=sc: e.tensor_scalar(
                        dst, cent, sc[:, 5:6], None, op0=ALU.mult), waits=[trs])
                return tz

            r1w = [tok_y2ev[(g, ttl, 0)], tok_y2ev[(g, ttl, 1)]]
            if n_out2[tb] > 0:
                r1w.append((q_out2[tb], n_out2[tb]))
            t_r1 = S("vector", lambda e, yb=yb, xr=xr: e.tensor_tensor(
                yb, yb, xr, op=ALU.add), waits=r1w)
            t_z = ln_chain([t_r1], yb, cb, yb, EPS2, True)
            t_r2 = S("vector", lambda e, yb=yb, xr=xr: e.tensor_tensor(
                yb, xr, yb, op=ALU.add), waits=[t_z])
            tok_tail_res[(g, ttl)] = t_r2
            t_o = ln_chain([t_r2], yb, cb, cb, EPS1, False)
            tok_tail_last[(g, ttl)] = t_o

            n_out2[tb] += 16
            S("sync", lambda e, tt=tt, cb=cb, tb=tb: e.dma_start(
                d_out[tt * 128:(tt + 1) * 128, :], cb).then_inc(q_out2[tb], 16),
              waits=[t_o])

    S("sync", lambda e: e.nop(), waits=[(q_out2[0], n_out2[0]), (q_out2[1], n_out2[1])])

    sched.finalize()
    for blk_ops in sched.blocks:
        with nc.Block() as block:
            for ename, deco in (("sync", block.sync), ("gpsimd", block.gpsimd),
                                ("scalar", block.scalar), ("vector", block.vector),
                                ("tensor", block.tensor)):
                def body(engine, ename=ename, blk_ops=blk_ops):
                    sched.emit_block(blk_ops, ename, engine, eng_sems)
                deco(body)

    b_st.close()
    st.close()
    return nc


_NC_CACHE = {}


def _get_nc():
    if "nc" not in _NC_CACHE:
        _NC_CACHE["nc"] = build_program()
    return _NC_CACHE["nc"]


def _build_masks(seq0):
    p = np.arange(128)[:, None]
    f = np.arange(W3)[None, :]
    out = np.zeros((3, 128, W3), dtype=np.float32)
    for v in range(3):
        tt = v if v < 2 else 2
        allowed = (f >= p) & (f <= p + 256)
        if v < 2:
            kseq = seq0 - 256 + tt * 128 + f
            allowed = allowed & (kseq >= 0)
        out[v] = np.where(allowed, 0.0, NEG)
    return out


def kernel(L0_attn_x, L0_attn_W_attn, L0_attn_b_attn, L0_attn_W_proj, L0_attn_b_proj,
           L0_ffn_Wr, L0_ffn_W1, L0_ffn_b1, L0_ffn_Wo, L0_ffn_bo, L0_ffn_gamma,
           L0_ffn_beta):
    x = np.ascontiguousarray(np.asarray(L0_attn_x, dtype=np.float32).reshape(B * T, C))
    wa = np.ascontiguousarray(np.asarray(L0_attn_W_attn, dtype=np.float32))
    ba = np.ascontiguousarray(np.asarray(L0_attn_b_attn, dtype=np.float32))
    wp = np.ascontiguousarray(np.asarray(L0_attn_W_proj, dtype=np.float32))
    bp = np.ascontiguousarray(np.asarray(L0_attn_b_proj, dtype=np.float32))
    wrt = np.ascontiguousarray(np.asarray(L0_ffn_Wr, dtype=np.float32).T)
    w1 = np.ascontiguousarray(np.asarray(L0_ffn_W1, dtype=np.float32))
    b1 = np.ascontiguousarray(np.asarray(L0_ffn_b1, dtype=np.float32))
    wo = np.ascontiguousarray(np.asarray(L0_ffn_Wo, dtype=np.float32))
    bo = np.ascontiguousarray(np.asarray(L0_ffn_bo, dtype=np.float32))
    gam = np.ascontiguousarray(np.asarray(L0_ffn_gamma, dtype=np.float32))
    bet = np.ascontiguousarray(np.asarray(L0_ffn_beta, dtype=np.float32))

    in_maps = []
    for c in range(NCORES):
        base = c * TCORE
        seq0 = (c % 4) * TCORE
        xe = np.zeros((NTQ * 128, C), dtype=np.float32)
        if seq0 == 0:
            xe[HALO:] = x[base:base + TCORE]
        else:
            xe[:] = x[base - HALO:base + TCORE]
        in_maps.append({
            "x_ext": xe, "w_attn": wa, "b_attn": ba, "w_proj": wp, "b_proj": bp,
            "wr_t": wrt, "w1": w1, "b1": b1, "wo": wo, "bo": bo, "gamma": gam,
            "beta": bet, "masks": _build_masks(seq0),
        })

    nc = _get_nc()
    res = run_bass_kernel_spmd(nc, in_maps, core_ids=list(range(NCORES)))
    out = np.concatenate([r["out"] for r in res.results], axis=0)
    return out.reshape(B, T, C)


# version 03
# speedup vs baseline: 1.0830x; 1.0830x over previous
import numpy as np
import ml_dtypes
from contextlib import ExitStack

import concourse.bass as bass
import concourse.mybir as mybir
from concourse.bass_utils import run_bass_kernel_spmd

F32 = mybir.dt.float32
BF16 = mybir.dt.bfloat16
U8 = mybir.dt.uint8
U32 = mybir.dt.uint32
AF = mybir.ActivationFunctionType
ALU = mybir.AluOpType
AX = mybir.AxisListType

NCORES = 8
B, T, C = 2, 8192, 1024
TCORE = 2048
HALO = 256
NTT = 16
NTQ = 18
NH, HD = 16, 64
W3 = 384
FF = 4096
NE = 8
NKC = 8
NEG = -1.0e30
EPS1 = 1e-5
EPS2 = 1e-12

ENGINES = ("sync", "gpsimd", "scalar", "vector", "tensor")


class Sched:

    def __init__(self):
        self.blocks = []
        self.cur = None

    def new_block(self):
        self.cur = []
        self.blocks.append(self.cur)

    def op(self, eng, fn, waits=()):
        rec = {"eng": eng, "fn": fn, "waits": [w for w in waits if w is not None],
               "inc": None}
        self.cur.append(rec)
        return rec

    def finalize(self):
        for blk in self.blocks:
            for rec in blk:
                for wtg in rec["waits"]:
                    if isinstance(wtg, dict) and wtg["inc"] is None:
                        wtg["inc"] = -1
        counters = {e: 0 for e in ENGINES}
        for blk in self.blocks:
            for rec in blk:
                if rec["inc"] == -1:
                    counters[rec["eng"]] += 1
                    rec["inc"] = counters[rec["eng"]]

    def emit_block(self, blk_ops, eng_name, engine, eng_sems):
        highwater = {}
        for rec in blk_ops:
            if rec["eng"] != eng_name:
                continue
            for wtg in rec["waits"]:
                if isinstance(wtg, dict):
                    sem, val, key = eng_sems[wtg["eng"]], wtg["inc"], ("e", wtg["eng"])
                else:
                    sem, val = wtg
                    key = ("s", id(sem))
                if highwater.get(key, -1) >= val:
                    continue
                highwater[key] = val
                engine.wait_ge(sem, val)
            inst = rec["fn"](engine)
            if rec["inc"] is not None:
                assert inst is not None, f"wait-target op returned no inst on {eng_name}"
                inst.then_inc(eng_sems[rec["eng"]], 1)


def build_program():
    nc = bass.Bass("TRN2", target_bir_lowering=False)

    d_x = nc.dram_tensor("x_ext", [NTQ * 128, C], F32, kind="ExternalInput").ap()
    d_wa = nc.dram_tensor("w_attn", [C, 3 * C], F32, kind="ExternalInput").ap()
    d_ba = nc.dram_tensor("b_attn", [3 * C], F32, kind="ExternalInput").ap()
    d_wp = nc.dram_tensor("w_proj", [C, C], F32, kind="ExternalInput").ap()
    d_bp = nc.dram_tensor("b_proj", [C], F32, kind="ExternalInput").ap()
    d_wrt = nc.dram_tensor("wr_t", [C, NE], F32, kind="ExternalInput").ap()
    d_w1 = nc.dram_tensor("w1", [NE, C, FF], F32, kind="ExternalInput").ap()
    d_b1 = nc.dram_tensor("b1", [NE, FF], F32, kind="ExternalInput").ap()
    d_wo = nc.dram_tensor("wo", [FF, C], F32, kind="ExternalInput").ap()
    d_bo = nc.dram_tensor("bo", [C], F32, kind="ExternalInput").ap()
    d_gam = nc.dram_tensor("gamma", [C], F32, kind="ExternalInput").ap()
    d_bet = nc.dram_tensor("beta", [C], F32, kind="ExternalInput").ap()
    d_msk = nc.dram_tensor("masks", [3, 128, W3], F32, kind="ExternalInput").ap()
    d_out = nc.dram_tensor("out", [TCORE, C], F32, kind="ExternalOutput").ap()
    d_x1 = nc.dram_tensor("x1_scratch", [TCORE, C], F32, kind="Internal").ap()
    d_idx = nc.dram_tensor("idx_scratch", [TCORE], F32, kind="Internal").ap()

    ident_bf = nc.inline_tensor(np.eye(128, dtype=ml_dtypes.bfloat16), "ident_bf").ap()
    ident_f32 = nc.inline_tensor(np.eye(128, dtype=np.float32), "ident_f32").ap()

    st = ExitStack()
    sb = lambda name, shape, dt: st.enter_context(nc.sbuf_tensor(name, shape, dt))
    psal = lambda name, shape, dt: st.enter_context(nc.psum_tensor(name, shape, dt))
    sem = lambda name: st.enter_context(nc.semaphore(name))

    stream = sb("stream", [128, NTT * C], F32)
    acc_full = stream[:].bitcast(BF16)
    t_identb = sb("identb", [128, 128], BF16)
    t_identf = sb("identf", [128, 128], F32)
    t_bqk = sb("bqk", [128, 16], F32)
    t_bqs = sb("bqs", [128, 8], F32)
    t_wrt = sb("wrt", [128, NKC * NE], F32)
    t_b1c = sb("b1cols", [128, NE * 32], F32)
    t_bo = sb("bo_bc", [128, C], F32)
    t_gam = sb("gam_bc", [128, C], F32)
    t_bet = sb("bet_bc", [128, C], F32)

    eng_sems = {e: sem("e_" + e) for e in ENGINES}
    q_setup = sem("q_setup")
    q_wsetup = sem("q_wsetup")
    q_x = [sem(f"q_x{i}") for i in range(6)]
    q_spill = sem("q_spill")
    q_idx = sem("q_idx")
    q_idxbc = sem("q_idxbc")
    q_b1c = sem("q_b1c")
    q_xrel = [sem(f"q_xrel{i}") for i in range(4)]
    q_xrel2 = [sem(f"q_xrel2{i}") for i in range(4)]
    q_slot = [sem(f"q_slot{i}") for i in range(32)]
    q_out2 = [sem("q_out0"), sem("q_out1")]

    sched = Sched()

    a_st = ExitStack()
    asb = lambda name, shape, dt: a_st.enter_context(nc.sbuf_tensor(name, shape, dt))

    t_wab = asb("wab", [128, NKC * 3 * C], BF16)
    t_wpb = asb("wpb", [128, NKC * C], BF16)
    t_kring = asb("kring", [128, 3 * C], BF16)
    t_vring = asb("vring", [128, 3 * C], BF16)
    t_masks = asb("masks_sb", [128, 3 * W3], F32)
    t_bvbc = asb("bv_bc", [128, C], F32)
    t_bpbc = asb("bp_bc", [128, C], F32)
    t_xhalo = asb("xhalo", [128, 2 * C], F32)
    t_xc = asb("xc", [128, 2 * C], BF16)
    t_xTb = asb("xTb", [128, 2 * C], BF16)
    t_q = asb("qtile", [128, C], BF16)
    t_scf = asb("scf", [128, 2 * W3], F32)
    t_prob = asb("prob", [128, 2 * W3], BF16)
    t_probT = asb("probT", [128, 2 * W3], BF16)
    t_ao = asb("attn_out", [128, C], BF16)
    t_aoT = asb("attn_outT", [128, C], BF16)
    t_nmx = asb("negmax", [128, NH], F32)
    t_den = asb("den", [128, NH], F32)
    t_rden = asb("rden", [128, NH], F32)
    t_w = asb("wbuf", [128, C], F32)
    t_cent = asb("centbuf", [128, C], F32)
    t_xT32 = asb("xT32", [128, C], F32)
    t_lg = asb("lgs", [128, 8], F32)
    t_mx8 = asb("mx8", [128, 8], F32)
    t_idxu = asb("idxu", [128, 8], U32)
    t_idxf = asb("idxf", [128, 1], F32)
    t_stat = asb("stat", [128, 8], F32)

    apsal = lambda name, shape, dt: a_st.enter_context(nc.psum_tensor(name, shape, dt))
    p_tra = apsal("p_tra", [128, 1024], BF16)
    p_trb = apsal("p_trb", [128, 1024], BF16)
    p_kq = [apsal("p_kq0", [128, 128], F32), apsal("p_kq1", [128, 128], F32)]
    p_v = apsal("p_v", [128, 512], F32)
    p_sc = [apsal("p_sc0", [128, W3], F32), apsal("p_sc1", [128, W3], F32)]
    p_misc = apsal("p_misc", [128, 512], F32)

    sched.new_block()
    S = sched.op

    n_setup = [0]

    def setup_dma(engname, fn):
        S(engname, lambda e, fn=fn: fn(e).then_inc(q_setup, 16))
        n_setup[0] += 16

    setup_dma("sync", lambda e: e.dma_start(t_identb[:], ident_bf))
    setup_dma("sync", lambda e: e.dma_start(t_identf[:], ident_f32))
    for v in range(3):
        setup_dma("sync", lambda e, v=v: e.dma_start(
            t_masks[:, v * W3:(v + 1) * W3], d_msk[v]))
    setup_dma("sync", lambda e: e.dma_start(
        t_bqk[:], bass.AP(d_ba.tensor, 0, [[1, 128], [128, 16]]),
        allow_slow_non_contiguous=True))

    def bcast_src(dap, off, n):
        return bass.AP(dap.tensor, off, [[0, 128], [1, n]])

    setup_dma("sync", lambda e: e.dma_start(t_bvbc[:], bcast_src(d_ba, 2048, C)))
    setup_dma("sync", lambda e: e.dma_start(t_bpbc[:], bcast_src(d_bp, 0, C)))
    setup_dma("sync", lambda e: e.dma_start(t_bo[:], bcast_src(d_bo, 0, C)))
    setup_dma("sync", lambda e: e.dma_start(t_gam[:], bcast_src(d_gam, 0, C)))
    setup_dma("sync", lambda e: e.dma_start(t_bet[:], bcast_src(d_bet, 0, C)))
    for kc in range(NKC):
        setup_dma("sync", lambda e, kc=kc: e.dma_start(
            t_wrt[:, kc * NE:(kc + 1) * NE], d_wrt[kc * 128:(kc + 1) * 128, :]))
    SETUP_TOTAL = n_setup[0]

    n_b1c = [0]
    for j in range(32):
        S("scalar", lambda e, j=j: e.dma_start(
            t_b1c[:, j * 8:(j + 1) * 8],
            bass.AP(d_b1.tensor, j * 128, [[1, 128], [FF, 8]]),
            allow_slow_non_contiguous=True).then_inc(q_b1c, 16))
        n_b1c[0] += 16
    w_b1c = (q_b1c, n_b1c[0])

    n_wsetup = [0]
    for kc in range(NKC):
        S("gpsimd", lambda e, kc=kc: e.dma_start(
            t_wab[:, kc * 3 * C:(kc + 1) * 3 * C],
            d_wa[kc * 128:(kc + 1) * 128, :]).then_inc(q_wsetup, 16))
        n_wsetup[0] += 16
    for kc in range(NKC):
        S("gpsimd", lambda e, kc=kc: e.dma_start(
            t_wpb[:, kc * C:(kc + 1) * C],
            d_wp[kc * 128:(kc + 1) * 128, :]).then_inc(q_wsetup, 16))
        n_wsetup[0] += 16
    w_setup = (q_setup, SETUP_TOTAL)
    w_wsetup = (q_wsetup, n_wsetup[0])

    tok_bqs = S("vector", lambda e: e.tensor_scalar(
        t_bqs[:], t_bqk[:, 0:8], 0.125, None, op0=ALU.mult), waits=[w_setup])

    trb_ev = [None, None]

    tok_cast = {}
    tok_trE = {}
    tok_kev = {}
    tok_vev = {}
    tok_qev = {}
    tok_kq_mm = {}
    tok_v_mm = {}
    tok_sc = {}
    tok_maskTT = {}
    tok_exp = {}
    tok_recip = {}
    tok_probT_pe = {}
    tok_probT_ev = {}
    tok_av = {}
    tok_norm = {}
    tok_aoT_pe = {}
    tok_aoT_ev = {}
    tok_y1_mm = {}
    tok_proj_ev = {}
    tok_logit_ev = {}
    tok_xnew = {}
    tok_trP = {}
    tok_heads_last_pe = {}

    nx_load = [0] * 6

    for tq in range(NTQ):
        tt = tq - 2
        kslot = tq % 3

        qi = tq % 6
        nx_load[qi] += 16
        xw = (q_x[qi], nx_load[qi])
        dst = stream[:, tt * C:(tt + 1) * C] if tq >= 2 else t_xhalo[:, tq * C:(tq + 1) * C]
        lw = [tok_cast[tq - 6]] if tq - 6 >= 0 else []
        S("sync", lambda e, dst=dst, tq=tq, qi=qi: e.dma_start(
            dst, d_x[tq * 128:(tq + 1) * 128, :]).then_inc(q_x[qi], 16), waits=lw)

        cbuf = t_xc[:, (tq % 2) * C:(tq % 2 + 1) * C]
        cw = [xw]
        if tq - 2 >= 0:
            cw.append(tok_trP[tq - 2])
        tok_cast[tq] = S("scalar", lambda e, cbuf=cbuf, dst=dst: e.copy(cbuf, dst), waits=cw)

        def mk_tr(eng, cbuf=cbuf):
            last = None
            for j in range(NKC):
                last = eng.transpose(p_tra[:, j * 128:(j + 1) * 128],
                                     cbuf[:, j * 128:(j + 1) * 128], t_identb[:])
            return last
        tok_trP[tq] = S("tensor", mk_tr, waits=[tok_cast[tq], w_setup, trb_ev[0]])
        xTbuf = t_xTb[:, (tq % 2) * C:(tq % 2 + 1) * C]
        evw = [tok_trP[tq]]
        if tq - 2 >= 2:
            evw.append(tok_kq_mm[(tq - 2, "q", NKC - 1)])
        elif tq - 2 >= 0:
            evw.append(tok_v_mm[(tq - 2, 1)])
        tok_trE[tq] = S("vector", lambda e, xTbuf=xTbuf: e.tensor_copy(xTbuf, p_tra[:, 0:C]),
                        waits=evw)
        trb_ev[0] = tok_trE[tq]

        for fc in range(NKC):
            bank = p_kq[fc % 2]
            def mk_k(eng, fc=fc, bank=bank, xTbuf=xTbuf):
                last = None
                for kc in range(NKC):
                    last = eng.matmul(
                        bank[:],
                        t_wab[:, kc * 3 * C + C + fc * 128: kc * 3 * C + C + (fc + 1) * 128],
                        xTbuf[:, kc * 128:(kc + 1) * 128],
                        start=(kc == 0), stop=(kc == NKC - 1))
                return last
            kw = [tok_trE[tq], w_wsetup]
            if fc >= 2:
                kw.append(tok_kev[(tq, fc - 2)])
            elif tq >= 3:
                kw.append(tok_qev[(tq - 1, 6 + fc)])
            elif tq >= 1:
                kw.append(tok_kev[(tq - 1, 6 + fc)])
            tok_kq_mm[(tq, "k", fc)] = S("tensor", mk_k, waits=kw)
            kdst = t_kring[:, kslot * C + fc * 128: kslot * C + (fc + 1) * 128]
            ew = [tok_kq_mm[(tq, "k", fc)], w_setup]
            if tq - 1 >= 2:
                ew.append(tok_heads_last_pe[tq - 1])
            tok_kev[(tq, fc)] = S("scalar", lambda e, kdst=kdst, bank=bank, fc=fc:
                                  e.activation(kdst, bank[:], AF.Identity,
                                               bias=t_bqk[:, 8 + fc:9 + fc]), waits=ew)

        for vh in range(2):
            def mk_v(eng, vh=vh, xTbuf=xTbuf):
                last = None
                for kc in range(NKC):
                    last = eng.matmul(
                        p_v[:],
                        xTbuf[:, kc * 128:(kc + 1) * 128],
                        t_wab[:, kc * 3 * C + 2 * C + vh * 512: kc * 3 * C + 2 * C + (vh + 1) * 512],
                        start=(kc == 0), stop=(kc == NKC - 1))
                return last
            vw = [tok_trE[tq], w_wsetup]
            if vh == 1:
                vw.append(tok_vev[(tq, 0)])
            elif (tq - 1, 1) in tok_vev:
                vw.append(tok_vev[(tq - 1, 1)])
            tok_v_mm[(tq, vh)] = S("tensor", mk_v, waits=vw)
            vdst = t_vring[:, kslot * C + vh * 512: kslot * C + (vh + 1) * 512]
            ew = [tok_v_mm[(tq, vh)], w_setup]
            if tq - 1 >= 2:
                ew.append(tok_heads_last_pe[tq - 1])
            tok_vev[(tq, vh)] = S("vector", lambda e, vdst=vdst, vh=vh:
                                  e.scalar_tensor_tensor(
                                      vdst, p_v[:], 1.0, t_bvbc[:, vh * 512:(vh + 1) * 512],
                                      op0=ALU.mult, op1=ALU.add), waits=ew)

        if tt < 0:
            continue

        for fc in range(NKC):
            bank = p_kq[fc % 2]
            def mk_q(eng, fc=fc, bank=bank, xTbuf=xTbuf):
                last = None
                for kc in range(NKC):
                    last = eng.matmul(
                        bank[:],
                        t_wab[:, kc * 3 * C + fc * 128: kc * 3 * C + (fc + 1) * 128],
                        xTbuf[:, kc * 128:(kc + 1) * 128],
                        start=(kc == 0), stop=(kc == NKC - 1))
                return last
            qw = [tok_trE[tq], tok_kev[(tq, 6 + (fc % 2))], tok_bqs]
            if fc >= 2:
                qw.append(tok_qev[(tq, fc - 2)])
            tok_kq_mm[(tq, "q", fc)] = S("tensor", mk_q, waits=qw)
            qdst = t_q[:, fc * 128:(fc + 1) * 128]
            ew = [tok_kq_mm[(tq, "q", fc)]]
            if tt >= 1:
                ew.append(tok_sc[(tt - 1, NH - 1)])
            tok_qev[(tq, fc)] = S("scalar", lambda e, qdst=qdst, bank=bank, fc=fc:
                                  e.activation(qdst, bank[:], AF.Identity, scale=0.125,
                                               bias=t_bqs[:, fc:fc + 1]), waits=ew)

        mvar = tt if tt < 2 else 2
        for h in range(NH + 1):
            if h < NH:
                fc, r0, hb = h // 2, (h % 2) * 64, h % 2

                def mk_sc(eng, fc=fc, r0=r0, hb=hb, tq=tq):
                    last = None
                    for kr in range(3):
                        ks = (tq - 2 + kr) % 3
                        last = eng.matmul(
                            p_sc[hb][:, kr * 128:(kr + 1) * 128],
                            t_q[r0:r0 + 64, fc * 128:(fc + 1) * 128],
                            t_kring[r0:r0 + 64, ks * C + fc * 128: ks * C + (fc + 1) * 128],
                            start=True, stop=True)
                    return last
                sw = [tok_qev[(tq, fc)], tok_kev[(tq, NKC - 1)]]
                if h >= 2:
                    sw.append(tok_maskTT[(tt, h - 2)])
                elif tt >= 1:
                    sw.append(tok_maskTT[(tt - 1, 14 + hb)])
                tok_sc[(tt, h)] = S("tensor", mk_sc, waits=sw)

                scbuf = t_scf[:, hb * W3:(hb + 1) * W3]
                mw = [tok_sc[(tt, h)], w_setup]
                if h >= 2:
                    mw.append(tok_exp[(tt, h - 2)])
                elif tt >= 1:
                    mw.append(tok_exp[(tt - 1, 14 + hb)])
                tok_maskTT[(tt, h)] = S("vector", lambda e, scbuf=scbuf, hb=hb, mvar=mvar:
                                        e.tensor_tensor(scbuf, p_sc[hb][:],
                                                        t_masks[:, mvar * W3:(mvar + 1) * W3],
                                                        op=ALU.add), waits=mw)
                tok_rmax = S("vector", lambda e, scbuf=scbuf, h=h:
                             e.reduce_max(t_nmx[:, h:h + 1], scbuf, axis=AX.X, negate=True),
                             waits=[tok_maskTT[(tt, h)]])

                pbuf = t_prob[:, hb * W3:(hb + 1) * W3]
                ew = [tok_rmax]
                if h >= 2:
                    ew.append(tok_probT_pe[(tt, h - 2)])
                elif tt >= 1:
                    ew.append(tok_probT_pe[(tt - 1, 14 + hb)])
                tok_exp[(tt, h)] = S("scalar", lambda e, pbuf=pbuf, scbuf=scbuf, h=h:
                                     e.activation(pbuf, scbuf, AF.Exp,
                                                  bias=t_nmx[:, h:h + 1],
                                                  accum_out=t_den[:, h:h + 1]), waits=ew)
                tok_recip[(tt, h)] = S("vector", lambda e, h=h:
                                       e.reciprocal(t_rden[:, h:h + 1], t_den[:, h:h + 1]),
                                       waits=[tok_exp[(tt, h)]])

            if h >= 1:
                hh = h - 1
                hb2 = hh % 2
                bankT = p_tra if hb2 == 0 else p_trb
                pbuf2 = t_prob[:, hb2 * W3:(hb2 + 1) * W3]

                def mk_pt(eng, pbuf2=pbuf2, bankT=bankT):
                    last = None
                    for kr in range(3):
                        last = eng.transpose(bankT[:, kr * 128:(kr + 1) * 128],
                                             pbuf2[:, kr * 128:(kr + 1) * 128], t_identb[:])
                    return last
                tok_probT_pe[(tt, hh)] = S("tensor", mk_pt,
                                           waits=[tok_exp[(tt, hh)], trb_ev[hb2]])
                ptbuf = t_probT[:, hb2 * W3:(hb2 + 1) * W3]
                ew = [tok_probT_pe[(tt, hh)]]
                if hh >= 2:
                    ew.append(tok_av[(tt, hh - 2)])
                elif tt >= 1:
                    ew.append(tok_av[(tt - 1, 14 + hb2)])
                tok_probT_ev[(tt, hh)] = S("vector", lambda e, ptbuf=ptbuf, bankT=bankT:
                                           e.tensor_copy(ptbuf, bankT[:, 0:W3]), waits=ew)
                trb_ev[hb2] = tok_probT_ev[(tt, hh)]

                def mk_av(eng, ptbuf=ptbuf, hh=hh, tq=tq):
                    last = None
                    for kr in range(3):
                        ks = (tq - 2 + kr) % 3
                        last = eng.matmul(
                            p_misc[:, 0:HD],
                            ptbuf[:, kr * 128:(kr + 1) * 128],
                            t_vring[:, ks * C + hh * 64: ks * C + hh * 64 + 64],
                            start=(kr == 0), stop=(kr == 2))
                    return last
                aw = [tok_probT_ev[(tt, hh)], tok_vev[(tq, 1)]]
                if hh >= 1:
                    aw.append(tok_norm[(tt, hh - 1)])
                elif tt >= 1:
                    aw.append(tok_logit_ev[tt - 1])
                tok_av[(tt, hh)] = S("tensor", mk_av, waits=aw)

                nw = [tok_av[(tt, hh)], tok_recip[(tt, hh)]]
                if tt >= 1 and hh == 0:
                    nw.append(tok_aoT_pe[tt - 1])
                tok_norm[(tt, hh)] = S("vector", lambda e, hh=hh:
                                       e.tensor_scalar(t_ao[:, hh * 64:(hh + 1) * 64],
                                                       p_misc[:, 0:HD],
                                                       t_rden[:, hh:hh + 1], None,
                                                       op0=ALU.mult), waits=nw)
        tok_heads_last_pe[tq] = tok_av[(tt, NH - 1)]

        def mk_aoT(eng):
            last = None
            for j in range(NKC):
                last = eng.transpose(p_trb[:, j * 128:(j + 1) * 128],
                                     t_ao[:, j * 128:(j + 1) * 128], t_identb[:])
            return last
        aw = [tok_norm[(tt, NH - 1)], trb_ev[1]]
        tok_aoT_pe[tt] = S("tensor", mk_aoT, waits=aw)
        ew = [tok_aoT_pe[tt]]
        if tt >= 1:
            ew.append(tok_y1_mm[(tt - 1, 1)])
        tok_aoT_ev[tt] = S("vector", lambda e: e.tensor_copy(t_aoT[:], p_trb[:, 0:C]),
                           waits=ew)
        trb_ev[1] = tok_aoT_ev[tt]

        for nh2 in range(2):
            def mk_pj(eng, nh2=nh2):
                last = None
                for kc in range(NKC):
                    last = eng.matmul(
                        p_misc[:],
                        t_aoT[:, kc * 128:(kc + 1) * 128],
                        t_wpb[:, kc * C + nh2 * 512: kc * C + (nh2 + 1) * 512],
                        start=(kc == 0), stop=(kc == NKC - 1))
                return last
            pw = [tok_aoT_ev[tt], w_wsetup, tok_norm[(tt, NH - 1)]]
            if nh2 == 1:
                pw.append(tok_proj_ev[(tt, 0)])
            tok_y1_mm[(tt, nh2)] = S("tensor", mk_pj, waits=pw)
            ew = [tok_y1_mm[(tt, nh2)], w_setup]
            tok_proj_ev[(tt, nh2)] = S("vector", lambda e, nh2=nh2:
                                       e.scalar_tensor_tensor(
                                           t_w[:, nh2 * 512:(nh2 + 1) * 512], p_misc[:], 1.0,
                                           t_bpbc[:, nh2 * 512:(nh2 + 1) * 512],
                                           op0=ALU.mult, op1=ALU.add), waits=ew)

        tok_res = S("vector", lambda e, tt=tt: e.tensor_tensor(
            t_w[:], t_w[:], stream[:, tt * C:(tt + 1) * C], op=ALU.add),
            waits=[tok_proj_ev[(tt, 0)], tok_proj_ev[(tt, 1)]])

        tok_ns = S("vector", lambda e: e.reduce_sum(t_stat[:, 0:1], t_w[:], axis=AX.X,
                                                    negate=True), waits=[tok_res])
        tok_nm = S("vector", lambda e: e.tensor_scalar(t_stat[:, 1:2], t_stat[:, 0:1],
                                                       1.0 / C, None, op0=ALU.mult),
                   waits=[tok_ns])
        tok_ct = S("vector", lambda e: e.tensor_scalar(t_cent[:], t_w[:], t_stat[:, 1:2],
                                                       None, op0=ALU.add), waits=[tok_nm])
        tok_sq = S("vector", lambda e: e.tensor_tensor(t_w[:], t_cent[:], t_cent[:],
                                                       op=ALU.mult), waits=[tok_ct])
        tok_vr = S("vector", lambda e: e.reduce_sum(t_stat[:, 2:3], t_w[:], axis=AX.X),
                   waits=[tok_sq])
        tok_v2 = S("vector", lambda e: e.tensor_scalar(t_stat[:, 3:4], t_stat[:, 2:3],
                                                       1.0 / C, EPS1, op0=ALU.mult,
                                                       op1=ALU.add), waits=[tok_vr])
        tok_ln = S("scalar", lambda e: e.activation(t_stat[:, 4:5], t_stat[:, 3:4], AF.Ln),
                   waits=[tok_v2])
        tok_rs = S("scalar", lambda e: e.activation(t_stat[:, 5:6], t_stat[:, 4:5], AF.Exp,
                                                    scale=-0.5), waits=[tok_ln])
        tok_xnew[tt] = S("vector", lambda e, tt=tt: e.tensor_scalar(
            stream[:, tt * C:(tt + 1) * C], t_cent[:], t_stat[:, 5:6], None, op0=ALU.mult),
            waits=[tok_rs])

        S("sync", lambda e, tt=tt: e.dma_start(
            d_x1[tt * 128:(tt + 1) * 128, :],
            stream[:, tt * C:(tt + 1) * C]).then_inc(q_spill, 16), waits=[tok_xnew[tt]])

        tok_rtr_ev = {}
        for bt in range(2):
            def mk_rtr(eng, bt=bt, tt=tt):
                last = None
                for j in range(4):
                    last = eng.transpose(
                        p_misc[:, j * 128:(j + 1) * 128],
                        stream[:, tt * C + (bt * 4 + j) * 128: tt * C + (bt * 4 + j + 1) * 128],
                        t_identf[:])
                return last
            rw = [tok_xnew[tt]]
            rw.append(tok_proj_ev[(tt, 1)] if bt == 0 else tok_rtr_ev[0])
            tok_rtr_pe = S("tensor", mk_rtr, waits=rw)
            tok_rtr_ev[bt] = S("vector", lambda e, bt=bt: e.tensor_copy(
                t_xT32[:, bt * 512:(bt + 1) * 512], p_misc[:]), waits=[tok_rtr_pe])

        def mk_rmm(eng):
            last = None
            for kc in range(NKC):
                last = eng.matmul(p_misc[:, 0:NE],
                                  t_xT32[:, kc * 128:(kc + 1) * 128],
                                  t_wrt[:, kc * NE:(kc + 1) * NE],
                                  start=(kc == 0), stop=(kc == NKC - 1))
            return last
        tok_rmm = S("tensor", mk_rmm, waits=[tok_rtr_ev[0], tok_rtr_ev[1], w_setup])
        tok_logit_ev[tt] = S("vector", lambda e: e.tensor_copy(t_lg[:], p_misc[:, 0:NE]),
                             waits=[tok_rmm])
        tok_m8 = S("vector", lambda e: e.max(t_mx8[:], t_lg[:]), waits=[tok_logit_ev[tt]])
        tok_mi = S("vector", lambda e: e.max_index(t_idxu[:], t_mx8[:], t_lg[:]),
                   waits=[tok_m8])
        tok_if = S("vector", lambda e: e.tensor_scalar(t_idxf[:], t_idxu[:, 0:1], 0.0,
                                                       None, op0=ALU.add),
                   waits=[tok_mi] + ([(q_idx, 16 * tt)] if tt >= 1 else []))
        S("sync", lambda e, tt=tt: e.dma_start(
            d_idx.rearrange("(a p) -> a p", p=128)[tt].rearrange("(p o) -> p o", o=1),
            t_idxf[:]).then_inc(q_idx, 16), waits=[tok_if])

    S("sync", lambda e: e.nop(), waits=[(q_spill, 16 * NTT), (q_idx, 16 * NTT)])

    a_st.close()
    b_st = ExitStack()
    bsb = lambda name, shape, dt: b_st.enter_context(nc.sbuf_tensor(name, shape, dt))

    t_ring = bsb("ring", [128, 32 * 1024], BF16)
    t_xTg = bsb("xTg", [128, NKC * 1024], BF16)
    t_idxbc = bsb("idx_bc", [128, 1024], F32)
    t_eq = bsb("eq", [128, 2 * 1024], U8)
    t_gelu = bsb("gelu", [128, 3 * 512], BF16)
    t_xrel2 = bsb("xrel2", [128, 2 * C], F32)
    t_xrb = bsb("xrb", [128, 2 * C], BF16)
    t_y2 = bsb("y2", [128, 2 * C], F32)
    t_ct2 = bsb("ct2", [128, 2 * C], F32)
    t_st2 = bsb("stat2", [128, 16], F32)

    bpsal = lambda name, shape, dt: b_st.enter_context(nc.psum_tensor(name, shape, dt))
    p_i = [bpsal(f"p_i{j}", [128, 512], F32) for j in range(4)]
    p_y2 = [bpsal(f"p_y2{j}", [128, 512], F32) for j in range(2)]
    p_trB = bpsal("p_trB", [128, 1024], BF16)

    sched.new_block()
    S = sched.op

    nw_slot = [0] * 32
    slot_last_reader = [None] * 32
    load_ctr = [0]
    n_xrel = [0] * 4
    n_xrel2 = [0] * 4
    n_out2 = [0, 0]

    tok_sel = {}
    tok_gelu = {}
    tok_immP = {}
    tok_y2mm = {}
    tok_y2ev = {}
    tok_tail_last = {}
    tok_eqtok = {}
    tok_xTg_ev = {}
    tok_trB_tok = {}
    tok_tail_res = {}
    idxbc_readers = []
    gelu_slot_reader = [None, None, None]
    eq_slot_reader = [None, None]

    tok_acc0 = S("vector", lambda e: e.memset(acc_full[:], 0.0),
                 waits=[(q_spill, 16 * NTT), tok_logit_ev[NTT - 1]])

    for g in range(2):
        iw = [(q_idx, 16 * NTT)] + idxbc_readers
        idxbc_readers = []
        S("sync", lambda e, g=g: e.dma_start(
            t_idxbc[:], bass.AP(d_idx.tensor, g * 1024, [[0, 128], [1, 1024]])).then_inc(
            q_idxbc, 16), waits=iw)
        w_idxbc = (q_idxbc, 16 * (g + 1))

        for ttl in range(8):
            tt = g * 8 + ttl
            qi = ttl % 4
            rw = [(q_spill, 16 * NTT)]
            if (g, ttl - 2) in tok_trB_tok:
                rw.append(tok_trB_tok[(g, ttl - 2)])
            elif g == 1 and ttl < 2:
                rw.append(tok_trB_tok[(0, 6 + ttl)])
            n_xrel[qi] += 1
            S("gpsimd", lambda e, tt=tt, ttl=ttl, qi=qi: e.dma_start(
                t_xrb[:, (ttl % 2) * C:(ttl % 2 + 1) * C],
                d_x1[tt * 128:(tt + 1) * 128, :]).then_inc(q_xrel[qi], 16), waits=rw)
            xrw = (q_xrel[qi], 16 * n_xrel[qi])

            def mk_trB(eng, ttl=ttl):
                last = None
                for j in range(NKC):
                    last = eng.transpose(p_trB[:, j * 128:(j + 1) * 128],
                                         t_xrb[:, (ttl % 2) * C + j * 128:
                                               (ttl % 2) * C + (j + 1) * 128], t_identb[:])
                return last
            tw = [xrw, w_setup]
            if (g, ttl - 1) in tok_xTg_ev:
                tw.append(tok_xTg_ev[(g, ttl - 1)])
            elif g == 1 and ttl == 0:
                tw.append(tok_xTg_ev[(0, 7)])
            tok_trB_pe = S("tensor", mk_trB, waits=tw)
            tok_trB_tok[(g, ttl)] = tok_trB_pe
            xTg_v = t_xTg.rearrange("p (k t) -> p k t", k=NKC)
            ew = [tok_trB_pe]
            if g == 1 and ttl == 0:
                ew.append(tok_y2mm[(0, 7, 1)])
            tok_xTg_ev[(g, ttl)] = S("vector", lambda e, xTg_v=xTg_v, ttl=ttl:
                                     e.tensor_copy(xTg_v[:, :, ttl * 128:(ttl + 1) * 128],
                                                   p_trB[:, 0:C]), waits=ew)
        w_xTg = [tok_xTg_ev[(g, t2)] for t2 in range(8)]

        for eix in range(NE):
            eqbuf = t_eq[:, (eix % 2) * 1024:(eix % 2 + 1) * 1024]
            eqw = [w_idxbc]
            if eq_slot_reader[eix % 2] is not None:
                eqw.append(eq_slot_reader[eix % 2])
            tok_eqtok[(g, eix)] = S("vector", lambda e, eqbuf=eqbuf, eix=eix:
                                    e.tensor_scalar(eqbuf, t_idxbc[:], float(eix), None,
                                                    op0=ALU.is_equal), waits=eqw)
            idxbc_readers.append(tok_eqtok[(g, eix)])

            for iq in range(4):
                slot_of_kc = {}
                for kc in range(NKC):
                    L = load_ctr[0]
                    s = L % 32
                    nw_slot[s] += 1
                    slot_of_kc[kc] = (s, nw_slot[s])
                    lw = [slot_last_reader[s]] if slot_last_reader[s] is not None else []
                    S("gpsimd", lambda e, s=s, eix=eix, kc=kc, iq=iq: e.dma_start(
                        t_ring[:, s * 1024:(s + 1) * 1024],
                        d_w1[eix, kc * 128:(kc + 1) * 128,
                             iq * 1024:(iq + 1) * 1024]).then_inc(q_slot[s], 16), waits=lw)
                    load_ctr[0] += 1

                for ps2 in range(2):
                    for icl in range(8):
                        seq = ps2 * 8 + icl
                        bank = p_i[seq % 4]
                        icg = iq * 8 + icl
                        key = (g, eix, iq, ps2, icl)

                        def mk_im(eng, bank=bank, icl=icl, ps2=ps2,
                                  slot_of_kc=dict(slot_of_kc)):
                            last = None
                            for kc in range(NKC):
                                s2, _ = slot_of_kc[kc]
                                last = eng.matmul(
                                    bank[:],
                                    t_ring[:, s2 * 1024 + icl * 128: s2 * 1024 + (icl + 1) * 128],
                                    t_xTg[:, kc * 1024 + ps2 * 512: kc * 1024 + (ps2 + 1) * 512],
                                    start=(kc == 0), stop=(kc == NKC - 1))
                            return last

                        def back_key(delta):
                            sq = seq - delta
                            e2, i2, g2 = eix, iq, g
                            while sq < 0:
                                sq += 16
                                i2 -= 1
                                if i2 < 0:
                                    i2 = 3
                                    e2 -= 1
                                    if e2 < 0:
                                        e2 = NE - 1
                                        g2 -= 1
                                        if g2 < 0:
                                            return None
                            return (g2, e2, i2, sq // 8, sq % 8)

                        mw = list(w_xTg)
                        mw += [(q_slot[s2], 16 * n2) for (s2, n2) in slot_of_kc.values()]
                        bk = back_key(4)
                        if bk is not None and bk in tok_gelu:
                            mw.append(tok_gelu[bk])
                        tok_immP[key] = S("tensor", mk_im, waits=mw)

                        gb = seq % 3
                        gbuf = t_gelu[:, gb * 512:(gb + 1) * 512]
                        ew = [tok_immP[key]]
                        if gelu_slot_reader[gb] is not None:
                            ew.append(gelu_slot_reader[gb])
                        tok_gelu[key] = S("scalar", lambda e, gbuf=gbuf, bank=bank,
                                          eix=eix, icg=icg:
                                          e.activation(gbuf, bank[:], AF.Gelu,
                                                       bias=t_b1c[:, icg * 8 + eix:
                                                                  icg * 8 + eix + 1]),
                                          waits=ew + [w_b1c])
                        accsl = acc_full[:, icg * 1024 + ps2 * 512: icg * 1024 + (ps2 + 1) * 512]
                        sw = [tok_gelu[key], tok_eqtok[(g, eix)]]
                        if g == 1 and eix == 0:
                            sw.append(tok_y2ev[(0, 7, 1)])
                        tok_sel[key] = S("vector", lambda e, accsl=accsl, eqbuf=eqbuf,
                                         gbuf=gbuf, ps2=ps2:
                                         e.copy_predicated(
                                             accsl, eqbuf[:, ps2 * 512:(ps2 + 1) * 512],
                                             gbuf), waits=sw)
                        gelu_slot_reader[gb] = tok_sel[key]
                for kc in range(NKC):
                    s2, _ = slot_of_kc[kc]
                    slot_last_reader[s2] = tok_immP[(g, eix, iq, 1, 7)]
            eq_slot_reader[eix % 2] = tok_sel[(g, eix, 3, 1, 7)]

        last_sel = tok_sel[(g, NE - 1, 3, 1, 7)]

        wo_slot = {}
        for ic in range(32):
            L = load_ctr[0]
            s = L % 32
            nw_slot[s] += 1
            wo_slot[ic] = (s, nw_slot[s])
            lw = [slot_last_reader[s]] if slot_last_reader[s] is not None else []
            S("gpsimd", lambda e, s=s, ic=ic: e.dma_start(
                t_ring[:, s * 1024:(s + 1) * 1024],
                d_wo[ic * 128:(ic + 1) * 128, :]).then_inc(q_slot[s], 16), waits=lw)
            load_ctr[0] += 1

        for ttl in range(8):
            tt = g * 8 + ttl
            qi = ttl % 4
            tb = ttl % 2
            rw = [(q_spill, 16 * NTT)]
            if (g, ttl - 2) in tok_tail_res:
                rw.append(tok_tail_res[(g, ttl - 2)])
            elif g == 1 and ttl < 2:
                rw.append(tok_tail_res[(0, 6 + ttl)])
            n_xrel2[qi] += 1
            S("sync", lambda e, tt=tt, tb=tb, qi=qi: e.dma_start(
                t_xrel2[:, tb * C:(tb + 1) * C],
                d_x1[tt * 128:(tt + 1) * 128, :]).then_inc(q_xrel2[qi], 16), waits=rw)
            xr2w = (q_xrel2[qi], 16 * n_xrel2[qi])

            for ch in range(2):
                def mk_wo(eng, ttl=ttl, ch=ch, wo_slot=dict(wo_slot)):
                    last = None
                    for ic in range(32):
                        s2, _ = wo_slot[ic]
                        last = eng.matmul(
                            p_y2[ch][:],
                            acc_full[:, ic * 1024 + ttl * 128: ic * 1024 + (ttl + 1) * 128],
                            t_ring[:, s2 * 1024 + ch * 512: s2 * 1024 + (ch + 1) * 512],
                            start=(ic == 0), stop=(ic == 31))
                    return last
                ww = [last_sel]
                ww += [(q_slot[s2], 16 * n2) for (s2, n2) in wo_slot.values()]
                if ttl >= 1:
                    ww.append(tok_y2ev[(g, ttl - 1, ch)])
                elif g == 1:
                    ww.append(tok_y2ev[(0, 7, ch)])
                tok_y2mm[(g, ttl, ch)] = S("tensor", mk_wo, waits=ww)
                ew = [tok_y2mm[(g, ttl, ch)], w_setup, xr2w]
                if (g, ttl - 2) in tok_tail_last:
                    ew.append(tok_tail_last[(g, ttl - 2)])
                elif g == 1 and ttl < 2:
                    ew.append(tok_tail_last[(0, 6 + ttl)])
                tok_y2ev[(g, ttl, ch)] = S("vector", lambda e, ch=ch, tb=tb:
                                           e.scalar_tensor_tensor(
                                               t_y2[:, tb * C + ch * 512:
                                                    tb * C + (ch + 1) * 512],
                                               p_y2[ch][:], 1.0,
                                               t_bo[:, ch * 512:(ch + 1) * 512],
                                               op0=ALU.mult, op1=ALU.add), waits=ew)
            if ttl == 7:
                for ic in range(32):
                    s2, _ = wo_slot[ic]
                    slot_last_reader[s2] = tok_y2mm[(g, 7, 1)]

            yb = t_y2[:, tb * C:(tb + 1) * C]
            cb = t_ct2[:, tb * C:(tb + 1) * C]
            xr = t_xrel2[:, tb * C:(tb + 1) * C]
            sc = t_st2[:, tb * 8:(tb + 1) * 8]

            def ln_chain(first_w, src, cent, dst, eps, affine, sc=sc):
                tns = S("vector", lambda e, src=src, sc=sc: e.reduce_sum(
                    sc[:, 0:1], src, axis=AX.X, negate=True), waits=first_w)
                tnm = S("vector", lambda e, sc=sc: e.tensor_scalar(
                    sc[:, 1:2], sc[:, 0:1], 1.0 / C, None, op0=ALU.mult), waits=[tns])
                tct = S("vector", lambda e, src=src, cent=cent, sc=sc: e.tensor_scalar(
                    cent, src, sc[:, 1:2], None, op0=ALU.add), waits=[tnm])
                tsq = S("vector", lambda e, src=src, cent=cent: e.tensor_tensor(
                    src, cent, cent, op=ALU.mult), waits=[tct])
                tvr = S("vector", lambda e, src=src, sc=sc: e.reduce_sum(
                    sc[:, 2:3], src, axis=AX.X), waits=[tsq])
                tv2 = S("vector", lambda e, sc=sc, eps=eps: e.tensor_scalar(
                    sc[:, 3:4], sc[:, 2:3], 1.0 / C, eps, op0=ALU.mult, op1=ALU.add),
                    waits=[tvr])
                tln = S("scalar", lambda e, sc=sc: e.activation(sc[:, 4:5], sc[:, 3:4],
                                                                AF.Ln), waits=[tv2])
                trs = S("scalar", lambda e, sc=sc: e.activation(sc[:, 5:6], sc[:, 4:5],
                                                                AF.Exp, scale=-0.5),
                        waits=[tln])
                if affine:
                    tz = S("vector", lambda e, cent=cent, dst=dst, sc=sc:
                           e.scalar_tensor_tensor(dst, cent, sc[:, 5:6], t_gam[:],
                                                  op0=ALU.mult, op1=ALU.mult), waits=[trs])
                    tz = S("vector", lambda e, dst=dst: e.tensor_tensor(
                        dst, dst, t_bet[:], op=ALU.add), waits=[tz])
                else:
                    tz = S("vector", lambda e, cent=cent, dst=dst, sc=sc: e.tensor_scalar(
                        dst, cent, sc[:, 5:6], None, op0=ALU.mult), waits=[trs])
                return tz

            r1w = [tok_y2ev[(g, ttl, 0)], tok_y2ev[(g, ttl, 1)]]
            if n_out2[tb] > 0:
                r1w.append((q_out2[tb], n_out2[tb]))
            t_r1 = S("vector", lambda e, yb=yb, xr=xr: e.tensor_tensor(
                yb, yb, xr, op=ALU.add), waits=r1w)
            t_z = ln_chain([t_r1], yb, cb, yb, EPS2, True)
            t_r2 = S("vector", lambda e, yb=yb, xr=xr: e.tensor_tensor(
                yb, xr, yb, op=ALU.add), waits=[t_z])
            tok_tail_res[(g, ttl)] = t_r2
            t_o = ln_chain([t_r2], yb, cb, cb, EPS1, False)
            tok_tail_last[(g, ttl)] = t_o

            n_out2[tb] += 16
            S("sync", lambda e, tt=tt, cb=cb, tb=tb: e.dma_start(
                d_out[tt * 128:(tt + 1) * 128, :], cb).then_inc(q_out2[tb], 16),
              waits=[t_o])

    S("sync", lambda e: e.nop(), waits=[(q_out2[0], n_out2[0]), (q_out2[1], n_out2[1])])

    sched.finalize()
    for blk_ops in sched.blocks:
        with nc.Block() as block:
            for ename, deco in (("sync", block.sync), ("gpsimd", block.gpsimd),
                                ("scalar", block.scalar), ("vector", block.vector),
                                ("tensor", block.tensor)):
                def body(engine, ename=ename, blk_ops=blk_ops):
                    sched.emit_block(blk_ops, ename, engine, eng_sems)
                deco(body)

    b_st.close()
    st.close()
    return nc


_NC_CACHE = {}


def _get_nc():
    if "nc" not in _NC_CACHE:
        _NC_CACHE["nc"] = build_program()
    return _NC_CACHE["nc"]


def _build_masks(seq0):
    p = np.arange(128)[:, None]
    f = np.arange(W3)[None, :]
    out = np.zeros((3, 128, W3), dtype=np.float32)
    for v in range(3):
        tt = v if v < 2 else 2
        allowed = (f >= p) & (f <= p + 256)
        if v < 2:
            kseq = seq0 - 256 + tt * 128 + f
            allowed = allowed & (kseq >= 0)
        out[v] = np.where(allowed, 0.0, NEG)
    return out


def kernel(L0_attn_x, L0_attn_W_attn, L0_attn_b_attn, L0_attn_W_proj, L0_attn_b_proj,
           L0_ffn_Wr, L0_ffn_W1, L0_ffn_b1, L0_ffn_Wo, L0_ffn_bo, L0_ffn_gamma,
           L0_ffn_beta):
    x = np.ascontiguousarray(np.asarray(L0_attn_x, dtype=np.float32).reshape(B * T, C))
    wa = np.ascontiguousarray(np.asarray(L0_attn_W_attn, dtype=np.float32))
    ba = np.ascontiguousarray(np.asarray(L0_attn_b_attn, dtype=np.float32))
    wp = np.ascontiguousarray(np.asarray(L0_attn_W_proj, dtype=np.float32))
    bp = np.ascontiguousarray(np.asarray(L0_attn_b_proj, dtype=np.float32))
    wrt = np.ascontiguousarray(np.asarray(L0_ffn_Wr, dtype=np.float32).T)
    w1 = np.ascontiguousarray(np.asarray(L0_ffn_W1, dtype=np.float32))
    b1 = np.ascontiguousarray(np.asarray(L0_ffn_b1, dtype=np.float32))
    wo = np.ascontiguousarray(np.asarray(L0_ffn_Wo, dtype=np.float32))
    bo = np.ascontiguousarray(np.asarray(L0_ffn_bo, dtype=np.float32))
    gam = np.ascontiguousarray(np.asarray(L0_ffn_gamma, dtype=np.float32))
    bet = np.ascontiguousarray(np.asarray(L0_ffn_beta, dtype=np.float32))

    in_maps = []
    for c in range(NCORES):
        base = c * TCORE
        seq0 = (c % 4) * TCORE
        xe = np.zeros((NTQ * 128, C), dtype=np.float32)
        if seq0 == 0:
            xe[HALO:] = x[base:base + TCORE]
        else:
            xe[:] = x[base - HALO:base + TCORE]
        in_maps.append({
            "x_ext": xe, "w_attn": wa, "b_attn": ba, "w_proj": wp, "b_proj": bp,
            "wr_t": wrt, "w1": w1, "b1": b1, "wo": wo, "bo": bo, "gamma": gam,
            "beta": bet, "masks": _build_masks(seq0),
        })

    nc = _get_nc()
    res = run_bass_kernel_spmd(nc, in_maps, core_ids=list(range(NCORES)))
    out = np.concatenate([r["out"] for r in res.results], axis=0)
    return out.reshape(B, T, C)


# version 06
# speedup vs baseline: 1.0989x; 1.0148x over previous
import numpy as np
import ml_dtypes
from contextlib import ExitStack

import concourse.bass as bass
import concourse.mybir as mybir
from concourse.bass_utils import run_bass_kernel_spmd

F32 = mybir.dt.float32
BF16 = mybir.dt.bfloat16
U8 = mybir.dt.uint8
U32 = mybir.dt.uint32
AF = mybir.ActivationFunctionType
ALU = mybir.AluOpType
AX = mybir.AxisListType

NCORES = 8
B, T, C = 2, 8192, 1024
TCORE = 2048
HALO = 256
NTT = 16
NTQ = 18
NH, HD = 16, 64
W3 = 384
FF = 4096
NE = 8
NKC = 8
NEG = -1.0e30
EPS1 = 1e-5
EPS2 = 1e-12

ENGINES = ("sync", "gpsimd", "scalar", "vector", "tensor")


class Sched:

    def __init__(self):
        self.blocks = []
        self.cur = None

    def new_block(self):
        self.cur = []
        self.blocks.append(self.cur)

    def op(self, eng, fn, waits=()):
        rec = {"eng": eng, "fn": fn, "waits": [w for w in waits if w is not None],
               "inc": None}
        self.cur.append(rec)
        return rec

    def finalize(self):
        for blk in self.blocks:
            for rec in blk:
                for wtg in rec["waits"]:
                    if isinstance(wtg, dict) and wtg["inc"] is None:
                        wtg["inc"] = -1
        counters = {e: 0 for e in ENGINES}
        for blk in self.blocks:
            for rec in blk:
                if rec["inc"] == -1:
                    counters[rec["eng"]] += 1
                    rec["inc"] = counters[rec["eng"]]

    def emit_block(self, blk_ops, eng_name, engine, eng_sems):
        highwater = {}
        for rec in blk_ops:
            if rec["eng"] != eng_name:
                continue
            for wtg in rec["waits"]:
                if isinstance(wtg, dict):
                    sem, val, key = eng_sems[wtg["eng"]], wtg["inc"], ("e", wtg["eng"])
                else:
                    sem, val = wtg
                    key = ("s", id(sem))
                if highwater.get(key, -1) >= val:
                    continue
                highwater[key] = val
                engine.wait_ge(sem, val)
            inst = rec["fn"](engine)
            if rec["inc"] is not None:
                assert inst is not None, f"wait-target op returned no inst on {eng_name}"
                inst.then_inc(eng_sems[rec["eng"]], 1)


def build_program():
    nc = bass.Bass("TRN2", target_bir_lowering=False)

    d_x = nc.dram_tensor("x_ext", [NTQ * 128, C], F32, kind="ExternalInput").ap()
    d_wa = nc.dram_tensor("w_attn", [C, 3 * C], F32, kind="ExternalInput").ap()
    d_ba = nc.dram_tensor("b_attn", [3 * C], F32, kind="ExternalInput").ap()
    d_wp = nc.dram_tensor("w_proj", [C, C], F32, kind="ExternalInput").ap()
    d_bp = nc.dram_tensor("b_proj", [C], F32, kind="ExternalInput").ap()
    d_wrt = nc.dram_tensor("wr_t", [C, NE], F32, kind="ExternalInput").ap()
    d_w1 = nc.dram_tensor("w1", [NE, C, FF], F32, kind="ExternalInput").ap()
    d_b1 = nc.dram_tensor("b1", [NE, FF], F32, kind="ExternalInput").ap()
    d_wo = nc.dram_tensor("wo", [FF, C], F32, kind="ExternalInput").ap()
    d_bo = nc.dram_tensor("bo", [C], F32, kind="ExternalInput").ap()
    d_gam = nc.dram_tensor("gamma", [C], F32, kind="ExternalInput").ap()
    d_bet = nc.dram_tensor("beta", [C], F32, kind="ExternalInput").ap()
    d_msk = nc.dram_tensor("masks", [3, 128, W3], F32, kind="ExternalInput").ap()
    d_out = nc.dram_tensor("out", [TCORE, C], F32, kind="ExternalOutput").ap()
    d_x1 = nc.dram_tensor("x1_scratch", [TCORE, C], F32, kind="Internal").ap()
    d_idx = nc.dram_tensor("idx_scratch", [TCORE], F32, kind="Internal").ap()

    ident_bf = nc.inline_tensor(np.eye(128, dtype=ml_dtypes.bfloat16), "ident_bf").ap()
    ident_f32 = nc.inline_tensor(np.eye(128, dtype=np.float32), "ident_f32").ap()

    st = ExitStack()
    sb = lambda name, shape, dt: st.enter_context(nc.sbuf_tensor(name, shape, dt))
    psal = lambda name, shape, dt: st.enter_context(nc.psum_tensor(name, shape, dt))
    sem = lambda name: st.enter_context(nc.semaphore(name))

    stream = sb("stream", [128, NTT * C], F32)
    acc_full = stream[:].bitcast(BF16)
    t_identb = sb("identb", [128, 128], BF16)
    t_identf = sb("identf", [128, 128], F32)
    t_bqk = sb("bqk", [128, 16], F32)
    t_bqs = sb("bqs", [128, 8], F32)
    t_wrt = sb("wrt", [128, NKC * NE], F32)
    t_b1c = sb("b1cols", [128, NE * 32], F32)
    t_bo = sb("bo_bc", [128, C], F32)
    t_gam = sb("gam_bc", [128, C], F32)
    t_bet = sb("bet_bc", [128, C], F32)

    eng_sems = {e: sem("e_" + e) for e in ENGINES}
    q_setup = sem("q_setup")
    q_wsetup = sem("q_wsetup")
    q_x = [sem(f"q_x{i}") for i in range(6)]
    q_spill = sem("q_spill")
    q_idx = sem("q_idx")
    q_idxbc = sem("q_idxbc")
    q_b1c = sem("q_b1c")
    q_gsetup = sem("q_gsetup")
    q_xrel = [sem(f"q_xrel{i}") for i in range(4)]
    q_xrel2 = [sem(f"q_xrel2{i}") for i in range(4)]
    q_slot = [sem(f"q_slot{i}") for i in range(32)]
    q_out2 = [sem("q_out0"), sem("q_out1")]

    sched = Sched()

    a_st = ExitStack()
    asb = lambda name, shape, dt: a_st.enter_context(nc.sbuf_tensor(name, shape, dt))

    t_wab = asb("wab", [128, NKC * 3 * C], BF16)
    t_wpb = asb("wpb", [128, NKC * C], BF16)
    t_kring = asb("kring", [128, 3 * C], BF16)
    t_vring = asb("vring", [128, 3 * C], BF16)
    t_masks = asb("masks_sb", [128, 3 * W3], BF16)
    t_bvbc = asb("bv_bc", [128, C], BF16)
    t_bpbc = asb("bp_bc", [128, C], BF16)
    t_xc = asb("xc", [128, 2 * C], BF16)
    t_xTb = asb("xTb", [128, 2 * C], BF16)
    t_q = asb("qtile", [128, C], BF16)
    t_scf = asb("scf", [128, 4 * W3], F32)
    t_prob = asb("prob", [128, 2 * 4 * W3], BF16)
    t_probT = asb("probT", [128, 2 * 4 * W3], BF16)
    t_ao = asb("attn_out", [128, C], BF16)
    t_aon = asb("attn_out_n", [128, C], BF16)
    t_aoT = asb("attn_outT", [128, C], BF16)
    t_den = asb("den", [128, NH], F32)
    t_rden = asb("rden", [128, NH], F32)
    t_w = asb("wbuf", [128, C], F32)
    t_cent = asb("centbuf", [128, C], F32)
    t_xT32 = asb("xT32", [128, C], F32)
    t_lg = asb("lgs", [128, 8], F32)
    t_mx8 = asb("mx8", [128, 8], F32)
    t_idxu = asb("idxu", [128, 8], U32)
    t_idxf = asb("idxf", [128, 1], F32)
    t_stat = asb("stat", [128, 8], F32)

    apsal = lambda name, shape, dt: a_st.enter_context(nc.psum_tensor(name, shape, dt))
    p_tra = apsal("p_tra", [128, 1024], BF16)
    p_trb = apsal("p_trb", [128, 1024], BF16)
    p_v = apsal("p_v", [128, 512], F32)
    p_sc4 = [apsal(f"p_sc4_{i}", [128, 512], F32) for i in range(4)]
    p_misc = apsal("p_misc", [128, 512], F32)

    sched.new_block()
    S = sched.op

    n_setup = [0]
    n_gsetup = [0]

    def setup_dma(engname, fn):
        if engname == "gpsimd":
            S(engname, lambda e, fn=fn: fn(e).then_inc(q_gsetup, 16))
            n_gsetup[0] += 16
        else:
            S(engname, lambda e, fn=fn: fn(e).then_inc(q_setup, 16))
            n_setup[0] += 16

    setup_dma("sync", lambda e: e.dma_start(t_identb[:], ident_bf))
    setup_dma("sync", lambda e: e.dma_start(t_identf[:], ident_f32))
    for v in range(3):
        setup_dma("gpsimd", lambda e, v=v: e.dma_start(
            t_masks[:, v * W3:(v + 1) * W3], d_msk[v]))
    setup_dma("sync", lambda e: e.dma_start(
        t_bqk[:], bass.AP(d_ba.tensor, 0, [[1, 128], [128, 16]]),
        allow_slow_non_contiguous=True))

    def bcast_src(dap, off, n):
        return bass.AP(dap.tensor, off, [[0, 128], [1, n]])

    setup_dma("gpsimd", lambda e: e.dma_start(t_bvbc[:], bcast_src(d_ba, 2048, C)))
    setup_dma("gpsimd", lambda e: e.dma_start(t_bpbc[:], bcast_src(d_bp, 0, C)))
    setup_dma("sync", lambda e: e.dma_start(t_bo[:], bcast_src(d_bo, 0, C)))
    setup_dma("sync", lambda e: e.dma_start(t_gam[:], bcast_src(d_gam, 0, C)))
    setup_dma("sync", lambda e: e.dma_start(t_bet[:], bcast_src(d_bet, 0, C)))
    for kc in range(NKC):
        setup_dma("sync", lambda e, kc=kc: e.dma_start(
            t_wrt[:, kc * NE:(kc + 1) * NE], d_wrt[kc * 128:(kc + 1) * 128, :]))
    SETUP_TOTAL = n_setup[0]

    n_b1c = [0]
    for j in range(32):
        S("scalar", lambda e, j=j: e.dma_start(
            t_b1c[:, j * 8:(j + 1) * 8],
            bass.AP(d_b1.tensor, j * 128, [[1, 128], [FF, 8]]),
            allow_slow_non_contiguous=True).then_inc(q_b1c, 16))
        n_b1c[0] += 16
    w_b1c = (q_b1c, n_b1c[0])

    n_wsetup = [0]
    for kc in range(NKC):
        S("gpsimd", lambda e, kc=kc: e.dma_start(
            t_wab[:, kc * 3 * C:(kc + 1) * 3 * C],
            d_wa[kc * 128:(kc + 1) * 128, :]).then_inc(q_wsetup, 16))
        n_wsetup[0] += 16
    for kc in range(NKC):
        S("gpsimd", lambda e, kc=kc: e.dma_start(
            t_wpb[:, kc * C:(kc + 1) * C],
            d_wp[kc * 128:(kc + 1) * 128, :]).then_inc(q_wsetup, 16))
        n_wsetup[0] += 16
    w_setup = (q_setup, SETUP_TOTAL)
    w_gsetup = (q_gsetup, n_gsetup[0])
    w_wsetup = (q_wsetup, n_wsetup[0])

    tok_bqs = S("vector", lambda e: e.tensor_scalar(
        t_bqs[:], t_bqk[:, 0:8], 0.125, None, op0=ALU.mult), waits=[w_setup])

    trb_ev = [None, None]

    tok_cast = {}
    tok_trE = {}
    tok_kev = {}
    tok_vev = {}
    tok_qev = {}
    tok_kq_mm = {}
    tok_v_mm = {}
    tok_scg = {}
    tok_mask = {}
    tok_exp = {}
    tok_red = {}
    tok_recip = {}
    tok_ptpe = {}
    tok_ptev = {}
    tok_av = {}
    tok_aoev = {}
    tok_normao = {}
    tok_aoT_pe = {}
    tok_aoT_ev = {}
    tok_y1_mm = {}
    tok_proj_ev = {}
    tok_logit_ev = {}
    tok_xnew = {}
    tok_trP = {}
    tok_heads_last_pe = {}

    nx_load = [0] * 6

    for tq in range(NTQ):
        tt = tq - 2
        kslot = tq % 3

        qi = tq % 6
        nx_load[qi] += 16
        xw = (q_x[qi], nx_load[qi])
        dst = (stream[:, tt * C:(tt + 1) * C] if tq >= 2
               else stream[:, (14 + tq) * C:(15 + tq) * C])
        lw = [tok_cast[tq - 6]] if tq - 6 >= 0 else []
        if tq >= 16:
            lw.append(tok_cast[tq - 16])
        S("sync", lambda e, dst=dst, tq=tq, qi=qi: e.dma_start(
            dst, d_x[tq * 128:(tq + 1) * 128, :]).then_inc(q_x[qi], 16), waits=lw)

        cbuf = t_xc[:, (tq % 2) * C:(tq % 2 + 1) * C]
        cw = [xw]
        if tq - 2 >= 0:
            cw.append(tok_trP[tq - 2])
        tok_cast[tq] = S("scalar", lambda e, cbuf=cbuf, dst=dst: e.copy(cbuf, dst), waits=cw)

        def mk_tr(eng, cbuf=cbuf):
            last = None
            for j in range(NKC):
                last = eng.transpose(p_tra[:, j * 128:(j + 1) * 128],
                                     cbuf[:, j * 128:(j + 1) * 128], t_identb[:])
            return last
        tok_trP[tq] = S("tensor", mk_tr, waits=[tok_cast[tq], w_setup, trb_ev[0]])
        xTbuf = t_xTb[:, (tq % 2) * C:(tq % 2 + 1) * C]
        evw = [tok_trP[tq]]
        if tq - 2 >= 2:
            evw.append(tok_kq_mm[(tq - 2, "q", NKC - 1)])
        elif tq - 2 >= 0:
            evw.append(tok_v_mm[(tq - 2, 1)])
        tok_trE[tq] = S("vector", lambda e, xTbuf=xTbuf: e.tensor_copy(xTbuf, p_tra[:, 0:C]),
                        waits=evw)
        trb_ev[0] = tok_trE[tq]

        for fc in range(NKC):
            def mk_k(eng, fc=fc, xTbuf=xTbuf):
                last = None
                for kc in range(NKC):
                    last = eng.matmul(
                        p_sc4[fc % 4][:, (fc // 4) * 128:(fc // 4 + 1) * 128],
                        t_wab[:, kc * 3 * C + C + fc * 128: kc * 3 * C + C + (fc + 1) * 128],
                        xTbuf[:, kc * 128:(kc + 1) * 128],
                        start=(kc == 0), stop=(kc == NKC - 1))
                return last
            kw = [tok_trE[tq], w_wsetup]
            if fc == 0:
                if tq >= 3:
                    kw.append(tok_mask[(tq - 3, 3)])
                elif tq >= 1:
                    kw.append(tok_kev[(tq - 1, NKC - 1)])
            tok_kq_mm[(tq, "k", fc)] = S("tensor", mk_k, waits=kw)
        for fc in range(NKC):
            kdst = t_kring[:, kslot * C + fc * 128: kslot * C + (fc + 1) * 128]
            ew = [tok_kq_mm[(tq, "k", NKC - 1)], w_setup]
            if tq - 1 >= 2:
                ew.append(tok_heads_last_pe[tq - 1])
            tok_kev[(tq, fc)] = S("scalar", lambda e, kdst=kdst, fc=fc:
                                  e.activation(kdst, p_sc4[fc % 4][:, (fc // 4) * 128:(fc // 4 + 1) * 128],
                                               AF.Identity,
                                               bias=t_bqk[:, 8 + fc:9 + fc]), waits=ew)

        for vh in range(2):
            def mk_v(eng, vh=vh, xTbuf=xTbuf):
                last = None
                for kc in range(NKC):
                    last = eng.matmul(
                        p_v[:],
                        xTbuf[:, kc * 128:(kc + 1) * 128],
                        t_wab[:, kc * 3 * C + 2 * C + vh * 512: kc * 3 * C + 2 * C + (vh + 1) * 512],
                        start=(kc == 0), stop=(kc == NKC - 1))
                return last
            vw = [tok_trE[tq], w_wsetup]
            if vh == 1:
                vw.append(tok_vev[(tq, 0)])
            elif (tq - 1, 1) in tok_vev:
                vw.append(tok_vev[(tq - 1, 1)])
            tok_v_mm[(tq, vh)] = S("tensor", mk_v, waits=vw)
            vdst = t_vring[:, kslot * C + vh * 512: kslot * C + (vh + 1) * 512]
            ew = [tok_v_mm[(tq, vh)], w_setup, w_gsetup]
            if tq - 1 >= 2:
                ew.append(tok_heads_last_pe[tq - 1])
            tok_vev[(tq, vh)] = S("vector", lambda e, vdst=vdst, vh=vh:
                                  e.scalar_tensor_tensor(
                                      vdst, p_v[:], 1.0, t_bvbc[:, vh * 512:(vh + 1) * 512],
                                      op0=ALU.mult, op1=ALU.add), waits=ew)

        if tt < 0:
            continue

        for fc in range(NKC):
            def mk_q(eng, fc=fc, xTbuf=xTbuf):
                last = None
                for kc in range(NKC):
                    last = eng.matmul(
                        p_sc4[fc % 4][:, (fc // 4) * 128:(fc // 4 + 1) * 128],
                        t_wab[:, kc * 3 * C + fc * 128: kc * 3 * C + (fc + 1) * 128],
                        xTbuf[:, kc * 128:(kc + 1) * 128],
                        start=(kc == 0), stop=(kc == NKC - 1))
                return last
            qw = [tok_trE[tq], tok_bqs]
            if fc == 0:
                qw.append(tok_kev[(tq, NKC - 1)])
            tok_kq_mm[(tq, "q", fc)] = S("tensor", mk_q, waits=qw)
        for fc in range(NKC):
            qdst = t_q[:, fc * 128:(fc + 1) * 128]
            ew = [tok_kq_mm[(tq, "q", NKC - 1)]]
            if tt >= 1:
                ew.append(tok_scg[(tt - 1, 3)])
            tok_qev[(tq, fc)] = S("scalar", lambda e, qdst=qdst, fc=fc:
                                  e.activation(qdst, p_sc4[fc % 4][:, (fc // 4) * 128:(fc // 4 + 1) * 128],
                                               AF.Identity, scale=0.125,
                                               bias=t_bqs[:, fc:fc + 1]), waits=ew)

        mvar = tt if tt < 2 else 2
        for G in range(4):
            g2 = G % 2

            def mk_sc(eng, G=G, tq=tq):
                last = None
                for hloc in range(4):
                    h = G * 4 + hloc
                    fc, r0 = h // 2, (h % 2) * 64
                    for kr in range(3):
                        ks = (tq - 2 + kr) % 3
                        last = eng.matmul(
                            p_sc4[hloc][:, kr * 128:(kr + 1) * 128],
                            t_q[r0:r0 + 64, fc * 128:(fc + 1) * 128],
                            t_kring[r0:r0 + 64, ks * C + fc * 128: ks * C + (fc + 1) * 128],
                            start=True, stop=True)
                return last
            sw = [tok_qev[(tq, NKC - 1)]]
            if G >= 1:
                sw.append(tok_mask[(tt, G - 1)])
            elif tt >= 1:
                sw.append(tok_mask[(tt - 1, 3)])
            tok_scg[(tt, G)] = S("tensor", mk_sc, waits=sw)

            mw = [tok_scg[(tt, G)], w_setup, w_gsetup]
            if G == 0 and tt >= 1:
                mw.append(tok_exp[(tt - 1, 3)])
            pbuf = t_prob[:, g2 * 4 * W3:(g2 + 1) * 4 * W3]
            pk = (tt, G - 2) if G >= 2 else (tt - 1, G + 2)
            for hloc in range(4):
                mwh = list(mw) if hloc == 0 else []
                tok_mask[(tt, G)] = S("vector", lambda e, hloc=hloc, mvar=mvar:
                    e.tensor_tensor(
                        t_scf[:, hloc * W3:(hloc + 1) * W3],
                        p_sc4[hloc][:, 0:W3],
                        t_masks[:, mvar * W3:(mvar + 1) * W3], op=ALU.add),
                    waits=mwh + ([tok_exp[(tt, G - 1)]] if hloc == 0 and G >= 1 else []))
            for hloc in range(4):
                ew = [tok_mask[(tt, G)]]
                if hloc == 0 and pk in tok_red:
                    ew += [tok_red[pk], tok_ptpe[pk + (1,)]]
                tok_exp[(tt, G)] = S("scalar", lambda e, hloc=hloc, pbuf=pbuf: e.activation(
                    pbuf[:, hloc * W3:(hloc + 1) * W3],
                    t_scf[:, hloc * W3:(hloc + 1) * W3], AF.Exp), waits=ew)
            for hloc in range(4):
                tok_red[(tt, G)] = S("vector", lambda e, G=G, hloc=hloc, pbuf=pbuf:
                    e.reduce_sum(t_den[:, G * 4 + hloc:G * 4 + hloc + 1],
                                 pbuf[:, hloc * W3:(hloc + 1) * W3], axis=AX.X),
                    waits=[tok_exp[(tt, G)]])

            for sub in range(2):
                def mk_pt(eng, pbuf=pbuf, sub=sub):
                    last = None
                    for hl in range(2):
                        for kr in range(3):
                            last = eng.transpose(
                                p_trb[:, (hl * 3 + kr) * 128:(hl * 3 + kr + 1) * 128],
                                pbuf[:, (2 * sub + hl) * W3 + kr * 128:
                                     (2 * sub + hl) * W3 + (kr + 1) * 128],
                                t_identb[:])
                    return last
                tok_ptpe[(tt, G, sub)] = S("tensor", mk_pt,
                                           waits=[tok_exp[(tt, G)], trb_ev[1]])
                ptdst = t_probT[:, g2 * 4 * W3 + sub * 768: g2 * 4 * W3 + (sub + 1) * 768]
                ew = [tok_ptpe[(tt, G, sub)]]
                ak = (tt, G - 2, 2 * sub + 1) if G >= 2 else (tt - 1, G + 2, 2 * sub + 1)
                if ak in tok_av:
                    ew.append(tok_av[ak])
                tok_ptev[(tt, G, sub)] = S("vector", lambda e, ptdst=ptdst: e.tensor_copy(
                    ptdst, p_trb[:, 0:768]), waits=ew)
                trb_ev[1] = tok_ptev[(tt, G, sub)]

            for hloc in range(4):
                h = G * 4 + hloc
                sub = hloc // 2

                def mk_av(eng, g2=g2, hloc=hloc, h=h, G=G, tq=tq):
                    last = None
                    for kr in range(3):
                        ks = (tq - 2 + kr) % 3
                        last = eng.matmul(
                            p_misc[:, (G % 2) * 256 + hloc * 64:
                                   (G % 2) * 256 + (hloc + 1) * 64],
                            t_probT[:, g2 * 4 * W3 + hloc * W3 + kr * 128:
                                    g2 * 4 * W3 + hloc * W3 + (kr + 1) * 128],
                            t_vring[:, ks * C + h * 64: ks * C + (h + 1) * 64],
                            start=(kr == 0), stop=(kr == 2))
                    return last
                aw = [tok_ptev[(tt, G, sub)], tok_vev[(tq, 1)]]
                if G == 0 and hloc == 0 and tt >= 1:
                    aw.append(tok_logit_ev[tt - 1])
                pk2 = (tt, G - 1) if G >= 1 else (tt - 1, 3)
                if hloc == 0 and pk2 in tok_aoev:
                    aw.append(tok_aoev[pk2])
                tok_av[(tt, G, hloc)] = S("tensor", mk_av, waits=aw)

            tok_aoev[(tt, G)] = S("vector", lambda e, G=G: e.tensor_copy(
                t_ao[:, G * 256:(G + 1) * 256],
                p_misc[:, (G % 2) * 256:(G % 2 + 1) * 256]),
                waits=[tok_av[(tt, G, 3)]])

        tok_heads_last_pe[tq] = tok_av[(tt, 3, 3)]

        tok_recip[tt] = S("vector", lambda e: e.reciprocal(t_rden[:], t_den[:]),
                          waits=[tok_red[(tt, 3)]])
        nw = [tok_recip[tt], tok_aoev[(tt, 3)]]
        if tt >= 1:
            nw.append(tok_aoT_pe[tt - 1])
        tok_normao[tt] = S("vector", lambda e: e.tensor_tensor(
            t_aon[:].rearrange("p (h d) -> p h d", h=16),
            t_ao[:].rearrange("p (h d) -> p h d", h=16),
            t_rden[:].rearrange("p h -> p h ()").broadcast_to([128, NH, HD]),
            op=ALU.mult), waits=nw)

        def mk_aoT(eng):
            last = None
            for j in range(NKC):
                last = eng.transpose(p_trb[:, j * 128:(j + 1) * 128],
                                     t_aon[:, j * 128:(j + 1) * 128], t_identb[:])
            return last
        aw = [tok_normao[tt], trb_ev[1]]
        tok_aoT_pe[tt] = S("tensor", mk_aoT, waits=aw)
        ew = [tok_aoT_pe[tt]]
        if tt >= 1:
            ew.append(tok_y1_mm[(tt - 1, 1)])
        tok_aoT_ev[tt] = S("vector", lambda e: e.tensor_copy(t_aoT[:], p_trb[:, 0:C]),
                           waits=ew)
        trb_ev[1] = tok_aoT_ev[tt]

        for nh2 in range(2):
            def mk_pj(eng, nh2=nh2):
                last = None
                for kc in range(NKC):
                    last = eng.matmul(
                        p_misc[:],
                        t_aoT[:, kc * 128:(kc + 1) * 128],
                        t_wpb[:, kc * C + nh2 * 512: kc * C + (nh2 + 1) * 512],
                        start=(kc == 0), stop=(kc == NKC - 1))
                return last
            pw = [tok_aoT_ev[tt], w_wsetup]
            if nh2 == 1:
                pw.append(tok_proj_ev[(tt, 0)])
            tok_y1_mm[(tt, nh2)] = S("tensor", mk_pj, waits=pw)
            ew = [tok_y1_mm[(tt, nh2)], w_setup, w_gsetup]
            tok_proj_ev[(tt, nh2)] = S("vector", lambda e, nh2=nh2:
                                       e.scalar_tensor_tensor(
                                           t_w[:, nh2 * 512:(nh2 + 1) * 512], p_misc[:], 1.0,
                                           t_bpbc[:, nh2 * 512:(nh2 + 1) * 512],
                                           op0=ALU.mult, op1=ALU.add), waits=ew)

        tok_res = S("vector", lambda e, tt=tt: e.tensor_tensor(
            t_w[:], t_w[:], stream[:, tt * C:(tt + 1) * C], op=ALU.add),
            waits=[tok_proj_ev[(tt, 0)], tok_proj_ev[(tt, 1)]])

        tok_ns = S("vector", lambda e: e.reduce_sum(t_stat[:, 0:1], t_w[:], axis=AX.X,
                                                    negate=True), waits=[tok_res])
        tok_nm = S("vector", lambda e: e.tensor_scalar(t_stat[:, 1:2], t_stat[:, 0:1],
                                                       1.0 / C, None, op0=ALU.mult),
                   waits=[tok_ns])
        tok_ct = S("vector", lambda e: e.tensor_scalar(t_cent[:], t_w[:], t_stat[:, 1:2],
                                                       None, op0=ALU.add), waits=[tok_nm])
        tok_sq = S("vector", lambda e: e.tensor_tensor(t_w[:], t_cent[:], t_cent[:],
                                                       op=ALU.mult), waits=[tok_ct])
        tok_vr = S("vector", lambda e: e.reduce_sum(t_stat[:, 2:3], t_w[:], axis=AX.X),
                   waits=[tok_sq])
        tok_v2 = S("vector", lambda e: e.tensor_scalar(t_stat[:, 3:4], t_stat[:, 2:3],
                                                       1.0 / C, EPS1, op0=ALU.mult,
                                                       op1=ALU.add), waits=[tok_vr])
        tok_ln = S("scalar", lambda e: e.activation(t_stat[:, 4:5], t_stat[:, 3:4], AF.Ln),
                   waits=[tok_v2])
        tok_rs = S("scalar", lambda e: e.activation(t_stat[:, 5:6], t_stat[:, 4:5], AF.Exp,
                                                    scale=-0.5), waits=[tok_ln])
        tok_xnew[tt] = S("vector", lambda e, tt=tt: e.tensor_scalar(
            stream[:, tt * C:(tt + 1) * C], t_cent[:], t_stat[:, 5:6], None, op0=ALU.mult),
            waits=[tok_rs])

        S("sync", lambda e, tt=tt: e.dma_start(
            d_x1[tt * 128:(tt + 1) * 128, :],
            stream[:, tt * C:(tt + 1) * C]).then_inc(q_spill, 16), waits=[tok_xnew[tt]])

        tok_rtr_ev = {}
        for bt in range(2):
            def mk_rtr(eng, bt=bt, tt=tt):
                last = None
                for j in range(4):
                    last = eng.transpose(
                        p_misc[:, j * 128:(j + 1) * 128],
                        stream[:, tt * C + (bt * 4 + j) * 128: tt * C + (bt * 4 + j + 1) * 128],
                        t_identf[:])
                return last
            rw = [tok_xnew[tt]]
            rw.append(tok_proj_ev[(tt, 1)] if bt == 0 else tok_rtr_ev[0])
            tok_rtr_pe = S("tensor", mk_rtr, waits=rw)
            tok_rtr_ev[bt] = S("vector", lambda e, bt=bt: e.tensor_copy(
                t_xT32[:, bt * 512:(bt + 1) * 512], p_misc[:]), waits=[tok_rtr_pe])

        def mk_rmm(eng):
            last = None
            for kc in range(NKC):
                last = eng.matmul(p_misc[:, 0:NE],
                                  t_xT32[:, kc * 128:(kc + 1) * 128],
                                  t_wrt[:, kc * NE:(kc + 1) * NE],
                                  start=(kc == 0), stop=(kc == NKC - 1))
            return last
        tok_rmm = S("tensor", mk_rmm, waits=[tok_rtr_ev[0], tok_rtr_ev[1], w_setup])
        tok_logit_ev[tt] = S("vector", lambda e: e.tensor_copy(t_lg[:], p_misc[:, 0:NE]),
                             waits=[tok_rmm])
        tok_m8 = S("vector", lambda e: e.max(t_mx8[:], t_lg[:]), waits=[tok_logit_ev[tt]])
        tok_mi = S("vector", lambda e: e.max_index(t_idxu[:], t_mx8[:], t_lg[:]),
                   waits=[tok_m8])
        tok_if = S("vector", lambda e: e.tensor_scalar(t_idxf[:], t_idxu[:, 0:1], 0.0,
                                                       None, op0=ALU.add),
                   waits=[tok_mi] + ([(q_idx, 16 * tt)] if tt >= 1 else []))
        S("sync", lambda e, tt=tt: e.dma_start(
            d_idx.rearrange("(a p) -> a p", p=128)[tt].rearrange("(p o) -> p o", o=1),
            t_idxf[:]).then_inc(q_idx, 16), waits=[tok_if])

    S("sync", lambda e: e.nop(), waits=[(q_spill, 16 * NTT), (q_idx, 16 * NTT)])

    a_st.close()
    b_st = ExitStack()
    bsb = lambda name, shape, dt: b_st.enter_context(nc.sbuf_tensor(name, shape, dt))

    t_ring = bsb("ring", [128, 32 * 1024], BF16)
    t_xTg = bsb("xTg", [128, NKC * 1024], BF16)
    t_idxbc = bsb("idx_bc", [128, 1024], F32)
    t_eq = bsb("eq", [128, 2 * 1024], U8)
    t_gelu = bsb("gelu", [128, 3 * 512], BF16)
    t_xrel2 = bsb("xrel2", [128, 2 * C], F32)
    t_xrb = bsb("xrb", [128, 2 * C], BF16)
    t_y2 = bsb("y2", [128, 2 * C], F32)
    t_ct2 = bsb("ct2", [128, 2 * C], F32)
    t_st2 = bsb("stat2", [128, 16], F32)

    bpsal = lambda name, shape, dt: b_st.enter_context(nc.psum_tensor(name, shape, dt))
    p_i = [bpsal(f"p_i{j}", [128, 512], F32) for j in range(4)]
    p_y2 = [bpsal(f"p_y2{j}", [128, 512], F32) for j in range(2)]
    p_trB = bpsal("p_trB", [128, 1024], BF16)

    sched.new_block()
    S = sched.op

    nw_slot = [0] * 32
    slot_last_reader = [None] * 32
    load_ctr = [0]
    n_xrel = [0] * 4
    n_xrel2 = [0] * 4
    n_out2 = [0, 0]

    tok_sel = {}
    tok_gelu = {}
    tok_immP = {}
    tok_y2mm = {}
    tok_y2ev = {}
    tok_tail_last = {}
    tok_eqtok = {}
    tok_xTg_ev = {}
    tok_trB_tok = {}
    tok_tail_res = {}
    idxbc_readers = []
    gelu_slot_reader = [None, None, None]
    eq_slot_reader = [None, None]

    tok_acc0 = S("vector", lambda e: e.memset(acc_full[:], 0.0),
                 waits=[(q_spill, 16 * NTT), tok_logit_ev[NTT - 1]])

    for g in range(2):
        iw = [(q_idx, 16 * NTT)] + idxbc_readers
        idxbc_readers = []
        S("sync", lambda e, g=g: e.dma_start(
            t_idxbc[:], bass.AP(d_idx.tensor, g * 1024, [[0, 128], [1, 1024]])).then_inc(
            q_idxbc, 16), waits=iw)
        w_idxbc = (q_idxbc, 16 * (g + 1))

        for ttl in range(8):
            tt = g * 8 + ttl
            qi = ttl % 4
            rw = [(q_spill, 16 * NTT)]
            if (g, ttl - 2) in tok_trB_tok:
                rw.append(tok_trB_tok[(g, ttl - 2)])
            elif g == 1 and ttl < 2:
                rw.append(tok_trB_tok[(0, 6 + ttl)])
            n_xrel[qi] += 1
            S("gpsimd", lambda e, tt=tt, ttl=ttl, qi=qi: e.dma_start(
                t_xrb[:, (ttl % 2) * C:(ttl % 2 + 1) * C],
                d_x1[tt * 128:(tt + 1) * 128, :]).then_inc(q_xrel[qi], 16), waits=rw)
            xrw = (q_xrel[qi], 16 * n_xrel[qi])

            def mk_trB(eng, ttl=ttl):
                last = None
                for j in range(NKC):
                    last = eng.transpose(p_trB[:, j * 128:(j + 1) * 128],
                                         t_xrb[:, (ttl % 2) * C + j * 128:
                                               (ttl % 2) * C + (j + 1) * 128], t_identb[:])
                return last
            tw = [xrw, w_setup]
            if (g, ttl - 1) in tok_xTg_ev:
                tw.append(tok_xTg_ev[(g, ttl - 1)])
            elif g == 1 and ttl == 0:
                tw.append(tok_xTg_ev[(0, 7)])
            tok_trB_pe = S("tensor", mk_trB, waits=tw)
            tok_trB_tok[(g, ttl)] = tok_trB_pe
            xTg_v = t_xTg.rearrange("p (k t) -> p k t", k=NKC)
            ew = [tok_trB_pe]
            if g == 1 and ttl == 0:
                ew.append(tok_y2mm[(0, 7, 1)])
            tok_xTg_ev[(g, ttl)] = S("vector", lambda e, xTg_v=xTg_v, ttl=ttl:
                                     e.tensor_copy(xTg_v[:, :, ttl * 128:(ttl + 1) * 128],
                                                   p_trB[:, 0:C]), waits=ew)
        w_xTg = [tok_xTg_ev[(g, t2)] for t2 in range(8)]

        for eix in range(NE):
            eqbuf = t_eq[:, (eix % 2) * 1024:(eix % 2 + 1) * 1024]
            eqw = [w_idxbc]
            if eq_slot_reader[eix % 2] is not None:
                eqw.append(eq_slot_reader[eix % 2])
            tok_eqtok[(g, eix)] = S("vector", lambda e, eqbuf=eqbuf, eix=eix:
                                    e.tensor_scalar(eqbuf, t_idxbc[:], float(eix), None,
                                                    op0=ALU.is_equal), waits=eqw)
            idxbc_readers.append(tok_eqtok[(g, eix)])

            for iq in range(4):
                slot_of_kc = {}
                for kc in range(NKC):
                    L = load_ctr[0]
                    s = L % 32
                    nw_slot[s] += 1
                    slot_of_kc[kc] = (s, nw_slot[s])
                    lw = [slot_last_reader[s]] if slot_last_reader[s] is not None else []
                    S("gpsimd", lambda e, s=s, eix=eix, kc=kc, iq=iq: e.dma_start(
                        t_ring[:, s * 1024:(s + 1) * 1024],
                        d_w1[eix, kc * 128:(kc + 1) * 128,
                             iq * 1024:(iq + 1) * 1024]).then_inc(q_slot[s], 16), waits=lw)
                    load_ctr[0] += 1

                for ps2 in range(2):
                    for icl in range(8):
                        seq = ps2 * 8 + icl
                        bank = p_i[seq % 4]
                        icg = iq * 8 + icl
                        key = (g, eix, iq, ps2, icl)

                        def mk_im(eng, bank=bank, icl=icl, ps2=ps2,
                                  slot_of_kc=dict(slot_of_kc)):
                            last = None
                            for kc in range(NKC):
                                s2, _ = slot_of_kc[kc]
                                last = eng.matmul(
                                    bank[:],
                                    t_ring[:, s2 * 1024 + icl * 128: s2 * 1024 + (icl + 1) * 128],
                                    t_xTg[:, kc * 1024 + ps2 * 512: kc * 1024 + (ps2 + 1) * 512],
                                    start=(kc == 0), stop=(kc == NKC - 1))
                            return last

                        def back_key(delta):
                            sq = seq - delta
                            e2, i2, g2 = eix, iq, g
                            while sq < 0:
                                sq += 16
                                i2 -= 1
                                if i2 < 0:
                                    i2 = 3
                                    e2 -= 1
                                    if e2 < 0:
                                        e2 = NE - 1
                                        g2 -= 1
                                        if g2 < 0:
                                            return None
                            return (g2, e2, i2, sq // 8, sq % 8)

                        mw = list(w_xTg)
                        mw += [(q_slot[s2], 16 * n2) for (s2, n2) in slot_of_kc.values()]
                        bk = back_key(4)
                        if bk is not None and bk in tok_gelu:
                            mw.append(tok_gelu[bk])
                        tok_immP[key] = S("tensor", mk_im, waits=mw)

                        gb = seq % 3
                        gbuf = t_gelu[:, gb * 512:(gb + 1) * 512]
                        ew = [tok_immP[key]]
                        if gelu_slot_reader[gb] is not None:
                            ew.append(gelu_slot_reader[gb])
                        tok_gelu[key] = S("scalar", lambda e, gbuf=gbuf, bank=bank,
                                          eix=eix, icg=icg:
                                          e.activation(gbuf, bank[:], AF.Gelu,
                                                       bias=t_b1c[:, icg * 8 + eix:
                                                                  icg * 8 + eix + 1]),
                                          waits=ew + [w_b1c])
                        accsl = acc_full[:, icg * 1024 + ps2 * 512: icg * 1024 + (ps2 + 1) * 512]
                        sw = [tok_gelu[key], tok_eqtok[(g, eix)]]
                        if g == 1 and eix == 0:
                            sw.append(tok_y2ev[(0, 7, 1)])
                        tok_sel[key] = S("vector", lambda e, accsl=accsl, eqbuf=eqbuf,
                                         gbuf=gbuf, ps2=ps2:
                                         e.copy_predicated(
                                             accsl, eqbuf[:, ps2 * 512:(ps2 + 1) * 512],
                                             gbuf), waits=sw)
                        gelu_slot_reader[gb] = tok_sel[key]
                for kc in range(NKC):
                    s2, _ = slot_of_kc[kc]
                    slot_last_reader[s2] = tok_immP[(g, eix, iq, 1, 7)]
            eq_slot_reader[eix % 2] = tok_sel[(g, eix, 3, 1, 7)]

        last_sel = tok_sel[(g, NE - 1, 3, 1, 7)]

        wo_slot = {}
        for ic in range(32):
            L = load_ctr[0]
            s = L % 32
            nw_slot[s] += 1
            wo_slot[ic] = (s, nw_slot[s])
            lw = [slot_last_reader[s]] if slot_last_reader[s] is not None else []
            S("gpsimd", lambda e, s=s, ic=ic: e.dma_start(
                t_ring[:, s * 1024:(s + 1) * 1024],
                d_wo[ic * 128:(ic + 1) * 128, :]).then_inc(q_slot[s], 16), waits=lw)
            load_ctr[0] += 1

        for ttl in range(8):
            tt = g * 8 + ttl
            qi = ttl % 4
            tb = ttl % 2
            rw = [(q_spill, 16 * NTT)]
            if (g, ttl - 2) in tok_tail_res:
                rw.append(tok_tail_res[(g, ttl - 2)])
            elif g == 1 and ttl < 2:
                rw.append(tok_tail_res[(0, 6 + ttl)])
            n_xrel2[qi] += 1
            S("sync", lambda e, tt=tt, tb=tb, qi=qi: e.dma_start(
                t_xrel2[:, tb * C:(tb + 1) * C],
                d_x1[tt * 128:(tt + 1) * 128, :]).then_inc(q_xrel2[qi], 16), waits=rw)
            xr2w = (q_xrel2[qi], 16 * n_xrel2[qi])

            for ch in range(2):
                def mk_wo(eng, ttl=ttl, ch=ch, wo_slot=dict(wo_slot)):
                    last = None
                    for ic in range(32):
                        s2, _ = wo_slot[ic]
                        last = eng.matmul(
                            p_y2[ch][:],
                            acc_full[:, ic * 1024 + ttl * 128: ic * 1024 + (ttl + 1) * 128],
                            t_ring[:, s2 * 1024 + ch * 512: s2 * 1024 + (ch + 1) * 512],
                            start=(ic == 0), stop=(ic == 31))
                    return last
                ww = [last_sel]
                ww += [(q_slot[s2], 16 * n2) for (s2, n2) in wo_slot.values()]
                if ttl >= 1:
                    ww.append(tok_y2ev[(g, ttl - 1, ch)])
                elif g == 1:
                    ww.append(tok_y2ev[(0, 7, ch)])
                tok_y2mm[(g, ttl, ch)] = S("tensor", mk_wo, waits=ww)
                ew = [tok_y2mm[(g, ttl, ch)], w_setup, xr2w]
                if (g, ttl - 2) in tok_tail_last:
                    ew.append(tok_tail_last[(g, ttl - 2)])
                elif g == 1 and ttl < 2:
                    ew.append(tok_tail_last[(0, 6 + ttl)])
                tok_y2ev[(g, ttl, ch)] = S("vector", lambda e, ch=ch, tb=tb:
                                           e.scalar_tensor_tensor(
                                               t_y2[:, tb * C + ch * 512:
                                                    tb * C + (ch + 1) * 512],
                                               p_y2[ch][:], 1.0,
                                               t_bo[:, ch * 512:(ch + 1) * 512],
                                               op0=ALU.mult, op1=ALU.add), waits=ew)
            if ttl == 7:
                for ic in range(32):
                    s2, _ = wo_slot[ic]
                    slot_last_reader[s2] = tok_y2mm[(g, 7, 1)]

            yb = t_y2[:, tb * C:(tb + 1) * C]
            cb = t_ct2[:, tb * C:(tb + 1) * C]
            xr = t_xrel2[:, tb * C:(tb + 1) * C]
            sc = t_st2[:, tb * 8:(tb + 1) * 8]

            def ln_chain(first_w, src, cent, dst, eps, affine, sc=sc):
                tns = S("vector", lambda e, src=src, sc=sc: e.reduce_sum(
                    sc[:, 0:1], src, axis=AX.X, negate=True), waits=first_w)
                tnm = S("vector", lambda e, sc=sc: e.tensor_scalar(
                    sc[:, 1:2], sc[:, 0:1], 1.0 / C, None, op0=ALU.mult), waits=[tns])
                tct = S("vector", lambda e, src=src, cent=cent, sc=sc: e.tensor_scalar(
                    cent, src, sc[:, 1:2], None, op0=ALU.add), waits=[tnm])
                tsq = S("vector", lambda e, src=src, cent=cent: e.tensor_tensor(
                    src, cent, cent, op=ALU.mult), waits=[tct])
                tvr = S("vector", lambda e, src=src, sc=sc: e.reduce_sum(
                    sc[:, 2:3], src, axis=AX.X), waits=[tsq])
                tv2 = S("vector", lambda e, sc=sc, eps=eps: e.tensor_scalar(
                    sc[:, 3:4], sc[:, 2:3], 1.0 / C, eps, op0=ALU.mult, op1=ALU.add),
                    waits=[tvr])
                tln = S("scalar", lambda e, sc=sc: e.activation(sc[:, 4:5], sc[:, 3:4],
                                                                AF.Ln), waits=[tv2])
                trs = S("scalar", lambda e, sc=sc: e.activation(sc[:, 5:6], sc[:, 4:5],
                                                                AF.Exp, scale=-0.5),
                        waits=[tln])
                if affine:
                    tz = S("vector", lambda e, cent=cent, dst=dst, sc=sc:
                           e.scalar_tensor_tensor(dst, cent, sc[:, 5:6], t_gam[:],
                                                  op0=ALU.mult, op1=ALU.mult), waits=[trs])
                    tz = S("vector", lambda e, dst=dst: e.tensor_tensor(
                        dst, dst, t_bet[:], op=ALU.add), waits=[tz])
                else:
                    tz = S("vector", lambda e, cent=cent, dst=dst, sc=sc: e.tensor_scalar(
                        dst, cent, sc[:, 5:6], None, op0=ALU.mult), waits=[trs])
                return tz

            r1w = [tok_y2ev[(g, ttl, 0)], tok_y2ev[(g, ttl, 1)]]
            if n_out2[tb] > 0:
                r1w.append((q_out2[tb], n_out2[tb]))
            t_r1 = S("vector", lambda e, yb=yb, xr=xr: e.tensor_tensor(
                yb, yb, xr, op=ALU.add), waits=r1w)
            t_z = ln_chain([t_r1], yb, cb, yb, EPS2, True)
            t_r2 = S("vector", lambda e, yb=yb, xr=xr: e.tensor_tensor(
                yb, xr, yb, op=ALU.add), waits=[t_z])
            tok_tail_res[(g, ttl)] = t_r2
            t_o = ln_chain([t_r2], yb, cb, cb, EPS1, False)
            tok_tail_last[(g, ttl)] = t_o

            n_out2[tb] += 16
            S("sync", lambda e, tt=tt, cb=cb, tb=tb: e.dma_start(
                d_out[tt * 128:(tt + 1) * 128, :], cb).then_inc(q_out2[tb], 16),
              waits=[t_o])

    S("sync", lambda e: e.nop(), waits=[(q_out2[0], n_out2[0]), (q_out2[1], n_out2[1])])

    sched.finalize()
    for blk_ops in sched.blocks:
        with nc.Block() as block:
            for ename, deco in (("sync", block.sync), ("gpsimd", block.gpsimd),
                                ("scalar", block.scalar), ("vector", block.vector),
                                ("tensor", block.tensor)):
                def body(engine, ename=ename, blk_ops=blk_ops):
                    sched.emit_block(blk_ops, ename, engine, eng_sems)
                deco(body)

    b_st.close()
    st.close()
    return nc


_NC_CACHE = {}


def _get_nc():
    if "nc" not in _NC_CACHE:
        _NC_CACHE["nc"] = build_program()
    return _NC_CACHE["nc"]


def _build_masks(seq0):
    p = np.arange(128)[:, None]
    f = np.arange(W3)[None, :]
    out = np.zeros((3, 128, W3), dtype=np.float32)
    for v in range(3):
        tt = v if v < 2 else 2
        allowed = (f >= p) & (f <= p + 256)
        if v < 2:
            kseq = seq0 - 256 + tt * 128 + f
            allowed = allowed & (kseq >= 0)
        out[v] = np.where(allowed, 0.0, NEG)
    return out


def kernel(L0_attn_x, L0_attn_W_attn, L0_attn_b_attn, L0_attn_W_proj, L0_attn_b_proj,
           L0_ffn_Wr, L0_ffn_W1, L0_ffn_b1, L0_ffn_Wo, L0_ffn_bo, L0_ffn_gamma,
           L0_ffn_beta):
    x = np.ascontiguousarray(np.asarray(L0_attn_x, dtype=np.float32).reshape(B * T, C))
    wa = np.ascontiguousarray(np.asarray(L0_attn_W_attn, dtype=np.float32))
    ba = np.ascontiguousarray(np.asarray(L0_attn_b_attn, dtype=np.float32))
    wp = np.ascontiguousarray(np.asarray(L0_attn_W_proj, dtype=np.float32))
    bp = np.ascontiguousarray(np.asarray(L0_attn_b_proj, dtype=np.float32))
    wrt = np.ascontiguousarray(np.asarray(L0_ffn_Wr, dtype=np.float32).T)
    w1 = np.ascontiguousarray(np.asarray(L0_ffn_W1, dtype=np.float32))
    b1 = np.ascontiguousarray(np.asarray(L0_ffn_b1, dtype=np.float32))
    wo = np.ascontiguousarray(np.asarray(L0_ffn_Wo, dtype=np.float32))
    bo = np.ascontiguousarray(np.asarray(L0_ffn_bo, dtype=np.float32))
    gam = np.ascontiguousarray(np.asarray(L0_ffn_gamma, dtype=np.float32))
    bet = np.ascontiguousarray(np.asarray(L0_ffn_beta, dtype=np.float32))

    in_maps = []
    for c in range(NCORES):
        base = c * TCORE
        seq0 = (c % 4) * TCORE
        xe = np.zeros((NTQ * 128, C), dtype=np.float32)
        if seq0 == 0:
            xe[HALO:] = x[base:base + TCORE]
        else:
            xe[:] = x[base - HALO:base + TCORE]
        in_maps.append({
            "x_ext": xe, "w_attn": wa, "b_attn": ba, "w_proj": wp, "b_proj": bp,
            "wr_t": wrt, "w1": w1, "b1": b1, "wo": wo, "bo": bo, "gamma": gam,
            "beta": bet, "masks": _build_masks(seq0),
        })

    nc = _get_nc()
    res = run_bass_kernel_spmd(nc, in_maps, core_ids=list(range(NCORES)))
    out = np.concatenate([r["out"] for r in res.results], axis=0)
    return out.reshape(B, T, C)


# version 07
# speedup vs baseline: 1.2237x; 1.1135x over previous
import numpy as np
import ml_dtypes
from contextlib import ExitStack

import concourse.bass as bass
import concourse.mybir as mybir
from concourse.bass_utils import run_bass_kernel_spmd

F32 = mybir.dt.float32
BF16 = mybir.dt.bfloat16
U8 = mybir.dt.uint8
U32 = mybir.dt.uint32
AF = mybir.ActivationFunctionType
ALU = mybir.AluOpType
AX = mybir.AxisListType

NCORES = 8
B, T, C = 2, 8192, 1024
TCORE = 2048
HALO = 256
NTT = 16
NTQ = 18
NH, HD = 16, 64
W3 = 384
FF = 4096
NE = 8
NKC = 8
NEG = -1.0e30
NSL = 8
EPS1 = 1e-5
EPS2 = 1e-12

ENGINES = ("sync", "gpsimd", "scalar", "vector", "tensor")


class Sched:

    def __init__(self):
        self.blocks = []
        self.cur = None

    def new_block(self):
        self.cur = []
        self.blocks.append(self.cur)

    def op(self, eng, fn, waits=()):
        rec = {"eng": eng, "fn": fn, "waits": [w for w in waits if w is not None],
               "inc": None}
        self.cur.append(rec)
        return rec

    def finalize(self):
        for blk in self.blocks:
            for rec in blk:
                for wtg in rec["waits"]:
                    if isinstance(wtg, dict) and wtg["inc"] is None:
                        wtg["inc"] = -1
        counters = {e: 0 for e in ENGINES}
        for blk in self.blocks:
            for rec in blk:
                if rec["inc"] == -1:
                    counters[rec["eng"]] += 1
                    rec["inc"] = counters[rec["eng"]]

    def emit_block(self, blk_ops, eng_name, engine, eng_sems):
        highwater = {}
        for rec in blk_ops:
            if rec["eng"] != eng_name:
                continue
            for wtg in rec["waits"]:
                if isinstance(wtg, dict):
                    sem, val, key = eng_sems[wtg["eng"]], wtg["inc"], ("e", wtg["eng"])
                else:
                    sem, val = wtg
                    key = ("s", id(sem))
                if highwater.get(key, -1) >= val:
                    continue
                highwater[key] = val
                engine.wait_ge(sem, val)
            inst = rec["fn"](engine)
            if rec["inc"] is not None:
                assert inst is not None, f"wait-target op returned no inst on {eng_name}"
                inst.then_inc(eng_sems[rec["eng"]], 1)


def build_program():
    nc = bass.Bass("TRN2", target_bir_lowering=False)

    d_x = nc.dram_tensor("x_ext", [NTQ * 128, C], F32, kind="ExternalInput").ap()
    d_wa = nc.dram_tensor("w_attn", [C, 3 * C], F32, kind="ExternalInput").ap()
    d_ba = nc.dram_tensor("b_attn", [3 * C], F32, kind="ExternalInput").ap()
    d_wp = nc.dram_tensor("w_proj", [C, C], F32, kind="ExternalInput").ap()
    d_bp = nc.dram_tensor("b_proj", [C], F32, kind="ExternalInput").ap()
    d_wrt = nc.dram_tensor("wr_t", [C, NE], F32, kind="ExternalInput").ap()
    d_w1 = nc.dram_tensor("w1", [NE, C, FF], F32, kind="ExternalInput").ap()
    d_b1 = nc.dram_tensor("b1", [NE, FF], F32, kind="ExternalInput").ap()
    d_wo = nc.dram_tensor("wo", [FF, C], F32, kind="ExternalInput").ap()
    d_bo = nc.dram_tensor("bo", [C], F32, kind="ExternalInput").ap()
    d_gam = nc.dram_tensor("gamma", [C], F32, kind="ExternalInput").ap()
    d_bet = nc.dram_tensor("beta", [C], F32, kind="ExternalInput").ap()
    d_msk = nc.dram_tensor("masks", [3, 128, W3], F32, kind="ExternalInput").ap()
    d_out = nc.dram_tensor("out", [TCORE, C], F32, kind="ExternalOutput").ap()
    d_x1 = nc.dram_tensor("x1_scratch", [TCORE, C], F32, kind="Internal").ap()
    d_idx = nc.dram_tensor("idx_scratch", [TCORE], F32, kind="Internal").ap()

    ident_bf = nc.inline_tensor(np.eye(128, dtype=ml_dtypes.bfloat16), "ident_bf").ap()
    ident_f32 = nc.inline_tensor(np.eye(128, dtype=np.float32), "ident_f32").ap()

    st = ExitStack()
    sb = lambda name, shape, dt: st.enter_context(nc.sbuf_tensor(name, shape, dt))
    psal = lambda name, shape, dt: st.enter_context(nc.psum_tensor(name, shape, dt))
    sem = lambda name: st.enter_context(nc.semaphore(name))

    t_identb = sb("identb", [128, 128], BF16)
    t_identf = sb("identf", [128, 128], F32)
    t_bqk = sb("bqk", [128, 16], F32)
    t_bqs = sb("bqs", [128, 8], F32)
    t_wrt = sb("wrt", [128, NKC * NE], F32)
    t_b1c = sb("b1cols", [128, NE * 32], F32)
    t_bo = sb("bo_bc", [128, C], F32)
    t_gam = sb("gam_bc", [128, C], F32)
    t_bet = sb("bet_bc", [128, C], F32)

    eng_sems = {e: sem("e_" + e) for e in ENGINES}
    q_setup = sem("q_setup")
    q_wsetup = sem("q_wsetup")
    q_x = [sem(f"q_x{i}") for i in range(6)]
    q_spill = sem("q_spill")
    q_idx = sem("q_idx")
    q_idxbc = sem("q_idxbc")
    q_b1c = sem("q_b1c")
    q_gsetup = sem("q_gsetup")
    q_xrel = [sem(f"q_xrel{i}") for i in range(4)]
    q_xrel2 = [sem(f"q_xrel2{i}") for i in range(4)]
    q_slot = [sem(f"q_slot{i}") for i in range(32)]
    q_out2 = [sem("q_out0"), sem("q_out1")]

    sched = Sched()

    a_st = ExitStack()
    asb = lambda name, shape, dt: a_st.enter_context(nc.sbuf_tensor(name, shape, dt))

    t_wab = asb("wab", [128, NKC * 3 * C], BF16)
    t_wpb = asb("wpb", [128, NKC * C], BF16)
    t_kring = asb("kring", [128, NSL * C], BF16)
    t_vring = asb("vring", [128, NSL * C], BF16)
    t_masks = asb("masks_sb", [128, 3 * W3], BF16)
    t_bvbc = asb("bv_bc", [128, C], BF16)
    t_bpbc = asb("bp_bc", [128, C], BF16)
    t_xc = asb("xc", [128, 2 * C], BF16)
    t_xTb4 = asb("xTb4", [128, NKC * 512], BF16)
    t_q4 = asb("q4", [128, NKC * 512], BF16)
    stream = asb("stream", [128, NSL * C], F32)
    t_scf = asb("scf", [128, 4 * W3], F32)
    t_prob = asb("prob", [128, 2 * 4 * W3], BF16)
    t_probT = asb("probT", [128, 2 * 4 * W3], BF16)
    t_ao = asb("attn_out", [128, C], BF16)
    t_aon = asb("attn_out_n", [128, C], BF16)
    t_aoT = asb("attn_outT", [128, C], BF16)
    t_den = asb("den", [128, NH], F32)
    t_rden = asb("rden", [128, NH], F32)
    t_w = asb("wbuf", [128, C], F32)
    t_cent = asb("centbuf", [128, C], F32)
    t_xT32 = asb("xT32", [128, C], F32)
    t_lg = asb("lgs", [128, 8], F32)
    t_mx8 = asb("mx8", [128, 8], F32)
    t_idxu = asb("idxu", [128, 8], U32)
    t_idxf = asb("idxf", [128, 1], F32)
    t_stat = asb("stat", [128, 8], F32)

    apsal = lambda name, shape, dt: a_st.enter_context(nc.psum_tensor(name, shape, dt))
    p_tra = apsal("p_tra", [128, 1024], BF16)
    p_trb = apsal("p_trb", [128, 1024], BF16)
    p_v = apsal("p_v", [128, 512], F32)
    p_sc4 = [apsal(f"p_sc4_{i}", [128, 512], F32) for i in range(4)]
    p_misc = apsal("p_misc", [128, 512], F32)

    sched.new_block()
    S = sched.op

    n_setup = [0]
    n_gsetup = [0]

    def setup_dma(engname, fn):
        if engname == "gpsimd":
            S(engname, lambda e, fn=fn: fn(e).then_inc(q_gsetup, 16))
            n_gsetup[0] += 16
        else:
            S(engname, lambda e, fn=fn: fn(e).then_inc(q_setup, 16))
            n_setup[0] += 16

    setup_dma("sync", lambda e: e.dma_start(t_identb[:], ident_bf))
    setup_dma("sync", lambda e: e.dma_start(t_identf[:], ident_f32))
    for v in range(3):
        setup_dma("gpsimd", lambda e, v=v: e.dma_start(
            t_masks[:, v * W3:(v + 1) * W3], d_msk[v]))
    setup_dma("sync", lambda e: e.dma_start(
        t_bqk[:], bass.AP(d_ba.tensor, 0, [[1, 128], [128, 16]]),
        allow_slow_non_contiguous=True))

    def bcast_src(dap, off, n):
        return bass.AP(dap.tensor, off, [[0, 128], [1, n]])

    setup_dma("gpsimd", lambda e: e.dma_start(t_bvbc[:], bcast_src(d_ba, 2048, C)))
    setup_dma("gpsimd", lambda e: e.dma_start(t_bpbc[:], bcast_src(d_bp, 0, C)))
    setup_dma("sync", lambda e: e.dma_start(t_bo[:], bcast_src(d_bo, 0, C)))
    setup_dma("sync", lambda e: e.dma_start(t_gam[:], bcast_src(d_gam, 0, C)))
    setup_dma("sync", lambda e: e.dma_start(t_bet[:], bcast_src(d_bet, 0, C)))
    for kc in range(NKC):
        setup_dma("sync", lambda e, kc=kc: e.dma_start(
            t_wrt[:, kc * NE:(kc + 1) * NE], d_wrt[kc * 128:(kc + 1) * 128, :]))
    SETUP_TOTAL = n_setup[0]

    n_b1c = [0]
    for j in range(32):
        S("scalar", lambda e, j=j: e.dma_start(
            t_b1c[:, j * 8:(j + 1) * 8],
            bass.AP(d_b1.tensor, j * 128, [[1, 128], [FF, 8]]),
            allow_slow_non_contiguous=True).then_inc(q_b1c, 16))
        n_b1c[0] += 16
    w_b1c = (q_b1c, n_b1c[0])

    n_wsetup = [0]
    for kc in range(NKC):
        S("gpsimd", lambda e, kc=kc: e.dma_start(
            t_wab[:, kc * 3 * C:(kc + 1) * 3 * C],
            d_wa[kc * 128:(kc + 1) * 128, :]).then_inc(q_wsetup, 16))
        n_wsetup[0] += 16
    for kc in range(NKC):
        S("gpsimd", lambda e, kc=kc: e.dma_start(
            t_wpb[:, kc * C:(kc + 1) * C],
            d_wp[kc * 128:(kc + 1) * 128, :]).then_inc(q_wsetup, 16))
        n_wsetup[0] += 16
    w_setup = (q_setup, SETUP_TOTAL)
    w_gsetup = (q_gsetup, n_gsetup[0])
    w_wsetup = (q_wsetup, n_wsetup[0])

    tok_bqs = S("vector", lambda e: e.tensor_scalar(
        t_bqs[:], t_bqk[:, 0:8], 0.125, None, op0=ALU.mult), waits=[w_setup])

    trb_ev = [None, None]

    tok_cast = {}
    tok_trE = {}
    tok_kev = {}
    tok_vev = {}
    tok_qev = {}
    tok_kmm = {}
    tok_qmm = {}
    tok_v_mm = {}
    tok_scg = {}
    tok_mask = {}
    tok_exp = {}
    tok_red = {}
    tok_recip = {}
    tok_ptpe = {}
    tok_ptev = {}
    tok_av = {}
    tok_aoev = {}
    tok_normao = {}
    tok_aoT_pe = {}
    tok_aoT_ev = {}
    tok_y1_mm = {}
    tok_proj_ev = {}
    tok_logit_ev = {}
    tok_xnew = {}
    tok_trP = {}

    nx_load = [0] * 6
    n_spill = [0]
    avq = {}
    chunk_tiles = [list(range(4 * c2, min(4 * c2 + 4, NTQ))) for c2 in range(5)]

    for ch, tiles in enumerate(chunk_tiles):
        Wc = len(tiles) * 128
        qoff = 256 if ch == 0 else 0

        for ti, tq in enumerate(tiles):
            sl = tq % NSL
            qi = tq % 6
            nx_load[qi] += 16
            xw = (q_x[qi], nx_load[qi])
            dst = stream[:, sl * C:(sl + 1) * C]
            lw = []
            if tq - NSL >= 2:
                tt_old = tq - NSL - 2
                lw.append((q_spill, 16 * n_spill[0]))
                lw.append(tok_logit_ev[tt_old])
            elif tq - NSL >= 0:
                lw.append(tok_cast[tq - NSL])
            S("sync", lambda e, dst=dst, tq=tq, qi=qi: e.dma_start(
                dst, d_x[tq * 128:(tq + 1) * 128, :]).then_inc(q_x[qi], 16), waits=lw)

            cbuf = t_xc[:, (tq % 2) * C:(tq % 2 + 1) * C]
            cw = [xw]
            if tq - 2 >= 0:
                cw.append(tok_trP[tq - 2])
            tok_cast[tq] = S("scalar", lambda e, cbuf=cbuf, dst=dst: e.copy(cbuf, dst),
                             waits=cw)

            def mk_tr(eng, cbuf=cbuf):
                last = None
                for j in range(NKC):
                    last = eng.transpose(p_tra[:, j * 128:(j + 1) * 128],
                                         cbuf[:, j * 128:(j + 1) * 128], t_identb[:])
                return last
            trw = [tok_cast[tq], w_setup]
            if tq >= 1:
                trw.append(tok_trE[tq - 1])
            tok_trP[tq] = S("tensor", mk_tr, waits=trw)

            xdst = (t_xTb4[:].rearrange("p (kc t) -> p kc t", kc=NKC)
                    [:, :, ti * 128:(ti + 1) * 128])
            evw = [tok_trP[tq]]
            if ch >= 1 and ti == 0:
                evw.append(tok_qmm[(ch - 1, NKC - 1)])
            tok_trE[tq] = S("vector", lambda e, xdst=xdst: e.tensor_copy(
                xdst, p_tra[:, 0:C].rearrange("p (kc t) -> p kc t", kc=NKC)),
                waits=evw)

        for fc in range(NKC):
            def mk_k(eng, fc=fc, Wc=Wc):
                last = None
                for kc in range(NKC):
                    last = eng.matmul(
                        p_sc4[fc % 4][:, 0:Wc],
                        t_wab[:, kc * 3 * C + C + fc * 128: kc * 3 * C + C + (fc + 1) * 128],
                        t_xTb4[:, kc * 512:kc * 512 + Wc],
                        start=(kc == 0), stop=(kc == NKC - 1))
                return last
            kw = [w_wsetup]
            if fc == 0:
                kw += [tok_trE[t2] for t2 in tiles]
                if ch >= 1:
                    lastqb_prev = 3 if ch == 1 else 4 * (ch - 1) + 3
                    kw.append(tok_mask[(lastqb_prev - 2, 3)])
            elif fc == 4:
                kw.append(tok_kev[(ch, 3)])
            tok_kmm[(ch, fc)] = S("tensor", mk_k, waits=kw)
            if fc in (3, 7):
                for fc2 in range(fc - 3, fc + 1):
                    s0 = tiles[0] % NSL
                    kdst = (t_kring[:].rearrange("p (s c) -> p s c", s=NSL)
                            [:, s0:s0 + len(tiles), fc2 * 128:(fc2 + 1) * 128])
                    ksrc = (p_sc4[fc2 % 4][:, 0:Wc]
                            .rearrange("p (t c) -> p t c", c=128))
                    tok_kev[(ch, fc2)] = S("scalar", lambda e, kdst=kdst, ksrc=ksrc,
                                           fc2=fc2:
                                           e.activation(kdst, ksrc, AF.Identity,
                                                        bias=t_bqk[:, 8 + fc2:9 + fc2]),
                                           waits=[tok_kmm[(ch, fc)], w_setup])

        for ti, tq in enumerate(tiles):
            for vh in range(2):
                def mk_v(eng, vh=vh, ti=ti):
                    last = None
                    for kc in range(NKC):
                        last = eng.matmul(
                            p_v[:],
                            t_xTb4[:, kc * 512 + ti * 128: kc * 512 + (ti + 1) * 128],
                            t_wab[:, kc * 3 * C + 2 * C + vh * 512:
                                  kc * 3 * C + 2 * C + (vh + 1) * 512],
                            start=(kc == 0), stop=(kc == NKC - 1))
                    return last
                vw = [tok_trE[tq], w_wsetup]
                if vh == 1:
                    vw.append(tok_vev[(tq, 0)])
                elif (tq - 1, 1) in tok_vev:
                    vw.append(tok_vev[(tq - 1, 1)])
                tok_v_mm[(tq, vh)] = S("tensor", mk_v, waits=vw)
                vdst = t_vring[:, (tq % NSL) * C + vh * 512:
                               (tq % NSL) * C + (vh + 1) * 512]
                ew = [tok_v_mm[(tq, vh)], w_setup, w_gsetup]
                if tq - NSL in avq:
                    ew.append(avq[tq - NSL])
                tok_vev[(tq, vh)] = S("vector", lambda e, vdst=vdst, vh=vh:
                                      e.scalar_tensor_tensor(
                                          vdst, p_v[:], 1.0,
                                          t_bvbc[:, vh * 512:(vh + 1) * 512],
                                          op0=ALU.mult, op1=ALU.add), waits=ew)

        for fc in range(NKC):
            def mk_q(eng, fc=fc, Wc=Wc, qoff=qoff):
                last = None
                for kc in range(NKC):
                    last = eng.matmul(
                        p_sc4[fc % 4][:, qoff:Wc],
                        t_wab[:, kc * 3 * C + fc * 128: kc * 3 * C + (fc + 1) * 128],
                        t_xTb4[:, kc * 512 + qoff:kc * 512 + Wc],
                        start=(kc == 0), stop=(kc == NKC - 1))
                return last
            qw = [tok_bqs]
            if fc == 0:
                qw.append(tok_kev[(ch, NKC - 1)])
            elif fc == 4:
                qw.append(tok_qev[(ch, 3)])
            tok_qmm[(ch, fc)] = S("tensor", mk_q, waits=qw)
            if fc in (3, 7):
                for fc2 in range(fc - 3, fc + 1):
                    qdst = t_q4[:, fc2 * 512 + qoff: fc2 * 512 + Wc]
                    tok_qev[(ch, fc2)] = S("scalar", lambda e, qdst=qdst, fc2=fc2,
                                           Wc=Wc, qoff=qoff:
                                           e.activation(qdst,
                                                        p_sc4[fc2 % 4][:, qoff:Wc],
                                                        AF.Identity, scale=0.125,
                                                        bias=t_bqs[:, fc2:fc2 + 1]),
                                           waits=[tok_qmm[(ch, fc)]])

        qbs = [2, 3] if ch == 0 else list(tiles)
        for tq in qbs:
            tt = tq - 2
            ti = tq - 4 * ch
            mvar = tt if tt < 2 else 2
            for G in range(4):
                g2 = G % 2

                def mk_sc(eng, G=G, tq=tq, ti=ti):
                    last = None
                    for hloc in range(4):
                        h = G * 4 + hloc
                        fc, r0 = h // 2, (h % 2) * 64
                        for kr in range(3):
                            ks = (tq - 2 + kr) % NSL
                            last = eng.matmul(
                                p_sc4[hloc][:, kr * 128:(kr + 1) * 128],
                                t_q4[r0:r0 + 64, fc * 512 + ti * 128:
                                     fc * 512 + (ti + 1) * 128],
                                t_kring[r0:r0 + 64, ks * C + fc * 128: ks * C + (fc + 1) * 128],
                                start=True, stop=True)
                    return last
                sw = [tok_qev[(ch, NKC - 1)]]
                if G >= 1:
                    sw.append(tok_mask[(tt, G - 1)])
                elif tt >= 1:
                    sw.append(tok_mask[(tt - 1, 3)])
                tok_scg[(tt, G)] = S("tensor", mk_sc, waits=sw)

                mw = [tok_scg[(tt, G)], w_setup, w_gsetup]
                if G == 0 and tt >= 1:
                    mw.append(tok_exp[(tt - 1, 3)])
                pbuf = t_prob[:, g2 * 4 * W3:(g2 + 1) * 4 * W3]
                pk = (tt, G - 2) if G >= 2 else (tt - 1, G + 2)
                for hloc in range(4):
                    mwh = list(mw) if hloc == 0 else []
                    tok_mask[(tt, G)] = S("vector", lambda e, hloc=hloc, mvar=mvar:
                        e.tensor_tensor(
                            t_scf[:, hloc * W3:(hloc + 1) * W3],
                            p_sc4[hloc][:, 0:W3],
                            t_masks[:, mvar * W3:(mvar + 1) * W3], op=ALU.add),
                        waits=mwh + ([tok_exp[(tt, G - 1)]] if hloc == 0 and G >= 1 else []))
                for hloc in range(4):
                    ew = [tok_mask[(tt, G)]]
                    if hloc == 0 and pk in tok_red:
                        ew += [tok_red[pk], tok_ptpe[pk + (1,)]]
                    tok_exp[(tt, G)] = S("scalar", lambda e, hloc=hloc, pbuf=pbuf: e.activation(
                        pbuf[:, hloc * W3:(hloc + 1) * W3],
                        t_scf[:, hloc * W3:(hloc + 1) * W3], AF.Exp), waits=ew)
                for hloc in range(4):
                    tok_red[(tt, G)] = S("vector", lambda e, G=G, hloc=hloc, pbuf=pbuf:
                        e.reduce_sum(t_den[:, G * 4 + hloc:G * 4 + hloc + 1],
                                     pbuf[:, hloc * W3:(hloc + 1) * W3], axis=AX.X),
                        waits=[tok_exp[(tt, G)]])

                for sub in range(2):
                    def mk_pt(eng, pbuf=pbuf, sub=sub):
                        last = None
                        for hl in range(2):
                            for kr in range(3):
                                last = eng.transpose(
                                    p_trb[:, (hl * 3 + kr) * 128:(hl * 3 + kr + 1) * 128],
                                    pbuf[:, (2 * sub + hl) * W3 + kr * 128:
                                         (2 * sub + hl) * W3 + (kr + 1) * 128],
                                    t_identb[:])
                        return last
                    tok_ptpe[(tt, G, sub)] = S("tensor", mk_pt,
                                               waits=[tok_exp[(tt, G)], trb_ev[1]])
                    ptdst = t_probT[:, g2 * 4 * W3 + sub * 768: g2 * 4 * W3 + (sub + 1) * 768]
                    ew = [tok_ptpe[(tt, G, sub)]]
                    ak = (tt, G - 2, 2 * sub + 1) if G >= 2 else (tt - 1, G + 2, 2 * sub + 1)
                    if ak in tok_av:
                        ew.append(tok_av[ak])
                    tok_ptev[(tt, G, sub)] = S("vector", lambda e, ptdst=ptdst: e.tensor_copy(
                        ptdst, p_trb[:, 0:768]), waits=ew)
                    trb_ev[1] = tok_ptev[(tt, G, sub)]

                for hloc in range(4):
                    h = G * 4 + hloc
                    sub = hloc // 2

                    def mk_av(eng, g2=g2, hloc=hloc, h=h, G=G, tq=tq):
                        last = None
                        for kr in range(3):
                            ks = (tq - 2 + kr) % NSL
                            last = eng.matmul(
                                p_misc[:, (G % 2) * 256 + hloc * 64:
                                       (G % 2) * 256 + (hloc + 1) * 64],
                                t_probT[:, g2 * 4 * W3 + hloc * W3 + kr * 128:
                                        g2 * 4 * W3 + hloc * W3 + (kr + 1) * 128],
                                t_vring[:, ks * C + h * 64: ks * C + (h + 1) * 64],
                                start=(kr == 0), stop=(kr == 2))
                        return last
                    aw = [tok_ptev[(tt, G, sub)], tok_vev[(tq, 1)]]
                    if G == 0 and hloc == 0 and tt >= 1:
                        aw.append(tok_logit_ev[tt - 1])
                    pk2 = (tt, G - 1) if G >= 1 else (tt - 1, 3)
                    if hloc == 0 and pk2 in tok_aoev:
                        aw.append(tok_aoev[pk2])
                    tok_av[(tt, G, hloc)] = S("tensor", mk_av, waits=aw)

                tok_aoev[(tt, G)] = S("vector", lambda e, G=G: e.tensor_copy(
                    t_ao[:, G * 256:(G + 1) * 256],
                    p_misc[:, (G % 2) * 256:(G % 2 + 1) * 256]),
                    waits=[tok_av[(tt, G, 3)]])

            avq[tt] = tok_av[(tt, 3, 3)]

            tok_recip[tt] = S("vector", lambda e: e.reciprocal(t_rden[:], t_den[:]),
                              waits=[tok_red[(tt, 3)]])
            nw = [tok_recip[tt], tok_aoev[(tt, 3)]]
            if tt >= 1:
                nw.append(tok_aoT_pe[tt - 1])
            tok_normao[tt] = S("vector", lambda e: e.tensor_tensor(
                t_aon[:].rearrange("p (h d) -> p h d", h=16),
                t_ao[:].rearrange("p (h d) -> p h d", h=16),
                t_rden[:].rearrange("p h -> p h ()").broadcast_to([128, NH, HD]),
                op=ALU.mult), waits=nw)

            def mk_aoT(eng):
                last = None
                for j in range(NKC):
                    last = eng.transpose(p_trb[:, j * 128:(j + 1) * 128],
                                         t_aon[:, j * 128:(j + 1) * 128], t_identb[:])
                return last
            aw = [tok_normao[tt], trb_ev[1]]
            tok_aoT_pe[tt] = S("tensor", mk_aoT, waits=aw)
            ew = [tok_aoT_pe[tt]]
            if tt >= 1:
                ew.append(tok_y1_mm[(tt - 1, 1)])
            tok_aoT_ev[tt] = S("vector", lambda e: e.tensor_copy(t_aoT[:], p_trb[:, 0:C]),
                               waits=ew)
            trb_ev[1] = tok_aoT_ev[tt]

            for nh2 in range(2):
                def mk_pj(eng, nh2=nh2):
                    last = None
                    for kc in range(NKC):
                        last = eng.matmul(
                            p_misc[:],
                            t_aoT[:, kc * 128:(kc + 1) * 128],
                            t_wpb[:, kc * C + nh2 * 512: kc * C + (nh2 + 1) * 512],
                            start=(kc == 0), stop=(kc == NKC - 1))
                    return last
                pw = [tok_aoT_ev[tt], w_wsetup]
                if nh2 == 1:
                    pw.append(tok_proj_ev[(tt, 0)])
                tok_y1_mm[(tt, nh2)] = S("tensor", mk_pj, waits=pw)
                ew = [tok_y1_mm[(tt, nh2)], w_setup, w_gsetup]
                tok_proj_ev[(tt, nh2)] = S("vector", lambda e, nh2=nh2:
                                           e.scalar_tensor_tensor(
                                               t_w[:, nh2 * 512:(nh2 + 1) * 512], p_misc[:], 1.0,
                                               t_bpbc[:, nh2 * 512:(nh2 + 1) * 512],
                                               op0=ALU.mult, op1=ALU.add), waits=ew)

            tok_res = S("vector", lambda e, tq=tq: e.tensor_tensor(
                t_w[:], t_w[:], stream[:, (tq % NSL) * C:(tq % NSL + 1) * C], op=ALU.add),
                waits=[tok_proj_ev[(tt, 0)], tok_proj_ev[(tt, 1)]])

            tok_ns = S("vector", lambda e: e.reduce_sum(t_stat[:, 0:1], t_w[:], axis=AX.X,
                                                        negate=True), waits=[tok_res])
            tok_nm = S("vector", lambda e: e.tensor_scalar(t_stat[:, 1:2], t_stat[:, 0:1],
                                                           1.0 / C, None, op0=ALU.mult),
                       waits=[tok_ns])
            tok_ct = S("vector", lambda e: e.tensor_scalar(t_cent[:], t_w[:], t_stat[:, 1:2],
                                                           None, op0=ALU.add), waits=[tok_nm])
            tok_sq = S("vector", lambda e: e.tensor_tensor(t_w[:], t_cent[:], t_cent[:],
                                                           op=ALU.mult), waits=[tok_ct])
            tok_vr = S("vector", lambda e: e.reduce_sum(t_stat[:, 2:3], t_w[:], axis=AX.X),
                       waits=[tok_sq])
            tok_v2 = S("vector", lambda e: e.tensor_scalar(t_stat[:, 3:4], t_stat[:, 2:3],
                                                           1.0 / C, EPS1, op0=ALU.mult,
                                                           op1=ALU.add), waits=[tok_vr])
            tok_ln = S("scalar", lambda e: e.activation(t_stat[:, 4:5], t_stat[:, 3:4], AF.Ln),
                       waits=[tok_v2])
            tok_rs = S("scalar", lambda e: e.activation(t_stat[:, 5:6], t_stat[:, 4:5], AF.Exp,
                                                        scale=-0.5), waits=[tok_ln])
            tok_xnew[tt] = S("vector", lambda e, tq=tq: e.tensor_scalar(
                stream[:, (tq % NSL) * C:(tq % NSL + 1) * C], t_cent[:], t_stat[:, 5:6],
                None, op0=ALU.mult), waits=[tok_rs])

            n_spill[0] += 1
            S("sync", lambda e, tt=tt, tq=tq: e.dma_start(
                d_x1[tt * 128:(tt + 1) * 128, :],
                stream[:, (tq % NSL) * C:(tq % NSL + 1) * C]).then_inc(q_spill, 16),
              waits=[tok_xnew[tt]])

            tok_rtr_ev = {}
            for bt in range(2):
                def mk_rtr(eng, bt=bt, tq=tq):
                    last = None
                    for j in range(4):
                        last = eng.transpose(
                            p_misc[:, j * 128:(j + 1) * 128],
                            stream[:, (tq % NSL) * C + (bt * 4 + j) * 128: (tq % NSL) * C + (bt * 4 + j + 1) * 128],
                            t_identf[:])
                    return last
                rw = [tok_xnew[tt]]
                rw.append(tok_proj_ev[(tt, 1)] if bt == 0 else tok_rtr_ev[0])
                tok_rtr_pe = S("tensor", mk_rtr, waits=rw)
                tok_rtr_ev[bt] = S("vector", lambda e, bt=bt: e.tensor_copy(
                    t_xT32[:, bt * 512:(bt + 1) * 512], p_misc[:]), waits=[tok_rtr_pe])

            def mk_rmm(eng):
                last = None
                for kc in range(NKC):
                    last = eng.matmul(p_misc[:, 0:NE],
                                      t_xT32[:, kc * 128:(kc + 1) * 128],
                                      t_wrt[:, kc * NE:(kc + 1) * NE],
                                      start=(kc == 0), stop=(kc == NKC - 1))
                return last
            tok_rmm = S("tensor", mk_rmm, waits=[tok_rtr_ev[0], tok_rtr_ev[1], w_setup])
            tok_logit_ev[tt] = S("vector", lambda e: e.tensor_copy(t_lg[:], p_misc[:, 0:NE]),
                                 waits=[tok_rmm])
            tok_m8 = S("vector", lambda e: e.max(t_mx8[:], t_lg[:]), waits=[tok_logit_ev[tt]])
            tok_mi = S("vector", lambda e: e.max_index(t_idxu[:], t_mx8[:], t_lg[:]),
                       waits=[tok_m8])
            tok_if = S("vector", lambda e: e.tensor_scalar(t_idxf[:], t_idxu[:, 0:1], 0.0,
                                                           None, op0=ALU.add),
                       waits=[tok_mi] + ([(q_idx, 16 * tt)] if tt >= 1 else []))
            S("sync", lambda e, tt=tt: e.dma_start(
                d_idx.rearrange("(a p) -> a p", p=128)[tt].rearrange("(p o) -> p o", o=1),
                t_idxf[:]).then_inc(q_idx, 16), waits=[tok_if])

    S("sync", lambda e: e.nop(), waits=[(q_spill, 16 * NTT), (q_idx, 16 * NTT)])

    a_st.close()
    b_st = ExitStack()
    bsb = lambda name, shape, dt: b_st.enter_context(nc.sbuf_tensor(name, shape, dt))

    t_ring = bsb("ring", [128, 32 * 1024], BF16)
    t_acc = bsb("acc", [128, 32 * 1024], BF16)
    acc_full = t_acc
    t_xTg = bsb("xTg", [128, NKC * 1024], BF16)
    t_idxbc = bsb("idx_bc", [128, 1024], F32)
    t_eq = bsb("eq", [128, 2 * 1024], U8)
    t_gelu = bsb("gelu", [128, 3 * 512], BF16)
    t_xrel2 = bsb("xrel2", [128, 2 * C], F32)
    t_xrb = bsb("xrb", [128, 2 * C], BF16)
    t_y2 = bsb("y2", [128, 2 * C], F32)
    t_ct2 = bsb("ct2", [128, 2 * C], F32)
    t_st2 = bsb("stat2", [128, 16], F32)

    bpsal = lambda name, shape, dt: b_st.enter_context(nc.psum_tensor(name, shape, dt))
    p_i = [bpsal(f"p_i{j}", [128, 512], F32) for j in range(4)]
    p_y2 = [bpsal(f"p_y2{j}", [128, 512], F32) for j in range(2)]
    p_trB = bpsal("p_trB", [128, 1024], BF16)

    sched.new_block()
    S = sched.op

    nw_slot = [0] * 32
    slot_last_reader = [None] * 32
    load_ctr = [0]
    n_xrel = [0] * 4
    n_xrel2 = [0] * 4
    n_out2 = [0, 0]

    tok_sel = {}
    tok_gelu = {}
    tok_immP = {}
    tok_y2mm = {}
    tok_y2ev = {}
    tok_tail_last = {}
    tok_eqtok = {}
    tok_xTg_ev = {}
    tok_trB_tok = {}
    tok_tail_res = {}
    idxbc_readers = []
    gelu_slot_reader = [None, None, None]
    eq_slot_reader = [None, None]

    tok_acc0 = S("vector", lambda e: e.memset(acc_full[:], 0.0),
                 waits=[(q_spill, 16 * NTT), tok_logit_ev[NTT - 1]])

    for g in range(2):
        iw = [(q_idx, 16 * NTT)] + idxbc_readers
        idxbc_readers = []
        S("sync", lambda e, g=g: e.dma_start(
            t_idxbc[:], bass.AP(d_idx.tensor, g * 1024, [[0, 128], [1, 1024]])).then_inc(
            q_idxbc, 16), waits=iw)
        w_idxbc = (q_idxbc, 16 * (g + 1))

        for ttl in range(8):
            tt = g * 8 + ttl
            qi = ttl % 4
            rw = [(q_spill, 16 * NTT)]
            if (g, ttl - 2) in tok_trB_tok:
                rw.append(tok_trB_tok[(g, ttl - 2)])
            elif g == 1 and ttl < 2:
                rw.append(tok_trB_tok[(0, 6 + ttl)])
            n_xrel[qi] += 1
            S("gpsimd", lambda e, tt=tt, ttl=ttl, qi=qi: e.dma_start(
                t_xrb[:, (ttl % 2) * C:(ttl % 2 + 1) * C],
                d_x1[tt * 128:(tt + 1) * 128, :]).then_inc(q_xrel[qi], 16), waits=rw)
            xrw = (q_xrel[qi], 16 * n_xrel[qi])

            def mk_trB(eng, ttl=ttl):
                last = None
                for j in range(NKC):
                    last = eng.transpose(p_trB[:, j * 128:(j + 1) * 128],
                                         t_xrb[:, (ttl % 2) * C + j * 128:
                                               (ttl % 2) * C + (j + 1) * 128], t_identb[:])
                return last
            tw = [xrw, w_setup]
            if (g, ttl - 1) in tok_xTg_ev:
                tw.append(tok_xTg_ev[(g, ttl - 1)])
            elif g == 1 and ttl == 0:
                tw.append(tok_xTg_ev[(0, 7)])
            tok_trB_pe = S("tensor", mk_trB, waits=tw)
            tok_trB_tok[(g, ttl)] = tok_trB_pe
            xTg_v = t_xTg.rearrange("p (k t) -> p k t", k=NKC)
            ew = [tok_trB_pe]
            if g == 1 and ttl == 0:
                ew.append(tok_y2mm[(0, 7, 1)])
            tok_xTg_ev[(g, ttl)] = S("vector", lambda e, xTg_v=xTg_v, ttl=ttl:
                                     e.tensor_copy(xTg_v[:, :, ttl * 128:(ttl + 1) * 128],
                                                   p_trB[:, 0:C]), waits=ew)
        w_xTg = [tok_xTg_ev[(g, t2)] for t2 in range(8)]

        for eix in range(NE):
            eqbuf = t_eq[:, (eix % 2) * 1024:(eix % 2 + 1) * 1024]
            eqw = [w_idxbc]
            if eq_slot_reader[eix % 2] is not None:
                eqw.append(eq_slot_reader[eix % 2])
            tok_eqtok[(g, eix)] = S("vector", lambda e, eqbuf=eqbuf, eix=eix:
                                    e.tensor_scalar(eqbuf, t_idxbc[:], float(eix), None,
                                                    op0=ALU.is_equal), waits=eqw)
            idxbc_readers.append(tok_eqtok[(g, eix)])

            for iq in range(4):
                slot_of_kc = {}
                for kc in range(NKC):
                    L = load_ctr[0]
                    s = L % 32
                    nw_slot[s] += 1
                    slot_of_kc[kc] = (s, nw_slot[s])
                    lw = [slot_last_reader[s]] if slot_last_reader[s] is not None else []
                    S("gpsimd", lambda e, s=s, eix=eix, kc=kc, iq=iq: e.dma_start(
                        t_ring[:, s * 1024:(s + 1) * 1024],
                        d_w1[eix, kc * 128:(kc + 1) * 128,
                             iq * 1024:(iq + 1) * 1024]).then_inc(q_slot[s], 16), waits=lw)
                    load_ctr[0] += 1

                for ps2 in range(2):
                    for icl in range(8):
                        seq = ps2 * 8 + icl
                        bank = p_i[seq % 4]
                        icg = iq * 8 + icl
                        key = (g, eix, iq, ps2, icl)

                        def mk_im(eng, bank=bank, icl=icl, ps2=ps2,
                                  slot_of_kc=dict(slot_of_kc)):
                            last = None
                            for kc in range(NKC):
                                s2, _ = slot_of_kc[kc]
                                last = eng.matmul(
                                    bank[:],
                                    t_ring[:, s2 * 1024 + icl * 128: s2 * 1024 + (icl + 1) * 128],
                                    t_xTg[:, kc * 1024 + ps2 * 512: kc * 1024 + (ps2 + 1) * 512],
                                    start=(kc == 0), stop=(kc == NKC - 1))
                            return last

                        def back_key(delta):
                            sq = seq - delta
                            e2, i2, g2 = eix, iq, g
                            while sq < 0:
                                sq += 16
                                i2 -= 1
                                if i2 < 0:
                                    i2 = 3
                                    e2 -= 1
                                    if e2 < 0:
                                        e2 = NE - 1
                                        g2 -= 1
                                        if g2 < 0:
                                            return None
                            return (g2, e2, i2, sq // 8, sq % 8)

                        mw = list(w_xTg)
                        mw += [(q_slot[s2], 16 * n2) for (s2, n2) in slot_of_kc.values()]
                        bk = back_key(4)
                        if bk is not None and bk in tok_gelu:
                            mw.append(tok_gelu[bk])
                        tok_immP[key] = S("tensor", mk_im, waits=mw)

                        gb = seq % 3
                        gbuf = t_gelu[:, gb * 512:(gb + 1) * 512]
                        ew = [tok_immP[key]]
                        if gelu_slot_reader[gb] is not None:
                            ew.append(gelu_slot_reader[gb])
                        tok_gelu[key] = S("scalar", lambda e, gbuf=gbuf, bank=bank,
                                          eix=eix, icg=icg:
                                          e.activation(gbuf, bank[:], AF.Gelu,
                                                       bias=t_b1c[:, icg * 8 + eix:
                                                                  icg * 8 + eix + 1]),
                                          waits=ew + [w_b1c])
                        accsl = acc_full[:, icg * 1024 + ps2 * 512: icg * 1024 + (ps2 + 1) * 512]
                        sw = [tok_gelu[key], tok_eqtok[(g, eix)]]
                        if g == 1 and eix == 0:
                            sw.append(tok_y2ev[(0, 7, 1)])
                        tok_sel[key] = S("vector", lambda e, accsl=accsl, eqbuf=eqbuf,
                                         gbuf=gbuf, ps2=ps2:
                                         e.copy_predicated(
                                             accsl, eqbuf[:, ps2 * 512:(ps2 + 1) * 512],
                                             gbuf), waits=sw)
                        gelu_slot_reader[gb] = tok_sel[key]
                for kc in range(NKC):
                    s2, _ = slot_of_kc[kc]
                    slot_last_reader[s2] = tok_immP[(g, eix, iq, 1, 7)]
            eq_slot_reader[eix % 2] = tok_sel[(g, eix, 3, 1, 7)]

        last_sel = tok_sel[(g, NE - 1, 3, 1, 7)]

        wo_slot = {}
        for ic in range(32):
            L = load_ctr[0]
            s = L % 32
            nw_slot[s] += 1
            wo_slot[ic] = (s, nw_slot[s])
            lw = [slot_last_reader[s]] if slot_last_reader[s] is not None else []
            S("gpsimd", lambda e, s=s, ic=ic: e.dma_start(
                t_ring[:, s * 1024:(s + 1) * 1024],
                d_wo[ic * 128:(ic + 1) * 128, :]).then_inc(q_slot[s], 16), waits=lw)
            load_ctr[0] += 1

        for ttl in range(8):
            tt = g * 8 + ttl
            qi = ttl % 4
            tb = ttl % 2
            rw = [(q_spill, 16 * NTT)]
            if (g, ttl - 2) in tok_tail_res:
                rw.append(tok_tail_res[(g, ttl - 2)])
            elif g == 1 and ttl < 2:
                rw.append(tok_tail_res[(0, 6 + ttl)])
            n_xrel2[qi] += 1
            S("sync", lambda e, tt=tt, tb=tb, qi=qi: e.dma_start(
                t_xrel2[:, tb * C:(tb + 1) * C],
                d_x1[tt * 128:(tt + 1) * 128, :]).then_inc(q_xrel2[qi], 16), waits=rw)
            xr2w = (q_xrel2[qi], 16 * n_xrel2[qi])

            for ch in range(2):
                def mk_wo(eng, ttl=ttl, ch=ch, wo_slot=dict(wo_slot)):
                    last = None
                    for ic in range(32):
                        s2, _ = wo_slot[ic]
                        last = eng.matmul(
                            p_y2[ch][:],
                            acc_full[:, ic * 1024 + ttl * 128: ic * 1024 + (ttl + 1) * 128],
                            t_ring[:, s2 * 1024 + ch * 512: s2 * 1024 + (ch + 1) * 512],
                            start=(ic == 0), stop=(ic == 31))
                    return last
                ww = [last_sel]
                ww += [(q_slot[s2], 16 * n2) for (s2, n2) in wo_slot.values()]
                if ttl >= 1:
                    ww.append(tok_y2ev[(g, ttl - 1, ch)])
                elif g == 1:
                    ww.append(tok_y2ev[(0, 7, ch)])
                tok_y2mm[(g, ttl, ch)] = S("tensor", mk_wo, waits=ww)
                ew = [tok_y2mm[(g, ttl, ch)], w_setup, xr2w]
                if (g, ttl - 2) in tok_tail_last:
                    ew.append(tok_tail_last[(g, ttl - 2)])
                elif g == 1 and ttl < 2:
                    ew.append(tok_tail_last[(0, 6 + ttl)])
                tok_y2ev[(g, ttl, ch)] = S("vector", lambda e, ch=ch, tb=tb:
                                           e.scalar_tensor_tensor(
                                               t_y2[:, tb * C + ch * 512:
                                                    tb * C + (ch + 1) * 512],
                                               p_y2[ch][:], 1.0,
                                               t_bo[:, ch * 512:(ch + 1) * 512],
                                               op0=ALU.mult, op1=ALU.add), waits=ew)
            if ttl == 7:
                for ic in range(32):
                    s2, _ = wo_slot[ic]
                    slot_last_reader[s2] = tok_y2mm[(g, 7, 1)]

            yb = t_y2[:, tb * C:(tb + 1) * C]
            cb = t_ct2[:, tb * C:(tb + 1) * C]
            xr = t_xrel2[:, tb * C:(tb + 1) * C]
            sc = t_st2[:, tb * 8:(tb + 1) * 8]

            def ln_chain(first_w, src, cent, dst, eps, affine, sc=sc):
                tns = S("vector", lambda e, src=src, sc=sc: e.reduce_sum(
                    sc[:, 0:1], src, axis=AX.X, negate=True), waits=first_w)
                tnm = S("vector", lambda e, sc=sc: e.tensor_scalar(
                    sc[:, 1:2], sc[:, 0:1], 1.0 / C, None, op0=ALU.mult), waits=[tns])
                tct = S("vector", lambda e, src=src, cent=cent, sc=sc: e.tensor_scalar(
                    cent, src, sc[:, 1:2], None, op0=ALU.add), waits=[tnm])
                tsq = S("vector", lambda e, src=src, cent=cent: e.tensor_tensor(
                    src, cent, cent, op=ALU.mult), waits=[tct])
                tvr = S("vector", lambda e, src=src, sc=sc: e.reduce_sum(
                    sc[:, 2:3], src, axis=AX.X), waits=[tsq])
                tv2 = S("vector", lambda e, sc=sc, eps=eps: e.tensor_scalar(
                    sc[:, 3:4], sc[:, 2:3], 1.0 / C, eps, op0=ALU.mult, op1=ALU.add),
                    waits=[tvr])
                tln = S("scalar", lambda e, sc=sc: e.activation(sc[:, 4:5], sc[:, 3:4],
                                                                AF.Ln), waits=[tv2])
                trs = S("scalar", lambda e, sc=sc: e.activation(sc[:, 5:6], sc[:, 4:5],
                                                                AF.Exp, scale=-0.5),
                        waits=[tln])
                if affine:
                    tz = S("vector", lambda e, cent=cent, dst=dst, sc=sc:
                           e.scalar_tensor_tensor(dst, cent, sc[:, 5:6], t_gam[:],
                                                  op0=ALU.mult, op1=ALU.mult), waits=[trs])
                    tz = S("vector", lambda e, dst=dst: e.tensor_tensor(
                        dst, dst, t_bet[:], op=ALU.add), waits=[tz])
                else:
                    tz = S("vector", lambda e, cent=cent, dst=dst, sc=sc: e.tensor_scalar(
                        dst, cent, sc[:, 5:6], None, op0=ALU.mult), waits=[trs])
                return tz

            r1w = [tok_y2ev[(g, ttl, 0)], tok_y2ev[(g, ttl, 1)]]
            if n_out2[tb] > 0:
                r1w.append((q_out2[tb], n_out2[tb]))
            t_r1 = S("vector", lambda e, yb=yb, xr=xr: e.tensor_tensor(
                yb, yb, xr, op=ALU.add), waits=r1w)
            t_z = ln_chain([t_r1], yb, cb, yb, EPS2, True)
            t_r2 = S("vector", lambda e, yb=yb, xr=xr: e.tensor_tensor(
                yb, xr, yb, op=ALU.add), waits=[t_z])
            tok_tail_res[(g, ttl)] = t_r2
            t_o = ln_chain([t_r2], yb, cb, cb, EPS1, False)
            tok_tail_last[(g, ttl)] = t_o

            n_out2[tb] += 16
            S("sync", lambda e, tt=tt, cb=cb, tb=tb: e.dma_start(
                d_out[tt * 128:(tt + 1) * 128, :], cb).then_inc(q_out2[tb], 16),
              waits=[t_o])

    S("sync", lambda e: e.nop(), waits=[(q_out2[0], n_out2[0]), (q_out2[1], n_out2[1])])

    sched.finalize()
    for blk_ops in sched.blocks:
        with nc.Block() as block:
            for ename, deco in (("sync", block.sync), ("gpsimd", block.gpsimd),
                                ("scalar", block.scalar), ("vector", block.vector),
                                ("tensor", block.tensor)):
                def body(engine, ename=ename, blk_ops=blk_ops):
                    sched.emit_block(blk_ops, ename, engine, eng_sems)
                deco(body)

    b_st.close()
    st.close()
    return nc


_NC_CACHE = {}


def _get_nc():
    if "nc" not in _NC_CACHE:
        _NC_CACHE["nc"] = build_program()
    return _NC_CACHE["nc"]


def _build_masks(seq0):
    p = np.arange(128)[:, None]
    f = np.arange(W3)[None, :]
    out = np.zeros((3, 128, W3), dtype=np.float32)
    for v in range(3):
        tt = v if v < 2 else 2
        allowed = (f >= p) & (f <= p + 256)
        if v < 2:
            kseq = seq0 - 256 + tt * 128 + f
            allowed = allowed & (kseq >= 0)
        out[v] = np.where(allowed, 0.0, NEG)
    return out


def kernel(L0_attn_x, L0_attn_W_attn, L0_attn_b_attn, L0_attn_W_proj, L0_attn_b_proj,
           L0_ffn_Wr, L0_ffn_W1, L0_ffn_b1, L0_ffn_Wo, L0_ffn_bo, L0_ffn_gamma,
           L0_ffn_beta):
    x = np.ascontiguousarray(np.asarray(L0_attn_x, dtype=np.float32).reshape(B * T, C))
    wa = np.ascontiguousarray(np.asarray(L0_attn_W_attn, dtype=np.float32))
    ba = np.ascontiguousarray(np.asarray(L0_attn_b_attn, dtype=np.float32))
    wp = np.ascontiguousarray(np.asarray(L0_attn_W_proj, dtype=np.float32))
    bp = np.ascontiguousarray(np.asarray(L0_attn_b_proj, dtype=np.float32))
    wrt = np.ascontiguousarray(np.asarray(L0_ffn_Wr, dtype=np.float32).T)
    w1 = np.ascontiguousarray(np.asarray(L0_ffn_W1, dtype=np.float32))
    b1 = np.ascontiguousarray(np.asarray(L0_ffn_b1, dtype=np.float32))
    wo = np.ascontiguousarray(np.asarray(L0_ffn_Wo, dtype=np.float32))
    bo = np.ascontiguousarray(np.asarray(L0_ffn_bo, dtype=np.float32))
    gam = np.ascontiguousarray(np.asarray(L0_ffn_gamma, dtype=np.float32))
    bet = np.ascontiguousarray(np.asarray(L0_ffn_beta, dtype=np.float32))

    in_maps = []
    for c in range(NCORES):
        base = c * TCORE
        seq0 = (c % 4) * TCORE
        xe = np.zeros((NTQ * 128, C), dtype=np.float32)
        if seq0 == 0:
            xe[HALO:] = x[base:base + TCORE]
        else:
            xe[:] = x[base - HALO:base + TCORE]
        in_maps.append({
            "x_ext": xe, "w_attn": wa, "b_attn": ba, "w_proj": wp, "b_proj": bp,
            "wr_t": wrt, "w1": w1, "b1": b1, "wo": wo, "bo": bo, "gamma": gam,
            "beta": bet, "masks": _build_masks(seq0),
        })

    nc = _get_nc()
    res = run_bass_kernel_spmd(nc, in_maps, core_ids=list(range(NCORES)))
    out = np.concatenate([r["out"] for r in res.results], axis=0)
    return out.reshape(B, T, C)


# version 08
# speedup vs baseline: 1.2371x; 1.0110x over previous
import numpy as np
import ml_dtypes
from contextlib import ExitStack

import concourse.bass as bass
import concourse.mybir as mybir
from concourse.bass_utils import run_bass_kernel_spmd

F32 = mybir.dt.float32
BF16 = mybir.dt.bfloat16
U8 = mybir.dt.uint8
U32 = mybir.dt.uint32
AF = mybir.ActivationFunctionType
ALU = mybir.AluOpType
AX = mybir.AxisListType

NCORES = 8
B, T, C = 2, 8192, 1024
TCORE = 2048
HALO = 256
NTT = 16
NTQ = 18
NH, HD = 16, 64
W3 = 384
FF = 4096
NE = 8
NKC = 8
NEG = -1.0e30
NSL = 8
EPS1 = 1e-5
EPS2 = 1e-12

ENGINES = ("sync", "gpsimd", "scalar", "vector", "tensor")


class Sched:

    def __init__(self):
        self.blocks = []
        self.cur = None

    def new_block(self):
        self.cur = []
        self.blocks.append(self.cur)

    def op(self, eng, fn, waits=()):
        rec = {"eng": eng, "fn": fn, "waits": [w for w in waits if w is not None],
               "inc": None}
        self.cur.append(rec)
        return rec

    def finalize(self):
        for blk in self.blocks:
            for rec in blk:
                for wtg in rec["waits"]:
                    if isinstance(wtg, dict) and wtg["inc"] is None:
                        wtg["inc"] = -1
        counters = {e: 0 for e in ENGINES}
        for blk in self.blocks:
            for rec in blk:
                if rec["inc"] == -1:
                    counters[rec["eng"]] += 1
                    rec["inc"] = counters[rec["eng"]]

    def emit_block(self, blk_ops, eng_name, engine, eng_sems):
        highwater = {}
        for rec in blk_ops:
            if rec["eng"] != eng_name:
                continue
            for wtg in rec["waits"]:
                if isinstance(wtg, dict):
                    sem, val, key = eng_sems[wtg["eng"]], wtg["inc"], ("e", wtg["eng"])
                else:
                    sem, val = wtg
                    key = ("s", id(sem))
                if highwater.get(key, -1) >= val:
                    continue
                highwater[key] = val
                engine.wait_ge(sem, val)
            inst = rec["fn"](engine)
            if rec["inc"] is not None:
                assert inst is not None, f"wait-target op returned no inst on {eng_name}"
                inst.then_inc(eng_sems[rec["eng"]], 1)


def build_program():
    nc = bass.Bass("TRN2", target_bir_lowering=False)

    d_x = nc.dram_tensor("x_ext", [NTQ * 128, C], F32, kind="ExternalInput").ap()
    d_wa = nc.dram_tensor("w_attn", [C, 3 * C], F32, kind="ExternalInput").ap()
    d_ba = nc.dram_tensor("b_attn", [3 * C], F32, kind="ExternalInput").ap()
    d_wp = nc.dram_tensor("w_proj", [C, C], F32, kind="ExternalInput").ap()
    d_bp = nc.dram_tensor("b_proj", [C], F32, kind="ExternalInput").ap()
    d_wrt = nc.dram_tensor("wr_t", [C, NE], F32, kind="ExternalInput").ap()
    d_w1 = nc.dram_tensor("w1", [NE, C, FF], F32, kind="ExternalInput").ap()
    d_b1 = nc.dram_tensor("b1", [NE, FF], F32, kind="ExternalInput").ap()
    d_wo = nc.dram_tensor("wo", [FF, C], F32, kind="ExternalInput").ap()
    d_bo = nc.dram_tensor("bo", [C], F32, kind="ExternalInput").ap()
    d_gam = nc.dram_tensor("gamma", [C], F32, kind="ExternalInput").ap()
    d_bet = nc.dram_tensor("beta", [C], F32, kind="ExternalInput").ap()
    d_msk = nc.dram_tensor("masks", [3, 128, W3], F32, kind="ExternalInput").ap()
    d_out = nc.dram_tensor("out", [TCORE, C], F32, kind="ExternalOutput").ap()
    d_x1 = nc.dram_tensor("x1_scratch", [TCORE, C], F32, kind="Internal").ap()
    d_idx = nc.dram_tensor("idx_scratch", [TCORE], F32, kind="Internal").ap()

    ident_bf = nc.inline_tensor(np.eye(128, dtype=ml_dtypes.bfloat16), "ident_bf").ap()
    ident_f32 = nc.inline_tensor(np.eye(128, dtype=np.float32), "ident_f32").ap()

    st = ExitStack()
    sb = lambda name, shape, dt: st.enter_context(nc.sbuf_tensor(name, shape, dt))
    psal = lambda name, shape, dt: st.enter_context(nc.psum_tensor(name, shape, dt))
    sem = lambda name: st.enter_context(nc.semaphore(name))

    t_identb = sb("identb", [128, 128], BF16)
    t_identf = sb("identf", [128, 128], F32)
    t_bqk = sb("bqk", [128, 16], F32)
    t_bqs = sb("bqs", [128, 8], F32)
    t_wrt = sb("wrt", [128, NKC * NE], F32)
    t_b1c = sb("b1cols", [128, NE * 32], F32)
    t_bo = sb("bo_bc", [128, C], F32)
    t_gam = sb("gam_bc", [128, C], F32)
    t_bet = sb("bet_bc", [128, C], F32)

    eng_sems = {e: sem("e_" + e) for e in ENGINES}
    q_setup = sem("q_setup")
    q_wsetup = sem("q_wsetup")
    q_wk = sem("q_wk")
    q_wv = sem("q_wv")
    q_wq = sem("q_wq")
    q_x = [sem(f"q_x{i}") for i in range(6)]
    q_spill = sem("q_spill")
    q_idx = sem("q_idx")
    q_idxbc = sem("q_idxbc")
    q_b1c = sem("q_b1c")
    q_gsetup = sem("q_gsetup")
    q_xrel = [sem(f"q_xrel{i}") for i in range(4)]
    q_xrel2 = [sem(f"q_xrel2{i}") for i in range(4)]
    q_slot = [sem(f"q_slot{i}") for i in range(32)]
    q_out2 = [sem("q_out0"), sem("q_out1")]

    sched = Sched()

    a_st = ExitStack()
    asb = lambda name, shape, dt: a_st.enter_context(nc.sbuf_tensor(name, shape, dt))

    t_wab = asb("wab", [128, NKC * 3 * C], BF16)
    t_wpb = asb("wpb", [128, NKC * C], BF16)
    t_kring = asb("kring", [128, NSL * C], BF16)
    t_vring = asb("vring", [128, NSL * C], BF16)
    t_masks = asb("masks_sb", [128, 3 * W3], BF16)
    t_bvbc = asb("bv_bc", [128, C], BF16)
    t_bpbc = asb("bp_bc", [128, C], BF16)
    t_xc = asb("xc", [128, 2 * C], BF16)
    t_xTb4 = asb("xTb4", [128, NKC * 512], BF16)
    t_q4 = asb("q4", [128, NKC * 512], BF16)
    stream = asb("stream", [128, NSL * C], F32)
    t_scf = asb("scf", [128, 4 * W3], F32)
    t_prob = asb("prob", [128, 2 * 4 * W3], BF16)
    t_probT = asb("probT", [128, 2 * 4 * W3], BF16)
    t_ao = asb("attn_out", [128, C], BF16)
    t_aon = asb("attn_out_n", [128, C], BF16)
    t_aoT = asb("attn_outT", [128, C], BF16)
    t_den = asb("den", [128, NH], F32)
    t_rden = asb("rden", [128, NH], F32)
    t_w = asb("wbuf", [128, C], F32)
    t_cent = asb("centbuf", [128, C], F32)
    t_xT32 = asb("xT32", [128, C], F32)
    t_lg = asb("lgs", [128, 8], F32)
    t_mx8 = asb("mx8", [128, 8], F32)
    t_idxu = asb("idxu", [128, 8], U32)
    t_idxf = asb("idxf", [128, 1], F32)
    t_stat = asb("stat", [128, 8], F32)

    apsal = lambda name, shape, dt: a_st.enter_context(nc.psum_tensor(name, shape, dt))
    p_tra = apsal("p_tra", [128, 1024], BF16)
    p_trb = apsal("p_trb", [128, 1024], BF16)
    p_v = apsal("p_v", [128, 512], F32)
    p_sc4 = [apsal(f"p_sc4_{i}", [128, 512], F32) for i in range(4)]
    p_misc = apsal("p_misc", [128, 512], F32)

    sched.new_block()
    S = sched.op

    n_setup = [0]
    n_gsetup = [0]

    def setup_dma(engname, fn):
        if engname == "gpsimd":
            S(engname, lambda e, fn=fn: fn(e).then_inc(q_gsetup, 16))
            n_gsetup[0] += 16
        else:
            S(engname, lambda e, fn=fn: fn(e).then_inc(q_setup, 16))
            n_setup[0] += 16

    setup_dma("sync", lambda e: e.dma_start(t_identb[:], ident_bf))
    setup_dma("sync", lambda e: e.dma_start(t_identf[:], ident_f32))
    for v in range(3):
        setup_dma("gpsimd", lambda e, v=v: e.dma_start(
            t_masks[:, v * W3:(v + 1) * W3], d_msk[v]))
    setup_dma("sync", lambda e: e.dma_start(
        t_bqk[:], bass.AP(d_ba.tensor, 0, [[1, 128], [128, 16]]),
        allow_slow_non_contiguous=True))

    def bcast_src(dap, off, n):
        return bass.AP(dap.tensor, off, [[0, 128], [1, n]])

    setup_dma("gpsimd", lambda e: e.dma_start(t_bvbc[:], bcast_src(d_ba, 2048, C)))
    setup_dma("gpsimd", lambda e: e.dma_start(t_bpbc[:], bcast_src(d_bp, 0, C)))
    setup_dma("sync", lambda e: e.dma_start(t_bo[:], bcast_src(d_bo, 0, C)))
    setup_dma("sync", lambda e: e.dma_start(t_gam[:], bcast_src(d_gam, 0, C)))
    setup_dma("sync", lambda e: e.dma_start(t_bet[:], bcast_src(d_bet, 0, C)))
    for kc in range(NKC):
        setup_dma("sync", lambda e, kc=kc: e.dma_start(
            t_wrt[:, kc * NE:(kc + 1) * NE], d_wrt[kc * 128:(kc + 1) * 128, :]))
    SETUP_TOTAL = n_setup[0]

    n_b1c = [0]
    for j in range(32):
        S("scalar", lambda e, j=j: e.dma_start(
            t_b1c[:, j * 8:(j + 1) * 8],
            bass.AP(d_b1.tensor, j * 128, [[1, 128], [FF, 8]]),
            allow_slow_non_contiguous=True).then_inc(q_b1c, 16))
        n_b1c[0] += 16
    w_b1c = (q_b1c, n_b1c[0])

    n_wsetup = [0]
    for qsem, off in ((q_wk, C), (q_wv, 2 * C), (q_wq, 0)):
        for kc in range(NKC):
            S("gpsimd", lambda e, kc=kc, off=off, qsem=qsem: e.dma_start(
                t_wab[:, kc * 3 * C + off: kc * 3 * C + off + C],
                d_wa[kc * 128:(kc + 1) * 128, off:off + C]).then_inc(qsem, 16))
    w_wk = (q_wk, 8 * 16)
    w_wkv = (q_wv, 8 * 16)
    w_wkvq = (q_wq, 8 * 16)
    for kc in range(NKC):
        S("gpsimd", lambda e, kc=kc: e.dma_start(
            t_wpb[:, kc * C:(kc + 1) * C],
            d_wp[kc * 128:(kc + 1) * 128, :]).then_inc(q_wsetup, 16))
        n_wsetup[0] += 16
    w_setup = (q_setup, SETUP_TOTAL)
    w_gsetup = (q_gsetup, n_gsetup[0])
    w_wsetup = (q_wsetup, n_wsetup[0])

    tok_bqs = S("vector", lambda e: e.tensor_scalar(
        t_bqs[:], t_bqk[:, 0:8], 0.125, None, op0=ALU.mult), waits=[w_setup])

    trb_ev = [None, None]

    tok_cast = {}
    tok_trE = {}
    tok_kev = {}
    tok_vev = {}
    tok_qev = {}
    tok_kmm = {}
    tok_qmm = {}
    tok_v_mm = {}
    tok_scg = {}
    tok_mask = {}
    tok_exp = {}
    tok_red = {}
    tok_recip = {}
    tok_ptpe = {}
    tok_ptev = {}
    tok_av = {}
    tok_aoev = {}
    tok_normao = {}
    tok_aoT_pe = {}
    tok_aoT_ev = {}
    tok_y1_mm = {}
    tok_proj_ev = {}
    tok_logit_ev = {}
    tok_xnew = {}
    tok_trP = {}

    nx_load = [0] * 6
    n_spill = [0]
    avq = {}
    chunk_tiles = [list(range(4 * c2, min(4 * c2 + 4, NTQ))) for c2 in range(5)]

    for ch, tiles in enumerate(chunk_tiles):
        Wc = len(tiles) * 128
        qoff = 256 if ch == 0 else 0

        for ti, tq in enumerate(tiles):
            sl = tq % NSL
            qi = tq % 6
            nx_load[qi] += 16
            xw = (q_x[qi], nx_load[qi])
            dst = stream[:, sl * C:(sl + 1) * C]
            lw = []
            if tq - NSL >= 2:
                tt_old = tq - NSL - 2
                lw.append((q_spill, 16 * n_spill[0]))
                lw.append(tok_logit_ev[tt_old])
            elif tq - NSL >= 0:
                lw.append(tok_cast[tq - NSL])
            S("sync", lambda e, dst=dst, tq=tq, qi=qi: e.dma_start(
                dst, d_x[tq * 128:(tq + 1) * 128, :]).then_inc(q_x[qi], 16), waits=lw)

            cbuf = t_xc[:, (tq % 2) * C:(tq % 2 + 1) * C]
            cw = [xw]
            if tq - 2 >= 0:
                cw.append(tok_trP[tq - 2])
            tok_cast[tq] = S("scalar", lambda e, cbuf=cbuf, dst=dst: e.copy(cbuf, dst),
                             waits=cw)

            def mk_tr(eng, cbuf=cbuf):
                last = None
                for j in range(NKC):
                    last = eng.transpose(p_tra[:, j * 128:(j + 1) * 128],
                                         cbuf[:, j * 128:(j + 1) * 128], t_identb[:])
                return last
            trw = [tok_cast[tq], w_setup]
            if tq >= 1:
                trw.append(tok_trE[tq - 1])
            tok_trP[tq] = S("tensor", mk_tr, waits=trw)

            xdst = (t_xTb4[:].rearrange("p (kc t) -> p kc t", kc=NKC)
                    [:, :, ti * 128:(ti + 1) * 128])
            evw = [tok_trP[tq]]
            if ch >= 1 and ti == 0:
                evw.append(tok_qmm[(ch - 1, NKC - 1)])
            tok_trE[tq] = S("vector", lambda e, xdst=xdst: e.tensor_copy(
                xdst, p_tra[:, 0:C].rearrange("p (kc t) -> p kc t", kc=NKC)),
                waits=evw)

        for fc in range(NKC):
            def mk_k(eng, fc=fc, Wc=Wc):
                last = None
                for kc in range(NKC):
                    last = eng.matmul(
                        p_sc4[fc % 4][:, 0:Wc],
                        t_wab[:, kc * 3 * C + C + fc * 128: kc * 3 * C + C + (fc + 1) * 128],
                        t_xTb4[:, kc * 512:kc * 512 + Wc],
                        start=(kc == 0), stop=(kc == NKC - 1))
                return last
            kw = [w_wk]
            if fc == 0:
                kw += [tok_trE[t2] for t2 in tiles]
                if ch >= 1:
                    lastqb_prev = 3 if ch == 1 else 4 * (ch - 1) + 3
                    kw.append(tok_mask[(lastqb_prev - 2, 3)])
            elif fc == 4:
                kw.append(tok_kev[(ch, 3)])
            tok_kmm[(ch, fc)] = S("tensor", mk_k, waits=kw)
            if fc in (3, 7):
                for fc2 in range(fc - 3, fc + 1):
                    s0 = tiles[0] % NSL
                    kdst = (t_kring[:].rearrange("p (s c) -> p s c", s=NSL)
                            [:, s0:s0 + len(tiles), fc2 * 128:(fc2 + 1) * 128])
                    ksrc = (p_sc4[fc2 % 4][:, 0:Wc]
                            .rearrange("p (t c) -> p t c", c=128))
                    tok_kev[(ch, fc2)] = S("scalar", lambda e, kdst=kdst, ksrc=ksrc,
                                           fc2=fc2:
                                           e.activation(kdst, ksrc, AF.Identity,
                                                        bias=t_bqk[:, 8 + fc2:9 + fc2]),
                                           waits=[tok_kmm[(ch, fc)], w_setup])

        for ti, tq in enumerate(tiles):
            for vh in range(2):
                def mk_v(eng, vh=vh, ti=ti):
                    last = None
                    for kc in range(NKC):
                        last = eng.matmul(
                            p_v[:],
                            t_xTb4[:, kc * 512 + ti * 128: kc * 512 + (ti + 1) * 128],
                            t_wab[:, kc * 3 * C + 2 * C + vh * 512:
                                  kc * 3 * C + 2 * C + (vh + 1) * 512],
                            start=(kc == 0), stop=(kc == NKC - 1))
                    return last
                vw = [tok_trE[tq], w_wkv]
                if vh == 1:
                    vw.append(tok_vev[(tq, 0)])
                elif (tq - 1, 1) in tok_vev:
                    vw.append(tok_vev[(tq - 1, 1)])
                tok_v_mm[(tq, vh)] = S("tensor", mk_v, waits=vw)
                vdst = t_vring[:, (tq % NSL) * C + vh * 512:
                               (tq % NSL) * C + (vh + 1) * 512]
                ew = [tok_v_mm[(tq, vh)], w_setup, w_gsetup]
                if tq - NSL in avq:
                    ew.append(avq[tq - NSL])
                tok_vev[(tq, vh)] = S("vector", lambda e, vdst=vdst, vh=vh:
                                      e.scalar_tensor_tensor(
                                          vdst, p_v[:], 1.0,
                                          t_bvbc[:, vh * 512:(vh + 1) * 512],
                                          op0=ALU.mult, op1=ALU.add), waits=ew)

        for fc in range(NKC):
            def mk_q(eng, fc=fc, Wc=Wc, qoff=qoff):
                last = None
                for kc in range(NKC):
                    last = eng.matmul(
                        p_sc4[fc % 4][:, qoff:Wc],
                        t_wab[:, kc * 3 * C + fc * 128: kc * 3 * C + (fc + 1) * 128],
                        t_xTb4[:, kc * 512 + qoff:kc * 512 + Wc],
                        start=(kc == 0), stop=(kc == NKC - 1))
                return last
            qw = [tok_bqs, w_wkvq]
            if fc == 0:
                qw.append(tok_kev[(ch, NKC - 1)])
            elif fc == 4:
                qw.append(tok_qev[(ch, 3)])
            tok_qmm[(ch, fc)] = S("tensor", mk_q, waits=qw)
            if fc in (3, 7):
                for fc2 in range(fc - 3, fc + 1):
                    qdst = t_q4[:, fc2 * 512 + qoff: fc2 * 512 + Wc]
                    tok_qev[(ch, fc2)] = S("scalar", lambda e, qdst=qdst, fc2=fc2,
                                           Wc=Wc, qoff=qoff:
                                           e.activation(qdst,
                                                        p_sc4[fc2 % 4][:, qoff:Wc],
                                                        AF.Identity, scale=0.125,
                                                        bias=t_bqs[:, fc2:fc2 + 1]),
                                           waits=[tok_qmm[(ch, fc)]])

        qbs = [2, 3] if ch == 0 else list(tiles)
        for tq in qbs:
            tt = tq - 2
            ti = tq - 4 * ch
            mvar = tt if tt < 2 else 2
            for G in range(4):
                g2 = G % 2

                def mk_sc(eng, G=G, tq=tq, ti=ti):
                    last = None
                    for hloc in range(4):
                        h = G * 4 + hloc
                        fc, r0 = h // 2, (h % 2) * 64
                        for kr in range(3):
                            ks = (tq - 2 + kr) % NSL
                            last = eng.matmul(
                                p_sc4[hloc][:, kr * 128:(kr + 1) * 128],
                                t_q4[r0:r0 + 64, fc * 512 + ti * 128:
                                     fc * 512 + (ti + 1) * 128],
                                t_kring[r0:r0 + 64, ks * C + fc * 128: ks * C + (fc + 1) * 128],
                                start=True, stop=True)
                    return last
                sw = [tok_qev[(ch, NKC - 1)]]
                if G >= 1:
                    sw.append(tok_mask[(tt, G - 1)])
                elif tt >= 1:
                    sw.append(tok_mask[(tt - 1, 3)])
                tok_scg[(tt, G)] = S("tensor", mk_sc, waits=sw)

                mw = [tok_scg[(tt, G)], w_setup, w_gsetup]
                if G == 0 and tt >= 1:
                    mw.append(tok_exp[(tt - 1, 3)])
                pbuf = t_prob[:, g2 * 4 * W3:(g2 + 1) * 4 * W3]
                pk = (tt, G - 2) if G >= 2 else (tt - 1, G + 2)
                for hloc in range(4):
                    mwh = list(mw) if hloc == 0 else []
                    tok_mask[(tt, G)] = S("vector", lambda e, hloc=hloc, mvar=mvar:
                        e.tensor_tensor(
                            t_scf[:, hloc * W3:(hloc + 1) * W3],
                            p_sc4[hloc][:, 0:W3],
                            t_masks[:, mvar * W3:(mvar + 1) * W3], op=ALU.add),
                        waits=mwh + ([tok_exp[(tt, G - 1)]] if hloc == 0 and G >= 1 else []))
                ew = [tok_mask[(tt, G)]]
                if pk in tok_red:
                    ew += [tok_red[pk], tok_ptpe[pk + (1,)]]
                tok_exp[(tt, G)] = S("scalar", lambda e, pbuf=pbuf: e.activation(
                    pbuf, t_scf[:], AF.Exp), waits=ew)
                tok_red[(tt, G)] = S("vector", lambda e, G=G, pbuf=pbuf: e.reduce_sum(
                    t_den[:, G * 4:(G + 1) * 4].rearrange("p h -> p h ()"),
                    pbuf.rearrange("p (h j) -> p h j", h=4), axis=AX.X),
                    waits=[tok_exp[(tt, G)]])

                for sub in range(2):
                    def mk_pt(eng, pbuf=pbuf, sub=sub):
                        last = None
                        for hl in range(2):
                            for kr in range(3):
                                last = eng.transpose(
                                    p_trb[:, (hl * 3 + kr) * 128:(hl * 3 + kr + 1) * 128],
                                    pbuf[:, (2 * sub + hl) * W3 + kr * 128:
                                         (2 * sub + hl) * W3 + (kr + 1) * 128],
                                    t_identb[:])
                        return last
                    tok_ptpe[(tt, G, sub)] = S("tensor", mk_pt,
                                               waits=[tok_exp[(tt, G)], trb_ev[1]])
                    ptdst = t_probT[:, g2 * 4 * W3 + sub * 768: g2 * 4 * W3 + (sub + 1) * 768]
                    ew = [tok_ptpe[(tt, G, sub)]]
                    ak = (tt, G - 2, 2 * sub + 1) if G >= 2 else (tt - 1, G + 2, 2 * sub + 1)
                    if ak in tok_av:
                        ew.append(tok_av[ak])
                    tok_ptev[(tt, G, sub)] = S("vector", lambda e, ptdst=ptdst: e.tensor_copy(
                        ptdst, p_trb[:, 0:768]), waits=ew)
                    trb_ev[1] = tok_ptev[(tt, G, sub)]

                for hloc in range(4):
                    h = G * 4 + hloc
                    sub = hloc // 2

                    def mk_av(eng, g2=g2, hloc=hloc, h=h, G=G, tq=tq):
                        last = None
                        for kr in range(3):
                            ks = (tq - 2 + kr) % NSL
                            last = eng.matmul(
                                p_misc[:, (G % 2) * 256 + hloc * 64:
                                       (G % 2) * 256 + (hloc + 1) * 64],
                                t_probT[:, g2 * 4 * W3 + hloc * W3 + kr * 128:
                                        g2 * 4 * W3 + hloc * W3 + (kr + 1) * 128],
                                t_vring[:, ks * C + h * 64: ks * C + (h + 1) * 64],
                                start=(kr == 0), stop=(kr == 2))
                        return last
                    aw = [tok_ptev[(tt, G, sub)], tok_vev[(tq, 1)]]
                    if G == 0 and hloc == 0 and tt >= 1:
                        aw.append(tok_logit_ev[tt - 1])
                    pk2 = (tt, G - 1) if G >= 1 else (tt - 1, 3)
                    if hloc == 0 and pk2 in tok_aoev:
                        aw.append(tok_aoev[pk2])
                    tok_av[(tt, G, hloc)] = S("tensor", mk_av, waits=aw)

                tok_aoev[(tt, G)] = S("vector", lambda e, G=G: e.tensor_copy(
                    t_ao[:, G * 256:(G + 1) * 256],
                    p_misc[:, (G % 2) * 256:(G % 2 + 1) * 256]),
                    waits=[tok_av[(tt, G, 3)]])

            avq[tt] = tok_av[(tt, 3, 3)]

            tok_recip[tt] = S("vector", lambda e: e.reciprocal(t_rden[:], t_den[:]),
                              waits=[tok_red[(tt, 3)]])
            nw = [tok_recip[tt], tok_aoev[(tt, 3)]]
            if tt >= 1:
                nw.append(tok_aoT_pe[tt - 1])
            tok_normao[tt] = S("vector", lambda e: e.tensor_tensor(
                t_aon[:].rearrange("p (h d) -> p h d", h=16),
                t_ao[:].rearrange("p (h d) -> p h d", h=16),
                t_rden[:].rearrange("p h -> p h ()").broadcast_to([128, NH, HD]),
                op=ALU.mult), waits=nw)

            def mk_aoT(eng):
                last = None
                for j in range(NKC):
                    last = eng.transpose(p_trb[:, j * 128:(j + 1) * 128],
                                         t_aon[:, j * 128:(j + 1) * 128], t_identb[:])
                return last
            aw = [tok_normao[tt], trb_ev[1]]
            tok_aoT_pe[tt] = S("tensor", mk_aoT, waits=aw)
            ew = [tok_aoT_pe[tt]]
            if tt >= 1:
                ew.append(tok_y1_mm[(tt - 1, 1)])
            tok_aoT_ev[tt] = S("vector", lambda e: e.tensor_copy(t_aoT[:], p_trb[:, 0:C]),
                               waits=ew)
            trb_ev[1] = tok_aoT_ev[tt]

            for nh2 in range(2):
                def mk_pj(eng, nh2=nh2):
                    last = None
                    for kc in range(NKC):
                        last = eng.matmul(
                            p_misc[:],
                            t_aoT[:, kc * 128:(kc + 1) * 128],
                            t_wpb[:, kc * C + nh2 * 512: kc * C + (nh2 + 1) * 512],
                            start=(kc == 0), stop=(kc == NKC - 1))
                    return last
                pw = [tok_aoT_ev[tt], w_wsetup]
                if nh2 == 1:
                    pw.append(tok_proj_ev[(tt, 0)])
                tok_y1_mm[(tt, nh2)] = S("tensor", mk_pj, waits=pw)
                ew = [tok_y1_mm[(tt, nh2)], w_setup, w_gsetup]
                tok_proj_ev[(tt, nh2)] = S("vector", lambda e, nh2=nh2:
                                           e.scalar_tensor_tensor(
                                               t_w[:, nh2 * 512:(nh2 + 1) * 512], p_misc[:], 1.0,
                                               t_bpbc[:, nh2 * 512:(nh2 + 1) * 512],
                                               op0=ALU.mult, op1=ALU.add), waits=ew)

            tok_res = S("vector", lambda e, tq=tq: e.tensor_tensor(
                t_w[:], t_w[:], stream[:, (tq % NSL) * C:(tq % NSL + 1) * C], op=ALU.add),
                waits=[tok_proj_ev[(tt, 0)], tok_proj_ev[(tt, 1)]])

            tok_ns = S("vector", lambda e: e.reduce_sum(t_stat[:, 0:1], t_w[:], axis=AX.X,
                                                        negate=True), waits=[tok_res])
            tok_nm = S("vector", lambda e: e.tensor_scalar(t_stat[:, 1:2], t_stat[:, 0:1],
                                                           1.0 / C, None, op0=ALU.mult),
                       waits=[tok_ns])
            tok_ct = S("vector", lambda e: e.tensor_scalar(t_cent[:], t_w[:], t_stat[:, 1:2],
                                                           None, op0=ALU.add), waits=[tok_nm])
            tok_sq = S("vector", lambda e: e.tensor_tensor(t_w[:], t_cent[:], t_cent[:],
                                                           op=ALU.mult), waits=[tok_ct])
            tok_vr = S("vector", lambda e: e.reduce_sum(t_stat[:, 2:3], t_w[:], axis=AX.X),
                       waits=[tok_sq])
            tok_v2 = S("vector", lambda e: e.tensor_scalar(t_stat[:, 3:4], t_stat[:, 2:3],
                                                           1.0 / C, EPS1, op0=ALU.mult,
                                                           op1=ALU.add), waits=[tok_vr])
            tok_ln = S("scalar", lambda e: e.activation(t_stat[:, 4:5], t_stat[:, 3:4], AF.Ln),
                       waits=[tok_v2])
            tok_rs = S("scalar", lambda e: e.activation(t_stat[:, 5:6], t_stat[:, 4:5], AF.Exp,
                                                        scale=-0.5), waits=[tok_ln])
            tok_xnew[tt] = S("vector", lambda e, tq=tq: e.tensor_scalar(
                stream[:, (tq % NSL) * C:(tq % NSL + 1) * C], t_cent[:], t_stat[:, 5:6],
                None, op0=ALU.mult), waits=[tok_rs])

            n_spill[0] += 1
            S("sync", lambda e, tt=tt, tq=tq: e.dma_start(
                d_x1[tt * 128:(tt + 1) * 128, :],
                stream[:, (tq % NSL) * C:(tq % NSL + 1) * C]).then_inc(q_spill, 16),
              waits=[tok_xnew[tt]])

            tok_rtr_ev = {}
            for bt in range(2):
                def mk_rtr(eng, bt=bt, tq=tq):
                    last = None
                    for j in range(4):
                        last = eng.transpose(
                            p_misc[:, j * 128:(j + 1) * 128],
                            stream[:, (tq % NSL) * C + (bt * 4 + j) * 128: (tq % NSL) * C + (bt * 4 + j + 1) * 128],
                            t_identf[:])
                    return last
                rw = [tok_xnew[tt]]
                rw.append(tok_proj_ev[(tt, 1)] if bt == 0 else tok_rtr_ev[0])
                tok_rtr_pe = S("tensor", mk_rtr, waits=rw)
                tok_rtr_ev[bt] = S("vector", lambda e, bt=bt: e.tensor_copy(
                    t_xT32[:, bt * 512:(bt + 1) * 512], p_misc[:]), waits=[tok_rtr_pe])

            def mk_rmm(eng):
                last = None
                for kc in range(NKC):
                    last = eng.matmul(p_misc[:, 0:NE],
                                      t_xT32[:, kc * 128:(kc + 1) * 128],
                                      t_wrt[:, kc * NE:(kc + 1) * NE],
                                      start=(kc == 0), stop=(kc == NKC - 1))
                return last
            tok_rmm = S("tensor", mk_rmm, waits=[tok_rtr_ev[0], tok_rtr_ev[1], w_setup])
            tok_logit_ev[tt] = S("vector", lambda e: e.tensor_copy(t_lg[:], p_misc[:, 0:NE]),
                                 waits=[tok_rmm])
            tok_m8 = S("vector", lambda e: e.max(t_mx8[:], t_lg[:]), waits=[tok_logit_ev[tt]])
            tok_mi = S("vector", lambda e: e.max_index(t_idxu[:], t_mx8[:], t_lg[:]),
                       waits=[tok_m8])
            tok_if = S("vector", lambda e: e.tensor_scalar(t_idxf[:], t_idxu[:, 0:1], 0.0,
                                                           None, op0=ALU.add),
                       waits=[tok_mi] + ([(q_idx, 16 * tt)] if tt >= 1 else []))
            S("sync", lambda e, tt=tt: e.dma_start(
                d_idx.rearrange("(a p) -> a p", p=128)[tt].rearrange("(p o) -> p o", o=1),
                t_idxf[:]).then_inc(q_idx, 16), waits=[tok_if])

    S("sync", lambda e: e.nop(), waits=[(q_spill, 16 * NTT), (q_idx, 16 * NTT)])

    a_st.close()
    b_st = ExitStack()
    bsb = lambda name, shape, dt: b_st.enter_context(nc.sbuf_tensor(name, shape, dt))

    t_ring = bsb("ring", [128, 32 * 1024], BF16)
    t_acc = bsb("acc", [128, 32 * 1024], BF16)
    acc_full = t_acc
    t_xTg = bsb("xTg", [128, NKC * 1024], BF16)
    t_idxbc = bsb("idx_bc", [128, 1024], F32)
    t_eq = bsb("eq", [128, 2 * 1024], U8)
    t_gelu = bsb("gelu", [128, 3 * 512], BF16)
    t_xrel2 = bsb("xrel2", [128, 2 * C], F32)
    t_xrb = bsb("xrb", [128, 2 * C], BF16)
    t_y2 = bsb("y2", [128, 2 * C], F32)
    t_ct2 = bsb("ct2", [128, 2 * C], F32)
    t_st2 = bsb("stat2", [128, 16], F32)

    bpsal = lambda name, shape, dt: b_st.enter_context(nc.psum_tensor(name, shape, dt))
    p_i = [bpsal(f"p_i{j}", [128, 512], F32) for j in range(4)]
    p_y2 = [bpsal(f"p_y2{j}", [128, 512], F32) for j in range(2)]
    p_trB = bpsal("p_trB", [128, 1024], BF16)

    sched.new_block()
    S = sched.op

    nw_slot = [0] * 32
    slot_last_reader = [None] * 32
    load_ctr = [0]
    n_xrel = [0] * 4
    n_xrel2 = [0] * 4
    n_out2 = [0, 0]

    tok_sel = {}
    tok_gelu = {}
    tok_immP = {}
    tok_y2mm = {}
    tok_y2ev = {}
    tok_tail_last = {}
    tok_eqtok = {}
    tok_xTg_ev = {}
    tok_trB_tok = {}
    tok_tail_res = {}
    idxbc_readers = []
    gelu_slot_reader = [None, None, None]
    eq_slot_reader = [None, None]

    tok_acc0 = S("vector", lambda e: e.memset(acc_full[:], 0.0),
                 waits=[(q_spill, 16 * NTT), tok_logit_ev[NTT - 1]])

    for g in range(2):
        iw = [(q_idx, 16 * NTT)] + idxbc_readers
        idxbc_readers = []
        S("sync", lambda e, g=g: e.dma_start(
            t_idxbc[:], bass.AP(d_idx.tensor, g * 1024, [[0, 128], [1, 1024]])).then_inc(
            q_idxbc, 16), waits=iw)
        w_idxbc = (q_idxbc, 16 * (g + 1))

        for ttl in range(8):
            tt = g * 8 + ttl
            qi = ttl % 4
            rw = [(q_spill, 16 * NTT)]
            if (g, ttl - 2) in tok_trB_tok:
                rw.append(tok_trB_tok[(g, ttl - 2)])
            elif g == 1 and ttl < 2:
                rw.append(tok_trB_tok[(0, 6 + ttl)])
            n_xrel[qi] += 1
            S("gpsimd", lambda e, tt=tt, ttl=ttl, qi=qi: e.dma_start(
                t_xrb[:, (ttl % 2) * C:(ttl % 2 + 1) * C],
                d_x1[tt * 128:(tt + 1) * 128, :]).then_inc(q_xrel[qi], 16), waits=rw)
            xrw = (q_xrel[qi], 16 * n_xrel[qi])

            def mk_trB(eng, ttl=ttl):
                last = None
                for j in range(NKC):
                    last = eng.transpose(p_trB[:, j * 128:(j + 1) * 128],
                                         t_xrb[:, (ttl % 2) * C + j * 128:
                                               (ttl % 2) * C + (j + 1) * 128], t_identb[:])
                return last
            tw = [xrw, w_setup]
            if (g, ttl - 1) in tok_xTg_ev:
                tw.append(tok_xTg_ev[(g, ttl - 1)])
            elif g == 1 and ttl == 0:
                tw.append(tok_xTg_ev[(0, 7)])
            tok_trB_pe = S("tensor", mk_trB, waits=tw)
            tok_trB_tok[(g, ttl)] = tok_trB_pe
            xTg_v = t_xTg.rearrange("p (k t) -> p k t", k=NKC)
            ew = [tok_trB_pe]
            if g == 1 and ttl == 0:
                ew.append(tok_y2mm[(0, 7, 1)])
            tok_xTg_ev[(g, ttl)] = S("vector", lambda e, xTg_v=xTg_v, ttl=ttl:
                                     e.tensor_copy(xTg_v[:, :, ttl * 128:(ttl + 1) * 128],
                                                   p_trB[:, 0:C]), waits=ew)
        w_xTg = [tok_xTg_ev[(g, t2)] for t2 in range(8)]

        for eix in range(NE):
            eqbuf = t_eq[:, (eix % 2) * 1024:(eix % 2 + 1) * 1024]
            eqw = [w_idxbc]
            if eq_slot_reader[eix % 2] is not None:
                eqw.append(eq_slot_reader[eix % 2])
            tok_eqtok[(g, eix)] = S("vector", lambda e, eqbuf=eqbuf, eix=eix:
                                    e.tensor_scalar(eqbuf, t_idxbc[:], float(eix), None,
                                                    op0=ALU.is_equal), waits=eqw)
            idxbc_readers.append(tok_eqtok[(g, eix)])

            for iq in range(4):
                slot_of_kc = {}
                for kc in range(NKC):
                    L = load_ctr[0]
                    s = L % 32
                    nw_slot[s] += 1
                    slot_of_kc[kc] = (s, nw_slot[s])
                    lw = [slot_last_reader[s]] if slot_last_reader[s] is not None else []
                    S("gpsimd", lambda e, s=s, eix=eix, kc=kc, iq=iq: e.dma_start(
                        t_ring[:, s * 1024:(s + 1) * 1024],
                        d_w1[eix, kc * 128:(kc + 1) * 128,
                             iq * 1024:(iq + 1) * 1024]).then_inc(q_slot[s], 16), waits=lw)
                    load_ctr[0] += 1

                for ps2 in range(2):
                    for icl in range(8):
                        seq = ps2 * 8 + icl
                        bank = p_i[seq % 4]
                        icg = iq * 8 + icl
                        key = (g, eix, iq, ps2, icl)

                        def mk_im(eng, bank=bank, icl=icl, ps2=ps2,
                                  slot_of_kc=dict(slot_of_kc)):
                            last = None
                            for kc in range(NKC):
                                s2, _ = slot_of_kc[kc]
                                last = eng.matmul(
                                    bank[:],
                                    t_ring[:, s2 * 1024 + icl * 128: s2 * 1024 + (icl + 1) * 128],
                                    t_xTg[:, kc * 1024 + ps2 * 512: kc * 1024 + (ps2 + 1) * 512],
                                    start=(kc == 0), stop=(kc == NKC - 1))
                            return last

                        def back_key(delta):
                            sq = seq - delta
                            e2, i2, g2 = eix, iq, g
                            while sq < 0:
                                sq += 16
                                i2 -= 1
                                if i2 < 0:
                                    i2 = 3
                                    e2 -= 1
                                    if e2 < 0:
                                        e2 = NE - 1
                                        g2 -= 1
                                        if g2 < 0:
                                            return None
                            return (g2, e2, i2, sq // 8, sq % 8)

                        mw = list(w_xTg)
                        mw += [(q_slot[s2], 16 * n2) for (s2, n2) in slot_of_kc.values()]
                        bk = back_key(4)
                        if bk is not None and bk in tok_gelu:
                            mw.append(tok_gelu[bk])
                        tok_immP[key] = S("tensor", mk_im, waits=mw)

                        gb = seq % 3
                        gbuf = t_gelu[:, gb * 512:(gb + 1) * 512]
                        ew = [tok_immP[key]]
                        if gelu_slot_reader[gb] is not None:
                            ew.append(gelu_slot_reader[gb])
                        tok_gelu[key] = S("scalar", lambda e, gbuf=gbuf, bank=bank,
                                          eix=eix, icg=icg:
                                          e.activation(gbuf, bank[:], AF.Gelu,
                                                       bias=t_b1c[:, icg * 8 + eix:
                                                                  icg * 8 + eix + 1]),
                                          waits=ew + [w_b1c])
                        accsl = acc_full[:, icg * 1024 + ps2 * 512: icg * 1024 + (ps2 + 1) * 512]
                        sw = [tok_gelu[key], tok_eqtok[(g, eix)]]
                        if g == 1 and eix == 0:
                            sw.append(tok_y2ev[(0, 7, 1)])
                        tok_sel[key] = S("vector", lambda e, accsl=accsl, eqbuf=eqbuf,
                                         gbuf=gbuf, ps2=ps2:
                                         e.copy_predicated(
                                             accsl, eqbuf[:, ps2 * 512:(ps2 + 1) * 512],
                                             gbuf), waits=sw)
                        gelu_slot_reader[gb] = tok_sel[key]
                for kc in range(NKC):
                    s2, _ = slot_of_kc[kc]
                    slot_last_reader[s2] = tok_immP[(g, eix, iq, 1, 7)]
            eq_slot_reader[eix % 2] = tok_sel[(g, eix, 3, 1, 7)]

        last_sel = tok_sel[(g, NE - 1, 3, 1, 7)]

        wo_slot = {}
        for ic in range(32):
            L = load_ctr[0]
            s = L % 32
            nw_slot[s] += 1
            wo_slot[ic] = (s, nw_slot[s])
            lw = [slot_last_reader[s]] if slot_last_reader[s] is not None else []
            S("gpsimd", lambda e, s=s, ic=ic: e.dma_start(
                t_ring[:, s * 1024:(s + 1) * 1024],
                d_wo[ic * 128:(ic + 1) * 128, :]).then_inc(q_slot[s], 16), waits=lw)
            load_ctr[0] += 1

        for ttl in range(8):
            tt = g * 8 + ttl
            qi = ttl % 4
            tb = ttl % 2
            rw = [(q_spill, 16 * NTT)]
            if (g, ttl - 2) in tok_tail_res:
                rw.append(tok_tail_res[(g, ttl - 2)])
            elif g == 1 and ttl < 2:
                rw.append(tok_tail_res[(0, 6 + ttl)])
            n_xrel2[qi] += 1
            S("sync", lambda e, tt=tt, tb=tb, qi=qi: e.dma_start(
                t_xrel2[:, tb * C:(tb + 1) * C],
                d_x1[tt * 128:(tt + 1) * 128, :]).then_inc(q_xrel2[qi], 16), waits=rw)
            xr2w = (q_xrel2[qi], 16 * n_xrel2[qi])

            for ch in range(2):
                def mk_wo(eng, ttl=ttl, ch=ch, wo_slot=dict(wo_slot)):
                    last = None
                    for ic in range(32):
                        s2, _ = wo_slot[ic]
                        last = eng.matmul(
                            p_y2[ch][:],
                            acc_full[:, ic * 1024 + ttl * 128: ic * 1024 + (ttl + 1) * 128],
                            t_ring[:, s2 * 1024 + ch * 512: s2 * 1024 + (ch + 1) * 512],
                            start=(ic == 0), stop=(ic == 31))
                    return last
                ww = [last_sel]
                ww += [(q_slot[s2], 16 * n2) for (s2, n2) in wo_slot.values()]
                if ttl >= 1:
                    ww.append(tok_y2ev[(g, ttl - 1, ch)])
                elif g == 1:
                    ww.append(tok_y2ev[(0, 7, ch)])
                tok_y2mm[(g, ttl, ch)] = S("tensor", mk_wo, waits=ww)
                ew = [tok_y2mm[(g, ttl, ch)], w_setup, xr2w]
                if (g, ttl - 2) in tok_tail_last:
                    ew.append(tok_tail_last[(g, ttl - 2)])
                elif g == 1 and ttl < 2:
                    ew.append(tok_tail_last[(0, 6 + ttl)])
                tok_y2ev[(g, ttl, ch)] = S("vector", lambda e, ch=ch, tb=tb:
                                           e.scalar_tensor_tensor(
                                               t_y2[:, tb * C + ch * 512:
                                                    tb * C + (ch + 1) * 512],
                                               p_y2[ch][:], 1.0,
                                               t_bo[:, ch * 512:(ch + 1) * 512],
                                               op0=ALU.mult, op1=ALU.add), waits=ew)
            if ttl == 7:
                for ic in range(32):
                    s2, _ = wo_slot[ic]
                    slot_last_reader[s2] = tok_y2mm[(g, 7, 1)]

            yb = t_y2[:, tb * C:(tb + 1) * C]
            cb = t_ct2[:, tb * C:(tb + 1) * C]
            xr = t_xrel2[:, tb * C:(tb + 1) * C]
            sc = t_st2[:, tb * 8:(tb + 1) * 8]

            def ln_chain(first_w, src, cent, dst, eps, affine, sc=sc):
                tns = S("vector", lambda e, src=src, sc=sc: e.reduce_sum(
                    sc[:, 0:1], src, axis=AX.X, negate=True), waits=first_w)
                tnm = S("vector", lambda e, sc=sc: e.tensor_scalar(
                    sc[:, 1:2], sc[:, 0:1], 1.0 / C, None, op0=ALU.mult), waits=[tns])
                tct = S("vector", lambda e, src=src, cent=cent, sc=sc: e.tensor_scalar(
                    cent, src, sc[:, 1:2], None, op0=ALU.add), waits=[tnm])
                tsq = S("vector", lambda e, src=src, cent=cent: e.tensor_tensor(
                    src, cent, cent, op=ALU.mult), waits=[tct])
                tvr = S("vector", lambda e, src=src, sc=sc: e.reduce_sum(
                    sc[:, 2:3], src, axis=AX.X), waits=[tsq])
                tv2 = S("vector", lambda e, sc=sc, eps=eps: e.tensor_scalar(
                    sc[:, 3:4], sc[:, 2:3], 1.0 / C, eps, op0=ALU.mult, op1=ALU.add),
                    waits=[tvr])
                tln = S("scalar", lambda e, sc=sc: e.activation(sc[:, 4:5], sc[:, 3:4],
                                                                AF.Ln), waits=[tv2])
                trs = S("scalar", lambda e, sc=sc: e.activation(sc[:, 5:6], sc[:, 4:5],
                                                                AF.Exp, scale=-0.5),
                        waits=[tln])
                if affine:
                    tz = S("vector", lambda e, cent=cent, dst=dst, sc=sc:
                           e.scalar_tensor_tensor(dst, cent, sc[:, 5:6], t_gam[:],
                                                  op0=ALU.mult, op1=ALU.mult), waits=[trs])
                    tz = S("vector", lambda e, dst=dst: e.tensor_tensor(
                        dst, dst, t_bet[:], op=ALU.add), waits=[tz])
                else:
                    tz = S("vector", lambda e, cent=cent, dst=dst, sc=sc: e.tensor_scalar(
                        dst, cent, sc[:, 5:6], None, op0=ALU.mult), waits=[trs])
                return tz

            r1w = [tok_y2ev[(g, ttl, 0)], tok_y2ev[(g, ttl, 1)]]
            if n_out2[tb] > 0:
                r1w.append((q_out2[tb], n_out2[tb]))
            t_r1 = S("vector", lambda e, yb=yb, xr=xr: e.tensor_tensor(
                yb, yb, xr, op=ALU.add), waits=r1w)
            t_z = ln_chain([t_r1], yb, cb, yb, EPS2, True)
            t_r2 = S("vector", lambda e, yb=yb, xr=xr: e.tensor_tensor(
                yb, xr, yb, op=ALU.add), waits=[t_z])
            tok_tail_res[(g, ttl)] = t_r2
            t_o = ln_chain([t_r2], yb, cb, cb, EPS1, False)
            tok_tail_last[(g, ttl)] = t_o

            n_out2[tb] += 16
            S("sync", lambda e, tt=tt, cb=cb, tb=tb: e.dma_start(
                d_out[tt * 128:(tt + 1) * 128, :], cb).then_inc(q_out2[tb], 16),
              waits=[t_o])

    S("sync", lambda e: e.nop(), waits=[(q_out2[0], n_out2[0]), (q_out2[1], n_out2[1])])

    sched.finalize()
    for blk_ops in sched.blocks:
        with nc.Block() as block:
            for ename, deco in (("sync", block.sync), ("gpsimd", block.gpsimd),
                                ("scalar", block.scalar), ("vector", block.vector),
                                ("tensor", block.tensor)):
                def body(engine, ename=ename, blk_ops=blk_ops):
                    sched.emit_block(blk_ops, ename, engine, eng_sems)
                deco(body)

    b_st.close()
    st.close()
    return nc


_NC_CACHE = {}


def _get_nc():
    if "nc" not in _NC_CACHE:
        _NC_CACHE["nc"] = build_program()
    return _NC_CACHE["nc"]


def _build_masks(seq0):
    p = np.arange(128)[:, None]
    f = np.arange(W3)[None, :]
    out = np.zeros((3, 128, W3), dtype=np.float32)
    for v in range(3):
        tt = v if v < 2 else 2
        allowed = (f >= p) & (f <= p + 256)
        if v < 2:
            kseq = seq0 - 256 + tt * 128 + f
            allowed = allowed & (kseq >= 0)
        out[v] = np.where(allowed, 0.0, NEG)
    return out


def kernel(L0_attn_x, L0_attn_W_attn, L0_attn_b_attn, L0_attn_W_proj, L0_attn_b_proj,
           L0_ffn_Wr, L0_ffn_W1, L0_ffn_b1, L0_ffn_Wo, L0_ffn_bo, L0_ffn_gamma,
           L0_ffn_beta):
    x = np.ascontiguousarray(np.asarray(L0_attn_x, dtype=np.float32).reshape(B * T, C))
    wa = np.ascontiguousarray(np.asarray(L0_attn_W_attn, dtype=np.float32))
    ba = np.ascontiguousarray(np.asarray(L0_attn_b_attn, dtype=np.float32))
    wp = np.ascontiguousarray(np.asarray(L0_attn_W_proj, dtype=np.float32))
    bp = np.ascontiguousarray(np.asarray(L0_attn_b_proj, dtype=np.float32))
    wrt = np.ascontiguousarray(np.asarray(L0_ffn_Wr, dtype=np.float32).T)
    w1 = np.ascontiguousarray(np.asarray(L0_ffn_W1, dtype=np.float32))
    b1 = np.ascontiguousarray(np.asarray(L0_ffn_b1, dtype=np.float32))
    wo = np.ascontiguousarray(np.asarray(L0_ffn_Wo, dtype=np.float32))
    bo = np.ascontiguousarray(np.asarray(L0_ffn_bo, dtype=np.float32))
    gam = np.ascontiguousarray(np.asarray(L0_ffn_gamma, dtype=np.float32))
    bet = np.ascontiguousarray(np.asarray(L0_ffn_beta, dtype=np.float32))

    in_maps = []
    for c in range(NCORES):
        base = c * TCORE
        seq0 = (c % 4) * TCORE
        xe = np.zeros((NTQ * 128, C), dtype=np.float32)
        if seq0 == 0:
            xe[HALO:] = x[base:base + TCORE]
        else:
            xe[:] = x[base - HALO:base + TCORE]
        in_maps.append({
            "x_ext": xe, "w_attn": wa, "b_attn": ba, "w_proj": wp, "b_proj": bp,
            "wr_t": wrt, "w1": w1, "b1": b1, "wo": wo, "bo": bo, "gamma": gam,
            "beta": bet, "masks": _build_masks(seq0),
        })

    nc = _get_nc()
    res = run_bass_kernel_spmd(nc, in_maps, core_ids=list(range(NCORES)))
    out = np.concatenate([r["out"] for r in res.results], axis=0)
    return out.reshape(B, T, C)


# version 09
# speedup vs baseline: 1.2740x; 1.0299x over previous
import numpy as np
import ml_dtypes
from contextlib import ExitStack

import concourse.bass as bass
import concourse.mybir as mybir
from concourse.bass_utils import run_bass_kernel_spmd

F32 = mybir.dt.float32
BF16 = mybir.dt.bfloat16
U8 = mybir.dt.uint8
U32 = mybir.dt.uint32
AF = mybir.ActivationFunctionType
ALU = mybir.AluOpType
AX = mybir.AxisListType

NCORES = 8
B, T, C = 2, 8192, 1024
TCORE = 2048
HALO = 256
NTT = 16
NTQ = 18
NH, HD = 16, 64
W3 = 384
FF = 4096
NE = 8
NKC = 8
NEG = -1.0e30
NSL = 8
EPS1 = 1e-5
EPS2 = 1e-12

ENGINES = ("sync", "gpsimd", "scalar", "vector", "tensor")


class Sched:

    def __init__(self):
        self.blocks = []
        self.cur = None

    def new_block(self):
        self.cur = []
        self.blocks.append(self.cur)

    def op(self, eng, fn, waits=()):
        rec = {"eng": eng, "fn": fn, "waits": [w for w in waits if w is not None],
               "inc": None}
        self.cur.append(rec)
        return rec

    def finalize(self):
        for blk in self.blocks:
            for rec in blk:
                for wtg in rec["waits"]:
                    if isinstance(wtg, dict) and wtg["inc"] is None:
                        wtg["inc"] = -1
        counters = {e: 0 for e in ENGINES}
        for blk in self.blocks:
            for rec in blk:
                if rec["inc"] == -1:
                    counters[rec["eng"]] += 1
                    rec["inc"] = counters[rec["eng"]]

    def emit_block(self, blk_ops, eng_name, engine, eng_sems):
        highwater = {}
        for rec in blk_ops:
            if rec["eng"] != eng_name:
                continue
            for wtg in rec["waits"]:
                if isinstance(wtg, dict):
                    sem, val, key = eng_sems[wtg["eng"]], wtg["inc"], ("e", wtg["eng"])
                else:
                    sem, val = wtg
                    key = ("s", id(sem))
                if highwater.get(key, -1) >= val:
                    continue
                highwater[key] = val
                engine.wait_ge(sem, val)
            inst = rec["fn"](engine)
            if rec["inc"] is not None:
                assert inst is not None, f"wait-target op returned no inst on {eng_name}"
                inst.then_inc(eng_sems[rec["eng"]], 1)


def build_program():
    nc = bass.Bass("TRN2", target_bir_lowering=False)

    d_x = nc.dram_tensor("x_ext", [NTQ * 128, C], F32, kind="ExternalInput").ap()
    d_wa = nc.dram_tensor("w_attn", [C, 3 * C], F32, kind="ExternalInput").ap()
    d_ba = nc.dram_tensor("b_attn", [3 * C], F32, kind="ExternalInput").ap()
    d_wp = nc.dram_tensor("w_proj", [C, C], F32, kind="ExternalInput").ap()
    d_bp = nc.dram_tensor("b_proj", [C], F32, kind="ExternalInput").ap()
    d_wrt = nc.dram_tensor("wr_t", [C, NE], F32, kind="ExternalInput").ap()
    d_w1 = nc.dram_tensor("w1", [NE, C, FF], F32, kind="ExternalInput").ap()
    d_b1 = nc.dram_tensor("b1", [NE, FF], F32, kind="ExternalInput").ap()
    d_wo = nc.dram_tensor("wo", [FF, C], F32, kind="ExternalInput").ap()
    d_bo = nc.dram_tensor("bo", [C], F32, kind="ExternalInput").ap()
    d_gam = nc.dram_tensor("gamma", [C], F32, kind="ExternalInput").ap()
    d_bet = nc.dram_tensor("beta", [C], F32, kind="ExternalInput").ap()
    d_msk = nc.dram_tensor("masks", [3, 128, W3], F32, kind="ExternalInput").ap()
    d_out = nc.dram_tensor("out", [TCORE, C], F32, kind="ExternalOutput").ap()
    d_x1 = nc.dram_tensor("x1_scratch", [TCORE, C], F32, kind="Internal").ap()
    d_idx = nc.dram_tensor("idx_scratch", [TCORE], F32, kind="Internal").ap()

    ident_bf = nc.inline_tensor(np.eye(128, dtype=ml_dtypes.bfloat16), "ident_bf").ap()
    ident_f32 = nc.inline_tensor(np.eye(128, dtype=np.float32), "ident_f32").ap()

    st = ExitStack()
    sb = lambda name, shape, dt: st.enter_context(nc.sbuf_tensor(name, shape, dt))
    psal = lambda name, shape, dt: st.enter_context(nc.psum_tensor(name, shape, dt))
    sem = lambda name: st.enter_context(nc.semaphore(name))

    t_identb = sb("identb", [128, 128], BF16)
    t_identf = sb("identf", [128, 128], F32)
    t_bqk = sb("bqk", [128, 16], F32)
    t_bqs = sb("bqs", [128, 8], F32)
    t_wrt = sb("wrt", [128, NKC * NE], F32)
    t_b1c = sb("b1cols", [128, NE * 32], F32)
    t_bo = sb("bo_bc", [128, C], F32)
    t_gam = sb("gam_bc", [128, C], F32)
    t_bet = sb("bet_bc", [128, C], F32)

    eng_sems = {e: sem("e_" + e) for e in ENGINES}
    q_setup = sem("q_setup")
    q_wsetup = sem("q_wsetup")
    q_wk = sem("q_wk")
    q_wv = sem("q_wv")
    q_wq = sem("q_wq")
    q_x = [sem(f"q_x{i}") for i in range(6)]
    q_spill = sem("q_spill")
    q_idx = sem("q_idx")
    q_idxbc = sem("q_idxbc")
    q_b1c = sem("q_b1c")
    q_gsetup = sem("q_gsetup")
    q_xrel = [sem(f"q_xrel{i}") for i in range(4)]
    q_xrel2 = [sem(f"q_xrel2{i}") for i in range(4)]
    q_slot = [sem(f"q_slot{i}") for i in range(32)]
    q_out2 = [sem("q_out0"), sem("q_out1")]

    sched = Sched()

    a_st = ExitStack()
    asb = lambda name, shape, dt: a_st.enter_context(nc.sbuf_tensor(name, shape, dt))

    t_wab = asb("wab", [128, NKC * 3 * C], BF16)
    t_wpb = asb("wpb", [128, NKC * C], BF16)
    t_kring = asb("kring", [128, NSL * C], BF16)
    t_vring = asb("vring", [128, NSL * C], BF16)
    t_masks = asb("masks_sb", [128, 3 * W3], BF16)
    t_bvbc = asb("bv_bc", [128, C], BF16)
    t_bpbc = asb("bp_bc", [128, C], BF16)
    t_xc = asb("xc", [128, 2 * C], BF16)
    t_xTb4 = asb("xTb4", [128, NKC * 512], BF16)
    t_q4 = asb("q4", [128, NKC * 512], BF16)
    stream = asb("stream", [128, NSL * C], F32)
    t_scf = asb("scf", [128, 4 * W3], F32)
    t_prob = asb("prob", [128, 2 * 4 * W3], BF16)
    t_probT = asb("probT", [128, 2 * 4 * W3], BF16)
    t_ao = asb("attn_out", [128, C], BF16)
    t_aon = asb("attn_out_n", [128, C], BF16)
    t_aoT = asb("attn_outT", [128, C], BF16)
    t_den = asb("den", [128, NH], F32)
    t_rden = asb("rden", [128, NH], F32)
    t_w = asb("wbuf", [128, C], F32)
    t_cent = asb("centbuf", [128, C], F32)
    t_xT32 = asb("xT32", [128, C], F32)
    t_lg = asb("lgs", [128, 8], F32)
    t_mx8 = asb("mx8", [128, 8], F32)
    t_idxu = asb("idxu", [128, 8], U32)
    t_idxf = asb("idxf", [128, 1], F32)
    t_stat = asb("stat", [128, 8], F32)

    apsal = lambda name, shape, dt: a_st.enter_context(nc.psum_tensor(name, shape, dt))
    p_tra = apsal("p_tra", [128, 1024], BF16)
    p_trb = apsal("p_trb", [128, 1024], BF16)
    p_v = apsal("p_v", [128, 512], F32)
    p_sc4 = [apsal(f"p_sc4_{i}", [128, 512], F32) for i in range(4)]
    p_misc = apsal("p_misc", [128, 512], F32)

    sched.new_block()
    S = sched.op

    n_setup = [0]
    n_gsetup = [0]

    def setup_dma(engname, fn):
        if engname == "gpsimd":
            S(engname, lambda e, fn=fn: fn(e).then_inc(q_gsetup, 16))
            n_gsetup[0] += 16
        else:
            S(engname, lambda e, fn=fn: fn(e).then_inc(q_setup, 16))
            n_setup[0] += 16

    setup_dma("sync", lambda e: e.dma_start(t_identb[:], ident_bf))
    setup_dma("sync", lambda e: e.dma_start(t_identf[:], ident_f32))
    for v in range(3):
        setup_dma("gpsimd", lambda e, v=v: e.dma_start(
            t_masks[:, v * W3:(v + 1) * W3], d_msk[v]))
    setup_dma("sync", lambda e: e.dma_start(
        t_bqk[:], bass.AP(d_ba.tensor, 0, [[1, 128], [128, 16]]),
        allow_slow_non_contiguous=True))

    def bcast_src(dap, off, n):
        return bass.AP(dap.tensor, off, [[0, 128], [1, n]])

    setup_dma("gpsimd", lambda e: e.dma_start(t_bvbc[:], bcast_src(d_ba, 2048, C)))
    setup_dma("gpsimd", lambda e: e.dma_start(t_bpbc[:], bcast_src(d_bp, 0, C)))
    setup_dma("sync", lambda e: e.dma_start(t_bo[:], bcast_src(d_bo, 0, C)))
    setup_dma("sync", lambda e: e.dma_start(t_gam[:], bcast_src(d_gam, 0, C)))
    setup_dma("sync", lambda e: e.dma_start(t_bet[:], bcast_src(d_bet, 0, C)))
    for kc in range(NKC):
        setup_dma("sync", lambda e, kc=kc: e.dma_start(
            t_wrt[:, kc * NE:(kc + 1) * NE], d_wrt[kc * 128:(kc + 1) * 128, :]))
    SETUP_TOTAL = n_setup[0]

    n_b1c = [0]
    for j in range(32):
        S("scalar", lambda e, j=j: e.dma_start(
            t_b1c[:, j * 8:(j + 1) * 8],
            bass.AP(d_b1.tensor, j * 128, [[1, 128], [FF, 8]]),
            allow_slow_non_contiguous=True).then_inc(q_b1c, 16))
        n_b1c[0] += 16
    w_b1c = (q_b1c, n_b1c[0])

    n_wsetup = [0]
    for qsem, off in ((q_wk, C), (q_wv, 2 * C), (q_wq, 0)):
        for kc in range(NKC):
            S("gpsimd", lambda e, kc=kc, off=off, qsem=qsem: e.dma_start(
                t_wab[:, kc * 3 * C + off: kc * 3 * C + off + C],
                d_wa[kc * 128:(kc + 1) * 128, off:off + C]).then_inc(qsem, 16))
    w_wk = (q_wk, 8 * 16)
    w_wkv = (q_wv, 8 * 16)
    w_wkvq = (q_wq, 8 * 16)
    for kc in range(NKC):
        S("gpsimd", lambda e, kc=kc: e.dma_start(
            t_wpb[:, kc * C:(kc + 1) * C],
            d_wp[kc * 128:(kc + 1) * 128, :]).then_inc(q_wsetup, 16))
        n_wsetup[0] += 16
    w_setup = (q_setup, SETUP_TOTAL)
    w_gsetup = (q_gsetup, n_gsetup[0])
    w_wsetup = (q_wsetup, n_wsetup[0])

    tok_bqs = S("vector", lambda e: e.tensor_scalar(
        t_bqs[:], t_bqk[:, 0:8], 0.125, None, op0=ALU.mult), waits=[w_setup])

    trb_ev = [None, None]

    tok_cast = {}
    tok_trE = {}
    tok_kev = {}
    tok_vev = {}
    tok_qev = {}
    tok_kmm = {}
    tok_qmm = {}
    tok_v_mm = {}
    tok_scg = {}
    tok_scg_h = {}
    tok_mask = {}
    tok_mask_h = {}
    tok_exp = {}
    tok_red = {}
    tok_recip = {}
    tok_ptpe = {}
    tok_ptev = {}
    tok_av = {}
    tok_aoev = {}
    tok_normao = {}
    tok_aoT_pe = {}
    tok_aoT_ev = {}
    tok_y1_mm = {}
    tok_proj_ev = {}
    tok_logit_ev = {}
    tok_xnew = {}
    tok_trP = {}

    nx_load = [0] * 6
    n_spill = [0]
    avq = {}
    chunk_tiles = [list(range(4 * c2, min(4 * c2 + 4, NTQ))) for c2 in range(5)]

    for ch, tiles in enumerate(chunk_tiles):
        Wc = len(tiles) * 128
        qoff = 256 if ch == 0 else 0

        for ti, tq in enumerate(tiles):
            sl = tq % NSL
            qi = tq % 6
            nx_load[qi] += 16
            xw = (q_x[qi], nx_load[qi])
            dst = stream[:, sl * C:(sl + 1) * C]
            lw = []
            if tq - NSL >= 2:
                tt_old = tq - NSL - 2
                lw.append((q_spill, 16 * n_spill[0]))
                lw.append(tok_logit_ev[tt_old])
            elif tq - NSL >= 0:
                lw.append(tok_cast[tq - NSL])
            S("sync", lambda e, dst=dst, tq=tq, qi=qi: e.dma_start(
                dst, d_x[tq * 128:(tq + 1) * 128, :]).then_inc(q_x[qi], 16), waits=lw)

            cbuf = t_xc[:, (tq % 2) * C:(tq % 2 + 1) * C]
            cw = [xw]
            if tq - 2 >= 0:
                cw.append(tok_trP[tq - 2])
            tok_cast[tq] = S("scalar", lambda e, cbuf=cbuf, dst=dst: e.copy(cbuf, dst),
                             waits=cw)

            def mk_tr(eng, cbuf=cbuf):
                last = None
                for j in range(NKC):
                    last = eng.transpose(p_tra[:, j * 128:(j + 1) * 128],
                                         cbuf[:, j * 128:(j + 1) * 128], t_identb[:])
                return last
            trw = [tok_cast[tq], w_setup]
            if tq >= 1:
                trw.append(tok_trE[tq - 1])
            tok_trP[tq] = S("tensor", mk_tr, waits=trw)

            xdst = (t_xTb4[:].rearrange("p (kc t) -> p kc t", kc=NKC)
                    [:, :, ti * 128:(ti + 1) * 128])
            evw = [tok_trP[tq]]
            if ch >= 1 and ti == 0:
                evw.append(tok_qmm[(ch - 1, NKC - 1)])
            tok_trE[tq] = S("vector", lambda e, xdst=xdst: e.tensor_copy(
                xdst, p_tra[:, 0:C].rearrange("p (kc t) -> p kc t", kc=NKC)),
                waits=evw)

        for fc in range(NKC):
            def mk_k(eng, fc=fc, Wc=Wc):
                last = None
                for kc in range(NKC):
                    last = eng.matmul(
                        p_sc4[fc % 4][:, 0:Wc],
                        t_wab[:, kc * 3 * C + C + fc * 128: kc * 3 * C + C + (fc + 1) * 128],
                        t_xTb4[:, kc * 512:kc * 512 + Wc],
                        start=(kc == 0), stop=(kc == NKC - 1))
                return last
            kw = [w_wk]
            if fc == 0:
                kw += [tok_trE[t2] for t2 in tiles]
                if ch >= 1:
                    lastqb_prev = 3 if ch == 1 else 4 * (ch - 1) + 3
                    kw.append(tok_mask[(lastqb_prev - 2, 3)])
            elif fc == 4:
                kw.append(tok_kev[(ch, 3)])
            tok_kmm[(ch, fc)] = S("tensor", mk_k, waits=kw)
            if fc in (3, 7):
                for fc2 in range(fc - 3, fc + 1):
                    s0 = tiles[0] % NSL
                    kdst = (t_kring[:].rearrange("p (s c) -> p s c", s=NSL)
                            [:, s0:s0 + len(tiles), fc2 * 128:(fc2 + 1) * 128])
                    ksrc = (p_sc4[fc2 % 4][:, 0:Wc]
                            .rearrange("p (t c) -> p t c", c=128))
                    tok_kev[(ch, fc2)] = S("scalar", lambda e, kdst=kdst, ksrc=ksrc,
                                           fc2=fc2:
                                           e.activation(kdst, ksrc, AF.Identity,
                                                        bias=t_bqk[:, 8 + fc2:9 + fc2]),
                                           waits=[tok_kmm[(ch, fc)], w_setup])

        for ti, tq in enumerate(tiles):
            for vh in range(2):
                def mk_v(eng, vh=vh, ti=ti):
                    last = None
                    for kc in range(NKC):
                        last = eng.matmul(
                            p_v[:],
                            t_xTb4[:, kc * 512 + ti * 128: kc * 512 + (ti + 1) * 128],
                            t_wab[:, kc * 3 * C + 2 * C + vh * 512:
                                  kc * 3 * C + 2 * C + (vh + 1) * 512],
                            start=(kc == 0), stop=(kc == NKC - 1))
                    return last
                vw = [tok_trE[tq], w_wkv]
                if vh == 1:
                    vw.append(tok_vev[(tq, 0)])
                elif (tq - 1, 1) in tok_vev:
                    vw.append(tok_vev[(tq - 1, 1)])
                tok_v_mm[(tq, vh)] = S("tensor", mk_v, waits=vw)
                vdst = t_vring[:, (tq % NSL) * C + vh * 512:
                               (tq % NSL) * C + (vh + 1) * 512]
                ew = [tok_v_mm[(tq, vh)], w_setup, w_gsetup]
                if tq - NSL in avq:
                    ew.append(avq[tq - NSL])
                tok_vev[(tq, vh)] = S("vector", lambda e, vdst=vdst, vh=vh:
                                      e.scalar_tensor_tensor(
                                          vdst, p_v[:], 1.0,
                                          t_bvbc[:, vh * 512:(vh + 1) * 512],
                                          op0=ALU.mult, op1=ALU.add), waits=ew)

        for fc in range(NKC):
            def mk_q(eng, fc=fc, Wc=Wc, qoff=qoff):
                last = None
                for kc in range(NKC):
                    last = eng.matmul(
                        p_sc4[fc % 4][:, qoff:Wc],
                        t_wab[:, kc * 3 * C + fc * 128: kc * 3 * C + (fc + 1) * 128],
                        t_xTb4[:, kc * 512 + qoff:kc * 512 + Wc],
                        start=(kc == 0), stop=(kc == NKC - 1))
                return last
            qw = [tok_bqs, w_wkvq]
            if fc == 0:
                qw.append(tok_kev[(ch, NKC - 1)])
            elif fc == 4:
                qw.append(tok_qev[(ch, 3)])
            tok_qmm[(ch, fc)] = S("tensor", mk_q, waits=qw)
            if fc in (3, 7):
                for fc2 in range(fc - 3, fc + 1):
                    qdst = t_q4[:, fc2 * 512 + qoff: fc2 * 512 + Wc]
                    tok_qev[(ch, fc2)] = S("scalar", lambda e, qdst=qdst, fc2=fc2,
                                           Wc=Wc, qoff=qoff:
                                           e.activation(qdst,
                                                        p_sc4[fc2 % 4][:, qoff:Wc],
                                                        AF.Identity, scale=0.125,
                                                        bias=t_bqs[:, fc2:fc2 + 1]),
                                           waits=[tok_qmm[(ch, fc)]])

        qbs = [2, 3] if ch == 0 else list(tiles)
        for tq in qbs:
            tt = tq - 2
            ti = tq - 4 * ch
            mvar = tt if tt < 2 else 2
            for s in range(5):
                if s < 4:
                    G = s
                    g2 = G % 2
                    for hloc in range(4):
                        def mk_sc(eng, G=G, tq=tq, ti=ti, hloc=hloc):
                            last = None
                            h = G * 4 + hloc
                            fc, r0 = h // 2, (h % 2) * 64
                            for kr in range(3):
                                ks = (tq - 2 + kr) % NSL
                                last = eng.matmul(
                                    p_sc4[hloc][:, kr * 128:(kr + 1) * 128],
                                    t_q4[r0:r0 + 64, fc * 512 + ti * 128:
                                         fc * 512 + (ti + 1) * 128],
                                    t_kring[r0:r0 + 64,
                                            ks * C + fc * 128: ks * C + (fc + 1) * 128],
                                    start=True, stop=True)
                            return last
                        sw = []
                        if G == 0 and tq == (2 if ch == 0 else tiles[0]):
                            sw.append(tok_qev[(ch, NKC - 1)])
                        pmk = (tt, G - 1, hloc) if G >= 1 else (tt - 1, 3, hloc)
                        if pmk in tok_mask_h:
                            sw.append(tok_mask_h[pmk])
                        tok_scg_h[(tt, G, hloc)] = S("tensor", mk_sc, waits=sw)
                    tok_scg[(tt, G)] = tok_scg_h[(tt, G, 3)]

                    for hloc in range(4):
                        mwh = [tok_scg_h[(tt, G, hloc)]]
                        if hloc == 0:
                            mwh += [w_setup, w_gsetup]
                            if G >= 1:
                                mwh.append(tok_exp[(tt, G - 1)])
                            elif tt >= 1:
                                mwh.append(tok_exp[(tt - 1, 3)])
                        tok_mask_h[(tt, G, hloc)] = S(
                            "vector", lambda e, hloc=hloc, mvar=mvar:
                            e.tensor_tensor(
                                t_scf[:, hloc * W3:(hloc + 1) * W3],
                                p_sc4[hloc][:, 0:W3],
                                t_masks[:, mvar * W3:(mvar + 1) * W3], op=ALU.add),
                            waits=mwh)
                    tok_mask[(tt, G)] = tok_mask_h[(tt, G, 3)]

                    pbuf = t_prob[:, g2 * 4 * W3:(g2 + 1) * 4 * W3]
                    pk = (tt, G - 2) if G >= 2 else (tt - 1, G + 2)
                    ew = [tok_mask[(tt, G)]]
                    if pk in tok_red:
                        ew += [tok_red[pk], tok_ptpe[pk + (1,)]]
                    tok_exp[(tt, G)] = S("scalar", lambda e, pbuf=pbuf: e.activation(
                        pbuf, t_scf[:], AF.Exp), waits=ew)

                if s >= 1:
                    G = s - 1
                    g2 = G % 2
                    pbuf = t_prob[:, g2 * 4 * W3:(g2 + 1) * 4 * W3]
                    for sub in range(2):
                        def mk_pt(eng, pbuf=pbuf, sub=sub):
                            last = None
                            for hl in range(2):
                                for kr in range(3):
                                    last = eng.transpose(
                                        p_trb[:, (hl * 3 + kr) * 128:
                                              (hl * 3 + kr + 1) * 128],
                                        pbuf[:, (2 * sub + hl) * W3 + kr * 128:
                                             (2 * sub + hl) * W3 + (kr + 1) * 128],
                                        t_identb[:])
                            return last
                        tok_ptpe[(tt, G, sub)] = S("tensor", mk_pt,
                                                   waits=[tok_exp[(tt, G)], trb_ev[1]])
                        ptdst = t_probT[:, g2 * 4 * W3 + sub * 768:
                                        g2 * 4 * W3 + (sub + 1) * 768]
                        ew = [tok_ptpe[(tt, G, sub)]]
                        ak = ((tt, G - 2, 2 * sub + 1) if G >= 2
                              else (tt - 1, G + 2, 2 * sub + 1))
                        if ak in tok_av:
                            ew.append(tok_av[ak])
                        tok_ptev[(tt, G, sub)] = S("vector",
                                                   lambda e, ptdst=ptdst: e.tensor_copy(
                                                       ptdst, p_trb[:, 0:768]), waits=ew)
                        trb_ev[1] = tok_ptev[(tt, G, sub)]

                    for hloc in range(4):
                        h = G * 4 + hloc
                        sub = hloc // 2

                        def mk_av(eng, g2=g2, hloc=hloc, h=h, G=G, tq=tq):
                            last = None
                            for kr in range(3):
                                ks = (tq - 2 + kr) % NSL
                                last = eng.matmul(
                                    p_misc[:, (G % 2) * 256 + hloc * 64:
                                           (G % 2) * 256 + (hloc + 1) * 64],
                                    t_probT[:, g2 * 4 * W3 + hloc * W3 + kr * 128:
                                            g2 * 4 * W3 + hloc * W3 + (kr + 1) * 128],
                                    t_vring[:, ks * C + h * 64: ks * C + (h + 1) * 64],
                                    start=(kr == 0), stop=(kr == 2))
                            return last
                        aw = [tok_ptev[(tt, G, sub)], tok_vev[(tq, 1)]]
                        if G == 0 and hloc == 0 and tt >= 1:
                            aw.append(tok_logit_ev[tt - 1])
                        pk2 = (tt, G - 1) if G >= 1 else (tt - 1, 3)
                        if hloc == 0 and pk2 in tok_aoev:
                            aw.append(tok_aoev[pk2])
                        tok_av[(tt, G, hloc)] = S("tensor", mk_av, waits=aw)

                    tok_aoev[(tt, G)] = S("vector", lambda e, G=G: e.tensor_copy(
                        t_ao[:, G * 256:(G + 1) * 256],
                        p_misc[:, (G % 2) * 256:(G % 2 + 1) * 256]),
                        waits=[tok_av[(tt, G, 3)]])

                    tok_red[(tt, G)] = S("vector", lambda e, G=G, pbuf=pbuf:
                        e.reduce_sum(
                            t_den[:, G * 4:(G + 1) * 4].rearrange("p h -> p h ()"),
                            pbuf.rearrange("p (h j) -> p h j", h=4), axis=AX.X),
                        waits=[tok_exp[(tt, G)]])

            avq[tt] = tok_av[(tt, 3, 3)]

            tok_recip[tt] = S("vector", lambda e: e.reciprocal(t_rden[:], t_den[:]),
                              waits=[tok_red[(tt, 3)]])
            nw = [tok_recip[tt], tok_aoev[(tt, 3)]]
            if tt >= 1:
                nw.append(tok_aoT_pe[tt - 1])
            tok_normao[tt] = S("vector", lambda e: e.tensor_tensor(
                t_aon[:].rearrange("p (h d) -> p h d", h=16),
                t_ao[:].rearrange("p (h d) -> p h d", h=16),
                t_rden[:].rearrange("p h -> p h ()").broadcast_to([128, NH, HD]),
                op=ALU.mult), waits=nw)

            def mk_aoT(eng):
                last = None
                for j in range(NKC):
                    last = eng.transpose(p_trb[:, j * 128:(j + 1) * 128],
                                         t_aon[:, j * 128:(j + 1) * 128], t_identb[:])
                return last
            aw = [tok_normao[tt], trb_ev[1]]
            tok_aoT_pe[tt] = S("tensor", mk_aoT, waits=aw)
            ew = [tok_aoT_pe[tt]]
            if tt >= 1:
                ew.append(tok_y1_mm[(tt - 1, 1)])
            tok_aoT_ev[tt] = S("vector", lambda e: e.tensor_copy(t_aoT[:], p_trb[:, 0:C]),
                               waits=ew)
            trb_ev[1] = tok_aoT_ev[tt]

            for nh2 in range(2):
                def mk_pj(eng, nh2=nh2):
                    last = None
                    for kc in range(NKC):
                        last = eng.matmul(
                            p_misc[:],
                            t_aoT[:, kc * 128:(kc + 1) * 128],
                            t_wpb[:, kc * C + nh2 * 512: kc * C + (nh2 + 1) * 512],
                            start=(kc == 0), stop=(kc == NKC - 1))
                    return last
                pw = [tok_aoT_ev[tt], w_wsetup]
                if nh2 == 1:
                    pw.append(tok_proj_ev[(tt, 0)])
                tok_y1_mm[(tt, nh2)] = S("tensor", mk_pj, waits=pw)
                ew = [tok_y1_mm[(tt, nh2)], w_setup, w_gsetup]
                tok_proj_ev[(tt, nh2)] = S("vector", lambda e, nh2=nh2:
                                           e.scalar_tensor_tensor(
                                               t_w[:, nh2 * 512:(nh2 + 1) * 512], p_misc[:], 1.0,
                                               t_bpbc[:, nh2 * 512:(nh2 + 1) * 512],
                                               op0=ALU.mult, op1=ALU.add), waits=ew)

            tok_res = S("vector", lambda e, tq=tq: e.tensor_tensor(
                t_w[:], t_w[:], stream[:, (tq % NSL) * C:(tq % NSL + 1) * C], op=ALU.add),
                waits=[tok_proj_ev[(tt, 0)], tok_proj_ev[(tt, 1)]])

            tok_ns = S("vector", lambda e: e.reduce_sum(t_stat[:, 0:1], t_w[:], axis=AX.X,
                                                        negate=True), waits=[tok_res])
            tok_nm = S("vector", lambda e: e.tensor_scalar(t_stat[:, 1:2], t_stat[:, 0:1],
                                                           1.0 / C, None, op0=ALU.mult),
                       waits=[tok_ns])
            tok_ct = S("vector", lambda e: e.tensor_scalar(t_cent[:], t_w[:], t_stat[:, 1:2],
                                                           None, op0=ALU.add), waits=[tok_nm])
            tok_sq = S("vector", lambda e: e.tensor_tensor(t_w[:], t_cent[:], t_cent[:],
                                                           op=ALU.mult), waits=[tok_ct])
            tok_vr = S("vector", lambda e: e.reduce_sum(t_stat[:, 2:3], t_w[:], axis=AX.X),
                       waits=[tok_sq])
            tok_v2 = S("vector", lambda e: e.tensor_scalar(t_stat[:, 3:4], t_stat[:, 2:3],
                                                           1.0 / C, EPS1, op0=ALU.mult,
                                                           op1=ALU.add), waits=[tok_vr])
            tok_ln = S("scalar", lambda e: e.activation(t_stat[:, 4:5], t_stat[:, 3:4], AF.Ln),
                       waits=[tok_v2])
            tok_rs = S("scalar", lambda e: e.activation(t_stat[:, 5:6], t_stat[:, 4:5], AF.Exp,
                                                        scale=-0.5), waits=[tok_ln])
            tok_xnew[tt] = S("vector", lambda e, tq=tq: e.tensor_scalar(
                stream[:, (tq % NSL) * C:(tq % NSL + 1) * C], t_cent[:], t_stat[:, 5:6],
                None, op0=ALU.mult), waits=[tok_rs])

            n_spill[0] += 1
            S("sync", lambda e, tt=tt, tq=tq: e.dma_start(
                d_x1[tt * 128:(tt + 1) * 128, :],
                stream[:, (tq % NSL) * C:(tq % NSL + 1) * C]).then_inc(q_spill, 16),
              waits=[tok_xnew[tt]])

            tok_rtr_ev = {}
            for bt in range(2):
                def mk_rtr(eng, bt=bt, tq=tq):
                    last = None
                    for j in range(4):
                        last = eng.transpose(
                            p_misc[:, j * 128:(j + 1) * 128],
                            stream[:, (tq % NSL) * C + (bt * 4 + j) * 128: (tq % NSL) * C + (bt * 4 + j + 1) * 128],
                            t_identf[:])
                    return last
                rw = [tok_xnew[tt]]
                rw.append(tok_proj_ev[(tt, 1)] if bt == 0 else tok_rtr_ev[0])
                tok_rtr_pe = S("tensor", mk_rtr, waits=rw)
                tok_rtr_ev[bt] = S("vector", lambda e, bt=bt: e.tensor_copy(
                    t_xT32[:, bt * 512:(bt + 1) * 512], p_misc[:]), waits=[tok_rtr_pe])

            def mk_rmm(eng):
                last = None
                for kc in range(NKC):
                    last = eng.matmul(p_misc[:, 0:NE],
                                      t_xT32[:, kc * 128:(kc + 1) * 128],
                                      t_wrt[:, kc * NE:(kc + 1) * NE],
                                      start=(kc == 0), stop=(kc == NKC - 1))
                return last
            tok_rmm = S("tensor", mk_rmm, waits=[tok_rtr_ev[0], tok_rtr_ev[1], w_setup])
            tok_logit_ev[tt] = S("vector", lambda e: e.tensor_copy(t_lg[:], p_misc[:, 0:NE]),
                                 waits=[tok_rmm])
            tok_m8 = S("vector", lambda e: e.max(t_mx8[:], t_lg[:]), waits=[tok_logit_ev[tt]])
            tok_mi = S("vector", lambda e: e.max_index(t_idxu[:], t_mx8[:], t_lg[:]),
                       waits=[tok_m8])
            tok_if = S("vector", lambda e: e.tensor_scalar(t_idxf[:], t_idxu[:, 0:1], 0.0,
                                                           None, op0=ALU.add),
                       waits=[tok_mi] + ([(q_idx, 16 * tt)] if tt >= 1 else []))
            S("sync", lambda e, tt=tt: e.dma_start(
                d_idx.rearrange("(a p) -> a p", p=128)[tt].rearrange("(p o) -> p o", o=1),
                t_idxf[:]).then_inc(q_idx, 16), waits=[tok_if])

    S("sync", lambda e: e.nop(), waits=[(q_spill, 16 * NTT), (q_idx, 16 * NTT)])

    a_st.close()
    b_st = ExitStack()
    bsb = lambda name, shape, dt: b_st.enter_context(nc.sbuf_tensor(name, shape, dt))

    t_ring = bsb("ring", [128, 32 * 1024], BF16)
    t_acc = bsb("acc", [128, 32 * 1024], BF16)
    acc_full = t_acc
    t_xTg = bsb("xTg", [128, NKC * 1024], BF16)
    t_idxbc = bsb("idx_bc", [128, 1024], F32)
    t_eq = bsb("eq", [128, 2 * 1024], U8)
    t_gelu = bsb("gelu", [128, 3 * 512], BF16)
    t_xrel2 = bsb("xrel2", [128, 2 * C], F32)
    t_xrb = bsb("xrb", [128, 2 * C], BF16)
    t_y2 = bsb("y2", [128, 2 * C], F32)
    t_ct2 = bsb("ct2", [128, 2 * C], F32)
    t_st2 = bsb("stat2", [128, 16], F32)

    bpsal = lambda name, shape, dt: b_st.enter_context(nc.psum_tensor(name, shape, dt))
    p_i = [bpsal(f"p_i{j}", [128, 512], F32) for j in range(4)]
    p_y2 = [bpsal(f"p_y2{j}", [128, 512], F32) for j in range(2)]
    p_trB = bpsal("p_trB", [128, 1024], BF16)

    sched.new_block()
    S = sched.op

    nw_slot = [0] * 32
    slot_last_reader = [None] * 32
    load_ctr = [0]
    n_xrel = [0] * 4
    n_xrel2 = [0] * 4
    n_out2 = [0, 0]

    tok_sel = {}
    tok_gelu = {}
    tok_immP = {}
    tok_y2mm = {}
    tok_y2ev = {}
    tok_tail_last = {}
    tok_eqtok = {}
    tok_xTg_ev = {}
    tok_trB_tok = {}
    tok_tail_res = {}
    idxbc_readers = []
    gelu_slot_reader = [None, None, None]
    eq_slot_reader = [None, None]

    tok_acc0 = S("vector", lambda e: e.memset(acc_full[:], 0.0),
                 waits=[(q_spill, 16 * NTT), tok_logit_ev[NTT - 1]])

    for g in range(2):
        iw = [(q_idx, 16 * NTT)] + idxbc_readers
        idxbc_readers = []
        S("sync", lambda e, g=g: e.dma_start(
            t_idxbc[:], bass.AP(d_idx.tensor, g * 1024, [[0, 128], [1, 1024]])).then_inc(
            q_idxbc, 16), waits=iw)
        w_idxbc = (q_idxbc, 16 * (g + 1))

        for ttl in range(8):
            tt = g * 8 + ttl
            qi = ttl % 4
            rw = [(q_spill, 16 * NTT)]
            if (g, ttl - 2) in tok_trB_tok:
                rw.append(tok_trB_tok[(g, ttl - 2)])
            elif g == 1 and ttl < 2:
                rw.append(tok_trB_tok[(0, 6 + ttl)])
            n_xrel[qi] += 1
            S("gpsimd", lambda e, tt=tt, ttl=ttl, qi=qi: e.dma_start(
                t_xrb[:, (ttl % 2) * C:(ttl % 2 + 1) * C],
                d_x1[tt * 128:(tt + 1) * 128, :]).then_inc(q_xrel[qi], 16), waits=rw)
            xrw = (q_xrel[qi], 16 * n_xrel[qi])

            def mk_trB(eng, ttl=ttl):
                last = None
                for j in range(NKC):
                    last = eng.transpose(p_trB[:, j * 128:(j + 1) * 128],
                                         t_xrb[:, (ttl % 2) * C + j * 128:
                                               (ttl % 2) * C + (j + 1) * 128], t_identb[:])
                return last
            tw = [xrw, w_setup]
            if (g, ttl - 1) in tok_xTg_ev:
                tw.append(tok_xTg_ev[(g, ttl - 1)])
            elif g == 1 and ttl == 0:
                tw.append(tok_xTg_ev[(0, 7)])
            tok_trB_pe = S("tensor", mk_trB, waits=tw)
            tok_trB_tok[(g, ttl)] = tok_trB_pe
            xTg_v = t_xTg.rearrange("p (k t) -> p k t", k=NKC)
            ew = [tok_trB_pe]
            if g == 1 and ttl == 0:
                ew.append(tok_y2mm[(0, 7, 1)])
            tok_xTg_ev[(g, ttl)] = S("vector", lambda e, xTg_v=xTg_v, ttl=ttl:
                                     e.tensor_copy(xTg_v[:, :, ttl * 128:(ttl + 1) * 128],
                                                   p_trB[:, 0:C]), waits=ew)
        w_xTg = [tok_xTg_ev[(g, t2)] for t2 in range(8)]

        for eix in range(NE):
            eqbuf = t_eq[:, (eix % 2) * 1024:(eix % 2 + 1) * 1024]
            eqw = [w_idxbc]
            if eq_slot_reader[eix % 2] is not None:
                eqw.append(eq_slot_reader[eix % 2])
            tok_eqtok[(g, eix)] = S("vector", lambda e, eqbuf=eqbuf, eix=eix:
                                    e.tensor_scalar(eqbuf, t_idxbc[:], float(eix), None,
                                                    op0=ALU.is_equal), waits=eqw)
            idxbc_readers.append(tok_eqtok[(g, eix)])

            for iq in range(4):
                slot_of_kc = {}
                for kc in range(NKC):
                    L = load_ctr[0]
                    s = L % 32
                    nw_slot[s] += 1
                    slot_of_kc[kc] = (s, nw_slot[s])
                    lw = [slot_last_reader[s]] if slot_last_reader[s] is not None else []
                    S("gpsimd", lambda e, s=s, eix=eix, kc=kc, iq=iq: e.dma_start(
                        t_ring[:, s * 1024:(s + 1) * 1024],
                        d_w1[eix, kc * 128:(kc + 1) * 128,
                             iq * 1024:(iq + 1) * 1024]).then_inc(q_slot[s], 16), waits=lw)
                    load_ctr[0] += 1

                for ps2 in range(2):
                    for icl in range(8):
                        seq = ps2 * 8 + icl
                        bank = p_i[seq % 4]
                        icg = iq * 8 + icl
                        key = (g, eix, iq, ps2, icl)

                        def mk_im(eng, bank=bank, icl=icl, ps2=ps2,
                                  slot_of_kc=dict(slot_of_kc)):
                            last = None
                            for kc in range(NKC):
                                s2, _ = slot_of_kc[kc]
                                last = eng.matmul(
                                    bank[:],
                                    t_ring[:, s2 * 1024 + icl * 128: s2 * 1024 + (icl + 1) * 128],
                                    t_xTg[:, kc * 1024 + ps2 * 512: kc * 1024 + (ps2 + 1) * 512],
                                    start=(kc == 0), stop=(kc == NKC - 1))
                            return last

                        def back_key(delta):
                            sq = seq - delta
                            e2, i2, g2 = eix, iq, g
                            while sq < 0:
                                sq += 16
                                i2 -= 1
                                if i2 < 0:
                                    i2 = 3
                                    e2 -= 1
                                    if e2 < 0:
                                        e2 = NE - 1
                                        g2 -= 1
                                        if g2 < 0:
                                            return None
                            return (g2, e2, i2, sq // 8, sq % 8)

                        mw = list(w_xTg)
                        mw += [(q_slot[s2], 16 * n2) for (s2, n2) in slot_of_kc.values()]
                        bk = back_key(4)
                        if bk is not None and bk in tok_gelu:
                            mw.append(tok_gelu[bk])
                        tok_immP[key] = S("tensor", mk_im, waits=mw)

                        gb = seq % 3
                        gbuf = t_gelu[:, gb * 512:(gb + 1) * 512]
                        ew = [tok_immP[key]]
                        if gelu_slot_reader[gb] is not None:
                            ew.append(gelu_slot_reader[gb])
                        tok_gelu[key] = S("scalar", lambda e, gbuf=gbuf, bank=bank,
                                          eix=eix, icg=icg:
                                          e.activation(gbuf, bank[:], AF.Gelu,
                                                       bias=t_b1c[:, icg * 8 + eix:
                                                                  icg * 8 + eix + 1]),
                                          waits=ew + [w_b1c])
                        accsl = acc_full[:, icg * 1024 + ps2 * 512: icg * 1024 + (ps2 + 1) * 512]
                        sw = [tok_gelu[key], tok_eqtok[(g, eix)]]
                        if g == 1 and eix == 0:
                            sw.append(tok_y2ev[(0, 7, 1)])
                        tok_sel[key] = S("vector", lambda e, accsl=accsl, eqbuf=eqbuf,
                                         gbuf=gbuf, ps2=ps2:
                                         e.copy_predicated(
                                             accsl, eqbuf[:, ps2 * 512:(ps2 + 1) * 512],
                                             gbuf), waits=sw)
                        gelu_slot_reader[gb] = tok_sel[key]
                for kc in range(NKC):
                    s2, _ = slot_of_kc[kc]
                    slot_last_reader[s2] = tok_immP[(g, eix, iq, 1, 7)]
            eq_slot_reader[eix % 2] = tok_sel[(g, eix, 3, 1, 7)]

        last_sel = tok_sel[(g, NE - 1, 3, 1, 7)]

        wo_slot = {}
        for ic in range(32):
            L = load_ctr[0]
            s = L % 32
            nw_slot[s] += 1
            wo_slot[ic] = (s, nw_slot[s])
            lw = [slot_last_reader[s]] if slot_last_reader[s] is not None else []
            S("gpsimd", lambda e, s=s, ic=ic: e.dma_start(
                t_ring[:, s * 1024:(s + 1) * 1024],
                d_wo[ic * 128:(ic + 1) * 128, :]).then_inc(q_slot[s], 16), waits=lw)
            load_ctr[0] += 1

        for ttl in range(8):
            tt = g * 8 + ttl
            qi = ttl % 4
            tb = ttl % 2
            rw = [(q_spill, 16 * NTT)]
            if (g, ttl - 2) in tok_tail_res:
                rw.append(tok_tail_res[(g, ttl - 2)])
            elif g == 1 and ttl < 2:
                rw.append(tok_tail_res[(0, 6 + ttl)])
            n_xrel2[qi] += 1
            S("sync", lambda e, tt=tt, tb=tb, qi=qi: e.dma_start(
                t_xrel2[:, tb * C:(tb + 1) * C],
                d_x1[tt * 128:(tt + 1) * 128, :]).then_inc(q_xrel2[qi], 16), waits=rw)
            xr2w = (q_xrel2[qi], 16 * n_xrel2[qi])

            for ch in range(2):
                def mk_wo(eng, ttl=ttl, ch=ch, wo_slot=dict(wo_slot)):
                    last = None
                    for ic in range(32):
                        s2, _ = wo_slot[ic]
                        last = eng.matmul(
                            p_y2[ch][:],
                            acc_full[:, ic * 1024 + ttl * 128: ic * 1024 + (ttl + 1) * 128],
                            t_ring[:, s2 * 1024 + ch * 512: s2 * 1024 + (ch + 1) * 512],
                            start=(ic == 0), stop=(ic == 31))
                    return last
                ww = [last_sel]
                ww += [(q_slot[s2], 16 * n2) for (s2, n2) in wo_slot.values()]
                if ttl >= 1:
                    ww.append(tok_y2ev[(g, ttl - 1, ch)])
                elif g == 1:
                    ww.append(tok_y2ev[(0, 7, ch)])
                tok_y2mm[(g, ttl, ch)] = S("tensor", mk_wo, waits=ww)
                ew = [tok_y2mm[(g, ttl, ch)], w_setup, xr2w]
                if (g, ttl - 2) in tok_tail_last:
                    ew.append(tok_tail_last[(g, ttl - 2)])
                elif g == 1 and ttl < 2:
                    ew.append(tok_tail_last[(0, 6 + ttl)])
                tok_y2ev[(g, ttl, ch)] = S("vector", lambda e, ch=ch, tb=tb:
                                           e.scalar_tensor_tensor(
                                               t_y2[:, tb * C + ch * 512:
                                                    tb * C + (ch + 1) * 512],
                                               p_y2[ch][:], 1.0,
                                               t_bo[:, ch * 512:(ch + 1) * 512],
                                               op0=ALU.mult, op1=ALU.add), waits=ew)
            if ttl == 7:
                for ic in range(32):
                    s2, _ = wo_slot[ic]
                    slot_last_reader[s2] = tok_y2mm[(g, 7, 1)]

            yb = t_y2[:, tb * C:(tb + 1) * C]
            cb = t_ct2[:, tb * C:(tb + 1) * C]
            xr = t_xrel2[:, tb * C:(tb + 1) * C]
            sc = t_st2[:, tb * 8:(tb + 1) * 8]

            def ln_chain(first_w, src, cent, dst, eps, affine, sc=sc):
                tns = S("vector", lambda e, src=src, sc=sc: e.reduce_sum(
                    sc[:, 0:1], src, axis=AX.X, negate=True), waits=first_w)
                tnm = S("vector", lambda e, sc=sc: e.tensor_scalar(
                    sc[:, 1:2], sc[:, 0:1], 1.0 / C, None, op0=ALU.mult), waits=[tns])
                tct = S("vector", lambda e, src=src, cent=cent, sc=sc: e.tensor_scalar(
                    cent, src, sc[:, 1:2], None, op0=ALU.add), waits=[tnm])
                tsq = S("vector", lambda e, src=src, cent=cent: e.tensor_tensor(
                    src, cent, cent, op=ALU.mult), waits=[tct])
                tvr = S("vector", lambda e, src=src, sc=sc: e.reduce_sum(
                    sc[:, 2:3], src, axis=AX.X), waits=[tsq])
                tv2 = S("vector", lambda e, sc=sc, eps=eps: e.tensor_scalar(
                    sc[:, 3:4], sc[:, 2:3], 1.0 / C, eps, op0=ALU.mult, op1=ALU.add),
                    waits=[tvr])
                tln = S("scalar", lambda e, sc=sc: e.activation(sc[:, 4:5], sc[:, 3:4],
                                                                AF.Ln), waits=[tv2])
                trs = S("scalar", lambda e, sc=sc: e.activation(sc[:, 5:6], sc[:, 4:5],
                                                                AF.Exp, scale=-0.5),
                        waits=[tln])
                if affine:
                    tz = S("vector", lambda e, cent=cent, dst=dst, sc=sc:
                           e.scalar_tensor_tensor(dst, cent, sc[:, 5:6], t_gam[:],
                                                  op0=ALU.mult, op1=ALU.mult), waits=[trs])
                    tz = S("vector", lambda e, dst=dst: e.tensor_tensor(
                        dst, dst, t_bet[:], op=ALU.add), waits=[tz])
                else:
                    tz = S("vector", lambda e, cent=cent, dst=dst, sc=sc: e.tensor_scalar(
                        dst, cent, sc[:, 5:6], None, op0=ALU.mult), waits=[trs])
                return tz

            r1w = [tok_y2ev[(g, ttl, 0)], tok_y2ev[(g, ttl, 1)]]
            if n_out2[tb] > 0:
                r1w.append((q_out2[tb], n_out2[tb]))
            t_r1 = S("vector", lambda e, yb=yb, xr=xr: e.tensor_tensor(
                yb, yb, xr, op=ALU.add), waits=r1w)
            t_z = ln_chain([t_r1], yb, cb, yb, EPS2, True)
            t_r2 = S("vector", lambda e, yb=yb, xr=xr: e.tensor_tensor(
                yb, xr, yb, op=ALU.add), waits=[t_z])
            tok_tail_res[(g, ttl)] = t_r2
            t_o = ln_chain([t_r2], yb, cb, cb, EPS1, False)
            tok_tail_last[(g, ttl)] = t_o

            n_out2[tb] += 16
            S("sync", lambda e, tt=tt, cb=cb, tb=tb: e.dma_start(
                d_out[tt * 128:(tt + 1) * 128, :], cb).then_inc(q_out2[tb], 16),
              waits=[t_o])

    S("sync", lambda e: e.nop(), waits=[(q_out2[0], n_out2[0]), (q_out2[1], n_out2[1])])

    sched.finalize()
    for blk_ops in sched.blocks:
        with nc.Block() as block:
            for ename, deco in (("sync", block.sync), ("gpsimd", block.gpsimd),
                                ("scalar", block.scalar), ("vector", block.vector),
                                ("tensor", block.tensor)):
                def body(engine, ename=ename, blk_ops=blk_ops):
                    sched.emit_block(blk_ops, ename, engine, eng_sems)
                deco(body)

    b_st.close()
    st.close()
    return nc


_NC_CACHE = {}


def _get_nc():
    if "nc" not in _NC_CACHE:
        _NC_CACHE["nc"] = build_program()
    return _NC_CACHE["nc"]


def _build_masks(seq0):
    p = np.arange(128)[:, None]
    f = np.arange(W3)[None, :]
    out = np.zeros((3, 128, W3), dtype=np.float32)
    for v in range(3):
        tt = v if v < 2 else 2
        allowed = (f >= p) & (f <= p + 256)
        if v < 2:
            kseq = seq0 - 256 + tt * 128 + f
            allowed = allowed & (kseq >= 0)
        out[v] = np.where(allowed, 0.0, NEG)
    return out


def kernel(L0_attn_x, L0_attn_W_attn, L0_attn_b_attn, L0_attn_W_proj, L0_attn_b_proj,
           L0_ffn_Wr, L0_ffn_W1, L0_ffn_b1, L0_ffn_Wo, L0_ffn_bo, L0_ffn_gamma,
           L0_ffn_beta):
    x = np.ascontiguousarray(np.asarray(L0_attn_x, dtype=np.float32).reshape(B * T, C))
    wa = np.ascontiguousarray(np.asarray(L0_attn_W_attn, dtype=np.float32))
    ba = np.ascontiguousarray(np.asarray(L0_attn_b_attn, dtype=np.float32))
    wp = np.ascontiguousarray(np.asarray(L0_attn_W_proj, dtype=np.float32))
    bp = np.ascontiguousarray(np.asarray(L0_attn_b_proj, dtype=np.float32))
    wrt = np.ascontiguousarray(np.asarray(L0_ffn_Wr, dtype=np.float32).T)
    w1 = np.ascontiguousarray(np.asarray(L0_ffn_W1, dtype=np.float32))
    b1 = np.ascontiguousarray(np.asarray(L0_ffn_b1, dtype=np.float32))
    wo = np.ascontiguousarray(np.asarray(L0_ffn_Wo, dtype=np.float32))
    bo = np.ascontiguousarray(np.asarray(L0_ffn_bo, dtype=np.float32))
    gam = np.ascontiguousarray(np.asarray(L0_ffn_gamma, dtype=np.float32))
    bet = np.ascontiguousarray(np.asarray(L0_ffn_beta, dtype=np.float32))

    in_maps = []
    for c in range(NCORES):
        base = c * TCORE
        seq0 = (c % 4) * TCORE
        xe = np.zeros((NTQ * 128, C), dtype=np.float32)
        if seq0 == 0:
            xe[HALO:] = x[base:base + TCORE]
        else:
            xe[:] = x[base - HALO:base + TCORE]
        in_maps.append({
            "x_ext": xe, "w_attn": wa, "b_attn": ba, "w_proj": wp, "b_proj": bp,
            "wr_t": wrt, "w1": w1, "b1": b1, "wo": wo, "bo": bo, "gamma": gam,
            "beta": bet, "masks": _build_masks(seq0),
        })

    nc = _get_nc()
    res = run_bass_kernel_spmd(nc, in_maps, core_ids=list(range(NCORES)))
    out = np.concatenate([r["out"] for r in res.results], axis=0)
    return out.reshape(B, T, C)


# version 11
# speedup vs baseline: 1.4597x; 1.1457x over previous
import numpy as np
import ml_dtypes
from contextlib import ExitStack

import concourse.bass as bass
import concourse.mybir as mybir
from concourse.bass import IndirectOffsetOnAxis
from concourse.bass_utils import run_bass_kernel_spmd

F32 = mybir.dt.float32
BF16 = mybir.dt.bfloat16
U8 = mybir.dt.uint8
U32 = mybir.dt.uint32
I32 = mybir.dt.int32
AF = mybir.ActivationFunctionType
ALU = mybir.AluOpType
AX = mybir.AxisListType

NCORES = 8
B, T, C = 2, 8192, 1024
TCORE = 2048
HALO = 256
NTT = 16
NTQ = 18
NH, HD = 16, 64
W3 = 384
FF = 4096
NE = 8
NKC = 8
NEG = -1.0e30
NSL = 8
EPS1 = 1e-5
EPS2 = 1e-12

ENGINES = ("sync", "gpsimd", "scalar", "vector", "tensor")


class Sched:

    def __init__(self):
        self.blocks = []
        self.cur = None

    def new_block(self):
        self.cur = []
        self.blocks.append(self.cur)

    def op(self, eng, fn, waits=()):
        rec = {"eng": eng, "fn": fn, "waits": [w for w in waits if w is not None],
               "inc": None}
        self.cur.append(rec)
        return rec

    def finalize(self):
        for blk in self.blocks:
            for rec in blk:
                for wtg in rec["waits"]:
                    if isinstance(wtg, dict) and wtg["inc"] is None:
                        wtg["inc"] = -1
        counters = {e: 0 for e in ENGINES}
        for blk in self.blocks:
            for rec in blk:
                if rec["inc"] == -1:
                    counters[rec["eng"]] += 1
                    rec["inc"] = counters[rec["eng"]]

    def emit_block(self, blk_ops, eng_name, engine, eng_sems):
        highwater = {}
        for rec in blk_ops:
            if rec["eng"] != eng_name:
                continue
            for wtg in rec["waits"]:
                if isinstance(wtg, dict):
                    sem, val, key = eng_sems[wtg["eng"]], wtg["inc"], ("e", wtg["eng"])
                else:
                    sem, val = wtg
                    key = ("s", id(sem))
                if highwater.get(key, -1) >= val:
                    continue
                highwater[key] = val
                engine.wait_ge(sem, val)
            inst = rec["fn"](engine)
            if rec["inc"] is not None:
                assert inst is not None, f"wait-target op returned no inst on {eng_name}"
                inst.then_inc(eng_sems[rec["eng"]], 1)


def build_program():
    nc = bass.Bass("TRN2", target_bir_lowering=False)

    d_x = nc.dram_tensor("x_ext", [NTQ * 128, C], F32, kind="ExternalInput").ap()
    d_wa = nc.dram_tensor("w_attn", [C, 3 * C], F32, kind="ExternalInput").ap()
    d_ba = nc.dram_tensor("b_attn", [3 * C], F32, kind="ExternalInput").ap()
    d_wp = nc.dram_tensor("w_proj", [C, C], F32, kind="ExternalInput").ap()
    d_bp = nc.dram_tensor("b_proj", [C], F32, kind="ExternalInput").ap()
    d_wrt = nc.dram_tensor("wr_t", [C, NE], F32, kind="ExternalInput").ap()
    d_w1 = nc.dram_tensor("w1", [NE, C, FF], F32, kind="ExternalInput").ap()
    d_b1 = nc.dram_tensor("b1", [NE, FF], F32, kind="ExternalInput").ap()
    d_wo = nc.dram_tensor("wo", [FF, C], F32, kind="ExternalInput").ap()
    d_bo = nc.dram_tensor("bo", [C], F32, kind="ExternalInput").ap()
    d_gam = nc.dram_tensor("gamma", [C], F32, kind="ExternalInput").ap()
    d_bet = nc.dram_tensor("beta", [C], F32, kind="ExternalInput").ap()
    d_msk = nc.dram_tensor("masks", [3, 128, W3], F32, kind="ExternalInput").ap()
    d_out = nc.dram_tensor("out", [TCORE, C], F32, kind="ExternalOutput").ap()
    d_x1 = nc.dram_tensor("x1_scratch", [TCORE, C], F32, kind="Internal").ap()
    d_idx = nc.dram_tensor("idx_scratch", [TCORE], F32, kind="Internal").ap()
    d_eids = nc.dram_tensor("eids", [NE, 1], F32, kind="ExternalInput").ap()
    d_ones8 = nc.dram_tensor("ones8", [NE, 1], F32, kind="ExternalInput").ap()
    d_xg = nc.dram_tensor("xg_scratch", [NE * 512, C], BF16, kind="Internal").ap()
    d_yg = nc.dram_tensor("yg_scratch", [NE * 512, C], F32, kind="Internal").ap()
    d_dst = nc.dram_tensor("dst_scratch", [TCORE], F32, kind="Internal").ap()
    d_wob = nc.dram_tensor("wo_bf", [FF, C], BF16, kind="Internal").ap()

    ident_bf = nc.inline_tensor(np.eye(128, dtype=ml_dtypes.bfloat16), "ident_bf").ap()
    ident_f32 = nc.inline_tensor(np.eye(128, dtype=np.float32), "ident_f32").ap()

    st = ExitStack()
    sb = lambda name, shape, dt: st.enter_context(nc.sbuf_tensor(name, shape, dt))
    psal = lambda name, shape, dt: st.enter_context(nc.psum_tensor(name, shape, dt))
    sem = lambda name: st.enter_context(nc.semaphore(name))

    t_identb = sb("identb", [128, 128], BF16)
    t_identf = sb("identf", [128, 128], F32)
    t_bqk = sb("bqk", [128, 16], F32)
    t_bqs = sb("bqs", [128, 8], F32)
    t_wrt = sb("wrt", [128, NKC * NE], F32)
    t_b1c = sb("b1cols", [128, NE * 32], F32)
    t_eids = sb("eids_sb", [NE, 1], F32)
    t_ones8 = sb("ones8_sb", [NE, 1], F32)
    t_ebase = sb("ebase_sb", [NE, 1], F32)
    t_bo = sb("bo_bc", [128, C], F32)
    t_gam = sb("gam_bc", [128, C], F32)
    t_bet = sb("bet_bc", [128, C], F32)

    eng_sems = {e: sem("e_" + e) for e in ENGINES}
    q_setup = sem("q_setup")
    q_wsetup = sem("q_wsetup")
    q_wk = sem("q_wk")
    q_wv = sem("q_wv")
    q_wq = sem("q_wq")
    q_x = [sem(f"q_x{i}") for i in range(6)]
    q_spill = sem("q_spill")
    q_idx = sem("q_idx")
    q_idxbc = sem("q_idxbc")
    q_b1c = sem("q_b1c")
    q_gsetup = sem("q_gsetup")
    q_xrel = [sem(f"q_xrel{i}") for i in range(4)]
    q_xrel2 = [sem(f"q_xrel2{i}") for i in range(4)]
    q_slot = [sem(f"q_slot{i}") for i in range(32)]
    q_out2 = [sem("q_out0"), sem("q_out1")]
    q_xg2 = [sem("q_xg0"), sem("q_xg1")]
    q_zg = sem("q_zg")
    q_dst = sem("q_dst")
    q_dst2 = sem("q_dst2")
    q_wob = sem("q_wob")
    q_xgl = [sem("q_xgl0"), sem("q_xgl1")]
    q_ygs = [sem(f"q_ygs{i}") for i in range(4)]
    q_ygr = [sem("q_ygr0"), sem("q_ygr1")]

    sched = Sched()

    a_st = ExitStack()
    asb = lambda name, shape, dt: a_st.enter_context(nc.sbuf_tensor(name, shape, dt))

    t_wab = asb("wab", [128, NKC * 3 * C], BF16)
    t_wpb = asb("wpb", [128, NKC * C], BF16)
    t_kring = asb("kring", [128, NSL * C], BF16)
    t_vring = asb("vring", [128, NSL * C], BF16)
    t_masks = asb("masks_sb", [128, 3 * W3], BF16)
    t_bvbc = asb("bv_bc", [128, C], BF16)
    t_bpbc = asb("bp_bc", [128, C], BF16)
    t_xc = asb("xc", [128, 2 * C], BF16)
    t_xTb4 = asb("xTb4", [128, NKC * 512], BF16)
    t_q4 = asb("q4", [128, NKC * 512], BF16)
    stream = asb("stream", [128, NSL * C], F32)
    t_scf = asb("scf", [128, 4 * W3], F32)
    t_prob = asb("prob", [128, 2 * 4 * W3], BF16)
    t_probT = asb("probT", [128, 2 * 4 * W3], BF16)
    t_ao = asb("attn_out", [128, C], BF16)
    t_aon = asb("attn_out_n", [128, C], BF16)
    t_aoT = asb("attn_outT", [128, C], BF16)
    t_den = asb("den", [128, NH], F32)
    t_rden = asb("rden", [128, NH], F32)
    t_w = asb("wbuf", [128, C], F32)
    t_cent = asb("centbuf", [128, C], F32)
    t_xT32 = asb("xT32", [128, C], F32)
    t_lg = asb("lgs", [128, 8], F32)
    t_mx8 = asb("mx8", [128, 8], F32)
    t_idxu = asb("idxu", [128, 8], U32)
    t_idxf = asb("idxf", [128, 1], F32)
    t_stat = asb("stat", [128, 8], F32)

    apsal = lambda name, shape, dt: a_st.enter_context(nc.psum_tensor(name, shape, dt))
    p_tra = apsal("p_tra", [128, 1024], BF16)
    p_trb = apsal("p_trb", [128, 1024], BF16)
    p_v = apsal("p_v", [128, 512], F32)
    p_sc4 = [apsal(f"p_sc4_{i}", [128, 512], F32) for i in range(4)]
    p_misc = apsal("p_misc", [128, 512], F32)

    sched.new_block()
    S = sched.op

    n_setup = [0]
    n_gsetup = [0]

    def setup_dma(engname, fn):
        if engname == "gpsimd":
            S(engname, lambda e, fn=fn: fn(e).then_inc(q_gsetup, 16))
            n_gsetup[0] += 16
        else:
            S(engname, lambda e, fn=fn: fn(e).then_inc(q_setup, 16))
            n_setup[0] += 16

    setup_dma("sync", lambda e: e.dma_start(t_identb[:], ident_bf))
    setup_dma("sync", lambda e: e.dma_start(t_identf[:], ident_f32))
    for v in range(3):
        setup_dma("gpsimd", lambda e, v=v: e.dma_start(
            t_masks[:, v * W3:(v + 1) * W3], d_msk[v]))
    setup_dma("sync", lambda e: e.dma_start(
        t_bqk[:], bass.AP(d_ba.tensor, 0, [[1, 128], [128, 16]]),
        allow_slow_non_contiguous=True))

    def bcast_src(dap, off, n):
        return bass.AP(dap.tensor, off, [[0, 128], [1, n]])

    setup_dma("gpsimd", lambda e: e.dma_start(t_bvbc[:], bcast_src(d_ba, 2048, C)))
    setup_dma("gpsimd", lambda e: e.dma_start(t_bpbc[:], bcast_src(d_bp, 0, C)))
    setup_dma("sync", lambda e: e.dma_start(t_bo[:], bcast_src(d_bo, 0, C)))
    setup_dma("sync", lambda e: e.dma_start(t_gam[:], bcast_src(d_gam, 0, C)))
    setup_dma("sync", lambda e: e.dma_start(t_bet[:], bcast_src(d_bet, 0, C)))
    for kc in range(NKC):
        setup_dma("sync", lambda e, kc=kc: e.dma_start(
            t_wrt[:, kc * NE:(kc + 1) * NE], d_wrt[kc * 128:(kc + 1) * 128, :]))
    setup_dma("sync", lambda e: e.dma_start(t_eids[:], d_eids))
    setup_dma("sync", lambda e: e.dma_start(t_ones8[:], d_ones8))
    SETUP_TOTAL = n_setup[0]

    n_b1c = [0]
    for j in range(32):
        S("scalar", lambda e, j=j: e.dma_start(
            t_b1c[:, j * 8:(j + 1) * 8],
            bass.AP(d_b1.tensor, j * 128, [[1, 128], [FF, 8]]),
            allow_slow_non_contiguous=True).then_inc(q_b1c, 16))
        n_b1c[0] += 16
    w_b1c = (q_b1c, n_b1c[0])

    n_wsetup = [0]
    for qsem, off in ((q_wk, C), (q_wv, 2 * C), (q_wq, 0)):
        for kc in range(NKC):
            S("gpsimd", lambda e, kc=kc, off=off, qsem=qsem: e.dma_start(
                t_wab[:, kc * 3 * C + off: kc * 3 * C + off + C],
                d_wa[kc * 128:(kc + 1) * 128, off:off + C]).then_inc(qsem, 16))
    w_wk = (q_wk, 8 * 16)
    w_wkv = (q_wv, 8 * 16)
    w_wkvq = (q_wq, 8 * 16)
    for kc in range(NKC):
        S("gpsimd", lambda e, kc=kc: e.dma_start(
            t_wpb[:, kc * C:(kc + 1) * C],
            d_wp[kc * 128:(kc + 1) * 128, :]).then_inc(q_wsetup, 16))
        n_wsetup[0] += 16
    S("gpsimd", lambda e: e.dma_start(d_wob, d_wo).then_inc(q_wob, 16))
    w_setup = (q_setup, SETUP_TOTAL)
    w_gsetup = (q_gsetup, n_gsetup[0])
    w_wsetup = (q_wsetup, n_wsetup[0])

    tok_bqs = S("vector", lambda e: e.tensor_scalar(
        t_bqs[:], t_bqk[:, 0:8], 0.125, None, op0=ALU.mult), waits=[w_setup])

    trb_ev = [None, None]

    tok_cast = {}
    tok_trE = {}
    tok_kev = {}
    tok_vev = {}
    tok_qev = {}
    tok_kmm = {}
    tok_qmm = {}
    tok_v_mm = {}
    tok_scg = {}
    tok_scg_h = {}
    tok_mask = {}
    tok_mask_h = {}
    tok_exp = {}
    tok_red = {}
    tok_recip = {}
    tok_ptpe = {}
    tok_ptev = {}
    tok_av = {}
    tok_aoev = {}
    tok_normao = {}
    tok_aoT_pe = {}
    tok_aoT_ev = {}
    tok_y1_mm = {}
    tok_proj_ev = {}
    tok_logit_ev = {}
    tok_xnew = {}
    tok_trP = {}

    nx_load = [0] * 6
    n_spill = [0]
    avq = {}
    chunk_tiles = [list(range(4 * c2, min(4 * c2 + 4, NTQ))) for c2 in range(5)]

    for ch, tiles in enumerate(chunk_tiles):
        Wc = len(tiles) * 128
        qoff = 256 if ch == 0 else 0

        for ti, tq in enumerate(tiles):
            sl = tq % NSL
            qi = tq % 6
            nx_load[qi] += 16
            xw = (q_x[qi], nx_load[qi])
            dst = stream[:, sl * C:(sl + 1) * C]
            lw = []
            if tq - NSL >= 2:
                tt_old = tq - NSL - 2
                lw.append((q_spill, 16 * n_spill[0]))
                lw.append(tok_logit_ev[tt_old])
            elif tq - NSL >= 0:
                lw.append(tok_cast[tq - NSL])
            S("sync", lambda e, dst=dst, tq=tq, qi=qi: e.dma_start(
                dst, d_x[tq * 128:(tq + 1) * 128, :]).then_inc(q_x[qi], 16), waits=lw)

            cbuf = t_xc[:, (tq % 2) * C:(tq % 2 + 1) * C]
            cw = [xw]
            if tq - 2 >= 0:
                cw.append(tok_trP[tq - 2])
            tok_cast[tq] = S("scalar", lambda e, cbuf=cbuf, dst=dst: e.copy(cbuf, dst),
                             waits=cw)

            def mk_tr(eng, cbuf=cbuf):
                last = None
                for j in range(NKC):
                    last = eng.transpose(p_tra[:, j * 128:(j + 1) * 128],
                                         cbuf[:, j * 128:(j + 1) * 128], t_identb[:])
                return last
            trw = [tok_cast[tq], w_setup]
            if tq >= 1:
                trw.append(tok_trE[tq - 1])
            tok_trP[tq] = S("tensor", mk_tr, waits=trw)

            xdst = (t_xTb4[:].rearrange("p (kc t) -> p kc t", kc=NKC)
                    [:, :, ti * 128:(ti + 1) * 128])
            evw = [tok_trP[tq]]
            if ch >= 1 and ti == 0:
                evw.append(tok_qmm[(ch - 1, NKC - 1)])
            tok_trE[tq] = S("vector", lambda e, xdst=xdst: e.tensor_copy(
                xdst, p_tra[:, 0:C].rearrange("p (kc t) -> p kc t", kc=NKC)),
                waits=evw)

        for fc in range(NKC):
            def mk_k(eng, fc=fc, Wc=Wc):
                last = None
                for kc in range(NKC):
                    last = eng.matmul(
                        p_sc4[fc % 4][:, 0:Wc],
                        t_wab[:, kc * 3 * C + C + fc * 128: kc * 3 * C + C + (fc + 1) * 128],
                        t_xTb4[:, kc * 512:kc * 512 + Wc],
                        start=(kc == 0), stop=(kc == NKC - 1))
                return last
            kw = [w_wk]
            if fc == 0:
                kw += [tok_trE[t2] for t2 in tiles]
                if ch >= 1:
                    lastqb_prev = 3 if ch == 1 else 4 * (ch - 1) + 3
                    kw.append(tok_mask[(lastqb_prev - 2, 3)])
            elif fc == 4:
                kw.append(tok_kev[(ch, 3)])
            tok_kmm[(ch, fc)] = S("tensor", mk_k, waits=kw)
            if fc in (3, 7):
                for fc2 in range(fc - 3, fc + 1):
                    s0 = tiles[0] % NSL
                    kdst = (t_kring[:].rearrange("p (s c) -> p s c", s=NSL)
                            [:, s0:s0 + len(tiles), fc2 * 128:(fc2 + 1) * 128])
                    ksrc = (p_sc4[fc2 % 4][:, 0:Wc]
                            .rearrange("p (t c) -> p t c", c=128))
                    tok_kev[(ch, fc2)] = S("scalar", lambda e, kdst=kdst, ksrc=ksrc,
                                           fc2=fc2:
                                           e.activation(kdst, ksrc, AF.Identity,
                                                        bias=t_bqk[:, 8 + fc2:9 + fc2]),
                                           waits=[tok_kmm[(ch, fc)], w_setup])

        for ti, tq in enumerate(tiles):
            for vh in range(2):
                def mk_v(eng, vh=vh, ti=ti):
                    last = None
                    for kc in range(NKC):
                        last = eng.matmul(
                            p_v[:],
                            t_xTb4[:, kc * 512 + ti * 128: kc * 512 + (ti + 1) * 128],
                            t_wab[:, kc * 3 * C + 2 * C + vh * 512:
                                  kc * 3 * C + 2 * C + (vh + 1) * 512],
                            start=(kc == 0), stop=(kc == NKC - 1))
                    return last
                vw = [tok_trE[tq], w_wkv]
                if vh == 1:
                    vw.append(tok_vev[(tq, 0)])
                elif (tq - 1, 1) in tok_vev:
                    vw.append(tok_vev[(tq - 1, 1)])
                tok_v_mm[(tq, vh)] = S("tensor", mk_v, waits=vw)
                vdst = t_vring[:, (tq % NSL) * C + vh * 512:
                               (tq % NSL) * C + (vh + 1) * 512]
                ew = [tok_v_mm[(tq, vh)], w_setup, w_gsetup]
                if tq - NSL in avq:
                    ew.append(avq[tq - NSL])
                tok_vev[(tq, vh)] = S("vector", lambda e, vdst=vdst, vh=vh:
                                      e.scalar_tensor_tensor(
                                          vdst, p_v[:], 1.0,
                                          t_bvbc[:, vh * 512:(vh + 1) * 512],
                                          op0=ALU.mult, op1=ALU.add), waits=ew)

        for fc in range(NKC):
            def mk_q(eng, fc=fc, Wc=Wc, qoff=qoff):
                last = None
                for kc in range(NKC):
                    last = eng.matmul(
                        p_sc4[fc % 4][:, qoff:Wc],
                        t_wab[:, kc * 3 * C + fc * 128: kc * 3 * C + (fc + 1) * 128],
                        t_xTb4[:, kc * 512 + qoff:kc * 512 + Wc],
                        start=(kc == 0), stop=(kc == NKC - 1))
                return last
            qw = [tok_bqs, w_wkvq]
            if fc == 0:
                qw.append(tok_kev[(ch, NKC - 1)])
            elif fc == 4:
                qw.append(tok_qev[(ch, 3)])
            tok_qmm[(ch, fc)] = S("tensor", mk_q, waits=qw)
            if fc in (3, 7):
                for fc2 in range(fc - 3, fc + 1):
                    qdst = t_q4[:, fc2 * 512 + qoff: fc2 * 512 + Wc]
                    tok_qev[(ch, fc2)] = S("scalar", lambda e, qdst=qdst, fc2=fc2,
                                           Wc=Wc, qoff=qoff:
                                           e.activation(qdst,
                                                        p_sc4[fc2 % 4][:, qoff:Wc],
                                                        AF.Identity, scale=0.125,
                                                        bias=t_bqs[:, fc2:fc2 + 1]),
                                           waits=[tok_qmm[(ch, fc)]])

        qbs = [2, 3] if ch == 0 else list(tiles)
        for tq in qbs:
            tt = tq - 2
            ti = tq - 4 * ch
            mvar = tt if tt < 2 else 2
            for s in range(5):
                if s < 4:
                    G = s
                    g2 = G % 2
                    for hloc in range(4):
                        def mk_sc(eng, G=G, tq=tq, ti=ti, hloc=hloc):
                            last = None
                            h = G * 4 + hloc
                            fc, r0 = h // 2, (h % 2) * 64
                            for kr in range(3):
                                ks = (tq - 2 + kr) % NSL
                                last = eng.matmul(
                                    p_sc4[hloc][:, kr * 128:(kr + 1) * 128],
                                    t_q4[r0:r0 + 64, fc * 512 + ti * 128:
                                         fc * 512 + (ti + 1) * 128],
                                    t_kring[r0:r0 + 64,
                                            ks * C + fc * 128: ks * C + (fc + 1) * 128],
                                    start=True, stop=True)
                            return last
                        sw = []
                        if G == 0 and tq == (2 if ch == 0 else tiles[0]):
                            sw.append(tok_qev[(ch, NKC - 1)])
                        pmk = (tt, G - 1, hloc) if G >= 1 else (tt - 1, 3, hloc)
                        if pmk in tok_mask_h:
                            sw.append(tok_mask_h[pmk])
                        tok_scg_h[(tt, G, hloc)] = S("tensor", mk_sc, waits=sw)
                    tok_scg[(tt, G)] = tok_scg_h[(tt, G, 3)]

                    for hloc in range(4):
                        mwh = [tok_scg_h[(tt, G, hloc)]]
                        if hloc == 0:
                            mwh += [w_setup, w_gsetup]
                            if G >= 1:
                                mwh.append(tok_exp[(tt, G - 1)])
                            elif tt >= 1:
                                mwh.append(tok_exp[(tt - 1, 3)])
                        tok_mask_h[(tt, G, hloc)] = S(
                            "vector", lambda e, hloc=hloc, mvar=mvar:
                            e.tensor_tensor(
                                t_scf[:, hloc * W3:(hloc + 1) * W3],
                                p_sc4[hloc][:, 0:W3],
                                t_masks[:, mvar * W3:(mvar + 1) * W3], op=ALU.add),
                            waits=mwh)
                    tok_mask[(tt, G)] = tok_mask_h[(tt, G, 3)]

                    pbuf = t_prob[:, g2 * 4 * W3:(g2 + 1) * 4 * W3]
                    pk = (tt, G - 2) if G >= 2 else (tt - 1, G + 2)
                    ew = [tok_mask[(tt, G)]]
                    if pk in tok_red:
                        ew += [tok_red[pk], tok_ptpe[pk + (1,)]]
                    tok_exp[(tt, G)] = S("scalar", lambda e, pbuf=pbuf: e.activation(
                        pbuf, t_scf[:], AF.Exp), waits=ew)

                if s >= 1:
                    G = s - 1
                    g2 = G % 2
                    pbuf = t_prob[:, g2 * 4 * W3:(g2 + 1) * 4 * W3]
                    for sub in range(2):
                        def mk_pt(eng, pbuf=pbuf, sub=sub):
                            last = None
                            for hl in range(2):
                                for kr in range(3):
                                    last = eng.transpose(
                                        p_trb[:, (hl * 3 + kr) * 128:
                                              (hl * 3 + kr + 1) * 128],
                                        pbuf[:, (2 * sub + hl) * W3 + kr * 128:
                                             (2 * sub + hl) * W3 + (kr + 1) * 128],
                                        t_identb[:])
                            return last
                        tok_ptpe[(tt, G, sub)] = S("tensor", mk_pt,
                                                   waits=[tok_exp[(tt, G)], trb_ev[1]])
                        ptdst = t_probT[:, g2 * 4 * W3 + sub * 768:
                                        g2 * 4 * W3 + (sub + 1) * 768]
                        ew = [tok_ptpe[(tt, G, sub)]]
                        ak = ((tt, G - 2, 2 * sub + 1) if G >= 2
                              else (tt - 1, G + 2, 2 * sub + 1))
                        if ak in tok_av:
                            ew.append(tok_av[ak])
                        tok_ptev[(tt, G, sub)] = S("vector",
                                                   lambda e, ptdst=ptdst: e.tensor_copy(
                                                       ptdst, p_trb[:, 0:768]), waits=ew)
                        trb_ev[1] = tok_ptev[(tt, G, sub)]

                    for hloc in range(4):
                        h = G * 4 + hloc
                        sub = hloc // 2

                        def mk_av(eng, g2=g2, hloc=hloc, h=h, G=G, tq=tq):
                            last = None
                            for kr in range(3):
                                ks = (tq - 2 + kr) % NSL
                                last = eng.matmul(
                                    p_misc[:, (G % 2) * 256 + hloc * 64:
                                           (G % 2) * 256 + (hloc + 1) * 64],
                                    t_probT[:, g2 * 4 * W3 + hloc * W3 + kr * 128:
                                            g2 * 4 * W3 + hloc * W3 + (kr + 1) * 128],
                                    t_vring[:, ks * C + h * 64: ks * C + (h + 1) * 64],
                                    start=(kr == 0), stop=(kr == 2))
                            return last
                        aw = [tok_ptev[(tt, G, sub)], tok_vev[(tq, 1)]]
                        if G == 0 and hloc == 0 and tt >= 1:
                            aw.append(tok_logit_ev[tt - 1])
                        pk2 = (tt, G - 1) if G >= 1 else (tt - 1, 3)
                        if hloc == 0 and pk2 in tok_aoev:
                            aw.append(tok_aoev[pk2])
                        tok_av[(tt, G, hloc)] = S("tensor", mk_av, waits=aw)

                    tok_aoev[(tt, G)] = S("vector", lambda e, G=G: e.tensor_copy(
                        t_ao[:, G * 256:(G + 1) * 256],
                        p_misc[:, (G % 2) * 256:(G % 2 + 1) * 256]),
                        waits=[tok_av[(tt, G, 3)]])

                    tok_red[(tt, G)] = S("vector", lambda e, G=G, pbuf=pbuf:
                        e.reduce_sum(
                            t_den[:, G * 4:(G + 1) * 4].rearrange("p h -> p h ()"),
                            pbuf.rearrange("p (h j) -> p h j", h=4), axis=AX.X),
                        waits=[tok_exp[(tt, G)]])

            avq[tt] = tok_av[(tt, 3, 3)]

            tok_recip[tt] = S("vector", lambda e: e.reciprocal(t_rden[:], t_den[:]),
                              waits=[tok_red[(tt, 3)]])
            nw = [tok_recip[tt], tok_aoev[(tt, 3)]]
            if tt >= 1:
                nw.append(tok_aoT_pe[tt - 1])
            tok_normao[tt] = S("vector", lambda e: e.tensor_tensor(
                t_aon[:].rearrange("p (h d) -> p h d", h=16),
                t_ao[:].rearrange("p (h d) -> p h d", h=16),
                t_rden[:].rearrange("p h -> p h ()").broadcast_to([128, NH, HD]),
                op=ALU.mult), waits=nw)

            def mk_aoT(eng):
                last = None
                for j in range(NKC):
                    last = eng.transpose(p_trb[:, j * 128:(j + 1) * 128],
                                         t_aon[:, j * 128:(j + 1) * 128], t_identb[:])
                return last
            aw = [tok_normao[tt], trb_ev[1]]
            tok_aoT_pe[tt] = S("tensor", mk_aoT, waits=aw)
            ew = [tok_aoT_pe[tt]]
            if tt >= 1:
                ew.append(tok_y1_mm[(tt - 1, 1)])
            tok_aoT_ev[tt] = S("vector", lambda e: e.tensor_copy(t_aoT[:], p_trb[:, 0:C]),
                               waits=ew)
            trb_ev[1] = tok_aoT_ev[tt]

            for nh2 in range(2):
                def mk_pj(eng, nh2=nh2):
                    last = None
                    for kc in range(NKC):
                        last = eng.matmul(
                            p_misc[:],
                            t_aoT[:, kc * 128:(kc + 1) * 128],
                            t_wpb[:, kc * C + nh2 * 512: kc * C + (nh2 + 1) * 512],
                            start=(kc == 0), stop=(kc == NKC - 1))
                    return last
                pw = [tok_aoT_ev[tt], w_wsetup]
                if nh2 == 1:
                    pw.append(tok_proj_ev[(tt, 0)])
                tok_y1_mm[(tt, nh2)] = S("tensor", mk_pj, waits=pw)
                ew = [tok_y1_mm[(tt, nh2)], w_setup, w_gsetup]
                tok_proj_ev[(tt, nh2)] = S("vector", lambda e, nh2=nh2:
                                           e.scalar_tensor_tensor(
                                               t_w[:, nh2 * 512:(nh2 + 1) * 512], p_misc[:], 1.0,
                                               t_bpbc[:, nh2 * 512:(nh2 + 1) * 512],
                                               op0=ALU.mult, op1=ALU.add), waits=ew)

            tok_res = S("vector", lambda e, tq=tq: e.tensor_tensor(
                t_w[:], t_w[:], stream[:, (tq % NSL) * C:(tq % NSL + 1) * C], op=ALU.add),
                waits=[tok_proj_ev[(tt, 0)], tok_proj_ev[(tt, 1)]])

            tok_ns = S("vector", lambda e: e.reduce_sum(t_stat[:, 0:1], t_w[:], axis=AX.X,
                                                        negate=True), waits=[tok_res])
            tok_nm = S("vector", lambda e: e.tensor_scalar(t_stat[:, 1:2], t_stat[:, 0:1],
                                                           1.0 / C, None, op0=ALU.mult),
                       waits=[tok_ns])
            tok_ct = S("vector", lambda e: e.tensor_scalar(t_cent[:], t_w[:], t_stat[:, 1:2],
                                                           None, op0=ALU.add), waits=[tok_nm])
            tok_sq = S("vector", lambda e: e.tensor_tensor(t_w[:], t_cent[:], t_cent[:],
                                                           op=ALU.mult), waits=[tok_ct])
            tok_vr = S("vector", lambda e: e.reduce_sum(t_stat[:, 2:3], t_w[:], axis=AX.X),
                       waits=[tok_sq])
            tok_v2 = S("vector", lambda e: e.tensor_scalar(t_stat[:, 3:4], t_stat[:, 2:3],
                                                           1.0 / C, EPS1, op0=ALU.mult,
                                                           op1=ALU.add), waits=[tok_vr])
            tok_ln = S("scalar", lambda e: e.activation(t_stat[:, 4:5], t_stat[:, 3:4], AF.Ln),
                       waits=[tok_v2])
            tok_rs = S("scalar", lambda e: e.activation(t_stat[:, 5:6], t_stat[:, 4:5], AF.Exp,
                                                        scale=-0.5), waits=[tok_ln])
            tok_xnew[tt] = S("vector", lambda e, tq=tq: e.tensor_scalar(
                stream[:, (tq % NSL) * C:(tq % NSL + 1) * C], t_cent[:], t_stat[:, 5:6],
                None, op0=ALU.mult), waits=[tok_rs])

            n_spill[0] += 1
            S("sync", lambda e, tt=tt, tq=tq: e.dma_start(
                d_x1[tt * 128:(tt + 1) * 128, :],
                stream[:, (tq % NSL) * C:(tq % NSL + 1) * C]).then_inc(q_spill, 16),
              waits=[tok_xnew[tt]])

            tok_rtr_ev = {}
            for bt in range(2):
                def mk_rtr(eng, bt=bt, tq=tq):
                    last = None
                    for j in range(4):
                        last = eng.transpose(
                            p_misc[:, j * 128:(j + 1) * 128],
                            stream[:, (tq % NSL) * C + (bt * 4 + j) * 128: (tq % NSL) * C + (bt * 4 + j + 1) * 128],
                            t_identf[:])
                    return last
                rw = [tok_xnew[tt]]
                rw.append(tok_proj_ev[(tt, 1)] if bt == 0 else tok_rtr_ev[0])
                tok_rtr_pe = S("tensor", mk_rtr, waits=rw)
                tok_rtr_ev[bt] = S("vector", lambda e, bt=bt: e.tensor_copy(
                    t_xT32[:, bt * 512:(bt + 1) * 512], p_misc[:]), waits=[tok_rtr_pe])

            def mk_rmm(eng):
                last = None
                for kc in range(NKC):
                    last = eng.matmul(p_misc[:, 0:NE],
                                      t_xT32[:, kc * 128:(kc + 1) * 128],
                                      t_wrt[:, kc * NE:(kc + 1) * NE],
                                      start=(kc == 0), stop=(kc == NKC - 1))
                return last
            tok_rmm = S("tensor", mk_rmm, waits=[tok_rtr_ev[0], tok_rtr_ev[1], w_setup])
            tok_logit_ev[tt] = S("vector", lambda e: e.tensor_copy(t_lg[:], p_misc[:, 0:NE]),
                                 waits=[tok_rmm])
            tok_m8 = S("vector", lambda e: e.max(t_mx8[:], t_lg[:]), waits=[tok_logit_ev[tt]])
            tok_mi = S("vector", lambda e: e.max_index(t_idxu[:], t_mx8[:], t_lg[:]),
                       waits=[tok_m8])
            tok_if = S("vector", lambda e: e.tensor_scalar(t_idxf[:], t_idxu[:, 0:1], 0.0,
                                                           None, op0=ALU.add),
                       waits=[tok_mi] + ([(q_idx, 16 * tt)] if tt >= 1 else []))
            S("sync", lambda e, tt=tt: e.dma_start(
                d_idx.rearrange("(a p) -> a p", p=128)[tt].rearrange("(p o) -> p o", o=1),
                t_idxf[:]).then_inc(q_idx, 16), waits=[tok_if])

    S("sync", lambda e: e.nop(), waits=[(q_spill, 16 * NTT), (q_idx, 16 * NTT)])

    a_st.close()
    b_st = ExitStack()
    bsb = lambda name, shape, dt: b_st.enter_context(nc.sbuf_tensor(name, shape, dt))

    t_ring = bsb("ring", [128, 32 * 1024], BF16)
    t_idxbc = bsb("idx_bc", [128, 2048], F32)
    t_rank = bsb("rank", [128, 2048], F32)
    t_dsf = bsb("dsf", [128, 16], F32)
    t_dsi = bsb("dsi", [128, 16], I32)
    t_xrb = bsb("xrb", [128, 2 * C], BF16)
    t_xgl = bsb("xgl", [128, 2 * C], BF16)
    t_xgT = bsb("xgT", [128, 2 * NKC * 512], BF16)
    t_zg = bsb("zg", [128, 32 * 512], BF16)
    t_yst = bsb("yst", [128, 4 * C], F32)
    t_yg = bsb("yg", [128, 2 * C], F32)
    t_xrel2 = bsb("xrel2", [128, 2 * C], F32)
    t_y2 = bsb("y2", [128, 2 * C], F32)
    t_ct2 = bsb("ct2", [128, 2 * C], F32)
    t_st2 = bsb("stat2", [128, 16], F32)

    bpsal = lambda name, shape, dt: b_st.enter_context(nc.psum_tensor(name, shape, dt))
    p_i = [bpsal(f"p_i{j}", [128, 512], F32) for j in range(4)]
    p_y2 = [bpsal(f"p_y2{j}", [128, 512], F32) for j in range(2)]
    p_trB = bpsal("p_trB", [128, 1024], BF16)

    sched.new_block()
    S = sched.op

    nw_slot = [0] * 32
    slot_last_reader = [None] * 32
    load_ctr = [0]
    n_xrel = [0] * 4
    n_xrel2 = [0] * 4
    n_out2 = [0, 0]

    tok_idxbc = S("sync", lambda e: e.dma_start(
        t_idxbc[:], bass.AP(d_idx.tensor, 0, [[0, 128], [1, 2048]])).then_inc(
        q_idxbc, 16))
    w_idxbc = (q_idxbc, 16)

    tok_ebase = S("vector", lambda e: e.tensor_scalar(
        t_ebase[:], t_eids[:], 512.0, -1.0, op0=ALU.mult, op1=ALU.add),
        waits=[w_setup])
    tok_eq8 = S("vector", lambda e: e.tensor_scalar(
        t_rank[0:8, :], t_idxbc[0:8, :], t_eids[:], None, op0=ALU.is_equal),
        waits=[w_idxbc, w_setup])
    cur, oth = 0, 32
    tok_cs = tok_eq8
    for k in [1 << i for i in range(11)]:
        t1 = S("vector", lambda e, cur=cur, oth=oth, k=k: e.tensor_copy(
            t_rank[oth:oth + 8, 0:k], t_rank[cur:cur + 8, 0:k]), waits=[tok_cs])
        tok_cs = S("vector", lambda e, cur=cur, oth=oth, k=k: e.tensor_tensor(
            t_rank[oth:oth + 8, k:2048], t_rank[cur:cur + 8, k:2048],
            t_rank[cur:cur + 8, 0:2048 - k], op=ALU.add), waits=[t1])
        cur, oth = oth, cur
    assert cur == 32
    tok_eq8b = S("vector", lambda e: e.tensor_scalar(
        t_idxbc[0:8, :], t_idxbc[0:8, :], t_eids[:], None,
        op0=ALU.is_equal), waits=[tok_cs])
    tok_tmp = S("vector", lambda e, cur=cur: e.tensor_scalar(
        t_rank[0:8, :], t_rank[cur:cur + 8, :], t_ebase[:], None, op0=ALU.add),
        waits=[tok_cs, tok_ebase])
    tok_w8 = S("vector", lambda e: e.tensor_tensor(
        t_rank[0:8, :], t_rank[0:8, :], t_idxbc[0:8, :],
        op=ALU.mult), waits=[tok_tmp, tok_eq8b])
    w8base = 0
    tok_dsum_ev = {}
    for c4 in range(4):
        bank = p_y2[c4 % 2]
        dw = [tok_w8, w_setup]
        if c4 >= 2:
            dw.append(tok_dsum_ev[c4 - 2])
        tok_dmm = S("tensor", lambda e, bank=bank, c4=c4: e.matmul(
            bank[0:1, 0:512], t_ones8[:],
            t_rank[w8base:w8base + 8, c4 * 512:(c4 + 1) * 512],
            start=True, stop=True), waits=dw)
        tok_dsum_ev[c4] = S("vector", lambda e, bank=bank, c4=c4: e.tensor_copy(
            t_rank[64:65, c4 * 512:(c4 + 1) * 512], bank[0:1, 0:512]),
            waits=[tok_dmm])
    S("sync", lambda e: e.dma_start(d_dst, t_rank[64:65, :]).then_inc(q_dst, 16),
      waits=[tok_dsum_ev[2], tok_dsum_ev[3]])
    S("sync", lambda e: e.dma_start(
        t_dsf[:], bass.AP(d_dst.tensor, 0, [[1, 128], [128, 16]]),
        allow_slow_non_contiguous=True).then_inc(q_dst2, 16),
      waits=[(q_dst, 16)])
    tok_dsi = S("vector", lambda e: e.tensor_copy(t_dsi[:], t_dsf[:]),
                waits=[(q_dst2, 16)])

    tok_z0 = S("vector", lambda e: e.memset(t_xgl[:, 0:1024], 0.0))
    for j in range(32):
        S("sync", lambda e, j=j: e.dma_start(
            d_xg[j * 128:(j + 1) * 128, :], t_xgl[:, 0:1024]).then_inc(q_zg, 16),
          waits=[tok_z0])
    w_zg = (q_zg, 32 * 16)

    nscat = [0, 0]
    tok_scat = {}
    for t2 in range(NTT):
        qi = t2 % 4
        n_xrel[qi] += 1
        lw = [(q_spill, 16 * NTT)]
        if t2 - 2 >= 0:
            lw.append((q_xg2[t2 % 2], nscat[t2 % 2]))
        S("gpsimd", lambda e, t2=t2, qi=qi: e.dma_start(
            t_xrb[:, (t2 % 2) * C:(t2 % 2 + 1) * C],
            d_x1[t2 * 128:(t2 + 1) * 128, :]).then_inc(q_xrel[qi], 16), waits=lw)
        xrw = (q_xrel[qi], 16 * n_xrel[qi])
        nscat[t2 % 2] += 16
        tok_scat[t2] = S("gpsimd", lambda e, t2=t2: e.indirect_dma_start(
            d_xg, IndirectOffsetOnAxis(ap=t_dsi[:, t2:t2 + 1], axis=0),
            t_xrb[:, (t2 % 2) * C:(t2 % 2 + 1) * C], None,
            bounds_check=NE * 512 - 1, oob_is_err=False).then_inc(
            q_xg2[t2 % 2], 16), waits=[xrw, tok_dsi, w_zg])
    w_scat = [(q_xg2[0], nscat[0]), (q_xg2[1], nscat[1])]

    nxgl = [0, 0]
    tok_xgtr = {}
    tok_xgev = {}
    tok_hmm = {}
    tok_gelu = {}
    tok_womm = {}
    tok_yev = {}
    nyg = [0] * 4
    tok_hmm_last = {}
    tok_womm_last = {}
    tok_gelu_last = {}

    for eix in range(NE):
        for j in range(4):
            gb2 = (eix * 4 + j) % 2
            lw = list(w_scat) if (eix == 0 and j == 0) else []
            prev = (eix, j - 2) if j >= 2 else (eix - 1, j + 2)
            if prev in tok_xgtr:
                lw.append(tok_xgtr[prev])
            nxgl[gb2] += 16
            S("sync", lambda e, eix=eix, j=j, gb2=gb2: e.dma_start(
                t_xgl[:, gb2 * C:(gb2 + 1) * C],
                d_xg[eix * 512 + j * 128: eix * 512 + (j + 1) * 128, :]).then_inc(
                q_xgl[gb2], 16), waits=lw)
            xglw = (q_xgl[gb2], nxgl[gb2])

            def mk_xt(eng, gb2=gb2):
                last = None
                for kc in range(NKC):
                    last = eng.transpose(
                        p_trB[:, kc * 128:(kc + 1) * 128],
                        t_xgl[:, gb2 * C + kc * 128: gb2 * C + (kc + 1) * 128],
                        t_identb[:])
                return last
            tw = [xglw, w_setup]
            pj = (eix, j - 1) if j >= 1 else (eix - 1, 3)
            if pj in tok_xgev:
                tw.append(tok_xgev[pj])
            tok_xgtr[(eix, j)] = S("tensor", mk_xt, waits=tw)
            xdst = (t_xgT[:, (eix % 2) * NKC * 512:(eix % 2 + 1) * NKC * 512]
                    .rearrange("p (kc t) -> p kc t", kc=NKC)
                    [:, :, j * 128:(j + 1) * 128])
            ew = [tok_xgtr[(eix, j)]]
            if (eix - 2) in tok_hmm_last and j == 0:
                ew.append(tok_hmm_last[eix - 2])
            tok_xgev[(eix, j)] = S("vector", lambda e, xdst=xdst: e.tensor_copy(
                xdst, p_trB[:, 0:C].rearrange("p (kc t) -> p kc t", kc=NKC)),
                waits=ew)

        for iq in range(4):
            slot_of_kc = {}
            for kc in range(NKC):
                L = load_ctr[0]
                s = L % 32
                nw_slot[s] += 1
                slot_of_kc[kc] = (s, nw_slot[s])
                lw = [slot_last_reader[s]] if slot_last_reader[s] is not None else []
                S("gpsimd", lambda e, s=s, eix=eix, kc=kc, iq=iq: e.dma_start(
                    t_ring[:, s * 1024:(s + 1) * 1024],
                    d_w1[eix, kc * 128:(kc + 1) * 128,
                         iq * 1024:(iq + 1) * 1024]).then_inc(q_slot[s], 16),
                  waits=lw)
                load_ctr[0] += 1
            for icl in range(8):
                icg = iq * 8 + icl
                bank = p_i[icg % 4]

                def mk_im(eng, bank=bank, icl=icl, eix=eix,
                          slot_of_kc=dict(slot_of_kc)):
                    last = None
                    for kc in range(NKC):
                        s2, _ = slot_of_kc[kc]
                        last = eng.matmul(
                            bank[:],
                            t_ring[:, s2 * 1024 + icl * 128: s2 * 1024 + (icl + 1) * 128],
                            t_xgT[:, (eix % 2) * NKC * 512 + kc * 512:
                                  (eix % 2) * NKC * 512 + (kc + 1) * 512],
                            start=(kc == 0), stop=(kc == NKC - 1))
                    return last
                mw = [tok_xgev[(eix, 3)]]
                mw += [(q_slot[s2], 16 * n2) for (s2, n2) in slot_of_kc.values()]
                if (eix, icg - 4) in tok_gelu:
                    mw.append(tok_gelu[(eix, icg - 4)])
                elif eix >= 1 and icg < 4:
                    mw.append(tok_gelu[(eix - 1, 28 + icg)])
                tok_hmm[(eix, icg)] = S("tensor", mk_im, waits=mw)

                gw = [tok_hmm[(eix, icg)], w_b1c]
                if icg == 0 and (eix - 1) in tok_womm_last:
                    gw.append(tok_womm_last[eix - 1])
                tok_gelu[(eix, icg)] = S("scalar", lambda e, bank=bank, eix=eix,
                                         icg=icg:
                                         e.activation(
                                             t_zg[:, icg * 512:(icg + 1) * 512],
                                             bank[:], AF.Gelu,
                                             bias=t_b1c[:, icg * 8 + eix:
                                                        icg * 8 + eix + 1]),
                                         waits=gw)
            for kc in range(NKC):
                s2, _ = slot_of_kc[kc]
                slot_last_reader[s2] = tok_hmm[(eix, iq * 8 + 7)]
        tok_hmm_last[eix] = tok_hmm[(eix, 31)]
        tok_gelu_last[eix] = tok_gelu[(eix, 31)]

        wo_slot = {}
        for ic in range(32):
            L = load_ctr[0]
            s = L % 32
            nw_slot[s] += 1
            wo_slot[ic] = (s, nw_slot[s])
            lw = [slot_last_reader[s]] if slot_last_reader[s] is not None else []
            if ic == 0:
                lw.append((q_wob, 16))
            S("gpsimd", lambda e, s=s, ic=ic: e.dma_start(
                t_ring[:, s * 1024:(s + 1) * 1024],
                d_wob[ic * 128:(ic + 1) * 128, :]).then_inc(q_slot[s], 16),
              waits=lw)
            load_ctr[0] += 1

        for j in range(4):
            for ch2 in range(2):
                def mk_wo(eng, j=j, ch2=ch2, wo_slot=dict(wo_slot)):
                    last = None
                    for ic in range(32):
                        s2, _ = wo_slot[ic]
                        last = eng.matmul(
                            p_y2[ch2][:],
                            t_zg[:, ic * 512 + j * 128: ic * 512 + (j + 1) * 128],
                            t_ring[:, s2 * 1024 + ch2 * 512: s2 * 1024 + (ch2 + 1) * 512],
                            start=(ic == 0), stop=(ic == 31))
                    return last
                ww = [tok_gelu_last[eix]]
                ww += [(q_slot[s2], 16 * n2) for (s2, n2) in wo_slot.values()]
                if j >= 1:
                    ww.append(tok_yev[(eix, j - 1, ch2)])
                elif eix >= 1:
                    ww.append(tok_yev[(eix - 1, 3, ch2)])
                tok_womm[(eix, j, ch2)] = S("tensor", mk_wo, waits=ww)
                ew = [tok_womm[(eix, j, ch2)]]
                if eix >= 1:
                    ew.append((q_ygs[j], nyg[j]))
                tok_yev[(eix, j, ch2)] = S("vector", lambda e, j=j, ch2=ch2:
                                           e.tensor_copy(
                                               t_yst[:, j * C + ch2 * 512:
                                                     j * C + ch2 * 512 + 512],
                                               p_y2[ch2][:]), waits=ew)
        for ic in range(32):
            s2, _ = wo_slot[ic]
            slot_last_reader[s2] = tok_womm[(eix, 3, 1)]
        tok_womm_last[eix] = tok_womm[(eix, 3, 1)]
        for j in range(4):
            nyg[j] += 16
            S("sync", lambda e, eix=eix, j=j: e.dma_start(
                d_yg[eix * 512 + j * 128: eix * 512 + (j + 1) * 128, :],
                t_yst[:, j * C:(j + 1) * C]).then_inc(q_ygs[j], 16),
              waits=[tok_yev[(eix, j, 0)], tok_yev[(eix, j, 1)]])

    w_yall = [(q_ygs[j], nyg[j]) for j in range(4)]

    nygr = [0, 0]
    tok_tail_res = {}
    tok_tail_last = {}
    tok_r0 = {}
    for ttl in range(NTT):
        tb = ttl % 2
        gw = list(w_yall) if ttl < 2 else []
        if (ttl - 2) in tok_r0:
            gw.append(tok_tail_res[ttl - 2])
        gw.append(tok_dsi)
        nygr[tb] += 16
        S("gpsimd", lambda e, ttl=ttl, tb=tb: e.indirect_dma_start(
            t_yg[:, tb * C:(tb + 1) * C], None,
            d_yg, IndirectOffsetOnAxis(ap=t_dsi[:, ttl:ttl + 1], axis=0),
            bounds_check=NE * 512 - 1, oob_is_err=False).then_inc(
            q_ygr[tb], 16), waits=gw)
        ygw = (q_ygr[tb], nygr[tb])

        qi = ttl % 4
        rw = [(q_spill, 16 * NTT)]
        if (ttl - 2) in tok_tail_res:
            rw.append(tok_tail_res[ttl - 2])
        n_xrel2[qi] += 1
        S("sync", lambda e, ttl=ttl, tb=tb, qi=qi: e.dma_start(
            t_xrel2[:, tb * C:(tb + 1) * C],
            d_x1[ttl * 128:(ttl + 1) * 128, :]).then_inc(q_xrel2[qi], 16), waits=rw)
        xr2w = (q_xrel2[qi], 16 * n_xrel2[qi])

        yb = t_y2[:, tb * C:(tb + 1) * C]
        cb2 = t_ct2[:, tb * C:(tb + 1) * C]
        xr = t_xrel2[:, tb * C:(tb + 1) * C]
        sc = t_st2[:, tb * 8:(tb + 1) * 8]

        r0w = [ygw, w_setup]
        if (ttl - 2) in tok_tail_last:
            r0w.append(tok_tail_last[ttl - 2])
        if n_out2[tb] > 0:
            r0w.append((q_out2[tb], n_out2[tb]))
        tok_r0[ttl] = S("vector", lambda e, yb=yb, tb=tb: e.scalar_tensor_tensor(
            yb, t_yg[:, tb * C:(tb + 1) * C], 1.0, t_bo[:],
            op0=ALU.mult, op1=ALU.add), waits=r0w)

        def ln_chain(first_w, src, cent, dst, eps, affine, sc=sc):
            tns = S("vector", lambda e, src=src, sc=sc: e.reduce_sum(
                sc[:, 0:1], src, axis=AX.X, negate=True), waits=first_w)
            tnm = S("vector", lambda e, sc=sc: e.tensor_scalar(
                sc[:, 1:2], sc[:, 0:1], 1.0 / C, None, op0=ALU.mult), waits=[tns])
            tct = S("vector", lambda e, src=src, cent=cent, sc=sc: e.tensor_scalar(
                cent, src, sc[:, 1:2], None, op0=ALU.add), waits=[tnm])
            tsq = S("vector", lambda e, src=src, cent=cent: e.tensor_tensor(
                src, cent, cent, op=ALU.mult), waits=[tct])
            tvr = S("vector", lambda e, src=src, sc=sc: e.reduce_sum(
                sc[:, 2:3], src, axis=AX.X), waits=[tsq])
            tv2 = S("vector", lambda e, sc=sc, eps=eps: e.tensor_scalar(
                sc[:, 3:4], sc[:, 2:3], 1.0 / C, eps, op0=ALU.mult,
                op1=ALU.add), waits=[tvr])
            tln = S("scalar", lambda e, sc=sc: e.activation(sc[:, 4:5], sc[:, 3:4],
                                                            AF.Ln), waits=[tv2])
            trs = S("scalar", lambda e, sc=sc: e.activation(sc[:, 5:6], sc[:, 4:5],
                                                            AF.Exp, scale=-0.5),
                    waits=[tln])
            if affine:
                tz = S("vector", lambda e, cent=cent, dst=dst, sc=sc:
                       e.scalar_tensor_tensor(dst, cent, sc[:, 5:6], t_gam[:],
                                              op0=ALU.mult, op1=ALU.mult), waits=[trs])
                tz = S("vector", lambda e, dst=dst: e.tensor_tensor(
                    dst, dst, t_bet[:], op=ALU.add), waits=[tz])
            else:
                tz = S("vector", lambda e, cent=cent, dst=dst, sc=sc: e.tensor_scalar(
                    dst, cent, sc[:, 5:6], None, op0=ALU.mult), waits=[trs])
            return tz

        t_r1 = S("vector", lambda e, yb=yb, xr=xr: e.tensor_tensor(
            yb, yb, xr, op=ALU.add), waits=[tok_r0[ttl], xr2w])
        t_z = ln_chain([t_r1], yb, cb2, yb, EPS2, True)
        t_r2 = S("vector", lambda e, yb=yb, xr=xr: e.tensor_tensor(
            yb, xr, yb, op=ALU.add), waits=[t_z])
        tok_tail_res[ttl] = t_r2
        t_o = ln_chain([t_r2], yb, cb2, cb2, EPS1, False)
        tok_tail_last[ttl] = t_o

        n_out2[tb] += 16
        S("sync", lambda e, ttl=ttl, cb2=cb2, tb=tb: e.dma_start(
            d_out[ttl * 128:(ttl + 1) * 128, :], cb2).then_inc(q_out2[tb], 16),
          waits=[t_o])

    S("sync", lambda e: e.nop(), waits=[(q_out2[0], n_out2[0]), (q_out2[1], n_out2[1])])

    sched.finalize()
    for blk_ops in sched.blocks:
        with nc.Block() as block:
            for ename, deco in (("sync", block.sync), ("gpsimd", block.gpsimd),
                                ("scalar", block.scalar), ("vector", block.vector),
                                ("tensor", block.tensor)):
                def body(engine, ename=ename, blk_ops=blk_ops):
                    sched.emit_block(blk_ops, ename, engine, eng_sems)
                deco(body)

    b_st.close()
    st.close()
    return nc


_NC_CACHE = {}


def _get_nc():
    if "nc" not in _NC_CACHE:
        _NC_CACHE["nc"] = build_program()
    return _NC_CACHE["nc"]


def _build_masks(seq0):
    p = np.arange(128)[:, None]
    f = np.arange(W3)[None, :]
    out = np.zeros((3, 128, W3), dtype=np.float32)
    for v in range(3):
        tt = v if v < 2 else 2
        allowed = (f >= p) & (f <= p + 256)
        if v < 2:
            kseq = seq0 - 256 + tt * 128 + f
            allowed = allowed & (kseq >= 0)
        out[v] = np.where(allowed, 0.0, NEG)
    return out


def kernel(L0_attn_x, L0_attn_W_attn, L0_attn_b_attn, L0_attn_W_proj, L0_attn_b_proj,
           L0_ffn_Wr, L0_ffn_W1, L0_ffn_b1, L0_ffn_Wo, L0_ffn_bo, L0_ffn_gamma,
           L0_ffn_beta):
    x = np.ascontiguousarray(np.asarray(L0_attn_x, dtype=np.float32).reshape(B * T, C))
    wa = np.ascontiguousarray(np.asarray(L0_attn_W_attn, dtype=np.float32))
    ba = np.ascontiguousarray(np.asarray(L0_attn_b_attn, dtype=np.float32))
    wp = np.ascontiguousarray(np.asarray(L0_attn_W_proj, dtype=np.float32))
    bp = np.ascontiguousarray(np.asarray(L0_attn_b_proj, dtype=np.float32))
    wrt = np.ascontiguousarray(np.asarray(L0_ffn_Wr, dtype=np.float32).T)
    w1 = np.ascontiguousarray(np.asarray(L0_ffn_W1, dtype=np.float32))
    b1 = np.ascontiguousarray(np.asarray(L0_ffn_b1, dtype=np.float32))
    wo = np.ascontiguousarray(np.asarray(L0_ffn_Wo, dtype=np.float32))
    bo = np.ascontiguousarray(np.asarray(L0_ffn_bo, dtype=np.float32))
    gam = np.ascontiguousarray(np.asarray(L0_ffn_gamma, dtype=np.float32))
    bet = np.ascontiguousarray(np.asarray(L0_ffn_beta, dtype=np.float32))

    in_maps = []
    for c in range(NCORES):
        base = c * TCORE
        seq0 = (c % 4) * TCORE
        xe = np.zeros((NTQ * 128, C), dtype=np.float32)
        if seq0 == 0:
            xe[HALO:] = x[base:base + TCORE]
        else:
            xe[:] = x[base - HALO:base + TCORE]
        in_maps.append({
            "x_ext": xe, "w_attn": wa, "b_attn": ba, "w_proj": wp, "b_proj": bp,
            "wr_t": wrt, "w1": w1, "b1": b1, "wo": wo, "bo": bo, "gamma": gam,
            "beta": bet, "masks": _build_masks(seq0),
            "eids": np.arange(NE, dtype=np.float32).reshape(NE, 1),
            "ones8": np.ones((NE, 1), dtype=np.float32),
        })

    nc = _get_nc()
    res = run_bass_kernel_spmd(nc, in_maps, core_ids=list(range(NCORES)))
    out = np.concatenate([r["out"] for r in res.results], axis=0)
    return out.reshape(B, T, C)


# version 12
# speedup vs baseline: 1.4715x; 1.0081x over previous
import numpy as np
import ml_dtypes
from contextlib import ExitStack

import concourse.bass as bass
import concourse.mybir as mybir
from concourse.bass import IndirectOffsetOnAxis
from concourse.bass_utils import run_bass_kernel_spmd

F32 = mybir.dt.float32
BF16 = mybir.dt.bfloat16
U8 = mybir.dt.uint8
U32 = mybir.dt.uint32
I32 = mybir.dt.int32
AF = mybir.ActivationFunctionType
ALU = mybir.AluOpType
AX = mybir.AxisListType

NCORES = 8
B, T, C = 2, 8192, 1024
TCORE = 2048
HALO = 256
NTT = 16
NTQ = 18
NH, HD = 16, 64
W3 = 384
FF = 4096
NE = 8
NKC = 8
NEG = -1.0e30
NSL = 8
EPS1 = 1e-5
EPS2 = 1e-12

ENGINES = ("sync", "gpsimd", "scalar", "vector", "tensor")


class Sched:

    def __init__(self):
        self.blocks = []
        self.cur = None

    def new_block(self):
        self.cur = []
        self.blocks.append(self.cur)

    def op(self, eng, fn, waits=()):
        rec = {"eng": eng, "fn": fn, "waits": [w for w in waits if w is not None],
               "inc": None}
        self.cur.append(rec)
        return rec

    def finalize(self):
        for blk in self.blocks:
            for rec in blk:
                for wtg in rec["waits"]:
                    if isinstance(wtg, dict) and wtg["inc"] is None:
                        wtg["inc"] = -1
        counters = {e: 0 for e in ENGINES}
        for blk in self.blocks:
            for rec in blk:
                if rec["inc"] == -1:
                    counters[rec["eng"]] += 1
                    rec["inc"] = counters[rec["eng"]]

    def emit_block(self, blk_ops, eng_name, engine, eng_sems):
        highwater = {}
        for rec in blk_ops:
            if rec["eng"] != eng_name:
                continue
            for wtg in rec["waits"]:
                if isinstance(wtg, dict):
                    sem, val, key = eng_sems[wtg["eng"]], wtg["inc"], ("e", wtg["eng"])
                else:
                    sem, val = wtg
                    key = ("s", id(sem))
                if highwater.get(key, -1) >= val:
                    continue
                highwater[key] = val
                engine.wait_ge(sem, val)
            inst = rec["fn"](engine)
            if rec["inc"] is not None:
                assert inst is not None, f"wait-target op returned no inst on {eng_name}"
                inst.then_inc(eng_sems[rec["eng"]], 1)


def build_program():
    nc = bass.Bass("TRN2", target_bir_lowering=False)

    d_x = nc.dram_tensor("x_ext", [NTQ * 128, C], F32, kind="ExternalInput").ap()
    d_wa = nc.dram_tensor("w_attn", [C, 3 * C], F32, kind="ExternalInput").ap()
    d_ba = nc.dram_tensor("b_attn", [3 * C], F32, kind="ExternalInput").ap()
    d_wp = nc.dram_tensor("w_proj", [C, C], F32, kind="ExternalInput").ap()
    d_bp = nc.dram_tensor("b_proj", [C], F32, kind="ExternalInput").ap()
    d_wrt = nc.dram_tensor("wr_t", [C, NE], F32, kind="ExternalInput").ap()
    d_w1 = nc.dram_tensor("w1", [NE, C, FF], F32, kind="ExternalInput").ap()
    d_b1 = nc.dram_tensor("b1", [NE, FF], F32, kind="ExternalInput").ap()
    d_wo = nc.dram_tensor("wo", [FF, C], F32, kind="ExternalInput").ap()
    d_bo = nc.dram_tensor("bo", [C], F32, kind="ExternalInput").ap()
    d_gam = nc.dram_tensor("gamma", [C], F32, kind="ExternalInput").ap()
    d_bet = nc.dram_tensor("beta", [C], F32, kind="ExternalInput").ap()
    d_msk = nc.dram_tensor("masks", [3, 128, W3], F32, kind="ExternalInput").ap()
    d_out = nc.dram_tensor("out", [TCORE, C], F32, kind="ExternalOutput").ap()
    d_x1 = nc.dram_tensor("x1_scratch", [TCORE, C], F32, kind="Internal").ap()
    d_idx = nc.dram_tensor("idx_scratch", [TCORE], F32, kind="Internal").ap()
    d_eids = nc.dram_tensor("eids", [NE, 1], F32, kind="ExternalInput").ap()
    d_ones8 = nc.dram_tensor("ones8", [NE, 1], F32, kind="ExternalInput").ap()
    d_xg = nc.dram_tensor("xg_scratch", [NE * 512, C], BF16, kind="Internal").ap()
    d_yg = nc.dram_tensor("yg_scratch", [NE * 512, C], F32, kind="Internal").ap()
    d_dst = nc.dram_tensor("dst_scratch", [TCORE], F32, kind="Internal").ap()
    d_wob = nc.dram_tensor("wo_bf", [FF, C], BF16, kind="Internal").ap()

    ident_bf = nc.inline_tensor(np.eye(128, dtype=ml_dtypes.bfloat16), "ident_bf").ap()
    ident_f32 = nc.inline_tensor(np.eye(128, dtype=np.float32), "ident_f32").ap()

    st = ExitStack()
    sb = lambda name, shape, dt: st.enter_context(nc.sbuf_tensor(name, shape, dt))
    psal = lambda name, shape, dt: st.enter_context(nc.psum_tensor(name, shape, dt))
    sem = lambda name: st.enter_context(nc.semaphore(name))

    t_identb = sb("identb", [128, 128], BF16)
    t_identf = sb("identf", [128, 128], F32)
    t_bqk = sb("bqk", [128, 16], F32)
    t_bqs = sb("bqs", [128, 8], F32)
    t_wrt = sb("wrt", [128, NKC * NE], F32)
    t_b1c = sb("b1cols", [128, NE * 32], F32)
    t_eids = sb("eids_sb", [NE, 1], F32)
    t_ones8 = sb("ones8_sb", [NE, 1], F32)
    t_ebase = sb("ebase_sb", [NE, 1], F32)
    t_bo = sb("bo_bc", [128, C], F32)
    t_gam = sb("gam_bc", [128, C], F32)
    t_bet = sb("bet_bc", [128, C], F32)

    eng_sems = {e: sem("e_" + e) for e in ENGINES}
    q_setup = sem("q_setup")
    q_wsetup = sem("q_wsetup")
    q_wk = sem("q_wk")
    q_wv = sem("q_wv")
    q_wq = sem("q_wq")
    q_x = [sem(f"q_x{i}") for i in range(6)]
    q_spill = sem("q_spill")
    q_idx = sem("q_idx")
    q_idxbc = sem("q_idxbc")
    q_b1c = sem("q_b1c")
    q_gsetup = sem("q_gsetup")
    q_xrel = [sem(f"q_xrel{i}") for i in range(4)]
    q_xrel2 = [sem(f"q_xrel2{i}") for i in range(4)]
    q_slot = [sem(f"q_slot{i}") for i in range(32)]
    q_out2 = [sem("q_out0"), sem("q_out1")]
    q_xg2 = [sem("q_xg0"), sem("q_xg1")]
    q_zg = sem("q_zg")
    q_dst = sem("q_dst")
    q_dst2 = sem("q_dst2")
    q_wob = sem("q_wob")
    q_xgl = [sem("q_xgl0"), sem("q_xgl1")]
    q_ygs = [sem(f"q_ygs{i}") for i in range(4)]
    q_ygr = [sem("q_ygr0"), sem("q_ygr1")]

    sched = Sched()

    a_st = ExitStack()
    asb = lambda name, shape, dt: a_st.enter_context(nc.sbuf_tensor(name, shape, dt))

    t_wab = asb("wab", [128, NKC * 3 * C], BF16)
    t_wpb = asb("wpb", [128, NKC * C], BF16)
    t_kring = asb("kring", [128, NSL * C], BF16)
    t_vring = asb("vring", [128, NSL * C], BF16)
    t_masks = asb("masks_sb", [128, 3 * W3], BF16)
    t_bvbc = asb("bv_bc", [128, C], BF16)
    t_bpbc = asb("bp_bc", [128, C], BF16)
    t_xc = asb("xc", [128, 2 * C], BF16)
    t_xTb4 = asb("xTb4", [128, NKC * 512], BF16)
    t_q4 = asb("q4", [128, NKC * 512], BF16)
    stream = asb("stream", [128, NSL * C], F32)
    t_scf = asb("scf", [128, 4 * W3], F32)
    t_prob = asb("prob", [128, 2 * 4 * W3], BF16)
    t_probT = asb("probT", [128, 2 * 4 * W3], BF16)
    t_ao = asb("attn_out", [128, C], BF16)
    t_aon = asb("attn_out_n", [128, C], BF16)
    t_aoT = asb("attn_outT", [128, C], BF16)
    t_den = asb("den", [128, NH], F32)
    t_rden = asb("rden", [128, NH], F32)
    t_w = asb("wbuf", [128, C], F32)
    t_cent = asb("centbuf", [128, C], F32)
    t_xT32 = asb("xT32", [128, C], F32)
    t_lg = asb("lgs", [128, 8], F32)
    t_mx8 = asb("mx8", [128, 8], F32)
    t_idxu = asb("idxu", [128, 8], U32)
    t_idxf = asb("idxf", [128, 1], F32)
    t_stat = asb("stat", [128, 8], F32)

    apsal = lambda name, shape, dt: a_st.enter_context(nc.psum_tensor(name, shape, dt))
    p_tra = apsal("p_tra", [128, 1024], BF16)
    p_trb = apsal("p_trb", [128, 1024], BF16)
    p_v = apsal("p_v", [128, 512], F32)
    p_sc4 = [apsal(f"p_sc4_{i}", [128, 512], F32) for i in range(4)]
    p_misc = apsal("p_misc", [128, 512], F32)

    sched.new_block()
    S = sched.op

    n_setup = [0]
    n_gsetup = [0]

    def setup_dma(engname, fn):
        if engname == "gpsimd":
            S(engname, lambda e, fn=fn: fn(e).then_inc(q_gsetup, 16))
            n_gsetup[0] += 16
        else:
            S(engname, lambda e, fn=fn: fn(e).then_inc(q_setup, 16))
            n_setup[0] += 16

    setup_dma("sync", lambda e: e.dma_start(t_identb[:], ident_bf))
    setup_dma("sync", lambda e: e.dma_start(t_identf[:], ident_f32))
    for v in range(3):
        setup_dma("gpsimd", lambda e, v=v: e.dma_start(
            t_masks[:, v * W3:(v + 1) * W3], d_msk[v]))
    setup_dma("sync", lambda e: e.dma_start(
        t_bqk[:], bass.AP(d_ba.tensor, 0, [[1, 128], [128, 16]]),
        allow_slow_non_contiguous=True))

    def bcast_src(dap, off, n):
        return bass.AP(dap.tensor, off, [[0, 128], [1, n]])

    setup_dma("gpsimd", lambda e: e.dma_start(t_bvbc[:], bcast_src(d_ba, 2048, C)))
    setup_dma("gpsimd", lambda e: e.dma_start(t_bpbc[:], bcast_src(d_bp, 0, C)))
    setup_dma("sync", lambda e: e.dma_start(t_bo[:], bcast_src(d_bo, 0, C)))
    setup_dma("sync", lambda e: e.dma_start(t_gam[:], bcast_src(d_gam, 0, C)))
    setup_dma("sync", lambda e: e.dma_start(t_bet[:], bcast_src(d_bet, 0, C)))
    for kc in range(NKC):
        setup_dma("sync", lambda e, kc=kc: e.dma_start(
            t_wrt[:, kc * NE:(kc + 1) * NE], d_wrt[kc * 128:(kc + 1) * 128, :]))
    setup_dma("sync", lambda e: e.dma_start(t_eids[:], d_eids))
    setup_dma("sync", lambda e: e.dma_start(t_ones8[:], d_ones8))
    SETUP_TOTAL = n_setup[0]

    n_b1c = [0]
    for j in range(32):
        S("scalar", lambda e, j=j: e.dma_start(
            t_b1c[:, j * 8:(j + 1) * 8],
            bass.AP(d_b1.tensor, j * 128, [[1, 128], [FF, 8]]),
            allow_slow_non_contiguous=True).then_inc(q_b1c, 16))
        n_b1c[0] += 16
    w_b1c = (q_b1c, n_b1c[0])

    n_wsetup = [0]
    for qsem, off in ((q_wk, C), (q_wv, 2 * C), (q_wq, 0)):
        for kc in range(NKC):
            S("gpsimd", lambda e, kc=kc, off=off, qsem=qsem: e.dma_start(
                t_wab[:, kc * 3 * C + off: kc * 3 * C + off + C],
                d_wa[kc * 128:(kc + 1) * 128, off:off + C]).then_inc(qsem, 16))
    w_wk = (q_wk, 8 * 16)
    w_wkv = (q_wv, 8 * 16)
    w_wkvq = (q_wq, 8 * 16)
    for kc in range(NKC):
        S("gpsimd", lambda e, kc=kc: e.dma_start(
            t_wpb[:, kc * C:(kc + 1) * C],
            d_wp[kc * 128:(kc + 1) * 128, :]).then_inc(q_wsetup, 16))
        n_wsetup[0] += 16
    S("gpsimd", lambda e: e.dma_start(d_wob, d_wo).then_inc(q_wob, 16))
    w_setup = (q_setup, SETUP_TOTAL)
    w_gsetup = (q_gsetup, n_gsetup[0])
    w_wsetup = (q_wsetup, n_wsetup[0])

    tok_bqs = S("vector", lambda e: e.tensor_scalar(
        t_bqs[:], t_bqk[:, 0:8], 0.125, None, op0=ALU.mult), waits=[w_setup])

    trb_ev = [None, None]

    tok_cast = {}
    tok_trE = {}
    tok_kev = {}
    tok_vev = {}
    tok_qev = {}
    tok_kmm = {}
    tok_qmm = {}
    tok_v_mm = {}
    tok_scg = {}
    tok_scg_h = {}
    tok_mask = {}
    tok_mask_h = {}
    tok_exp = {}
    tok_red = {}
    tok_recip = {}
    tok_ptpe = {}
    tok_ptev = {}
    tok_av = {}
    tok_aoev = {}
    tok_normao = {}
    tok_aoT_pe = {}
    tok_aoT_ev = {}
    tok_y1_mm = {}
    tok_proj_ev = {}
    tok_logit_ev = {}
    tok_xnew = {}
    tok_trP = {}

    nx_load = [0] * 6
    n_spill = [0]
    avq = {}
    chunk_tiles = [list(range(4 * c2, min(4 * c2 + 4, NTQ))) for c2 in range(5)]

    for ch, tiles in enumerate(chunk_tiles):
        Wc = len(tiles) * 128
        qoff = 256 if ch == 0 else 0

        for ti, tq in enumerate(tiles):
            sl = tq % NSL
            qi = tq % 6
            nx_load[qi] += 16
            xw = (q_x[qi], nx_load[qi])
            dst = stream[:, sl * C:(sl + 1) * C]
            lw = []
            if tq - NSL >= 2:
                tt_old = tq - NSL - 2
                lw.append((q_spill, 16 * n_spill[0]))
                lw.append(tok_logit_ev[tt_old])
            elif tq - NSL >= 0:
                lw.append(tok_cast[tq - NSL])
            S("sync", lambda e, dst=dst, tq=tq, qi=qi: e.dma_start(
                dst, d_x[tq * 128:(tq + 1) * 128, :]).then_inc(q_x[qi], 16), waits=lw)

            cbuf = t_xc[:, (tq % 2) * C:(tq % 2 + 1) * C]
            cw = [xw]
            if tq - 2 >= 0:
                cw.append(tok_trP[tq - 2])
            tok_cast[tq] = S("scalar", lambda e, cbuf=cbuf, dst=dst: e.copy(cbuf, dst),
                             waits=cw)

            def mk_tr(eng, cbuf=cbuf):
                last = None
                for j in range(NKC):
                    last = eng.transpose(p_tra[:, j * 128:(j + 1) * 128],
                                         cbuf[:, j * 128:(j + 1) * 128], t_identb[:])
                return last
            trw = [tok_cast[tq], w_setup]
            if tq >= 1:
                trw.append(tok_trE[tq - 1])
            tok_trP[tq] = S("tensor", mk_tr, waits=trw)

            xdst = (t_xTb4[:].rearrange("p (kc t) -> p kc t", kc=NKC)
                    [:, :, ti * 128:(ti + 1) * 128])
            evw = [tok_trP[tq]]
            if ch >= 1 and ti == 0:
                evw.append(tok_qmm[(ch - 1, NKC - 1)])
            tok_trE[tq] = S("vector", lambda e, xdst=xdst: e.tensor_copy(
                xdst, p_tra[:, 0:C].rearrange("p (kc t) -> p kc t", kc=NKC)),
                waits=evw)

        for fc in range(NKC):
            def mk_k(eng, fc=fc, Wc=Wc):
                last = None
                for kc in range(NKC):
                    last = eng.matmul(
                        p_sc4[fc % 4][:, 0:Wc],
                        t_wab[:, kc * 3 * C + C + fc * 128: kc * 3 * C + C + (fc + 1) * 128],
                        t_xTb4[:, kc * 512:kc * 512 + Wc],
                        start=(kc == 0), stop=(kc == NKC - 1))
                return last
            kw = [w_wk]
            if fc == 0:
                kw += [tok_trE[t2] for t2 in tiles]
                if ch >= 1:
                    lastqb_prev = 3 if ch == 1 else 4 * (ch - 1) + 3
                    kw.append(tok_mask[(lastqb_prev - 2, 3)])
            elif fc == 4:
                kw.append(tok_kev[(ch, 3)])
            tok_kmm[(ch, fc)] = S("tensor", mk_k, waits=kw)
            if fc in (3, 7):
                for fc2 in range(fc - 3, fc + 1):
                    s0 = tiles[0] % NSL
                    kdst = (t_kring[:].rearrange("p (s c) -> p s c", s=NSL)
                            [:, s0:s0 + len(tiles), fc2 * 128:(fc2 + 1) * 128])
                    ksrc = (p_sc4[fc2 % 4][:, 0:Wc]
                            .rearrange("p (t c) -> p t c", c=128))
                    tok_kev[(ch, fc2)] = S("scalar", lambda e, kdst=kdst, ksrc=ksrc,
                                           fc2=fc2:
                                           e.activation(kdst, ksrc, AF.Identity,
                                                        bias=t_bqk[:, 8 + fc2:9 + fc2]),
                                           waits=[tok_kmm[(ch, fc)], w_setup])

        for ti, tq in enumerate(tiles):
            for vh in range(2):
                def mk_v(eng, vh=vh, ti=ti):
                    last = None
                    for kc in range(NKC):
                        last = eng.matmul(
                            p_v[:],
                            t_xTb4[:, kc * 512 + ti * 128: kc * 512 + (ti + 1) * 128],
                            t_wab[:, kc * 3 * C + 2 * C + vh * 512:
                                  kc * 3 * C + 2 * C + (vh + 1) * 512],
                            start=(kc == 0), stop=(kc == NKC - 1))
                    return last
                vw = [tok_trE[tq], w_wkv]
                if vh == 1:
                    vw.append(tok_vev[(tq, 0)])
                elif (tq - 1, 1) in tok_vev:
                    vw.append(tok_vev[(tq - 1, 1)])
                tok_v_mm[(tq, vh)] = S("tensor", mk_v, waits=vw)
                vdst = t_vring[:, (tq % NSL) * C + vh * 512:
                               (tq % NSL) * C + (vh + 1) * 512]
                ew = [tok_v_mm[(tq, vh)], w_setup, w_gsetup]
                if tq - NSL in avq:
                    ew.append(avq[tq - NSL])
                tok_vev[(tq, vh)] = S("vector", lambda e, vdst=vdst, vh=vh:
                                      e.scalar_tensor_tensor(
                                          vdst, p_v[:], 1.0,
                                          t_bvbc[:, vh * 512:(vh + 1) * 512],
                                          op0=ALU.mult, op1=ALU.add), waits=ew)

        for fc in range(NKC):
            def mk_q(eng, fc=fc, Wc=Wc, qoff=qoff):
                last = None
                for kc in range(NKC):
                    last = eng.matmul(
                        p_sc4[fc % 4][:, qoff:Wc],
                        t_wab[:, kc * 3 * C + fc * 128: kc * 3 * C + (fc + 1) * 128],
                        t_xTb4[:, kc * 512 + qoff:kc * 512 + Wc],
                        start=(kc == 0), stop=(kc == NKC - 1))
                return last
            qw = [tok_bqs, w_wkvq]
            if fc == 0:
                qw.append(tok_kev[(ch, NKC - 1)])
            elif fc == 4:
                qw.append(tok_qev[(ch, 3)])
            tok_qmm[(ch, fc)] = S("tensor", mk_q, waits=qw)
            if fc in (3, 7):
                for fc2 in range(fc - 3, fc + 1):
                    qdst = t_q4[:, fc2 * 512 + qoff: fc2 * 512 + Wc]
                    tok_qev[(ch, fc2)] = S("scalar", lambda e, qdst=qdst, fc2=fc2,
                                           Wc=Wc, qoff=qoff:
                                           e.activation(qdst,
                                                        p_sc4[fc2 % 4][:, qoff:Wc],
                                                        AF.Identity, scale=0.125,
                                                        bias=t_bqs[:, fc2:fc2 + 1]),
                                           waits=[tok_qmm[(ch, fc)]])

        qbs = [2, 3] if ch == 0 else list(tiles)
        for tq in qbs:
            tt = tq - 2
            ti = tq - 4 * ch
            mvar = tt if tt < 2 else 2
            for s in range(5):
                if s < 4:
                    G = s
                    g2 = G % 2
                    for hloc in range(4):
                        def mk_sc(eng, G=G, tq=tq, ti=ti, hloc=hloc):
                            last = None
                            h = G * 4 + hloc
                            fc, r0 = h // 2, (h % 2) * 64
                            for kr in range(3):
                                ks = (tq - 2 + kr) % NSL
                                last = eng.matmul(
                                    p_sc4[hloc][:, kr * 128:(kr + 1) * 128],
                                    t_q4[r0:r0 + 64, fc * 512 + ti * 128:
                                         fc * 512 + (ti + 1) * 128],
                                    t_kring[r0:r0 + 64,
                                            ks * C + fc * 128: ks * C + (fc + 1) * 128],
                                    start=True, stop=True)
                            return last
                        sw = []
                        if G == 0 and tq == (2 if ch == 0 else tiles[0]):
                            sw.append(tok_qev[(ch, NKC - 1)])
                        pmk = (tt, G - 1, hloc) if G >= 1 else (tt - 1, 3, hloc)
                        if pmk in tok_mask_h:
                            sw.append(tok_mask_h[pmk])
                        tok_scg_h[(tt, G, hloc)] = S("tensor", mk_sc, waits=sw)
                    tok_scg[(tt, G)] = tok_scg_h[(tt, G, 3)]

                    for hloc in range(4):
                        mwh = [tok_scg_h[(tt, G, hloc)]]
                        if hloc == 0:
                            mwh += [w_setup, w_gsetup]
                            if G >= 1:
                                mwh.append(tok_exp[(tt, G - 1)])
                            elif tt >= 1:
                                mwh.append(tok_exp[(tt - 1, 3)])
                        tok_mask_h[(tt, G, hloc)] = S(
                            "vector", lambda e, hloc=hloc, mvar=mvar:
                            e.tensor_tensor(
                                t_scf[:, hloc * W3:(hloc + 1) * W3],
                                p_sc4[hloc][:, 0:W3],
                                t_masks[:, mvar * W3:(mvar + 1) * W3], op=ALU.add),
                            waits=mwh)
                    tok_mask[(tt, G)] = tok_mask_h[(tt, G, 3)]

                    pbuf = t_prob[:, g2 * 4 * W3:(g2 + 1) * 4 * W3]
                    pk = (tt, G - 2) if G >= 2 else (tt - 1, G + 2)
                    ew = [tok_mask[(tt, G)]]
                    if pk in tok_red:
                        ew += [tok_red[pk], tok_ptpe[pk + (1,)]]
                    tok_exp[(tt, G)] = S("scalar", lambda e, pbuf=pbuf: e.activation(
                        pbuf, t_scf[:], AF.Exp), waits=ew)

                if s >= 1:
                    G = s - 1
                    g2 = G % 2
                    pbuf = t_prob[:, g2 * 4 * W3:(g2 + 1) * 4 * W3]
                    for sub in range(2):
                        def mk_pt(eng, pbuf=pbuf, sub=sub):
                            last = None
                            for hl in range(2):
                                for kr in range(3):
                                    last = eng.transpose(
                                        p_trb[:, (hl * 3 + kr) * 128:
                                              (hl * 3 + kr + 1) * 128],
                                        pbuf[:, (2 * sub + hl) * W3 + kr * 128:
                                             (2 * sub + hl) * W3 + (kr + 1) * 128],
                                        t_identb[:])
                            return last
                        tok_ptpe[(tt, G, sub)] = S("tensor", mk_pt,
                                                   waits=[tok_exp[(tt, G)], trb_ev[1]])
                        ptdst = t_probT[:, g2 * 4 * W3 + sub * 768:
                                        g2 * 4 * W3 + (sub + 1) * 768]
                        ew = [tok_ptpe[(tt, G, sub)]]
                        ak = ((tt, G - 2, 2 * sub + 1) if G >= 2
                              else (tt - 1, G + 2, 2 * sub + 1))
                        if ak in tok_av:
                            ew.append(tok_av[ak])
                        tok_ptev[(tt, G, sub)] = S("vector",
                                                   lambda e, ptdst=ptdst: e.tensor_copy(
                                                       ptdst, p_trb[:, 0:768]), waits=ew)
                        trb_ev[1] = tok_ptev[(tt, G, sub)]

                    for hloc in range(4):
                        h = G * 4 + hloc
                        sub = hloc // 2

                        def mk_av(eng, g2=g2, hloc=hloc, h=h, G=G, tq=tq):
                            last = None
                            for kr in range(3):
                                ks = (tq - 2 + kr) % NSL
                                last = eng.matmul(
                                    p_misc[:, (G % 2) * 256 + hloc * 64:
                                           (G % 2) * 256 + (hloc + 1) * 64],
                                    t_probT[:, g2 * 4 * W3 + hloc * W3 + kr * 128:
                                            g2 * 4 * W3 + hloc * W3 + (kr + 1) * 128],
                                    t_vring[:, ks * C + h * 64: ks * C + (h + 1) * 64],
                                    start=(kr == 0), stop=(kr == 2))
                            return last
                        aw = [tok_ptev[(tt, G, sub)], tok_vev[(tq, 1)]]
                        if G == 0 and hloc == 0 and tt >= 1:
                            aw.append(tok_logit_ev[tt - 1])
                        pk2 = (tt, G - 1) if G >= 1 else (tt - 1, 3)
                        if hloc == 0 and pk2 in tok_aoev:
                            aw.append(tok_aoev[pk2])
                        tok_av[(tt, G, hloc)] = S("tensor", mk_av, waits=aw)

                    tok_aoev[(tt, G)] = S("vector", lambda e, G=G: e.tensor_copy(
                        t_ao[:, G * 256:(G + 1) * 256],
                        p_misc[:, (G % 2) * 256:(G % 2 + 1) * 256]),
                        waits=[tok_av[(tt, G, 3)]])

                    tok_red[(tt, G)] = S("vector", lambda e, G=G, pbuf=pbuf:
                        e.reduce_sum(
                            t_den[:, G * 4:(G + 1) * 4].rearrange("p h -> p h ()"),
                            pbuf.rearrange("p (h j) -> p h j", h=4), axis=AX.X),
                        waits=[tok_exp[(tt, G)]])

            avq[tt] = tok_av[(tt, 3, 3)]

            tok_recip[tt] = S("vector", lambda e: e.reciprocal(t_rden[:], t_den[:]),
                              waits=[tok_red[(tt, 3)]])
            nw = [tok_recip[tt], tok_aoev[(tt, 3)]]
            if tt >= 1:
                nw.append(tok_aoT_pe[tt - 1])
            tok_normao[tt] = S("vector", lambda e: e.tensor_tensor(
                t_aon[:].rearrange("p (h d) -> p h d", h=16),
                t_ao[:].rearrange("p (h d) -> p h d", h=16),
                t_rden[:].rearrange("p h -> p h ()").broadcast_to([128, NH, HD]),
                op=ALU.mult), waits=nw)

            def mk_aoT(eng):
                last = None
                for j in range(NKC):
                    last = eng.transpose(p_trb[:, j * 128:(j + 1) * 128],
                                         t_aon[:, j * 128:(j + 1) * 128], t_identb[:])
                return last
            aw = [tok_normao[tt], trb_ev[1]]
            tok_aoT_pe[tt] = S("tensor", mk_aoT, waits=aw)
            ew = [tok_aoT_pe[tt]]
            if tt >= 1:
                ew.append(tok_y1_mm[(tt - 1, 1)])
            tok_aoT_ev[tt] = S("vector", lambda e: e.tensor_copy(t_aoT[:], p_trb[:, 0:C]),
                               waits=ew)
            trb_ev[1] = tok_aoT_ev[tt]

            for nh2 in range(2):
                def mk_pj(eng, nh2=nh2):
                    last = None
                    for kc in range(NKC):
                        last = eng.matmul(
                            p_misc[:],
                            t_aoT[:, kc * 128:(kc + 1) * 128],
                            t_wpb[:, kc * C + nh2 * 512: kc * C + (nh2 + 1) * 512],
                            start=(kc == 0), stop=(kc == NKC - 1))
                    return last
                pw = [tok_aoT_ev[tt], w_wsetup]
                if nh2 == 1:
                    pw.append(tok_proj_ev[(tt, 0)])
                tok_y1_mm[(tt, nh2)] = S("tensor", mk_pj, waits=pw)
                ew = [tok_y1_mm[(tt, nh2)], w_setup, w_gsetup]
                tok_proj_ev[(tt, nh2)] = S("vector", lambda e, nh2=nh2:
                                           e.scalar_tensor_tensor(
                                               t_w[:, nh2 * 512:(nh2 + 1) * 512], p_misc[:], 1.0,
                                               t_bpbc[:, nh2 * 512:(nh2 + 1) * 512],
                                               op0=ALU.mult, op1=ALU.add), waits=ew)

            tok_res = S("vector", lambda e, tq=tq: e.tensor_tensor(
                t_w[:], t_w[:], stream[:, (tq % NSL) * C:(tq % NSL + 1) * C], op=ALU.add),
                waits=[tok_proj_ev[(tt, 0)], tok_proj_ev[(tt, 1)]])

            tok_ns = S("vector", lambda e: e.reduce_sum(t_stat[:, 0:1], t_w[:], axis=AX.X,
                                                        negate=True), waits=[tok_res])
            tok_nm = S("vector", lambda e: e.tensor_scalar(t_stat[:, 1:2], t_stat[:, 0:1],
                                                           1.0 / C, None, op0=ALU.mult),
                       waits=[tok_ns])
            tok_ct = S("vector", lambda e: e.tensor_scalar(t_cent[:], t_w[:], t_stat[:, 1:2],
                                                           None, op0=ALU.add), waits=[tok_nm])
            tok_sq = S("vector", lambda e: e.tensor_tensor(t_w[:], t_cent[:], t_cent[:],
                                                           op=ALU.mult), waits=[tok_ct])
            tok_vr = S("vector", lambda e: e.reduce_sum(t_stat[:, 2:3], t_w[:], axis=AX.X),
                       waits=[tok_sq])
            tok_v2 = S("vector", lambda e: e.tensor_scalar(t_stat[:, 3:4], t_stat[:, 2:3],
                                                           1.0 / C, EPS1, op0=ALU.mult,
                                                           op1=ALU.add), waits=[tok_vr])
            tok_ln = S("scalar", lambda e: e.activation(t_stat[:, 4:5], t_stat[:, 3:4], AF.Ln),
                       waits=[tok_v2])
            tok_rs = S("scalar", lambda e: e.activation(t_stat[:, 5:6], t_stat[:, 4:5], AF.Exp,
                                                        scale=-0.5), waits=[tok_ln])
            tok_xnew[tt] = S("vector", lambda e, tq=tq: e.tensor_scalar(
                stream[:, (tq % NSL) * C:(tq % NSL + 1) * C], t_cent[:], t_stat[:, 5:6],
                None, op0=ALU.mult), waits=[tok_rs])

            n_spill[0] += 1
            S("sync", lambda e, tt=tt, tq=tq: e.dma_start(
                d_x1[tt * 128:(tt + 1) * 128, :],
                stream[:, (tq % NSL) * C:(tq % NSL + 1) * C]).then_inc(q_spill, 16),
              waits=[tok_xnew[tt]])

            tok_rtr_ev = {}
            for bt in range(2):
                def mk_rtr(eng, bt=bt, tq=tq):
                    last = None
                    for j in range(4):
                        last = eng.transpose(
                            p_misc[:, j * 128:(j + 1) * 128],
                            stream[:, (tq % NSL) * C + (bt * 4 + j) * 128: (tq % NSL) * C + (bt * 4 + j + 1) * 128],
                            t_identf[:])
                    return last
                rw = [tok_xnew[tt]]
                rw.append(tok_proj_ev[(tt, 1)] if bt == 0 else tok_rtr_ev[0])
                tok_rtr_pe = S("tensor", mk_rtr, waits=rw)
                tok_rtr_ev[bt] = S("vector", lambda e, bt=bt: e.tensor_copy(
                    t_xT32[:, bt * 512:(bt + 1) * 512], p_misc[:]), waits=[tok_rtr_pe])

            def mk_rmm(eng):
                last = None
                for kc in range(NKC):
                    last = eng.matmul(p_misc[:, 0:NE],
                                      t_xT32[:, kc * 128:(kc + 1) * 128],
                                      t_wrt[:, kc * NE:(kc + 1) * NE],
                                      start=(kc == 0), stop=(kc == NKC - 1))
                return last
            tok_rmm = S("tensor", mk_rmm, waits=[tok_rtr_ev[0], tok_rtr_ev[1], w_setup])
            tok_logit_ev[tt] = S("vector", lambda e: e.tensor_copy(t_lg[:], p_misc[:, 0:NE]),
                                 waits=[tok_rmm])
            tok_m8 = S("vector", lambda e: e.max(t_mx8[:], t_lg[:]), waits=[tok_logit_ev[tt]])
            tok_mi = S("vector", lambda e: e.max_index(t_idxu[:], t_mx8[:], t_lg[:]),
                       waits=[tok_m8])
            tok_if = S("vector", lambda e: e.tensor_scalar(t_idxf[:], t_idxu[:, 0:1], 0.0,
                                                           None, op0=ALU.add),
                       waits=[tok_mi] + ([(q_idx, 16 * tt)] if tt >= 1 else []))
            S("sync", lambda e, tt=tt: e.dma_start(
                d_idx.rearrange("(a p) -> a p", p=128)[tt].rearrange("(p o) -> p o", o=1),
                t_idxf[:]).then_inc(q_idx, 16), waits=[tok_if])

    S("sync", lambda e: e.nop(), waits=[(q_spill, 16 * NTT), (q_idx, 16 * NTT)])

    a_st.close()
    b_st = ExitStack()
    bsb = lambda name, shape, dt: b_st.enter_context(nc.sbuf_tensor(name, shape, dt))

    t_ring = bsb("ring", [128, 32 * 1024], BF16)
    t_idxbc = bsb("idx_bc", [128, 2048], F32)
    t_rank = bsb("rank", [128, 2048], F32)
    t_dsf = bsb("dsf", [128, 16], F32)
    t_dsi = bsb("dsi", [128, 16], I32)
    t_xrb = bsb("xrb", [128, 2 * C], BF16)
    t_xgl = bsb("xgl", [128, 2 * C], BF16)
    t_xgT = bsb("xgT", [128, 2 * NKC * 512], BF16)
    t_zg = bsb("zg", [128, 32 * 512], BF16)
    t_yst = bsb("yst", [128, 4 * C], F32)
    t_yg = bsb("yg", [128, 2 * C], F32)
    t_xrel2 = bsb("xrel2", [128, 2 * C], F32)
    t_y2 = bsb("y2", [128, 2 * C], F32)
    t_ct2 = bsb("ct2", [128, 2 * C], F32)
    t_st2 = bsb("stat2", [128, 16], F32)

    bpsal = lambda name, shape, dt: b_st.enter_context(nc.psum_tensor(name, shape, dt))
    p_i = [bpsal(f"p_i{j}", [128, 512], F32) for j in range(4)]
    p_y2 = [bpsal(f"p_y2{j}", [128, 512], F32) for j in range(2)]
    p_trB = bpsal("p_trB", [128, 1024], BF16)

    sched.new_block()
    S = sched.op

    nw_slot = [0] * 32
    slot_last_reader = [None] * 32
    load_ctr = [0]
    n_xrel = [0] * 4
    n_xrel2 = [0] * 4
    n_out2 = [0, 0]

    tok_idxbc = S("sync", lambda e: e.dma_start(
        t_idxbc[:], bass.AP(d_idx.tensor, 0, [[0, 128], [1, 2048]])).then_inc(
        q_idxbc, 16))
    w_idxbc = (q_idxbc, 16)

    tok_ebase = S("vector", lambda e: e.tensor_scalar(
        t_ebase[:], t_eids[:], 512.0, -1.0, op0=ALU.mult, op1=ALU.add),
        waits=[w_setup])
    tok_eq8 = S("vector", lambda e: e.tensor_scalar(
        t_rank[0:8, :], t_idxbc[0:8, :], t_eids[:], None, op0=ALU.is_equal),
        waits=[w_idxbc, w_setup])
    cur, oth = 0, 32
    tok_cs = tok_eq8
    for k in [1 << i for i in range(11)]:
        t1 = S("vector", lambda e, cur=cur, oth=oth, k=k: e.tensor_copy(
            t_rank[oth:oth + 8, 0:k], t_rank[cur:cur + 8, 0:k]), waits=[tok_cs])
        tok_cs = S("vector", lambda e, cur=cur, oth=oth, k=k: e.tensor_tensor(
            t_rank[oth:oth + 8, k:2048], t_rank[cur:cur + 8, k:2048],
            t_rank[cur:cur + 8, 0:2048 - k], op=ALU.add), waits=[t1])
        cur, oth = oth, cur
    assert cur == 32
    tok_eq8b = S("vector", lambda e: e.tensor_scalar(
        t_idxbc[0:8, :], t_idxbc[0:8, :], t_eids[:], None,
        op0=ALU.is_equal), waits=[tok_cs])
    tok_tmp = S("vector", lambda e, cur=cur: e.tensor_scalar(
        t_rank[0:8, :], t_rank[cur:cur + 8, :], t_ebase[:], None, op0=ALU.add),
        waits=[tok_cs, tok_ebase])
    tok_w8 = S("vector", lambda e: e.tensor_tensor(
        t_rank[0:8, :], t_rank[0:8, :], t_idxbc[0:8, :],
        op=ALU.mult), waits=[tok_tmp, tok_eq8b])
    w8base = 0
    tok_dsum_ev = {}
    for c4 in range(4):
        bank = p_y2[c4 % 2]
        dw = [tok_w8, w_setup]
        if c4 >= 2:
            dw.append(tok_dsum_ev[c4 - 2])
        tok_dmm = S("tensor", lambda e, bank=bank, c4=c4: e.matmul(
            bank[0:1, 0:512], t_ones8[:],
            t_rank[w8base:w8base + 8, c4 * 512:(c4 + 1) * 512],
            start=True, stop=True), waits=dw)
        tok_dsum_ev[c4] = S("vector", lambda e, bank=bank, c4=c4: e.tensor_copy(
            t_rank[64:65, c4 * 512:(c4 + 1) * 512], bank[0:1, 0:512]),
            waits=[tok_dmm])
    S("sync", lambda e: e.dma_start(d_dst, t_rank[64:65, :]).then_inc(q_dst, 16),
      waits=[tok_dsum_ev[2], tok_dsum_ev[3]])
    S("sync", lambda e: e.dma_start(
        t_dsf[:], bass.AP(d_dst.tensor, 0, [[1, 128], [128, 16]]),
        allow_slow_non_contiguous=True).then_inc(q_dst2, 16),
      waits=[(q_dst, 16)])
    tok_dsi = S("vector", lambda e: e.tensor_copy(t_dsi[:], t_dsf[:]),
                waits=[(q_dst2, 16)])

    tok_z0 = S("vector", lambda e: e.memset(t_xgl[:, 0:1024], 0.0))
    for j in range(32):
        S("sync", lambda e, j=j: e.dma_start(
            d_xg[j * 128:(j + 1) * 128, :], t_xgl[:, 0:1024]).then_inc(q_zg, 16),
          waits=[tok_z0])
    w_zg = (q_zg, 32 * 16)

    pre_slot = {}
    for iq in range(4):
        for kc in range(NKC):
            L = load_ctr[0]
            s = L % 32
            nw_slot[s] += 1
            pre_slot[(iq, kc)] = (s, nw_slot[s])
            S("gpsimd", lambda e, s=s, kc=kc, iq=iq: e.dma_start(
                t_ring[:, s * 1024:(s + 1) * 1024],
                d_w1[0, kc * 128:(kc + 1) * 128,
                     iq * 1024:(iq + 1) * 1024]).then_inc(q_slot[s], 16))
            load_ctr[0] += 1

    nscat = [0, 0]
    tok_scat = {}
    for t2 in range(NTT):
        qi = t2 % 4
        n_xrel[qi] += 1
        lw = [(q_spill, 16 * NTT)]
        if t2 - 2 >= 0:
            lw.append((q_xg2[t2 % 2], nscat[t2 % 2]))
        S("gpsimd", lambda e, t2=t2, qi=qi: e.dma_start(
            t_xrb[:, (t2 % 2) * C:(t2 % 2 + 1) * C],
            d_x1[t2 * 128:(t2 + 1) * 128, :]).then_inc(q_xrel[qi], 16), waits=lw)
        xrw = (q_xrel[qi], 16 * n_xrel[qi])
        nscat[t2 % 2] += 16
        tok_scat[t2] = S("gpsimd", lambda e, t2=t2: e.indirect_dma_start(
            d_xg, IndirectOffsetOnAxis(ap=t_dsi[:, t2:t2 + 1], axis=0),
            t_xrb[:, (t2 % 2) * C:(t2 % 2 + 1) * C], None,
            bounds_check=NE * 512 - 1, oob_is_err=False).then_inc(
            q_xg2[t2 % 2], 16), waits=[xrw, tok_dsi, w_zg])
    w_scat = [(q_xg2[0], nscat[0]), (q_xg2[1], nscat[1])]

    nxgl = [0, 0]
    tok_xgtr = {}
    tok_xgev = {}
    tok_hmm = {}
    tok_gelu = {}
    tok_womm = {}
    tok_yev = {}
    nyg = [0] * 4
    tok_hmm_last = {}
    tok_womm_last = {}
    tok_gelu_last = {}

    for eix in range(NE):
        for j in range(4):
            gb2 = (eix * 4 + j) % 2
            lw = list(w_scat) if (eix == 0 and j == 0) else []
            prev = (eix, j - 2) if j >= 2 else (eix - 1, j + 2)
            if prev in tok_xgtr:
                lw.append(tok_xgtr[prev])
            nxgl[gb2] += 16
            S("sync", lambda e, eix=eix, j=j, gb2=gb2: e.dma_start(
                t_xgl[:, gb2 * C:(gb2 + 1) * C],
                d_xg[eix * 512 + j * 128: eix * 512 + (j + 1) * 128, :]).then_inc(
                q_xgl[gb2], 16), waits=lw)
            xglw = (q_xgl[gb2], nxgl[gb2])

            def mk_xt(eng, gb2=gb2):
                last = None
                for kc in range(NKC):
                    last = eng.transpose(
                        p_trB[:, kc * 128:(kc + 1) * 128],
                        t_xgl[:, gb2 * C + kc * 128: gb2 * C + (kc + 1) * 128],
                        t_identb[:])
                return last
            tw = [xglw, w_setup]
            pj = (eix, j - 1) if j >= 1 else (eix - 1, 3)
            if pj in tok_xgev:
                tw.append(tok_xgev[pj])
            tok_xgtr[(eix, j)] = S("tensor", mk_xt, waits=tw)
            xdst = (t_xgT[:, (eix % 2) * NKC * 512:(eix % 2 + 1) * NKC * 512]
                    .rearrange("p (kc t) -> p kc t", kc=NKC)
                    [:, :, j * 128:(j + 1) * 128])
            ew = [tok_xgtr[(eix, j)]]
            if (eix - 2) in tok_hmm_last and j == 0:
                ew.append(tok_hmm_last[eix - 2])
            tok_xgev[(eix, j)] = S("vector", lambda e, xdst=xdst: e.tensor_copy(
                xdst, p_trB[:, 0:C].rearrange("p (kc t) -> p kc t", kc=NKC)),
                waits=ew)

        for iq in range(4):
            if eix == 0:
                slot_of_kc = {kc: pre_slot[(iq, kc)] for kc in range(NKC)}
            else:
                slot_of_kc = {}
                for kc in range(NKC):
                    L = load_ctr[0]
                    s = L % 32
                    nw_slot[s] += 1
                    slot_of_kc[kc] = (s, nw_slot[s])
                    lw = ([slot_last_reader[s]]
                          if slot_last_reader[s] is not None else [])
                    S("gpsimd", lambda e, s=s, eix=eix, kc=kc, iq=iq: e.dma_start(
                        t_ring[:, s * 1024:(s + 1) * 1024],
                        d_w1[eix, kc * 128:(kc + 1) * 128,
                             iq * 1024:(iq + 1) * 1024]).then_inc(q_slot[s], 16),
                      waits=lw)
                    load_ctr[0] += 1
            for icl in range(8):
                icg = iq * 8 + icl
                bank = p_i[icg % 4]

                def mk_im(eng, bank=bank, icl=icl, eix=eix,
                          slot_of_kc=dict(slot_of_kc)):
                    last = None
                    for kc in range(NKC):
                        s2, _ = slot_of_kc[kc]
                        last = eng.matmul(
                            bank[:],
                            t_ring[:, s2 * 1024 + icl * 128: s2 * 1024 + (icl + 1) * 128],
                            t_xgT[:, (eix % 2) * NKC * 512 + kc * 512:
                                  (eix % 2) * NKC * 512 + (kc + 1) * 512],
                            start=(kc == 0), stop=(kc == NKC - 1))
                    return last
                mw = [tok_xgev[(eix, 3)]]
                mw += [(q_slot[s2], 16 * n2) for (s2, n2) in slot_of_kc.values()]
                if (eix, icg - 4) in tok_gelu:
                    mw.append(tok_gelu[(eix, icg - 4)])
                elif eix >= 1 and icg < 4:
                    mw.append(tok_gelu[(eix - 1, 28 + icg)])
                tok_hmm[(eix, icg)] = S("tensor", mk_im, waits=mw)

                gw = [tok_hmm[(eix, icg)], w_b1c]
                if icg == 0 and (eix - 1) in tok_womm_last:
                    gw.append(tok_womm_last[eix - 1])
                tok_gelu[(eix, icg)] = S("scalar", lambda e, bank=bank, eix=eix,
                                         icg=icg:
                                         e.activation(
                                             t_zg[:, icg * 512:(icg + 1) * 512],
                                             bank[:], AF.Gelu,
                                             bias=t_b1c[:, icg * 8 + eix:
                                                        icg * 8 + eix + 1]),
                                         waits=gw)
            for kc in range(NKC):
                s2, _ = slot_of_kc[kc]
                slot_last_reader[s2] = tok_hmm[(eix, iq * 8 + 7)]
        tok_hmm_last[eix] = tok_hmm[(eix, 31)]
        tok_gelu_last[eix] = tok_gelu[(eix, 31)]

        wo_slot = {}
        for ic in range(32):
            L = load_ctr[0]
            s = L % 32
            nw_slot[s] += 1
            wo_slot[ic] = (s, nw_slot[s])
            lw = [slot_last_reader[s]] if slot_last_reader[s] is not None else []
            if ic == 0:
                lw.append((q_wob, 16))
            S("gpsimd", lambda e, s=s, ic=ic: e.dma_start(
                t_ring[:, s * 1024:(s + 1) * 1024],
                d_wob[ic * 128:(ic + 1) * 128, :]).then_inc(q_slot[s], 16),
              waits=lw)
            load_ctr[0] += 1

        for j in range(4):
            for ch2 in range(2):
                def mk_wo(eng, j=j, ch2=ch2, wo_slot=dict(wo_slot)):
                    last = None
                    for ic in range(32):
                        s2, _ = wo_slot[ic]
                        last = eng.matmul(
                            p_y2[ch2][:],
                            t_zg[:, ic * 512 + j * 128: ic * 512 + (j + 1) * 128],
                            t_ring[:, s2 * 1024 + ch2 * 512: s2 * 1024 + (ch2 + 1) * 512],
                            start=(ic == 0), stop=(ic == 31))
                    return last
                ww = [tok_gelu_last[eix]]
                ww += [(q_slot[s2], 16 * n2) for (s2, n2) in wo_slot.values()]
                if j >= 1:
                    ww.append(tok_yev[(eix, j - 1, ch2)])
                elif eix >= 1:
                    ww.append(tok_yev[(eix - 1, 3, ch2)])
                tok_womm[(eix, j, ch2)] = S("tensor", mk_wo, waits=ww)
                ew = [tok_womm[(eix, j, ch2)]]
                if eix >= 1:
                    ew.append((q_ygs[j], nyg[j]))
                tok_yev[(eix, j, ch2)] = S("vector", lambda e, j=j, ch2=ch2:
                                           e.tensor_copy(
                                               t_yst[:, j * C + ch2 * 512:
                                                     j * C + ch2 * 512 + 512],
                                               p_y2[ch2][:]), waits=ew)
        for ic in range(32):
            s2, _ = wo_slot[ic]
            slot_last_reader[s2] = tok_womm[(eix, 3, 1)]
        tok_womm_last[eix] = tok_womm[(eix, 3, 1)]
        for j in range(4):
            nyg[j] += 16
            S("sync", lambda e, eix=eix, j=j: e.dma_start(
                d_yg[eix * 512 + j * 128: eix * 512 + (j + 1) * 128, :],
                t_yst[:, j * C:(j + 1) * C]).then_inc(q_ygs[j], 16),
              waits=[tok_yev[(eix, j, 0)], tok_yev[(eix, j, 1)]])

    w_yall = [(q_ygs[j], nyg[j]) for j in range(4)]

    nygr = [0]
    tok_pair_res = None
    tok_pair_out = None
    for pr in range(8):
        pair = (2 * pr, 2 * pr + 1)
        gw = [tok_dsi]
        if pr == 0:
            gw += list(w_yall)
        if tok_pair_res is not None:
            gw.append(tok_pair_res)
        for half, tt2 in enumerate(pair):
            nygr[0] += 16
            S("gpsimd", lambda e, tt2=tt2, half=half: e.indirect_dma_start(
                t_yg[:, half * C:(half + 1) * C], None,
                d_yg, IndirectOffsetOnAxis(ap=t_dsi[:, tt2:tt2 + 1], axis=0),
                bounds_check=NE * 512 - 1, oob_is_err=False).then_inc(
                q_ygr[0], 16), waits=(gw if half == 0 else []))
        ygw = (q_ygr[0], nygr[0])
        rw = [(q_spill, 16 * NTT)]
        if tok_pair_res is not None:
            rw.append(tok_pair_res)
        for half, tt2 in enumerate(pair):
            n_xrel2[half] += 1
            S("sync", lambda e, tt2=tt2, half=half: e.dma_start(
                t_xrel2[:, half * C:(half + 1) * C],
                d_x1[tt2 * 128:(tt2 + 1) * 128, :]).then_inc(
                q_xrel2[half], 16), waits=(rw if half == 0 else []))
        xr2w = [(q_xrel2[0], 16 * n_xrel2[0]), (q_xrel2[1], 16 * n_xrel2[1])]

        Y, XR, CT, YG = t_y2, t_xrel2, t_ct2, t_yg
        sc = t_st2
        Y3 = Y[:].rearrange("p (h c) -> p h c", h=2)
        XR3 = XR[:].rearrange("p (h c) -> p h c", h=2)
        CT3 = CT[:].rearrange("p (h c) -> p h c", h=2)
        YG3 = YG[:].rearrange("p (h c) -> p h c", h=2)
        bo2 = t_bo[:].rearrange("p c -> p () c").broadcast_to([128, 2, C])
        gam2 = t_gam[:].rearrange("p c -> p () c").broadcast_to([128, 2, C])
        bet2 = t_bet[:].rearrange("p c -> p () c").broadcast_to([128, 2, C])

        r0w = [ygw, w_setup] + xr2w
        if tok_pair_out is not None:
            r0w += tok_pair_out
        t_a = S("vector", lambda e: e.tensor_tensor(Y3, YG3, bo2, op=ALU.add),
                waits=r0w)
        t_r1 = S("vector", lambda e: e.tensor_tensor(Y[:], Y[:], XR[:],
                                                     op=ALU.add), waits=[t_a])

        def ln_pair(first_w, eps, affine, outp):
            tns = S("vector", lambda e: e.reduce_sum(
                sc[:, 0:2].rearrange("p h -> p h ()"), Y3, axis=AX.X,
                negate=True), waits=first_w)
            tnm = S("vector", lambda e: e.tensor_scalar(
                sc[:, 2:4], sc[:, 0:2], 1.0 / C, None, op0=ALU.mult), waits=[tns])
            mb = sc[:, 2:4].rearrange("p h -> p h ()").broadcast_to([128, 2, C])
            tct = S("vector", lambda e, mb=mb: e.tensor_tensor(CT3, Y3, mb,
                                                               op=ALU.add),
                    waits=[tnm])
            tsq = S("vector", lambda e: e.tensor_tensor(Y[:], CT[:], CT[:],
                                                        op=ALU.mult), waits=[tct])
            tvr = S("vector", lambda e: e.reduce_sum(
                sc[:, 4:6].rearrange("p h -> p h ()"), Y3, axis=AX.X),
                waits=[tsq])
            tv2 = S("vector", lambda e, eps=eps: e.tensor_scalar(
                sc[:, 6:8], sc[:, 4:6], 1.0 / C, eps, op0=ALU.mult,
                op1=ALU.add), waits=[tvr])
            tln = S("scalar", lambda e: e.activation(sc[:, 8:10], sc[:, 6:8],
                                                     AF.Ln), waits=[tv2])
            trs = S("scalar", lambda e: e.activation(sc[:, 10:12], sc[:, 8:10],
                                                     AF.Exp, scale=-0.5),
                    waits=[tln])
            rb = sc[:, 10:12].rearrange("p h -> p h ()").broadcast_to([128, 2, C])
            if affine:
                tz = S("vector", lambda e, rb=rb: e.tensor_tensor(
                    Y3, CT3, rb, op=ALU.mult), waits=[trs])
                tz = S("vector", lambda e: e.tensor_tensor(Y3, Y3, gam2,
                                                           op=ALU.mult), waits=[tz])
                tz = S("vector", lambda e: e.tensor_tensor(Y3, Y3, bet2,
                                                           op=ALU.add), waits=[tz])
            else:
                tz = S("vector", lambda e, rb=rb, outp=outp: e.tensor_tensor(
                    outp, CT3, rb, op=ALU.mult), waits=[trs])
            return tz

        t_z = ln_pair([t_r1], EPS2, True, None)
        t_r2 = S("vector", lambda e: e.tensor_tensor(Y[:], XR[:], Y[:],
                                                     op=ALU.add), waits=[t_z])
        tok_pair_res = t_r2
        t_o = ln_pair([t_r2], EPS1, False, CT3)

        tok_pair_out = []
        for half, tt2 in enumerate(pair):
            n_out2[half] += 16
            S("sync", lambda e, tt2=tt2, half=half: e.dma_start(
                d_out[tt2 * 128:(tt2 + 1) * 128, :],
                t_ct2[:, half * C:(half + 1) * C]).then_inc(q_out2[half], 16),
              waits=[t_o])
            tok_pair_out.append((q_out2[half], n_out2[half]))

    S("sync", lambda e: e.nop(), waits=[(q_out2[0], n_out2[0]), (q_out2[1], n_out2[1])])

    sched.finalize()
    for blk_ops in sched.blocks:
        with nc.Block() as block:
            for ename, deco in (("sync", block.sync), ("gpsimd", block.gpsimd),
                                ("scalar", block.scalar), ("vector", block.vector),
                                ("tensor", block.tensor)):
                def body(engine, ename=ename, blk_ops=blk_ops):
                    sched.emit_block(blk_ops, ename, engine, eng_sems)
                deco(body)

    b_st.close()
    st.close()
    return nc


_NC_CACHE = {}


def _get_nc():
    if "nc" not in _NC_CACHE:
        _NC_CACHE["nc"] = build_program()
    return _NC_CACHE["nc"]


def _build_masks(seq0):
    p = np.arange(128)[:, None]
    f = np.arange(W3)[None, :]
    out = np.zeros((3, 128, W3), dtype=np.float32)
    for v in range(3):
        tt = v if v < 2 else 2
        allowed = (f >= p) & (f <= p + 256)
        if v < 2:
            kseq = seq0 - 256 + tt * 128 + f
            allowed = allowed & (kseq >= 0)
        out[v] = np.where(allowed, 0.0, NEG)
    return out


def kernel(L0_attn_x, L0_attn_W_attn, L0_attn_b_attn, L0_attn_W_proj, L0_attn_b_proj,
           L0_ffn_Wr, L0_ffn_W1, L0_ffn_b1, L0_ffn_Wo, L0_ffn_bo, L0_ffn_gamma,
           L0_ffn_beta):
    x = np.ascontiguousarray(np.asarray(L0_attn_x, dtype=np.float32).reshape(B * T, C))
    wa = np.ascontiguousarray(np.asarray(L0_attn_W_attn, dtype=np.float32))
    ba = np.ascontiguousarray(np.asarray(L0_attn_b_attn, dtype=np.float32))
    wp = np.ascontiguousarray(np.asarray(L0_attn_W_proj, dtype=np.float32))
    bp = np.ascontiguousarray(np.asarray(L0_attn_b_proj, dtype=np.float32))
    wrt = np.ascontiguousarray(np.asarray(L0_ffn_Wr, dtype=np.float32).T)
    w1 = np.ascontiguousarray(np.asarray(L0_ffn_W1, dtype=np.float32))
    b1 = np.ascontiguousarray(np.asarray(L0_ffn_b1, dtype=np.float32))
    wo = np.ascontiguousarray(np.asarray(L0_ffn_Wo, dtype=np.float32))
    bo = np.ascontiguousarray(np.asarray(L0_ffn_bo, dtype=np.float32))
    gam = np.ascontiguousarray(np.asarray(L0_ffn_gamma, dtype=np.float32))
    bet = np.ascontiguousarray(np.asarray(L0_ffn_beta, dtype=np.float32))

    in_maps = []
    for c in range(NCORES):
        base = c * TCORE
        seq0 = (c % 4) * TCORE
        xe = np.zeros((NTQ * 128, C), dtype=np.float32)
        if seq0 == 0:
            xe[HALO:] = x[base:base + TCORE]
        else:
            xe[:] = x[base - HALO:base + TCORE]
        in_maps.append({
            "x_ext": xe, "w_attn": wa, "b_attn": ba, "w_proj": wp, "b_proj": bp,
            "wr_t": wrt, "w1": w1, "b1": b1, "wo": wo, "bo": bo, "gamma": gam,
            "beta": bet, "masks": _build_masks(seq0),
            "eids": np.arange(NE, dtype=np.float32).reshape(NE, 1),
            "ones8": np.ones((NE, 1), dtype=np.float32),
        })

    nc = _get_nc()
    res = run_bass_kernel_spmd(nc, in_maps, core_ids=list(range(NCORES)))
    out = np.concatenate([r["out"] for r in res.results], axis=0)
    return out.reshape(B, T, C)
